# version 02
# baseline (speedup 1.0000x reference)
import numpy as np
import ml_dtypes
import concourse.bass as bass
import concourse.mybir as mybir
from concourse.bass_utils import run_bass_kernel_spmd
from contextlib import ExitStack

F32 = mybir.dt.float32
BF16 = mybir.dt.bfloat16
AF = mybir.ActivationFunctionType
ALU = mybir.AluOpType
AX = mybir.AxisListType
bfnp = ml_dtypes.bfloat16

MAGIC = float(np.float32(1.5 * 2 ** 23))
T, C, NH, DK, NSEQ, INNER = 2048, 1024, 16, 64, 2, 4096
NTB = T // 128
NQC = 4
NCC = C // 128
NPAIR = NH // 2
NT2 = NSEQ * NTB

DEBUG_OUT = False
UPTO = 99
A4_PAIRS = 8
A4_QCS = 4


class Sem:
    def __init__(self, es, nc, name):
        self.h = es.enter_context(nc.semaphore(name))
        self.n = 0

    def inc(self, instr, k=1):
        instr.then_inc(self.h, k)
        self.n += k
        return instr

    def dma(self, instr):
        return self.inc(instr, 16)


def build_program():
    nc = bass.Bass("TRN2", target_bir_lowering=False)

    xin = nc.dram_tensor("xin", [NSEQ * T, C], F32, kind="ExternalInput").ap()
    wqkv = nc.dram_tensor("wqkv", [C, 3 * C], F32, kind="ExternalInput").ap()
    wout = nc.dram_tensor("wout", [C, C], F32, kind="ExternalInput").ap()
    wg = nc.dram_tensor("wg", [INNER, C], F32, kind="ExternalInput").ap()
    wu = nc.dram_tensor("wu", [INNER, C], F32, kind="ExternalInput").ap()
    wd = nc.dram_tensor("wd", [C, INNER], F32, kind="ExternalInput").ap()
    c_idf = nc.dram_tensor("c_idf", [128, 128], F32, kind="ExternalInput").ap()
    c_idb = nc.dram_tensor("c_idb", [128, 128], BF16, kind="ExternalInput").ap()
    c_tri = nc.dram_tensor("c_tri", [128, 128], BF16, kind="ExternalInput").ap()
    c_hmap = nc.dram_tensor("c_hmap", [128, 128], BF16, kind="ExternalInput").ap()
    c_ones = nc.dram_tensor("c_ones", [128, 128], F32, kind="ExternalInput").ap()
    c_id16 = nc.dram_tensor("c_id16", [16, 16], F32, kind="ExternalInput").ap()
    y = nc.dram_tensor("y", [NSEQ * T, C], F32, kind="ExternalOutput").ap()

    KIND = "ExternalOutput" if DEBUG_OUT else "Internal"
    xn_hbm = nc.dram_tensor("xn_hbm", [NSEQ * T, C], F32, kind=KIND).ap()
    hq_hbm = nc.dram_tensor("hq_hbm", [NSEQ * T, INNER], BF16, kind=KIND).ap()
    wqkv_bh = nc.dram_tensor("wqkv_bh", [C, 3 * C], BF16, kind="Internal").ap()
    wout_bh = nc.dram_tensor("wout_bh", [C, C], BF16, kind="Internal").ap()
    wgT_h = nc.dram_tensor("wgT_h", [C, INNER], BF16, kind=KIND).ap()
    wuT_h = nc.dram_tensor("wuT_h", [C, INNER], BF16, kind=KIND).ap()
    wdT_h = nc.dram_tensor("wdT_h", [INNER, C], BF16, kind=KIND).ap()

    es = ExitStack()

    s_idf = es.enter_context(nc.sbuf_tensor("s_idf", [128, 128], F32))
    s_idb = es.enter_context(nc.sbuf_tensor("s_idb", [128, 128], BF16))
    s_tri = es.enter_context(nc.sbuf_tensor("s_tri", [128, 128], BF16))
    s_hmap = es.enter_context(nc.sbuf_tensor("s_hmap", [128, 128], BF16))
    s_ones = es.enter_context(nc.sbuf_tensor("s_ones", [128, 128], F32))
    s_onesc = es.enter_context(nc.sbuf_tensor("s_onesc", [128, 1], F32))
    s_id16 = es.enter_context(nc.sbuf_tensor("s_id16", [16, 16], F32))
    s_eps = es.enter_context(nc.sbuf_tensor("s_eps", [128, 1], F32))
    s_magic = es.enter_context(nc.sbuf_tensor("s_magic", [128, 1], F32))
    wsc = es.enter_context(nc.sbuf_tensor("wsc", [128, 8], F32))
    dsh_all = es.enter_context(nc.sbuf_tensor("dsh_all", [128, 64], F32))
    w11 = es.enter_context(nc.sbuf_tensor("w11", [1, 4], F32))

    SD = {}
    for name in ("dX dW dS sTx sEv sAc sQk sS sEx sMk sPV sRc sBc sNm "
                 "sWo sGp sF sFd sFa sOut").split():
        SD[name] = Sem(es, nc, name)
    dXb = [Sem(es, nc, "dXb%d" % i) for i in range(2)]
    dWq = [Sem(es, nc, "dWq%d" % i) for i in range(2)]
    dWt = [Sem(es, nc, "dWt%d" % i) for i in range(2)]
    dXn = [Sem(es, nc, "dXn%d" % i) for i in range(2)]
    dHq = [Sem(es, nc, "dHq%d" % i) for i in range(2)]
    dSc = [Sem(es, nc, "dSc%d" % i) for i in range(2)]
    dSg = [Sem(es, nc, "dSg%d" % i) for i in range(2)]
    dSx = [Sem(es, nc, "dSx%d" % i) for i in range(2)]
    dSt2 = [Sem(es, nc, "dSt2_%d" % i) for i in range(2)]
    dSh = Sem(es, nc, "dSh")
    dX, dW, dS = SD["dX"], SD["dW"], SD["dS"]
    sTx, sEv, sAc, sQk = SD["sTx"], SD["sEv"], SD["sAc"], SD["sQk"]
    sS_, sEx, sMk, sPV = SD["sS"], SD["sEx"], SD["sMk"], SD["sPV"]
    sRc, sBc, sNm = SD["sRc"], SD["sBc"], SD["sNm"]
    sWo, sGp = SD["sWo"], SD["sGp"]
    sF, sFd, sFa, sOut = SD["sF"], SD["sFd"], SD["sFa"], SD["sOut"]
    ALL = (list(SD.values()) + dXb + dWq + dWt + dXn + dHq + dSc + dSg + dSx
           + dSt2 + [dSh])

    block = es.enter_context(nc.Block())

    def barrier():
        snaps = [(s.h, s.n) for s in ALL if s.n > 0]
        for attr in ("sync", "tensor", "vector", "scalar", "gpsimd"):
            def f(eng, snaps=snaps):
                for h, v in snaps:
                    eng.wait_ge(h, v)
            getattr(block, attr)(f)

    esA = ExitStack()

    def sbA(name, shape, dt):
        return esA.enter_context(nc.sbuf_tensor(name, shape, dt))

    xbuf = sbA("xbuf", [128, 2 * C], F32)
    wst = sbA("wst", [128, 2 * 1024], F32)
    wcb = sbA("wcb", [128, 2 * 1024], BF16)
    ovl = sbA("ovl", [128, 8192], BF16)
    wqs = ovl.ap()[:, 0:4096]
    wout_b = ovl.ap()[:, 0:8192]
    xfm = sbA("xfm", [128, 8 * T], BF16)
    attnO = xfm
    qfm = sbA("qfm", [128, 8 * T], BF16)
    kfm = sbA("kfm", [128, 8 * T], BF16)
    vsb = sbA("vsb", [128, NTB * NH * 65], BF16)
    pt = ovl.ap()[:, 0:2048]
    qsq = ovl.ap()[:, 0:2048]
    rcp_t = sbA("rcp_t", [128, 1024], F32)
    rcpa = rcp_t.ap()[64:65, 0:512]
    rcpb = rcp_t.ap()[64:65, 512:1024]
    bcsb = sbA("bcsb", [64, 1024], F32)
    stg = sbA("stg", [64, 2 * 512], BF16)
    biasq = sbA("biasq", [16, 4], F32)
    nrow = sbA("nrow", [1, 16], F32)
    biasP = sbA("biasP", [128, 24], F32)
    r1 = sbA("r1", [128, 2 * C], F32)
    stat = sbA("stat", [128, 16], F32)
    sqs = r1.ap()[:, 0:C]
    xnb = sbA("xnb", [128, 2 * C], F32)
    absacc = sbA("absacc", [128, 33], F32)
    tern = wcb.ap()[:, 0:2048]
    terT = ovl.ap()[:, 4096:6144]

    def ph0():
        @block.sync
        def _(sy):
            dW.dma(sy.dma_start(out=s_idf[:], in_=c_idf))
            dW.dma(sy.dma_start(out=s_idb[:], in_=c_idb))
            dW.dma(sy.dma_start(out=s_tri[:], in_=c_tri))
            dW.dma(sy.dma_start(out=s_hmap[:], in_=c_hmap))
            dW.dma(sy.dma_start(out=s_ones[:], in_=c_ones))
            dW.dma(sy.dma_start(out=s_id16[:], in_=c_id16))
        cw = dW.n

        @block.vector
        def _(v):
            sEv.inc(v.memset(s_eps[:], 1e-5))
            sEv.inc(v.memset(s_magic[:], MAGIC))
            sEv.inc(v.memset(s_onesc[:], 1.0))
            sEv.inc(v.memset(
                vsb.ap().rearrange("p (g c) -> p g c", c=65)[:, :, 64:65], 1.0))

        jobs = [("qkv", t, h) for t in range(8) for h in range(3)]
        jobs += [("out", t, 0) for t in range(8)]
        NJ = len(jobs)
        ld_v = {}
        ev0 = sEv.n
        ev_v = [ev0 + i + 1 for i in range(NJ)]
        st_v = {}

        @block.sync
        def _(sy):
            for i, (tag, t, h) in enumerate(jobs):
                if i >= 2:
                    sy.wait_ge(sEv.h, ev_v[i - 2])
                b = wst[:, (i % 2) * 1024:(i % 2) * 1024 + 1024]
                if tag == "qkv":
                    src = wqkv[t * 128:(t + 1) * 128, h * 1024:(h + 1) * 1024]
                else:
                    src = wout[t * 128:(t + 1) * 128, :]
                dWt[i % 2].dma(sy.dma_start(out=b, in_=src))
                ld_v[i] = dWt[i % 2].n

        @block.vector
        def _(v):
            for i, (tag, t, h) in enumerate(jobs):
                v.wait_ge(dWt[i % 2].h, ld_v[i])
                b = wst[:, (i % 2) * 1024:(i % 2) * 1024 + 1024]
                if i >= 2 and (i - 2) in st_v:
                    v.wait_ge(dSc[i % 2].h, st_v[i - 2])
                dst = wcb[:, (i % 2) * 1024:(i % 2) * 1024 + 1024]
                sEv.inc(v.tensor_copy(dst, b))

        @block.gpsimd
        def _(g):
            for i, (tag, t, h) in enumerate(jobs):
                g.wait_ge(sEv.h, ev_v[i])
                src = wcb[:, (i % 2) * 1024:(i % 2) * 1024 + 1024]
                if tag == "qkv":
                    dst = wqkv_bh[t * 128:(t + 1) * 128, h * 1024:(h + 1) * 1024]
                else:
                    dst = wout_bh[t * 128:(t + 1) * 128, :]
                dSc[i % 2].dma(g.dma_start(out=dst, in_=src))
                st_v[i] = dSc[i % 2].n

    def a1(s):
        psc = ExitStack()
        ptx = psc.enter_context(nc.psum_tensor("ptx%d" % s, [128, 1024], F32))
        tx0 = sTx.n
        tx_v = [tx0 + i + 1 for i in range(NTB)]
        ev0 = sEv.n
        ev_v = [ev0 + i + 1 for i in range(NTB)]
        ld_v = {}

        @block.sync
        def _(sy):
            for tb in range(NTB):
                if tb >= 2:
                    sy.wait_ge(sTx.h, tx_v[tb - 2])
                dXb[tb % 2].dma(sy.dma_start(
                    out=xbuf[:, (tb % 2) * C:(tb % 2) * C + C],
                    in_=xin[s * T + tb * 128: s * T + (tb + 1) * 128, :]))
                ld_v[tb] = dXb[tb % 2].n

        @block.tensor
        def _(te):
            for tb in range(NTB):
                te.wait_ge(dXb[tb % 2].h, ld_v[tb])
                if tb >= 1:
                    te.wait_ge(sEv.h, ev_v[tb - 1])
                for cc in range(NCC):
                    ins = te.transpose(
                        ptx[:, cc * 128:(cc + 1) * 128],
                        xbuf[:, (tb % 2) * C + cc * 128:(tb % 2) * C + (cc + 1) * 128],
                        s_idf[:])
                    if cc == 7:
                        sTx.inc(ins)

        @block.vector
        def _(v):
            for tb in range(NTB):
                v.wait_ge(sTx.h, tx_v[tb])
                sEv.inc(v.tensor_copy(
                    xfm.ap().rearrange("p (g t) -> p g t", t=T)[:, :, tb * 128:(tb + 1) * 128],
                    ptx.ap().rearrange("p (g c) -> p g c", c=128)))
        assert sTx.n == tx_v[-1] and sEv.n == ev_v[-1]
        psc.close()

    def a2(s):
        psc = ExitStack()
        pqk = [psc.enter_context(nc.psum_tensor("pqk%d_%d" % (s, i), [128, 512], F32))
               for i in range(2)]
        pv = [psc.enter_context(nc.psum_tensor("pv%d_%d" % (s, i), [128, 512], F32))
              for i in range(2)]

        qk0 = sQk.n
        qk_v = {}
        n = qk0
        for of in range(16):
            for tc in range(4):
                n += 1
                qk_v[(of, tc)] = n
        v_v = {}
        for nv in range(2):
            for tb in range(NTB):
                n += 1
                v_v[(nv, tb)] = n
        ac0 = sAc.n
        ac_v = {}
        n = ac0
        for of in range(16):
            for tc in range(4):
                n += 1
                ac_v[(of, tc)] = n
        ev0 = sEv.n
        evv_v = {}
        n = ev0
        for nv in range(2):
            for tb in range(NTB):
                n += 1
                evv_v[(nv, tb)] = n
        wl_v = {}

        @block.sync
        def _(sy):
            for of in range(16):
                if of >= 2:
                    sy.wait_ge(sQk.h, qk_v[(of - 2, 3)])
                dWq[of % 2].dma(sy.dma_start(
                    out=wqs.rearrange("p (b f) -> p b f", b=2)[:, of % 2, 0:1024],
                    in_=wqkv_bh.rearrange("(g p) f -> p g f", p=128)[:, :, of * 128:(of + 1) * 128]))
                wl_v[of] = dWq[of % 2].n
            for nv in range(2):
                if nv == 0:
                    sy.wait_ge(sQk.h, qk_v[(14, 3)])
                    sy.wait_ge(sQk.h, qk_v[(15, 3)])
                else:
                    sy.wait_ge(sQk.h, v_v[(0, NTB - 1)])
                for hf in range(2):
                    dWq[hf].dma(sy.dma_start(
                        out=wqs.rearrange("p (b f) -> p b f", b=2)[:, hf, :],
                        in_=wqkv_bh.rearrange("(g p) f -> p g f", p=128)[:, hf * 4:(hf + 1) * 4, 2048 + nv * 512:2048 + (nv + 1) * 512]))
                wl_v[16 + nv] = (dWq[0].n, dWq[1].n)

        @block.tensor
        def _(te):
            for of in range(16):
                te.wait_ge(dWq[of % 2].h, wl_v[of])
                for tc in range(4):
                    m = of * 4 + tc
                    if m >= 2:
                        of2, tc2 = divmod(m - 2, 4)
                        te.wait_ge(sAc.h, ac_v[(of2, tc2)])
                    for cc in range(NCC):
                        ins = te.matmul(
                            pqk[m % 2][:],
                            wqs[:, (of % 2) * 2048 + cc * 128:(of % 2) * 2048 + (cc + 1) * 128],
                            xfm[:, cc * T + tc * 512:cc * T + (tc + 1) * 512],
                            start=(cc == 0), stop=(cc == 7))
                        if cc == 7:
                            sQk.inc(ins)
            for nv in range(2):
                te.wait_ge(dWq[0].h, wl_v[16 + nv][0])
                te.wait_ge(dWq[1].h, wl_v[16 + nv][1])
                for tb in range(NTB):
                    m = nv * NTB + tb
                    if m >= 2:
                        nv2, tb2 = divmod(m - 2, NTB)
                        te.wait_ge(sEv.h, evv_v[(nv2, tb2)])
                    for cc in range(NCC):
                        ins = te.matmul(
                            pv[m % 2][:],
                            xfm[:, cc * T + tb * 128:cc * T + (tb + 1) * 128],
                            wqs[:, (cc // 4) * 2048 + (cc % 4) * 512:(cc // 4) * 2048 + (cc % 4) * 512 + 512],
                            start=(cc == 0), stop=(cc == 7))
                        if cc == 7:
                            sQk.inc(ins)

        @block.scalar
        def _(sc):
            for of in range(16):
                dstt = qfm if of < 8 else kfm
                ch = of % 8
                for tc in range(4):
                    sc.wait_ge(sQk.h, qk_v[(of, tc)])
                    sAc.inc(sc.activation(
                        dstt[:, ch * T + tc * 512:ch * T + (tc + 1) * 512],
                        pqk[(of * 4 + tc) % 2][:], AF.Identity))

        @block.vector
        def _(v):
            for nv in range(2):
                for tb in range(NTB):
                    v.wait_ge(sQk.h, v_v[(nv, tb)])
                    sEv.inc(v.tensor_copy(
                        vsb.ap().rearrange("p (k h c) -> p k h c", h=NH, c=65)[:, tb, nv * 8:(nv + 1) * 8, 0:64],
                        pv[(nv * NTB + tb) % 2].ap().rearrange("p (h c) -> p h c", c=64)))
        assert sQk.n == v_v[(1, NTB - 1)]
        psc.close()

    def a3(s):
        psc = ExitStack()
        pn = psc.enter_context(nc.psum_tensor("pn%d" % s, [16, 2048], F32))
        pr = psc.enter_context(nc.psum_tensor("pr%d" % s, [1, 16], F32))
        pbb = psc.enter_context(nc.psum_tensor("pbb%d" % s, [128, 16], F32))

        base = sEv.n
        sq_v = {}
        for cc in range(NCC):
            sq_v[(0, cc)] = base + cc + 1
        redq_v = base + 9
        for cc in range(NCC):
            sq_v[(1, cc)] = base + 10 + cc
        redk_v = base + 18
        mul_v = base + 19
        negb_v = base + 20
        nrow_v = base + 21
        bcp_v = base + 22
        min_v = base + 23
        t0 = sTx.n
        mm_v = {}
        for cc in range(NCC):
            mm_v[(0, cc)] = t0 + cc + 1
            mm_v[(1, cc)] = t0 + 9 + cc
        pr_v = t0 + 17
        pbb_v = t0 + 18
        sqrt_v = sAc.n + 1

        @block.vector
        def _(v):
            for cc in range(NCC):
                if cc >= 1:
                    v.wait_ge(sTx.h, mm_v[(0, cc - 1)])
                sEv.inc(v.tensor_mul(qsq[:], qfm[:, cc * T:(cc + 1) * T],
                                     qfm[:, cc * T:(cc + 1) * T]))
            v.wait_ge(sTx.h, mm_v[(0, NCC - 1)])
            sEv.inc(v.tensor_reduce(biasq[:, 0:1], pn[0:16, :], AX.X, ALU.max))
            for cc in range(NCC):
                if cc >= 1:
                    v.wait_ge(sTx.h, mm_v[(1, cc - 1)])
                sEv.inc(v.tensor_mul(qsq[:], kfm[:, cc * T:(cc + 1) * T],
                                     kfm[:, cc * T:(cc + 1) * T]))
            v.wait_ge(sTx.h, mm_v[(1, NCC - 1)])
            sEv.inc(v.tensor_reduce(biasq[:, 1:2], pn[0:16, :], AX.X, ALU.max))
            v.drain()
            sEv.inc(v.tensor_mul(biasq[:, 2:3], biasq[:, 0:1], biasq[:, 1:2]))
            v.drain()
            v.wait_ge(sAc.h, sqrt_v)
            sEv.inc(v.tensor_scalar_mul(biasq[:, 0:1], biasq[:, 3:4], -0.125))
            v.drain()
            v.wait_ge(sTx.h, pr_v)
            sEv.inc(v.tensor_copy(nrow[:], pr[:]))
            v.drain()
            v.wait_ge(sTx.h, pbb_v)
            sEv.inc(v.tensor_copy(biasP[:, 0:16], pbb[:]))
            v.drain()
            sEv.inc(v.tensor_tensor(biasP[:, 16:24], biasP[:, 0:16:2],
                                    biasP[:, 1:16:2], ALU.min))
            v.drain()

        @block.tensor
        def _(te):
            for w in range(2):
                if w == 1:
                    te.wait_ge(sEv.h, redq_v)
                for cc in range(NCC):
                    te.wait_ge(sEv.h, sq_v[(w, cc)])
                    for nq in range(4):
                        ins = te.matmul(pn[:, nq * 512:(nq + 1) * 512],
                                        s_hmap[:, cc * 16:(cc + 1) * 16],
                                        qsq[:, nq * 512:(nq + 1) * 512],
                                        start=(cc == 0), stop=(cc == 7))
                        if nq == 3:
                            sTx.inc(ins)
            te.wait_ge(sEv.h, negb_v)
            sTx.inc(te.matmul(pr[:], biasq[:, 0:1], s_id16[:], start=True, stop=True))
            te.wait_ge(sEv.h, nrow_v)
            sTx.inc(te.matmul(pbb[:], s_ones[0:1, :], nrow[:], start=True, stop=True))

        @block.scalar
        def _(sc):
            sc.wait_ge(sEv.h, mul_v)
            sAc.inc(sc.activation(biasq[:, 3:4], biasq[:, 2:3], AF.Sqrt))

        assert sEv.n == min_v, (sEv.n, min_v)
        assert sTx.n == pbb_v
        psc.close()

    def a4(s):
        psc = ExitStack()
        pS = psc.enter_context(nc.psum_tensor("pS%d" % s, [128, 1024], F32))
        pO = [psc.enter_context(nc.psum_tensor("pO%d_%d" % (s, i), [65, 1024], F32))
              for i in range(2)]
        pBc = psc.enter_context(nc.psum_tensor("pBc%d" % s, [64, 1024], F32))

        QCS = A4_QCS
        events = []
        for pair in range(A4_PAIRS):
            for qc in range(QCS):
                for kb in range(4 * qc + 4):
                    events.append((pair, qc, kb))
        E = len(events)
        eidx = {ev: i for i, ev in enumerate(events)}
        part = [kb >= 4 * qc for (pair, qc, kb) in events]
        jof = [kb - 4 * qc for (pair, qc, kb) in events]

        exp0, s0_, pv0, mk0 = sEx.n, sS_.n, sPV.n, sMk.n
        exp_v = [exp0 + i + 1 for i in range(E)]
        s_v = [s0_ + i + 1 for i in range(E)]
        pv_v = [pv0 + i + 1 for i in range(E)]
        mk_v = {}
        n = mk0
        for i in range(E):
            if part[i]:
                n += 1
                mk_v[i] = n
        NQG = A4_PAIRS * QCS
        nm0, rc0, bc0 = sNm.n, sRc.n, sBc.n
        nm_v = [nm0 + i + 1 for i in range(NQG)]
        rc_v = [rc0 + i + 1 for i in range(NQG)]
        bc_v = [bc0 + i + 1 for i in range(NQG)]
        stg_v = {}
        stg_base = [dSg[0].n, dSg[1].n]
        for i in range(NQG):
            stg_v[i] = stg_base[i % 2] + 16 * (i // 2 + 1)
        bias_rdy = sEv.n
        ev0 = sEv.n
        evac_v = [ev0 + i + 1 for i in range(NQG)]

        def cols(i):
            return (jof[i] * 128 if part[i] else 0, 512)

        def emit_pv(te, ip):
            pair, qc, kb = events[ip]
            gq = pair * QCS + qc
            c0, c1 = cols(ip)
            te.wait_ge(sEx.h, exp_v[ip])
            if part[ip]:
                te.wait_ge(sMk.h, mk_v[ip])
            if kb == 0 and gq >= 2:
                te.wait_ge(sNm.h, nm_v[gq - 2])
            for hh in range(2):
                h = pair * 2 + hh
                ins = te.matmul(
                    pO[gq % 2][0:65, hh * 512 + c0:hh * 512 + c1],
                    vsb.ap().rearrange("p (k h c) -> p k h c", h=NH, c=65)[:, kb, h, :],
                    pt[:, (ip % 2) * 1024 + hh * 512 + c0:(ip % 2) * 1024 + hh * 512 + c1],
                    start=(kb == 0), stop=(kb == 4 * qc + 3),
                    skip_group_check=True)
                if hh == 1:
                    sPV.inc(ins)

        @block.tensor
        def _(te):
            pend = []

            def do_bc(gq):
                te.wait_ge(sRc.h, rc_v[gq])
                if gq >= 1:
                    te.wait_ge(sNm.h, nm_v[gq - 1])
                te.matmul(pBc[:, 0:512], s_ones[64:65, 0:64], rcpa,
                          start=True, stop=True)
                sBc.inc(te.matmul(pBc[:, 512:1024], s_ones[64:65, 0:64], rcpb,
                                  start=True, stop=True))

            for i, (pair, qc, kb) in enumerate(events):
                c0, c1 = cols(i)
                if i >= 1:
                    te.wait_ge(sEx.h, exp_v[i - 1])
                for hh in range(2):
                    ins = te.matmul(
                        pS[:, hh * 512 + c0:hh * 512 + c1],
                        kfm[hh * 64:(hh + 1) * 64, pair * T + kb * 128:pair * T + (kb + 1) * 128],
                        qfm[hh * 64:(hh + 1) * 64, pair * T + qc * 512 + c0:pair * T + qc * 512 + c1],
                        start=True, stop=True, tile_position=(hh * 64, 0),
                        skip_group_check=True)
                    if hh == 1:
                        sS_.inc(ins)
                while pend:
                    do_bc(pend.pop(0))
                if kb >= 1:
                    emit_pv(te, i - 1)
                if kb == 4 * qc + 3:
                    emit_pv(te, i)
                    pend.append(pair * QCS + qc)
            while pend:
                do_bc(pend.pop(0))

        @block.scalar
        def _(sc):
            sc.wait_ge(sEv.h, bias_rdy)
            for i, (pair, qc, kb) in enumerate(events):
                c0, c1 = cols(i)
                sc.wait_ge(sS_.h, s_v[i])
                if i >= 2:
                    sc.wait_ge(sPV.h, pv_v[i - 2])
                sEx.inc(sc.activation(
                    pt.rearrange("p (b h c) -> p b h c", b=2, c=512)[:, i % 2, :, c0:c1],
                    pS.ap().rearrange("p (h c) -> p h c", c=512)[:, :, c0:c1],
                    AF.Exp, bias=biasP[:, 16 + pair:17 + pair], scale=0.125))

        @block.gpsimd
        def _(g):
            for i, (pair, qc, kb) in enumerate(events):
                if part[i]:
                    j = jof[i]
                    g.wait_ge(sEx.h, exp_v[i])
                    sMk.inc(g.tensor_tensor(
                        pt.rearrange("p (b h c) -> p b h c", b=2, c=512)[:, i % 2, :, j * 128:(j + 1) * 128],
                        pt.rearrange("p (b h c) -> p b h c", b=2, c=512)[:, i % 2, :, j * 128:(j + 1) * 128],
                        s_tri.ap().unsqueeze(1).broadcast_to([128, 2, 128]),
                        ALU.mult))
                if kb == 4 * qc + 3:
                    gq = pair * QCS + qc
                    g.wait_ge(sNm.h, nm_v[gq])
                    dSg[gq % 2].dma(g.dma_start(
                        out=attnO[64:128, pair * T + qc * 512:pair * T + (qc + 1) * 512],
                        in_=stg[:, (gq % 2) * 512:(gq % 2) * 512 + 512]))

        @block.vector
        def _(v):
            for gq in range(NQG):
                pair, qc = divmod(gq, QCS)
                i_last = eidx[(pair, qc, 4 * qc + 3)]
                v.wait_ge(sPV.h, pv_v[i_last])
                if gq >= 1:
                    v.wait_ge(sBc.h, bc_v[gq - 1])
                v.reciprocal(rcpa, pO[gq % 2][64:65, 0:512])
                sRc.inc(v.reciprocal(rcpb, pO[gq % 2][64:65, 512:1024]))
                v.wait_ge(sBc.h, bc_v[gq])
                sEv.inc(v.tensor_copy(bcsb[:], pBc[:]))
                v.drain()
                if gq >= 2:
                    v.wait_ge(dSg[gq % 2].h, stg_v[gq - 2])
                v.tensor_mul(attnO[0:64, pair * T + qc * 512:pair * T + (qc + 1) * 512],
                             pO[gq % 2][0:64, 0:512], bcsb[0:64, 0:512])
                sNm.inc(v.tensor_mul(stg[:, (gq % 2) * 512:(gq % 2) * 512 + 512],
                                     pO[gq % 2][0:64, 512:1024], bcsb[0:64, 512:1024]))

        assert sEx.n == exp_v[-1] and sPV.n == pv_v[-1]
        assert sNm.n == nm_v[-1] and sBc.n == bc_v[-1]
        psc.close()

    def a5(s):
        psc = ExitStack()
        pw = [psc.enter_context(nc.psum_tensor("pw%d_%d" % (s, i), [128, 1024], F32))
              for i in range(2)]
        psq = psc.enter_context(nc.psum_tensor("psq%d" % s, [128, 1024], F32))
        wo0 = sWo.n
        wo_v = [wo0 + i + 1 for i in range(NTB)]
        ac0 = sAc.n
        ac_sumsq = [ac0 + 2 * i + 1 for i in range(NTB)]
        ac_sqrt = [ac0 + 2 * i + 2 for i in range(NTB)]
        ev0 = sEv.n
        stt_v, var_v, xn_v = [], [], []
        n = ev0
        for tb in range(NTB):
            n += 1
            stt_v.append(n)
            n += 1
            n += 4
            var_v.append(n)
            n += 1
            n += 1
            xn_v.append(n)
        xnd_v = {}
        xnd_base = [dSx[0].n, dSx[1].n]
        for i in range(NTB):
            xnd_v[i] = xnd_base[i % 2] + 16 * (i // 2 + 1)
        ld_v = {}

        @block.sync
        def _(sy):
            dW.dma(sy.dma_start(
                out=wout_b.rearrange("p (g f) -> p g f", g=8),
                in_=wout_bh.rearrange("(g p) f -> p g f", p=128)))
            for tb in range(NTB):
                if tb >= 2:
                    sy.wait_ge(sEv.h, xn_v[tb - 2])
                dXb[tb % 2].dma(sy.dma_start(
                    out=xbuf[:, (tb % 2) * C:(tb % 2) * C + C],
                    in_=xin[s * T + tb * 128: s * T + (tb + 1) * 128, :]))
                ld_v[tb] = dXb[tb % 2].n
        wob_ld = dW.n

        @block.tensor
        def _(te):
            te.wait_ge(dW.h, wob_ld)
            for tb in range(NTB):
                if tb >= 2:
                    te.wait_ge(sEv.h, stt_v[tb - 2])
                for cc in range(NCC):
                    for n2 in range(2):
                        ins = te.matmul(
                            pw[tb % 2][:, n2 * 512:(n2 + 1) * 512],
                            attnO[:, cc * T + tb * 128:cc * T + (tb + 1) * 128],
                            wout_b[:, cc * C + n2 * 512:cc * C + (n2 + 1) * 512],
                            start=(cc == 0), stop=(cc == 7))
                        if cc == 7 and n2 == 1:
                            sWo.inc(ins)

        @block.scalar
        def _(sc):
            for tb in range(NTB):
                o = (tb % 2) * 8
                sc.wait_ge(sEv.h, stt_v[tb])
                sAc.inc(sc.activation(psq[:], r1[:, (tb % 2) * C:(tb % 2) * C + C],
                                      AF.Square, accum_out=stat[:, o + 1:o + 2]))
                sc.wait_ge(sEv.h, var_v[tb])
                sAc.inc(sc.activation(stat[:, o + 6:o + 7], stat[:, o + 5:o + 6],
                                      AF.Sqrt, bias=s_eps[:, 0:1]))

        @block.vector
        def _(v):
            for tb in range(NTB):
                o = (tb % 2) * 8
                r1t = r1[:, (tb % 2) * C:(tb % 2) * C + C]
                v.wait_ge(sWo.h, wo_v[tb])
                v.wait_ge(dXb[tb % 2].h, ld_v[tb])
                if tb >= 2:
                    v.wait_ge(sAc.h, ac_sumsq[tb - 2])
                sEv.inc(v.scalar_tensor_tensor(
                    r1t, pw[tb % 2][:], 1.0, xbuf[:, (tb % 2) * C:(tb % 2) * C + C],
                    ALU.mult, ALU.add))
                v.drain()
                sEv.inc(v.reduce_sum(stat[:, o:o + 1], r1t, AX.X))
                v.drain()
                v.wait_ge(sAc.h, ac_sumsq[tb])
                sEv.inc(v.tensor_scalar_mul(stat[:, o + 2:o + 3], stat[:, o:o + 1], 1.0 / C))
                sEv.inc(v.tensor_scalar_mul(stat[:, o + 3:o + 4], stat[:, o + 1:o + 2], 1.0 / C))
                v.drain()
                sEv.inc(v.tensor_mul(stat[:, o + 4:o + 5], stat[:, o + 2:o + 3], stat[:, o + 2:o + 3]))
                v.drain()
                sEv.inc(v.tensor_sub(stat[:, o + 5:o + 6], stat[:, o + 3:o + 4], stat[:, o + 4:o + 5]))
                v.drain()
                v.wait_ge(sAc.h, ac_sqrt[tb])
                sEv.inc(v.reciprocal(stat[:, o + 7:o + 8], stat[:, o + 6:o + 7]))
                v.drain()
                if tb >= 2:
                    v.wait_ge(dSx[tb % 2].h, xnd_v[tb - 2])
                sEv.inc(v.tensor_scalar(xnb[:, (tb % 2) * C:(tb % 2) * C + C], r1t,
                                        stat[:, o + 2:o + 3], stat[:, o + 7:o + 8],
                                        ALU.subtract, ALU.mult))

        @block.gpsimd
        def _(g):
            for tb in range(NTB):
                g.wait_ge(sEv.h, xn_v[tb])
                dSx[tb % 2].dma(g.dma_start(
                    out=xn_hbm[s * T + tb * 128: s * T + (tb + 1) * 128, :],
                    in_=xnb[:, (tb % 2) * C:(tb % 2) * C + C]))

        assert sEv.n == xn_v[-1]
        psc.close()

    def wtern(wi, wsrc, wdst_T, R, Ccols):
        tiles = [(r, c) for r in range(R // 128) for c in range(Ccols // 1024)]
        NT = len(tiles)
        psc = ExitStack()
        p1 = psc.enter_context(nc.psum_tensor("pww%d" % wi, [128, 1024], BF16))
        pm = psc.enter_context(nc.psum_tensor("pwm%d" % wi, [1, 1], F32))
        pb = psc.enter_context(nc.psum_tensor("pwb%d" % wi, [128, 2], F32))

        ev0 = sEv.n
        red_v = [ev0 + i + 1 for i in range(NT)]
        redall_v = ev0 + NT + 1
        m1_v = redall_v + 1
        m2_v = redall_v + 2
        sc_v = redall_v + 3
        tx0 = sTx.n
        pm_v = tx0 + 1
        pb_v = tx0 + 2
        ld1, ld2 = {}, {}

        @block.sync
        def _(sy):
            for i, (r, c) in enumerate(tiles):
                if i >= 2:
                    sy.wait_ge(sEv.h, red_v[i - 2])
                dWt[i % 2].dma(sy.dma_start(
                    out=wst[:, (i % 2) * 1024:(i % 2) * 1024 + 1024],
                    in_=wsrc[r * 128:(r + 1) * 128, c * 1024:(c + 1) * 1024]))
                ld1[i] = dWt[i % 2].n

        @block.vector
        def _(v):
            for i in range(NT):
                v.wait_ge(dWt[i % 2].h, ld1[i])
                sEv.inc(v.reduce_sum(absacc[:, i:i + 1],
                                     wst[:, (i % 2) * 1024:(i % 2) * 1024 + 1024],
                                     AX.X, apply_absolute_value=True))
            v.drain()
            sEv.inc(v.reduce_sum(absacc[:, 32:33], absacc[:, 0:NT], AX.X))
            v.drain()
            v.wait_ge(sTx.h, pm_v)
            sEv.inc(v.tensor_scalar(w11[:, 0:1], pm[:], 1.0 / (R * Ccols), 1e-5,
                                    ALU.mult, ALU.max))
            v.drain()
            sEv.inc(v.reciprocal(w11[:, 1:2], w11[:, 0:1]))
            v.drain()
            v.wait_ge(sTx.h, pb_v)
            sEv.inc(v.tensor_copy(wsc[:, wi * 2:wi * 2 + 2], pb[:]))

        @block.tensor
        def _(te):
            te.wait_ge(sEv.h, redall_v)
            sTx.inc(te.matmul(pm[:], s_onesc[:], absacc[:, 32:33],
                              start=True, stop=True))
            te.wait_ge(sEv.h, m2_v)
            sTx.inc(te.matmul(pb[:], s_ones[0:1, :], w11[0:1, 0:2],
                              start=True, stop=True))

        ev1 = sEv.n
        q1_v, q3_v, evT_v = [], [], []
        n = ev1
        for i in range(NT):
            n += 1
            q1_v.append(n)
            n += 1
            q3_v.append(n)
            n += 1
            evT_v.append(n)
        gp0 = sGp.n
        g_v = [gp0 + i + 1 for i in range(NT)]
        tx1 = sTx.n
        tx_v = [tx1 + i + 1 for i in range(NT)]
        st_v = {}
        st_base = [dSt2[0].n, dSt2[1].n]
        for i in range(NT):
            st_v[i] = st_base[i % 2] + 16 * (i // 2 + 1)

        @block.sync
        def _(sy):
            for i, (r, c) in enumerate(tiles):
                if i < 2:
                    sy.wait_ge(sEv.h, red_v[NT - 1])
                if i >= 2:
                    sy.wait_ge(sEv.h, q3_v[i - 2])
                dWt[i % 2].dma(sy.dma_start(
                    out=wst[:, (i % 2) * 1024:(i % 2) * 1024 + 1024],
                    in_=wsrc[r * 128:(r + 1) * 128, c * 1024:(c + 1) * 1024]))
                ld2[i] = dWt[i % 2].n

        @block.vector
        def _(v):
            v.wait_ge(sEv.h, sc_v)
            for i in range(NT):
                v.wait_ge(dWt[i % 2].h, ld2[i])
                src = wst[:, (i % 2) * 1024:(i % 2) * 1024 + 1024]
                if i >= 1:
                    v.wait_ge(sGp.h, g_v[i - 1])
                sEv.inc(v.tensor_scalar(sqs[:], src, wsc[:, wi * 2 + 1:wi * 2 + 2],
                                        -1.0, ALU.mult, ALU.max))
                v.wait_ge(sGp.h, g_v[i])
                if i >= 2:
                    v.wait_ge(sTx.h, tx_v[i - 2])
                sEv.inc(v.tensor_scalar_add(
                    tern[:, (i % 2) * 1024:(i % 2) * 1024 + 1024], sqs[:], -MAGIC))
                v.wait_ge(sTx.h, tx_v[i])
                if i >= 2:
                    v.wait_ge(dSt2[i % 2].h, st_v[i - 2])
                sEv.inc(v.tensor_copy(
                    terT[:, (i % 2) * 1024:(i % 2) * 1024 + 1024], p1.ap()))

        @block.gpsimd
        def _(g):
            for i, (r, c) in enumerate(tiles):
                g.wait_ge(sEv.h, q1_v[i])
                sGp.inc(g.tensor_scalar(sqs[:], sqs[:], 1.0, MAGIC,
                                        ALU.min, ALU.add))
                if i >= 2:
                    g.wait_ge(sEv.h, evT_v[i - 2])
                    r2, c2 = tiles[i - 2]
                    dSt2[(i - 2) % 2].dma(g.dma_start(
                        out=wdst_T.rearrange("(g p) f -> p g f", p=128)[:, c2 * 8:(c2 + 1) * 8, r2 * 128:(r2 + 1) * 128],
                        in_=terT.rearrange("p (b g f) -> p b g f", b=2, g=8)[:, (i - 2) % 2, :, :]))
            for i in (NT - 2, NT - 1):
                g.wait_ge(sEv.h, evT_v[i])
                r2, c2 = tiles[i]
                dSt2[i % 2].dma(g.dma_start(
                    out=wdst_T.rearrange("(g p) f -> p g f", p=128)[:, c2 * 8:(c2 + 1) * 8, r2 * 128:(r2 + 1) * 128],
                    in_=terT.rearrange("p (b g f) -> p b g f", b=2, g=8)[:, i % 2, :, :]))

        @block.tensor
        def _(te):
            for i in range(NT):
                te.wait_ge(sEv.h, q3_v[i])
                if i >= 1:
                    te.wait_ge(sEv.h, evT_v[i - 1])
                for cc in range(8):
                    ins = te.transpose(
                        p1[:, cc * 128:(cc + 1) * 128],
                        tern[:, (i % 2) * 1024 + cc * 128:(i % 2) * 1024 + (cc + 1) * 128],
                        s_idb[:])
                    if cc == 7:
                        sTx.inc(ins)

        assert sEv.n == evT_v[NT - 1] and sTx.n == tx_v[-1]
        psc.close()

    ph_list = [lambda: ph0()]
    for s in range(NSEQ):
        for f in (a1, a2, a3, a4, a5):
            ph_list.append(lambda f=f, s=s: f(s))
    ph_list.append(lambda: wtern(0, wg, wgT_h, INNER, C))
    ph_list.append(lambda: wtern(1, wu, wuT_h, INNER, C))
    ph_list.append(lambda: wtern(2, wd, wdT_h, C, INNER))
    nph = 0
    for f in ph_list:
        if nph >= UPTO:
            break
        f()
        barrier()
        nph += 1
    esA.close()

    esF = ExitStack()

    def sbF(name, shape, dt):
        return esF.enter_context(nc.sbuf_tensor(name, shape, dt))

    wgT = sbF("wgT", [128, 8 * INNER], BF16)
    wuT = sbF("wuT", [128, 8 * INNER], BF16)
    xnf = sbF("xnf", [128, 2 * C], F32)
    hbf = sbF("hbf", [128, INNER], F32)
    thf = sbF("thf", [128, INNER], F32)
    fst = sbF("fst", [128, 32], F32)
    esF1 = ExitStack()
    hqb = esF1.enter_context(nc.sbuf_tensor("hqb", [128, INNER], BF16))
    xqb = esF1.enter_context(nc.sbuf_tensor("xqb", [128, 2 * C], BF16))
    xqT = esF1.enter_context(nc.sbuf_tensor("xqT", [128, 2 * C], BF16))
    sgf = esF1.enter_context(nc.sbuf_tensor("sgf", [128, 512], F32))
    hqi_yb = [None, None]

    def f1():
        psc = ExitStack()
        pT = psc.enter_context(nc.psum_tensor("pT1", [128, 1024], BF16))
        pG = [psc.enter_context(nc.psum_tensor("pG%d" % i, [128, 512], F32)) for i in range(2)]
        pU = [psc.enter_context(nc.psum_tensor("pU%d" % i, [128, 512], F32)) for i in range(2)]

        fd0 = sFd.n
        xq_v, evT_v, hml_v, hs_v, hq_v = [], [], [], [], []
        n = fd0
        for tb in range(NT2):
            n += 7
            n += 2
            xq_v.append(n)
            n += 1
            evT_v.append(n)
            hml = []
            for oc in range(8):
                n += 1
                hml.append(n)
            hml_v.append(hml)
            n += 4
            hs_v.append(n)
            n += 1
            hq_v.append(n)
        fa0 = sFa.n
        silu_v = [[fa0 + tb * 9 + oc + 1 for oc in range(8)] for tb in range(NT2)]
        th_v = [fa0 + tb * 9 + 9 for tb in range(NT2)]
        f0 = sF.n
        tx_v, g_v, u_v = [], [], []
        n = f0
        for tb in range(NT2):
            n += 1
            tx_v.append(n)
            gl, ul = [], []
            for oc in range(8):
                n += 1
                gl.append(n)
                n += 1
                ul.append(n)
            g_v.append(gl)
            u_v.append(ul)
        hd0 = dSh.n
        hqd_v = [hd0 + 16 * (tb + 1) for tb in range(NT2)]
        xn_ld = {}

        @block.sync
        def _(sy):
            for half in range(2):
                dW.dma(sy.dma_start(
                    out=wgT.ap().rearrange("p (g f) -> p g f", g=8)[:, half * 4:(half + 1) * 4, :],
                    in_=wgT_h.rearrange("(g p) f -> p g f", p=128)[:, half * 4:(half + 1) * 4, :]))
                dW.dma(sy.dma_start(
                    out=wuT.ap().rearrange("p (g f) -> p g f", g=8)[:, half * 4:(half + 1) * 4, :],
                    in_=wuT_h.rearrange("(g p) f -> p g f", p=128)[:, half * 4:(half + 1) * 4, :]))
            w_ld = dW.n
            for tb in range(NT2):
                if tb >= 2:
                    sy.wait_ge(sFd.h, xq_v[tb - 2])
                dXn[tb % 2].dma(sy.dma_start(out=xnf[:, (tb % 2) * C:(tb % 2) * C + C],
                                    in_=xn_hbm[tb * 128:(tb + 1) * 128, :]))
                xn_ld[tb] = dXn[tb % 2].n

        w_ld = dW.n

        @block.vector
        def _(v):
            for tb in range(NT2):
                o = (tb % 2) * 16
                xt = xnf[:, (tb % 2) * C:(tb % 2) * C + C]
                v.wait_ge(dXn[tb % 2].h, xn_ld[tb])
                if tb >= 2:
                    v.wait_ge(sFa.h, th_v[tb - 2])
                sFd.inc(v.tensor_reduce(fst[:, o:o + 1], xt, AX.X, ALU.max,
                                        apply_absolute_value=True))
                v.drain()
                sFd.inc(v.tensor_scalar_max(fst[:, o:o + 1], fst[:, o:o + 1], 1e-5))
                v.drain()
                sFd.inc(v.reciprocal(fst[:, o + 1:o + 2], fst[:, o:o + 1]))
                v.drain()
                sFd.inc(v.tensor_scalar_mul(fst[:, o + 2:o + 3], fst[:, o + 1:o + 2], 127.0))
                sFd.inc(v.tensor_scalar_mul(fst[:, o + 3:o + 4], fst[:, o:o + 1], 1.0 / 127.0))
                v.drain()
                sFd.inc(v.tensor_mul(fst[:, o + 4:o + 5], fst[:, o + 3:o + 4], wsc[:, 0:1]))
                sFd.inc(v.tensor_mul(fst[:, o + 5:o + 6], fst[:, o + 3:o + 4], wsc[:, 2:3]))
                v.drain()
                sFd.inc(v.tensor_scalar(thf[:, 0:C], xt, fst[:, o + 2:o + 3], MAGIC,
                                        ALU.mult, ALU.add))
                v.drain()
                if tb >= 2:
                    v.wait_ge(sF.h, tx_v[tb - 2])
                sFd.inc(v.tensor_scalar_add(xqb[:, (tb % 2) * C:(tb % 2) * C + C],
                                            thf[:, 0:C], -MAGIC))
                v.wait_ge(sF.h, tx_v[tb])
                if tb >= 2:
                    v.wait_ge(sF.h, u_v[tb - 2][7])
                sFd.inc(v.tensor_copy(xqT[:, (tb % 2) * C:(tb % 2) * C + C], pT.ap()))
                for oc in range(8):
                    v.wait_ge(sF.h, u_v[tb][oc])
                    v.wait_ge(sFa.h, silu_v[tb][oc])
                    sFd.inc(v.scalar_tensor_tensor(
                        hbf[:, oc * 512:(oc + 1) * 512], pU[oc % 2][:],
                        fst[:, o + 5:o + 6], sgf[:], ALU.mult, ALU.mult))
                v.drain()
                sFd.inc(v.tensor_reduce(fst[:, o + 6:o + 7], hbf[:], AX.X, ALU.max,
                                        apply_absolute_value=True))
                v.drain()
                sFd.inc(v.reciprocal(fst[:, o + 7:o + 8], fst[:, o + 6:o + 7]))
                v.drain()
                sFd.inc(v.tensor_scalar_mul(fst[:, o + 8:o + 9], fst[:, o + 7:o + 8], 127.0))
                sFd.inc(v.tensor_scalar_mul(dsh_all[:, tb:tb + 1], fst[:, o + 6:o + 7], 1.0 / 127.0))
                v.drain()
                v.wait_ge(sFa.h, th_v[tb])
                if tb >= 1:
                    v.wait_ge(dSh.h, hqd_v[tb - 1])
                sFd.inc(v.tensor_scalar_add(hqb[:], thf[:], -MAGIC))

        @block.tensor
        def _(te):
            te.wait_ge(dW.h, w_ld)
            for tb in range(NT2):
                te.wait_ge(sFd.h, xq_v[tb])
                if tb >= 1:
                    te.wait_ge(sFd.h, evT_v[tb - 1])
                for cc in range(8):
                    ins = te.transpose(pT[:, cc * 128:(cc + 1) * 128],
                                       xqb[:, (tb % 2) * C + cc * 128:(tb % 2) * C + (cc + 1) * 128],
                                       s_idb[:])
                    if cc == 7:
                        sF.inc(ins)
                for oc in range(8):
                    gc = tb * 8 + oc
                    te.wait_ge(sFd.h, evT_v[tb])
                    if gc >= 2:
                        tb2, oc2 = divmod(gc - 2, 8)
                        te.wait_ge(sFa.h, silu_v[tb2][oc2])
                    for cc in range(8):
                        ins = te.matmul(pG[oc % 2][:],
                                        xqT[:, (tb % 2) * C + cc * 128:(tb % 2) * C + (cc + 1) * 128],
                                        wgT[:, cc * INNER + oc * 512:cc * INNER + (oc + 1) * 512],
                                        start=(cc == 0), stop=(cc == 7))
                        if cc == 7:
                            sF.inc(ins)
                    if gc >= 2:
                        tb2, oc2 = divmod(gc - 2, 8)
                        te.wait_ge(sFd.h, hml_v[tb2][oc2])
                    for cc in range(8):
                        ins = te.matmul(pU[oc % 2][:],
                                        xqT[:, (tb % 2) * C + cc * 128:(tb % 2) * C + (cc + 1) * 128],
                                        wuT[:, cc * INNER + oc * 512:cc * INNER + (oc + 1) * 512],
                                        start=(cc == 0), stop=(cc == 7))
                        if cc == 7:
                            sF.inc(ins)

        @block.scalar
        def _(sc):
            for tb in range(NT2):
                o = (tb % 2) * 16
                for oc in range(8):
                    gc = tb * 8 + oc
                    sc.wait_ge(sF.h, g_v[tb][oc])
                    if gc >= 1:
                        tb2, oc2 = divmod(gc - 1, 8)
                        sc.wait_ge(sFd.h, hml_v[tb2][oc2])
                    sFa.inc(sc.activation(sgf[:], pG[oc % 2][:], AF.Silu,
                                          scale=fst[:, o + 4:o + 5]))
                sc.wait_ge(sFd.h, hs_v[tb])
                sFa.inc(sc.activation(thf[:], hbf[:], AF.Identity,
                                      scale=fst[:, o + 8:o + 9], bias=s_magic[:, 0:1]))

        @block.gpsimd
        def _(g):
            for tb in range(NT2):
                g.wait_ge(sFd.h, hq_v[tb])
                dSh.dma(g.dma_start(out=hq_hbm[tb * 128:(tb + 1) * 128, :],
                                   in_=hqb[:]))

        assert sFd.n == hq_v[-1], (sFd.n, hq_v[-1])
        assert sF.n == u_v[-1][7]
        assert sFa.n == th_v[-1]
        psc.close()

    def f2():
        hqi_, yb_ = hqi_yb[0], hqi_yb[1]
        psc = ExitStack()
        pH = [psc.enter_context(nc.psum_tensor("pH%d" % i, [128, 1024], BF16)) for i in range(2)]
        pD = [psc.enter_context(nc.psum_tensor("pD%d" % i, [128, 1024], F32)) for i in range(2)]
        psq2 = psc.enter_context(nc.psum_tensor("psq2", [128, 1024], F32))
        wdT = wgT
        hqT = wuT.ap()[:, 0:2 * INNER]

        fd0 = sFd.n
        dshmd_v = fd0 + 1
        evh_v, ln_v = [], []
        n = fd0 + 1
        for tb in range(NT2):
            el = []
            for grp in range(4):
                n += 1
                el.append(n)
            evh_v.append(el)
            n += 8
            ln_v.append(n)
        fa0 = sFa.n
        sumsq_v = [fa0 + 2 * tb + 1 for tb in range(NT2)]
        sqrt_v = [fa0 + 2 * tb + 2 for tb in range(NT2)]
        f0 = sF.n
        txg_v, dmm_v = [], []
        n = f0
        for tb in range(NT2):
            tl = []
            for grp in range(4):
                n += 1
                tl.append(n)
            txg_v.append(tl)
            dl = []
            for n2 in range(2):
                n += 1
                dl.append(n)
            dmm_v.append(dl)
        dS0 = sOut.n
        y_v = [dS0 + 16 * (tb + 1) for tb in range(NT2)]
        hq_ld, xn_ld = {}, {}

        @block.vector
        def _(v):
            sFd.inc(v.tensor_scalar_mul(dsh_all[:, 32:64], dsh_all[:, 0:32],
                                        wsc[:, 4:5]))

        @block.sync
        def _(sy):
            for q in range(4):
                dW.dma(sy.dma_start(
                    out=wdT.ap().rearrange("p (g f) -> p g f", g=32)[:, q * 8:(q + 1) * 8, 0:C],
                    in_=wdT_h.rearrange("(g p) f -> p g f", p=128)[:, q * 8:(q + 1) * 8, :]))
            for tb in range(NT2):
                if tb >= 2:
                    sy.wait_ge(sF.h, dmm_v[tb - 2][1])
                dHq[tb % 2].dma(sy.dma_start(out=hqi_[:, (tb % 2) * INNER:(tb % 2) * INNER + INNER],
                                    in_=hq_hbm[tb * 128:(tb + 1) * 128, :]))
                hq_ld[tb] = dHq[tb % 2].n
                if tb >= 2:
                    sy.wait_ge(sFd.h, ln_v[tb - 2])
                dXn[tb % 2].dma(sy.dma_start(out=xnf[:, (tb % 2) * C:(tb % 2) * C + C],
                                    in_=xn_hbm[tb * 128:(tb + 1) * 128, :]))
                xn_ld[tb] = dXn[tb % 2].n

        wd_ld = dW.n

        @block.tensor
        def _(te):
            te.wait_ge(dW.h, wd_ld)
            te.wait_ge(sFd.h, dshmd_v)
            for tb in range(NT2):
                te.wait_ge(dHq[tb % 2].h, hq_ld[tb])
                for grp in range(4):
                    gi = tb * 4 + grp
                    if gi >= 2:
                        tbp, gp_ = divmod(gi - 2, 4)
                        te.wait_ge(sFd.h, evh_v[tbp][gp_])
                    for k in range(8):
                        ic = grp * 8 + k
                        ins = te.transpose(
                            pH[grp % 2][:, k * 128:(k + 1) * 128],
                            hqi_[:, (tb % 2) * INNER + ic * 128:(tb % 2) * INNER + (ic + 1) * 128],
                            s_idb[:])
                        if k == 7:
                            sF.inc(ins)
                for n2 in range(2):
                    if tb >= 2:
                        te.wait_ge(sFd.h, ln_v[tb - 2])
                    for ic in range(32):
                        if ic == 0 and n2 == 0:
                            te.wait_ge(sFd.h, evh_v[tb][3])
                        ins = te.matmul(
                            pD[tb % 2][:, n2 * 512:(n2 + 1) * 512],
                            hqT[:, (tb % 2) * INNER + ic * 128:(tb % 2) * INNER + (ic + 1) * 128],
                            wdT[:, ic * 1024 + n2 * 512:ic * 1024 + (n2 + 1) * 512],
                            start=(ic == 0), stop=(ic == 31))
                        if ic == 31:
                            sF.inc(ins)

        @block.vector
        def _(v):
            for tb in range(NT2):
                o = (tb % 2) * 16
                for grp in range(4):
                    v.wait_ge(sF.h, txg_v[tb][grp])
                    if tb >= 2:
                        v.wait_ge(sF.h, dmm_v[tb - 2][1])
                    sFd.inc(v.tensor_copy(
                        hqT.rearrange("p (b g f) -> p b g f", b=2, g=32)[:, tb % 2, grp * 8:(grp + 1) * 8, :],
                        pH[grp % 2].ap().rearrange("p (g f) -> p g f", g=8)))
                v.wait_ge(sF.h, dmm_v[tb][1])
                v.wait_ge(dXn[tb % 2].h, xn_ld[tb])
                if tb >= 1:
                    v.wait_ge(sFa.h, sumsq_v[tb - 1])
                r2t = thf[:, 0:C]
                sFd.inc(v.scalar_tensor_tensor(
                    r2t, pD[tb % 2][:], dsh_all[:, 32 + tb:33 + tb],
                    xnf[:, (tb % 2) * C:(tb % 2) * C + C], ALU.mult, ALU.add))
                v.drain()
                sFd.inc(v.reduce_sum(fst[:, o:o + 1], r2t, AX.X))
                v.drain()
                v.wait_ge(sFa.h, sumsq_v[tb])
                sFd.inc(v.tensor_scalar_mul(fst[:, o + 2:o + 3], fst[:, o:o + 1], 1.0 / C))
                sFd.inc(v.tensor_scalar_mul(fst[:, o + 3:o + 4], fst[:, o + 1:o + 2], 1.0 / C))
                v.drain()
                sFd.inc(v.tensor_mul(fst[:, o + 4:o + 5], fst[:, o + 2:o + 3], fst[:, o + 2:o + 3]))
                v.drain()
                sFd.inc(v.tensor_sub(fst[:, o + 5:o + 6], fst[:, o + 3:o + 4], fst[:, o + 4:o + 5]))
                v.drain()
                v.wait_ge(sFa.h, sqrt_v[tb])
                sFd.inc(v.reciprocal(fst[:, o + 7:o + 8], fst[:, o + 6:o + 7]))
                v.drain()
                if tb >= 1:
                    v.wait_ge(sOut.h, y_v[tb - 1])
                sFd.inc(v.tensor_scalar(yb_[:], r2t,
                                        fst[:, o + 2:o + 3], fst[:, o + 7:o + 8],
                                        ALU.subtract, ALU.mult))

        @block.scalar
        def _(sc):
            for tb in range(NT2):
                o = (tb % 2) * 16
                sc.wait_ge(sFd.h, ln_v[tb] - 7)
                sFa.inc(sc.activation(psq2[:], thf[:, 0:C], AF.Square,
                                      accum_out=fst[:, o + 1:o + 2]))
                sc.wait_ge(sFd.h, ln_v[tb] - 2)
                sFa.inc(sc.activation(fst[:, o + 6:o + 7], fst[:, o + 5:o + 6],
                                      AF.Sqrt, bias=s_eps[:, 0:1]))

        @block.gpsimd
        def _(g):
            for tb in range(NT2):
                g.wait_ge(sFd.h, ln_v[tb])
                sOut.dma(g.dma_start(out=y[tb * 128:(tb + 1) * 128, :],
                                     in_=yb_[:]))

        assert sFd.n == ln_v[-1]
        assert sOut.n == y_v[-1]
        psc.close()

    if nph < UPTO:
        f1()
        barrier()
        nph += 1
    esF1.close()
    hqi_yb = [esF.enter_context(nc.sbuf_tensor("hqi", [128, 2 * INNER], BF16)),
              esF.enter_context(nc.sbuf_tensor("yb", [128, C], F32))]
    if nph < UPTO:
        f2()
        barrier()
        nph += 1

    @block.sync
    def _(sy):
        sy.wait_ge(sOut.h, sOut.n)

    esF.close()
    es.close()
    return nc


def make_consts():
    idf = np.eye(128, dtype=np.float32)
    idb = np.eye(128, dtype=bfnp)
    tri = (np.arange(128)[:, None] <= np.arange(128)[None, :]).astype(bfnp)
    hmap = np.zeros((128, 128), dtype=bfnp)
    for cc in range(8):
        for p in range(128):
            hmap[p, cc * 16 + (2 * cc + p // 64)] = 1
    ones = np.ones((128, 128), dtype=np.float32)
    id16 = np.eye(16, dtype=np.float32)
    return dict(c_idf=idf, c_idb=idb, c_tri=tri, c_hmap=hmap, c_ones=ones,
                c_id16=id16)


def kernel(L0_attn_x, L0_attn_w_qkv, L0_attn_w_out, L0_ffn_w_gate, L0_ffn_w_up,
           L0_ffn_w_down):
    x = np.ascontiguousarray(np.asarray(L0_attn_x, dtype=np.float32))
    B = x.shape[0]
    ncore = 8
    per = B // ncore
    nc = build_program()
    base = dict(
        wqkv=np.ascontiguousarray(np.asarray(L0_attn_w_qkv, dtype=np.float32)),
        wout=np.ascontiguousarray(np.asarray(L0_attn_w_out, dtype=np.float32)),
        wg=np.ascontiguousarray(np.asarray(L0_ffn_w_gate, dtype=np.float32)),
        wu=np.ascontiguousarray(np.asarray(L0_ffn_w_up, dtype=np.float32)),
        wd=np.ascontiguousarray(np.asarray(L0_ffn_w_down, dtype=np.float32)),
        **make_consts())
    in_maps = []
    for c in range(ncore):
        m = dict(base)
        m["xin"] = np.ascontiguousarray(x[c * per:(c + 1) * per].reshape(per * T, C))
        in_maps.append(m)
    res = run_bass_kernel_spmd(nc, in_maps, core_ids=list(range(ncore)))
    outs = [r["y"].reshape(per, T, C) for r in res.results]
    return np.concatenate(outs, axis=0).astype(np.float32)


# version 03
# speedup vs baseline: 1.2533x; 1.2533x over previous
import numpy as np
import ml_dtypes
import concourse.bass as bass
import concourse.mybir as mybir
from concourse.bass_utils import run_bass_kernel_spmd
from contextlib import ExitStack

F32 = mybir.dt.float32
BF16 = mybir.dt.bfloat16
AF = mybir.ActivationFunctionType
ALU = mybir.AluOpType
AX = mybir.AxisListType
bfnp = ml_dtypes.bfloat16

MAGIC = float(np.float32(1.5 * 2 ** 23))
T, C, NH, DK, NSEQ, INNER = 2048, 1024, 16, 64, 2, 4096
NTB = T // 128
NQC = 4
NCC = C // 128
NPAIR = NH // 2
NT2 = NSEQ * NTB

DEBUG_OUT = False
UPTO = 99
A4_PAIRS = 8
A4_QCS = 4


class Sem:
    def __init__(self, es, nc, name):
        self.h = es.enter_context(nc.semaphore(name))
        self.n = 0

    def inc(self, instr, k=1):
        instr.then_inc(self.h, k)
        self.n += k
        return instr

    def dma(self, instr):
        return self.inc(instr, 16)


def build_program():
    nc = bass.Bass("TRN2", target_bir_lowering=False)

    xin = nc.dram_tensor("xin", [NSEQ * T, C], F32, kind="ExternalInput").ap()
    wqkv = nc.dram_tensor("wqkv", [C, 3 * C], F32, kind="ExternalInput").ap()
    wout = nc.dram_tensor("wout", [C, C], F32, kind="ExternalInput").ap()
    wg = nc.dram_tensor("wg", [INNER, C], F32, kind="ExternalInput").ap()
    wu = nc.dram_tensor("wu", [INNER, C], F32, kind="ExternalInput").ap()
    wd = nc.dram_tensor("wd", [C, INNER], F32, kind="ExternalInput").ap()
    c_idf = nc.dram_tensor("c_idf", [128, 128], F32, kind="ExternalInput").ap()
    c_idb = nc.dram_tensor("c_idb", [128, 128], BF16, kind="ExternalInput").ap()
    c_tri = nc.dram_tensor("c_tri", [128, 128], BF16, kind="ExternalInput").ap()
    c_hmap = nc.dram_tensor("c_hmap", [128, 128], BF16, kind="ExternalInput").ap()
    c_ones = nc.dram_tensor("c_ones", [128, 128], F32, kind="ExternalInput").ap()
    c_id16 = nc.dram_tensor("c_id16", [16, 16], F32, kind="ExternalInput").ap()
    y = nc.dram_tensor("y", [NSEQ * T, C], F32, kind="ExternalOutput").ap()

    KIND = "ExternalOutput" if DEBUG_OUT else "Internal"
    xn_hbm = nc.dram_tensor("xn_hbm", [NSEQ * T, C], F32, kind=KIND).ap()
    hq_hbm = nc.dram_tensor("hq_hbm", [NSEQ * T, INNER], BF16, kind=KIND).ap()
    wqkv_bh = nc.dram_tensor("wqkv_bh", [C, 3 * C], BF16, kind="Internal").ap()
    wout_bh = nc.dram_tensor("wout_bh", [C, C], BF16, kind="Internal").ap()
    wgT_h = nc.dram_tensor("wgT_h", [C, INNER], BF16, kind=KIND).ap()
    wuT_h = nc.dram_tensor("wuT_h", [C, INNER], BF16, kind=KIND).ap()
    wdT_h = nc.dram_tensor("wdT_h", [INNER, C], BF16, kind=KIND).ap()

    es = ExitStack()

    s_idf = es.enter_context(nc.sbuf_tensor("s_idf", [128, 128], F32))
    s_idb = es.enter_context(nc.sbuf_tensor("s_idb", [128, 128], BF16))
    s_tri = es.enter_context(nc.sbuf_tensor("s_tri", [128, 128], BF16))
    s_hmap = es.enter_context(nc.sbuf_tensor("s_hmap", [128, 128], BF16))
    s_ones = es.enter_context(nc.sbuf_tensor("s_ones", [128, 128], F32))
    s_onesc = es.enter_context(nc.sbuf_tensor("s_onesc", [128, 1], F32))
    s_id16 = es.enter_context(nc.sbuf_tensor("s_id16", [16, 16], F32))
    s_eps = es.enter_context(nc.sbuf_tensor("s_eps", [128, 1], F32))
    s_magic = es.enter_context(nc.sbuf_tensor("s_magic", [128, 1], F32))
    wsc = es.enter_context(nc.sbuf_tensor("wsc", [128, 8], F32))
    dsh_all = es.enter_context(nc.sbuf_tensor("dsh_all", [128, 64], F32))
    w11 = es.enter_context(nc.sbuf_tensor("w11", [1, 4], F32))

    SD = {}
    for name in ("dX dW dS sTx sEv sAc sQk sS sEx sMk sPV sRc sBc sNm "
                 "sWo sGp sF sFd sFa sOut").split():
        SD[name] = Sem(es, nc, name)
    dXb = [Sem(es, nc, "dXb%d" % i) for i in range(2)]
    dWq = [Sem(es, nc, "dWq%d" % i) for i in range(2)]
    dWt = [Sem(es, nc, "dWt%d" % i) for i in range(2)]
    dXn = [Sem(es, nc, "dXn%d" % i) for i in range(2)]
    dHq = [Sem(es, nc, "dHq%d" % i) for i in range(2)]
    dSc = [Sem(es, nc, "dSc%d" % i) for i in range(2)]
    dSg = [Sem(es, nc, "dSg%d" % i) for i in range(2)]
    dSx = [Sem(es, nc, "dSx%d" % i) for i in range(2)]
    dSt2 = [Sem(es, nc, "dSt2_%d" % i) for i in range(2)]
    dSh = Sem(es, nc, "dSh")
    dX, dW, dS = SD["dX"], SD["dW"], SD["dS"]
    sTx, sEv, sAc, sQk = SD["sTx"], SD["sEv"], SD["sAc"], SD["sQk"]
    sS_, sEx, sMk, sPV = SD["sS"], SD["sEx"], SD["sMk"], SD["sPV"]
    sRc, sBc, sNm = SD["sRc"], SD["sBc"], SD["sNm"]
    sWo, sGp = SD["sWo"], SD["sGp"]
    sF, sFd, sFa, sOut = SD["sF"], SD["sFd"], SD["sFa"], SD["sOut"]
    ALL = (list(SD.values()) + dXb + dWq + dWt + dXn + dHq + dSc + dSg + dSx
           + dSt2 + [dSh])

    block = es.enter_context(nc.Block())

    def barrier():
        snaps = [(s.h, s.n) for s in ALL if s.n > 0]
        for attr in ("sync", "tensor", "vector", "scalar", "gpsimd"):
            def f(eng, snaps=snaps):
                for h, v in snaps:
                    eng.wait_ge(h, v)
            getattr(block, attr)(f)

    esA = ExitStack()

    def sbA(name, shape, dt):
        return esA.enter_context(nc.sbuf_tensor(name, shape, dt))

    xbuf = sbA("xbuf", [128, 2 * C], F32)
    wst = sbA("wst", [128, 2 * 1024], F32)
    wcb = sbA("wcb", [128, 2 * 1024], BF16)
    ovl = sbA("ovl", [128, 8192], BF16)
    wqs = ovl.ap()[:, 0:4096]
    wout_b = ovl.ap()[:, 0:8192]
    xfm = sbA("xfm", [128, 8 * T], BF16)
    attnO = xfm
    qfm = sbA("qfm", [128, 8 * T], BF16)
    kfm = sbA("kfm", [128, 8 * T], BF16)
    vsb = sbA("vsb", [128, NTB * NH * 65], BF16)
    pt = ovl.ap()[:, 0:2048]
    qsq = ovl.ap()[:, 0:2048]
    rcp_t = sbA("rcp_t", [128, 1024], F32)
    rcpa = rcp_t.ap()[64:65, 0:512]
    rcpb = rcp_t.ap()[64:65, 512:1024]
    bcsb = sbA("bcsb", [64, 1024], F32)
    stg = sbA("stg", [64, 2 * 512], BF16)
    biasq = sbA("biasq", [16, 4], F32)
    nrow = sbA("nrow", [1, 16], F32)
    biasP = sbA("biasP", [128, 24], F32)
    r1 = sbA("r1", [128, 2 * C], F32)
    stat = sbA("stat", [128, 16], F32)
    sqs = r1.ap()[:, 0:C]
    xnb = sbA("xnb", [128, 2 * C], F32)
    absacc = sbA("absacc", [128, 33], F32)
    tern = wcb.ap()[:, 0:2048]
    terT = ovl.ap()[:, 4096:6144]

    def ph0():
        @block.sync
        def _(sy):
            dW.dma(sy.dma_start(out=s_idf[:], in_=c_idf))
            dW.dma(sy.dma_start(out=s_idb[:], in_=c_idb))
            dW.dma(sy.dma_start(out=s_tri[:], in_=c_tri))
            dW.dma(sy.dma_start(out=s_hmap[:], in_=c_hmap))
            dW.dma(sy.dma_start(out=s_ones[:], in_=c_ones))
            dW.dma(sy.dma_start(out=s_id16[:], in_=c_id16))
        cw = dW.n

        @block.vector
        def _(v):
            sEv.inc(v.memset(s_eps[:], 1e-5))
            sEv.inc(v.memset(s_magic[:], MAGIC))
            sEv.inc(v.memset(s_onesc[:], 1.0))
            sEv.inc(v.memset(
                vsb.ap().rearrange("p (g c) -> p g c", c=65)[:, :, 64:65], 1.0))

        jobs = [("qkv", t, h) for t in range(8) for h in range(3)]
        jobs += [("out", t, 0) for t in range(8)]
        NJ = len(jobs)
        ld_v = {}
        ev0 = sEv.n
        ev_v = [ev0 + i + 1 for i in range(NJ)]
        st_v = {}

        @block.sync
        def _(sy):
            for i, (tag, t, h) in enumerate(jobs):
                if i >= 2:
                    sy.wait_ge(sEv.h, ev_v[i - 2])
                b = wst[:, (i % 2) * 1024:(i % 2) * 1024 + 1024]
                if tag == "qkv":
                    src = wqkv[t * 128:(t + 1) * 128, h * 1024:(h + 1) * 1024]
                else:
                    src = wout[t * 128:(t + 1) * 128, :]
                dWt[i % 2].dma(sy.dma_start(out=b, in_=src))
                ld_v[i] = dWt[i % 2].n

        @block.vector
        def _(v):
            for i, (tag, t, h) in enumerate(jobs):
                v.wait_ge(dWt[i % 2].h, ld_v[i])
                b = wst[:, (i % 2) * 1024:(i % 2) * 1024 + 1024]
                if i >= 2 and (i - 2) in st_v:
                    v.wait_ge(dSc[i % 2].h, st_v[i - 2])
                dst = wcb[:, (i % 2) * 1024:(i % 2) * 1024 + 1024]
                sEv.inc(v.tensor_copy(dst, b))

        @block.gpsimd
        def _(g):
            for i, (tag, t, h) in enumerate(jobs):
                g.wait_ge(sEv.h, ev_v[i])
                src = wcb[:, (i % 2) * 1024:(i % 2) * 1024 + 1024]
                if tag == "qkv":
                    dst = wqkv_bh[t * 128:(t + 1) * 128, h * 1024:(h + 1) * 1024]
                else:
                    dst = wout_bh[t * 128:(t + 1) * 128, :]
                dSc[i % 2].dma(g.dma_start(out=dst, in_=src))
                st_v[i] = dSc[i % 2].n

    def a1(s):
        psc = ExitStack()
        ptx = psc.enter_context(nc.psum_tensor("ptx%d" % s, [128, 1024], F32))
        tx0 = sTx.n
        tx_v = [tx0 + i + 1 for i in range(NTB)]
        ev0 = sEv.n
        ev_v = [ev0 + i + 1 for i in range(NTB)]
        ld_v = {}

        @block.sync
        def _(sy):
            for tb in range(NTB):
                if tb >= 2:
                    sy.wait_ge(sTx.h, tx_v[tb - 2])
                dXb[tb % 2].dma(sy.dma_start(
                    out=xbuf[:, (tb % 2) * C:(tb % 2) * C + C],
                    in_=xin[s * T + tb * 128: s * T + (tb + 1) * 128, :]))
                ld_v[tb] = dXb[tb % 2].n

        @block.tensor
        def _(te):
            for tb in range(NTB):
                te.wait_ge(dXb[tb % 2].h, ld_v[tb])
                if tb >= 1:
                    te.wait_ge(sEv.h, ev_v[tb - 1])
                for cc in range(NCC):
                    ins = te.transpose(
                        ptx[:, cc * 128:(cc + 1) * 128],
                        xbuf[:, (tb % 2) * C + cc * 128:(tb % 2) * C + (cc + 1) * 128],
                        s_idf[:])
                    if cc == 7:
                        sTx.inc(ins)

        @block.vector
        def _(v):
            for tb in range(NTB):
                v.wait_ge(sTx.h, tx_v[tb])
                sEv.inc(v.tensor_copy(
                    xfm.ap().rearrange("p (g t) -> p g t", t=T)[:, :, tb * 128:(tb + 1) * 128],
                    ptx.ap().rearrange("p (g c) -> p g c", c=128)))
        assert sTx.n == tx_v[-1] and sEv.n == ev_v[-1]
        psc.close()

    def a2(s):
        psc = ExitStack()
        pqk = [psc.enter_context(nc.psum_tensor("pqk%d_%d" % (s, i), [128, 512], F32))
               for i in range(2)]
        pv = [psc.enter_context(nc.psum_tensor("pv%d_%d" % (s, i), [128, 512], F32))
              for i in range(2)]

        qk0 = sQk.n
        qk_v = {}
        n = qk0
        for of in range(16):
            for tc in range(4):
                n += 1
                qk_v[(of, tc)] = n
        v_v = {}
        for nv in range(2):
            for tb in range(NTB):
                n += 1
                v_v[(nv, tb)] = n
        ac0 = sAc.n
        ac_v = {}
        n = ac0
        for of in range(16):
            for tc in range(4):
                n += 1
                ac_v[(of, tc)] = n
        ev0 = sEv.n
        evv_v = {}
        n = ev0
        for nv in range(2):
            for tb in range(NTB):
                n += 1
                evv_v[(nv, tb)] = n
        wl_v = {}

        @block.sync
        def _(sy):
            for of in range(16):
                if of >= 2:
                    sy.wait_ge(sQk.h, qk_v[(of - 2, 3)])
                dWq[of % 2].dma(sy.dma_start(
                    out=wqs.rearrange("p (b f) -> p b f", b=2)[:, of % 2, 0:1024],
                    in_=wqkv_bh.rearrange("(g p) f -> p g f", p=128)[:, :, of * 128:(of + 1) * 128]))
                wl_v[of] = dWq[of % 2].n
            for nv in range(2):
                if nv == 0:
                    sy.wait_ge(sQk.h, qk_v[(14, 3)])
                    sy.wait_ge(sQk.h, qk_v[(15, 3)])
                else:
                    sy.wait_ge(sQk.h, v_v[(0, NTB - 1)])
                for hf in range(2):
                    dWq[hf].dma(sy.dma_start(
                        out=wqs.rearrange("p (b f) -> p b f", b=2)[:, hf, :],
                        in_=wqkv_bh.rearrange("(g p) f -> p g f", p=128)[:, hf * 4:(hf + 1) * 4, 2048 + nv * 512:2048 + (nv + 1) * 512]))
                wl_v[16 + nv] = (dWq[0].n, dWq[1].n)

        @block.tensor
        def _(te):
            for of in range(16):
                te.wait_ge(dWq[of % 2].h, wl_v[of])
                for tc in range(4):
                    m = of * 4 + tc
                    if m >= 2:
                        of2, tc2 = divmod(m - 2, 4)
                        te.wait_ge(sAc.h, ac_v[(of2, tc2)])
                    for cc in range(NCC):
                        ins = te.matmul(
                            pqk[m % 2][:],
                            wqs[:, (of % 2) * 2048 + cc * 128:(of % 2) * 2048 + (cc + 1) * 128],
                            xfm[:, cc * T + tc * 512:cc * T + (tc + 1) * 512],
                            start=(cc == 0), stop=(cc == 7))
                        if cc == 7:
                            sQk.inc(ins)
            for nv in range(2):
                te.wait_ge(dWq[0].h, wl_v[16 + nv][0])
                te.wait_ge(dWq[1].h, wl_v[16 + nv][1])
                for tb in range(NTB):
                    m = nv * NTB + tb
                    if m >= 2:
                        nv2, tb2 = divmod(m - 2, NTB)
                        te.wait_ge(sEv.h, evv_v[(nv2, tb2)])
                    for cc in range(NCC):
                        ins = te.matmul(
                            pv[m % 2][:],
                            xfm[:, cc * T + tb * 128:cc * T + (tb + 1) * 128],
                            wqs[:, (cc // 4) * 2048 + (cc % 4) * 512:(cc // 4) * 2048 + (cc % 4) * 512 + 512],
                            start=(cc == 0), stop=(cc == 7))
                        if cc == 7:
                            sQk.inc(ins)

        @block.scalar
        def _(sc):
            for of in range(16):
                dstt = qfm if of < 8 else kfm
                ch = of % 8
                for tc in range(4):
                    sc.wait_ge(sQk.h, qk_v[(of, tc)])
                    sAc.inc(sc.activation(
                        dstt[:, ch * T + tc * 512:ch * T + (tc + 1) * 512],
                        pqk[(of * 4 + tc) % 2][:], AF.Identity))

        @block.vector
        def _(v):
            for nv in range(2):
                for tb in range(NTB):
                    v.wait_ge(sQk.h, v_v[(nv, tb)])
                    sEv.inc(v.tensor_copy(
                        vsb.ap().rearrange("p (k h c) -> p k h c", h=NH, c=65)[:, tb, nv * 8:(nv + 1) * 8, 0:64],
                        pv[(nv * NTB + tb) % 2].ap().rearrange("p (h c) -> p h c", c=64)))
        assert sQk.n == v_v[(1, NTB - 1)]
        psc.close()

    def a3(s):
        psc = ExitStack()
        pn = psc.enter_context(nc.psum_tensor("pn%d" % s, [16, 2048], F32))
        pr = psc.enter_context(nc.psum_tensor("pr%d" % s, [1, 16], F32))
        pbb = psc.enter_context(nc.psum_tensor("pbb%d" % s, [128, 16], F32))

        base = sEv.n
        sq_v = {}
        for cc in range(NCC):
            sq_v[(0, cc)] = base + cc + 1
        redq_v = base + 9
        for cc in range(NCC):
            sq_v[(1, cc)] = base + 10 + cc
        redk_v = base + 18
        mul_v = base + 19
        negb_v = base + 20
        nrow_v = base + 21
        bcp_v = base + 22
        min_v = base + 23
        t0 = sTx.n
        mm_v = {}
        for cc in range(NCC):
            mm_v[(0, cc)] = t0 + cc + 1
            mm_v[(1, cc)] = t0 + 9 + cc
        pr_v = t0 + 17
        pbb_v = t0 + 18
        sqrt_v = sAc.n + 1

        @block.vector
        def _(v):
            for cc in range(NCC):
                if cc >= 1:
                    v.wait_ge(sTx.h, mm_v[(0, cc - 1)])
                sEv.inc(v.tensor_mul(qsq[:], qfm[:, cc * T:(cc + 1) * T],
                                     qfm[:, cc * T:(cc + 1) * T]))
            v.wait_ge(sTx.h, mm_v[(0, NCC - 1)])
            sEv.inc(v.tensor_reduce(biasq[:, 0:1], pn[0:16, :], AX.X, ALU.max))
            for cc in range(NCC):
                if cc >= 1:
                    v.wait_ge(sTx.h, mm_v[(1, cc - 1)])
                sEv.inc(v.tensor_mul(qsq[:], kfm[:, cc * T:(cc + 1) * T],
                                     kfm[:, cc * T:(cc + 1) * T]))
            v.wait_ge(sTx.h, mm_v[(1, NCC - 1)])
            sEv.inc(v.tensor_reduce(biasq[:, 1:2], pn[0:16, :], AX.X, ALU.max))
            v.drain()
            sEv.inc(v.tensor_mul(biasq[:, 2:3], biasq[:, 0:1], biasq[:, 1:2]))
            v.drain()
            v.wait_ge(sAc.h, sqrt_v)
            sEv.inc(v.tensor_scalar_mul(biasq[:, 0:1], biasq[:, 3:4], -0.125))
            v.drain()
            v.wait_ge(sTx.h, pr_v)
            sEv.inc(v.tensor_copy(nrow[:], pr[:]))
            v.drain()
            v.wait_ge(sTx.h, pbb_v)
            sEv.inc(v.tensor_copy(biasP[:, 0:16], pbb[:]))
            v.drain()
            sEv.inc(v.tensor_tensor(biasP[:, 16:24], biasP[:, 0:16:2],
                                    biasP[:, 1:16:2], ALU.min))
            v.drain()

        @block.tensor
        def _(te):
            for w in range(2):
                if w == 1:
                    te.wait_ge(sEv.h, redq_v)
                for cc in range(NCC):
                    te.wait_ge(sEv.h, sq_v[(w, cc)])
                    for nq in range(4):
                        ins = te.matmul(pn[:, nq * 512:(nq + 1) * 512],
                                        s_hmap[:, cc * 16:(cc + 1) * 16],
                                        qsq[:, nq * 512:(nq + 1) * 512],
                                        start=(cc == 0), stop=(cc == 7))
                        if nq == 3:
                            sTx.inc(ins)
            te.wait_ge(sEv.h, negb_v)
            sTx.inc(te.matmul(pr[:], biasq[:, 0:1], s_id16[:], start=True, stop=True))
            te.wait_ge(sEv.h, nrow_v)
            sTx.inc(te.matmul(pbb[:], s_ones[0:1, :], nrow[:], start=True, stop=True))

        @block.scalar
        def _(sc):
            sc.wait_ge(sEv.h, mul_v)
            sAc.inc(sc.activation(biasq[:, 3:4], biasq[:, 2:3], AF.Sqrt))

        assert sEv.n == min_v, (sEv.n, min_v)
        assert sTx.n == pbb_v
        psc.close()

    def a4(s):
        psc = ExitStack()
        pS = psc.enter_context(nc.psum_tensor("pS%d" % s, [128, 1024], F32))
        pO = [psc.enter_context(nc.psum_tensor("pO%d_%d" % (s, i), [65, 1024], F32))
              for i in range(2)]
        pBc = psc.enter_context(nc.psum_tensor("pBc%d" % s, [64, 1024], F32))

        QCS = A4_QCS
        events = []
        for pair in range(A4_PAIRS):
            for qc in range(QCS):
                for kb in range(4 * qc + 4):
                    events.append((pair, qc, kb))
        E = len(events)
        eidx = {ev: i for i, ev in enumerate(events)}
        part = [kb >= 4 * qc for (pair, qc, kb) in events]
        jof = [kb - 4 * qc for (pair, qc, kb) in events]

        exp0, s0_, pv0, mk0 = sEx.n, sS_.n, sPV.n, sMk.n
        exp_v = [exp0 + i + 1 for i in range(E)]
        s_v = [s0_ + i + 1 for i in range(E)]
        pv_v = [pv0 + i + 1 for i in range(E)]
        mk_v = {}
        n = mk0
        for i in range(E):
            if part[i]:
                n += 1
                mk_v[i] = n
        NQG = A4_PAIRS * QCS
        nm0, rc0, bc0 = sNm.n, sRc.n, sBc.n
        nm_v = [nm0 + i + 1 for i in range(NQG)]
        rc_v = [rc0 + i + 1 for i in range(NQG)]
        bc_v = [bc0 + i + 1 for i in range(NQG)]
        stg_v = {}
        stg_base = [dSg[0].n, dSg[1].n]
        for i in range(NQG):
            stg_v[i] = stg_base[i % 2] + 16 * (i // 2 + 1)
        bias_rdy = sEv.n
        ev0 = sEv.n
        evac_v = [ev0 + i + 1 for i in range(NQG)]

        def cols(i):
            return (jof[i] * 128 if part[i] else 0, 512)

        def emit_pv(te, ip):
            pair, qc, kb = events[ip]
            gq = pair * QCS + qc
            c0, c1 = cols(ip)
            te.wait_ge(sEx.h, exp_v[ip])
            if part[ip]:
                te.wait_ge(sMk.h, mk_v[ip])
            if kb == 0 and gq >= 2:
                te.wait_ge(sNm.h, nm_v[gq - 2])
            for hh in range(2):
                h = pair * 2 + hh
                ins = te.matmul(
                    pO[gq % 2][0:65, hh * 512 + c0:hh * 512 + c1],
                    vsb.ap().rearrange("p (k h c) -> p k h c", h=NH, c=65)[:, kb, h, :],
                    pt[:, (ip % 2) * 1024 + hh * 512 + c0:(ip % 2) * 1024 + hh * 512 + c1],
                    start=(kb == 0), stop=(kb == 4 * qc + 3),
                    skip_group_check=True)
                if hh == 1:
                    sPV.inc(ins)

        @block.tensor
        def _(te):
            pend = []

            def do_bc(gq):
                te.wait_ge(sRc.h, rc_v[gq])
                if gq >= 1:
                    te.wait_ge(sNm.h, nm_v[gq - 1])
                te.matmul(pBc[:, 0:512], s_ones[64:65, 0:64], rcpa,
                          start=True, stop=True)
                sBc.inc(te.matmul(pBc[:, 512:1024], s_ones[64:65, 0:64], rcpb,
                                  start=True, stop=True))

            for i, (pair, qc, kb) in enumerate(events):
                c0, c1 = cols(i)
                if i >= 1:
                    te.wait_ge(sEx.h, exp_v[i - 1])
                for hh in range(2):
                    ins = te.matmul(
                        pS[:, hh * 512 + c0:hh * 512 + c1],
                        kfm[hh * 64:(hh + 1) * 64, pair * T + kb * 128:pair * T + (kb + 1) * 128],
                        qfm[hh * 64:(hh + 1) * 64, pair * T + qc * 512 + c0:pair * T + qc * 512 + c1],
                        start=True, stop=True, tile_position=(hh * 64, 0),
                        skip_group_check=True)
                    if hh == 1:
                        sS_.inc(ins)
                while pend:
                    do_bc(pend.pop(0))
                if kb >= 1:
                    emit_pv(te, i - 1)
                if kb == 4 * qc + 3:
                    emit_pv(te, i)
                    pend.append(pair * QCS + qc)
            while pend:
                do_bc(pend.pop(0))

        @block.scalar
        def _(sc):
            sc.wait_ge(sEv.h, bias_rdy)
            for i, (pair, qc, kb) in enumerate(events):
                c0, c1 = cols(i)
                sc.wait_ge(sS_.h, s_v[i])
                if i >= 2:
                    sc.wait_ge(sPV.h, pv_v[i - 2])
                sEx.inc(sc.activation(
                    pt.rearrange("p (b h c) -> p b h c", b=2, c=512)[:, i % 2, :, c0:c1],
                    pS.ap().rearrange("p (h c) -> p h c", c=512)[:, :, c0:c1],
                    AF.Exp, bias=biasP[:, 16 + pair:17 + pair], scale=0.125))

        @block.gpsimd
        def _(g):
            for i, (pair, qc, kb) in enumerate(events):
                if part[i]:
                    j = jof[i]
                    g.wait_ge(sEx.h, exp_v[i])
                    sMk.inc(g.tensor_tensor(
                        pt.rearrange("p (b h c) -> p b h c", b=2, c=512)[:, i % 2, :, j * 128:(j + 1) * 128],
                        pt.rearrange("p (b h c) -> p b h c", b=2, c=512)[:, i % 2, :, j * 128:(j + 1) * 128],
                        s_tri.ap().unsqueeze(1).broadcast_to([128, 2, 128]),
                        ALU.mult))
                if kb == 4 * qc + 3:
                    gq = pair * QCS + qc
                    g.wait_ge(sNm.h, nm_v[gq])
                    dSg[gq % 2].dma(g.dma_start(
                        out=attnO[64:128, pair * T + qc * 512:pair * T + (qc + 1) * 512],
                        in_=stg[:, (gq % 2) * 512:(gq % 2) * 512 + 512]))

        @block.vector
        def _(v):
            for gq in range(NQG):
                pair, qc = divmod(gq, QCS)
                i_last = eidx[(pair, qc, 4 * qc + 3)]
                v.wait_ge(sPV.h, pv_v[i_last])
                if gq >= 1:
                    v.wait_ge(sBc.h, bc_v[gq - 1])
                v.reciprocal(rcpa, pO[gq % 2][64:65, 0:512])
                sRc.inc(v.reciprocal(rcpb, pO[gq % 2][64:65, 512:1024]))
                v.wait_ge(sBc.h, bc_v[gq])
                sEv.inc(v.tensor_copy(bcsb[:], pBc[:]))
                v.drain()
                if gq >= 2:
                    v.wait_ge(dSg[gq % 2].h, stg_v[gq - 2])
                v.tensor_mul(attnO[0:64, pair * T + qc * 512:pair * T + (qc + 1) * 512],
                             pO[gq % 2][0:64, 0:512], bcsb[0:64, 0:512])
                sNm.inc(v.tensor_mul(stg[:, (gq % 2) * 512:(gq % 2) * 512 + 512],
                                     pO[gq % 2][0:64, 512:1024], bcsb[0:64, 512:1024]))

        assert sEx.n == exp_v[-1] and sPV.n == pv_v[-1]
        assert sNm.n == nm_v[-1] and sBc.n == bc_v[-1]
        psc.close()

    def a5(s):
        psc = ExitStack()
        pw = [psc.enter_context(nc.psum_tensor("pw%d_%d" % (s, i), [128, 1024], F32))
              for i in range(2)]
        psq = psc.enter_context(nc.psum_tensor("psq%d" % s, [128, 1024], F32))
        wo0 = sWo.n
        wo_v = [wo0 + i + 1 for i in range(NTB)]
        ac0 = sAc.n
        ac_sumsq = [ac0 + 2 * i + 1 for i in range(NTB)]
        ac_sqrt = [ac0 + 2 * i + 2 for i in range(NTB)]
        ev0 = sEv.n
        stt_v, var_v, xn_v = [], [], []
        n = ev0
        for tb in range(NTB):
            n += 1
            stt_v.append(n)
            n += 1
            n += 4
            var_v.append(n)
            n += 1
            n += 1
            xn_v.append(n)
        xnd_v = {}
        xnd_base = [dSx[0].n, dSx[1].n]
        for i in range(NTB):
            xnd_v[i] = xnd_base[i % 2] + 16 * (i // 2 + 1)
        ld_v = {}

        @block.sync
        def _(sy):
            dW.dma(sy.dma_start(
                out=wout_b.rearrange("p (g f) -> p g f", g=8),
                in_=wout_bh.rearrange("(g p) f -> p g f", p=128)))
            for tb in range(NTB):
                if tb >= 2:
                    sy.wait_ge(sEv.h, xn_v[tb - 2])
                dXb[tb % 2].dma(sy.dma_start(
                    out=xbuf[:, (tb % 2) * C:(tb % 2) * C + C],
                    in_=xin[s * T + tb * 128: s * T + (tb + 1) * 128, :]))
                ld_v[tb] = dXb[tb % 2].n
        wob_ld = dW.n

        @block.tensor
        def _(te):
            te.wait_ge(dW.h, wob_ld)
            for tb in range(NTB):
                if tb >= 2:
                    te.wait_ge(sEv.h, stt_v[tb - 2])
                for cc in range(NCC):
                    for n2 in range(2):
                        ins = te.matmul(
                            pw[tb % 2][:, n2 * 512:(n2 + 1) * 512],
                            attnO[:, cc * T + tb * 128:cc * T + (tb + 1) * 128],
                            wout_b[:, cc * C + n2 * 512:cc * C + (n2 + 1) * 512],
                            start=(cc == 0), stop=(cc == 7))
                        if cc == 7 and n2 == 1:
                            sWo.inc(ins)

        @block.scalar
        def _(sc):
            for tb in range(NTB):
                o = (tb % 2) * 8
                sc.wait_ge(sEv.h, stt_v[tb])
                sAc.inc(sc.activation(psq[:], r1[:, (tb % 2) * C:(tb % 2) * C + C],
                                      AF.Square, accum_out=stat[:, o + 1:o + 2]))
                sc.wait_ge(sEv.h, var_v[tb])
                sAc.inc(sc.activation(stat[:, o + 6:o + 7], stat[:, o + 5:o + 6],
                                      AF.Sqrt, bias=s_eps[:, 0:1]))

        @block.vector
        def _(v):
            for tb in range(NTB):
                o = (tb % 2) * 8
                r1t = r1[:, (tb % 2) * C:(tb % 2) * C + C]
                v.wait_ge(sWo.h, wo_v[tb])
                v.wait_ge(dXb[tb % 2].h, ld_v[tb])
                if tb >= 2:
                    v.wait_ge(sAc.h, ac_sumsq[tb - 2])
                sEv.inc(v.scalar_tensor_tensor(
                    r1t, pw[tb % 2][:], 1.0, xbuf[:, (tb % 2) * C:(tb % 2) * C + C],
                    ALU.mult, ALU.add))
                v.drain()
                sEv.inc(v.reduce_sum(stat[:, o:o + 1], r1t, AX.X))
                v.drain()
                v.wait_ge(sAc.h, ac_sumsq[tb])
                sEv.inc(v.tensor_scalar_mul(stat[:, o + 2:o + 3], stat[:, o:o + 1], 1.0 / C))
                sEv.inc(v.tensor_scalar_mul(stat[:, o + 3:o + 4], stat[:, o + 1:o + 2], 1.0 / C))
                v.drain()
                sEv.inc(v.tensor_mul(stat[:, o + 4:o + 5], stat[:, o + 2:o + 3], stat[:, o + 2:o + 3]))
                v.drain()
                sEv.inc(v.tensor_sub(stat[:, o + 5:o + 6], stat[:, o + 3:o + 4], stat[:, o + 4:o + 5]))
                v.drain()
                v.wait_ge(sAc.h, ac_sqrt[tb])
                sEv.inc(v.reciprocal(stat[:, o + 7:o + 8], stat[:, o + 6:o + 7]))
                v.drain()
                if tb >= 2:
                    v.wait_ge(dSx[tb % 2].h, xnd_v[tb - 2])
                sEv.inc(v.tensor_scalar(xnb[:, (tb % 2) * C:(tb % 2) * C + C], r1t,
                                        stat[:, o + 2:o + 3], stat[:, o + 7:o + 8],
                                        ALU.subtract, ALU.mult))

        @block.gpsimd
        def _(g):
            for tb in range(NTB):
                g.wait_ge(sEv.h, xn_v[tb])
                dSx[tb % 2].dma(g.dma_start(
                    out=xn_hbm[s * T + tb * 128: s * T + (tb + 1) * 128, :],
                    in_=xnb[:, (tb % 2) * C:(tb % 2) * C + C]))

        assert sEv.n == xn_v[-1]
        psc.close()

    def wtern(wi, wsrc, wdst_T, R, Ccols):
        tiles = [(r, c) for r in range(R // 128) for c in range(Ccols // 1024)]
        NT = len(tiles)
        psc = ExitStack()
        p1 = psc.enter_context(nc.psum_tensor("pww%d" % wi, [128, 1024], BF16))
        pm = psc.enter_context(nc.psum_tensor("pwm%d" % wi, [1, 1], F32))
        pb = psc.enter_context(nc.psum_tensor("pwb%d" % wi, [128, 2], F32))

        ev0 = sEv.n
        red_v = [ev0 + i + 1 for i in range(NT)]
        redall_v = ev0 + NT + 1
        m1_v = redall_v + 1
        m2_v = redall_v + 2
        sc_v = redall_v + 3
        tx0 = sTx.n
        pm_v = tx0 + 1
        pb_v = tx0 + 2
        ld1, ld2 = {}, {}

        @block.sync
        def _(sy):
            for i, (r, c) in enumerate(tiles):
                if i >= 2:
                    sy.wait_ge(sEv.h, red_v[i - 2])
                dWt[i % 2].dma(sy.dma_start(
                    out=wst[:, (i % 2) * 1024:(i % 2) * 1024 + 1024],
                    in_=wsrc[r * 128:(r + 1) * 128, c * 1024:(c + 1) * 1024]))
                ld1[i] = dWt[i % 2].n

        @block.vector
        def _(v):
            for i in range(NT):
                v.wait_ge(dWt[i % 2].h, ld1[i])
                sEv.inc(v.reduce_sum(absacc[:, i:i + 1],
                                     wst[:, (i % 2) * 1024:(i % 2) * 1024 + 1024],
                                     AX.X, apply_absolute_value=True))
            v.drain()
            sEv.inc(v.reduce_sum(absacc[:, 32:33], absacc[:, 0:NT], AX.X))
            v.drain()
            v.wait_ge(sTx.h, pm_v)
            sEv.inc(v.tensor_scalar(w11[:, 0:1], pm[:], 1.0 / (R * Ccols), 1e-5,
                                    ALU.mult, ALU.max))
            v.drain()
            sEv.inc(v.reciprocal(w11[:, 1:2], w11[:, 0:1]))
            v.drain()
            v.wait_ge(sTx.h, pb_v)
            sEv.inc(v.tensor_copy(wsc[:, wi * 2:wi * 2 + 2], pb[:]))

        @block.tensor
        def _(te):
            te.wait_ge(sEv.h, redall_v)
            sTx.inc(te.matmul(pm[:], s_onesc[:], absacc[:, 32:33],
                              start=True, stop=True))
            te.wait_ge(sEv.h, m2_v)
            sTx.inc(te.matmul(pb[:], s_ones[0:1, :], w11[0:1, 0:2],
                              start=True, stop=True))

        ev1 = sEv.n
        q3_v, evT_v = [], []
        n = ev1
        for i in range(NT):
            n += 3
            q3_v.append(n)
            n += 1
            evT_v.append(n)
        tx1 = sTx.n
        tx_v = [tx1 + i + 1 for i in range(NT)]
        st_v = {}
        st_base = [dSt2[0].n, dSt2[1].n]
        for i in range(NT):
            st_v[i] = st_base[i % 2] + 16 * (i // 2 + 1)

        @block.sync
        def _(sy):
            for i, (r, c) in enumerate(tiles):
                if i < 2:
                    sy.wait_ge(sEv.h, red_v[NT - 1])
                if i >= 2:
                    sy.wait_ge(sEv.h, q3_v[i - 2])
                dWt[i % 2].dma(sy.dma_start(
                    out=wst[:, (i % 2) * 1024:(i % 2) * 1024 + 1024],
                    in_=wsrc[r * 128:(r + 1) * 128, c * 1024:(c + 1) * 1024]))
                ld2[i] = dWt[i % 2].n

        @block.vector
        def _(v):
            v.wait_ge(sEv.h, sc_v)
            for i in range(NT):
                v.wait_ge(dWt[i % 2].h, ld2[i])
                src = wst[:, (i % 2) * 1024:(i % 2) * 1024 + 1024]
                sEv.inc(v.tensor_scalar(sqs[:], src, wsc[:, wi * 2 + 1:wi * 2 + 2],
                                        -1.0, ALU.mult, ALU.max))
                v.drain()
                sEv.inc(v.tensor_scalar(sqs[:], sqs[:], 1.0, MAGIC,
                                        ALU.min, ALU.add))
                v.drain()
                if i >= 2:
                    v.wait_ge(sTx.h, tx_v[i - 2])
                sEv.inc(v.tensor_scalar_add(
                    tern[:, (i % 2) * 1024:(i % 2) * 1024 + 1024], sqs[:], -MAGIC))
                v.wait_ge(sTx.h, tx_v[i])
                if i >= 2:
                    v.wait_ge(dSt2[i % 2].h, st_v[i - 2])
                sEv.inc(v.tensor_copy(
                    terT[:, (i % 2) * 1024:(i % 2) * 1024 + 1024], p1.ap()))

        @block.gpsimd
        def _(g):
            for i, (r, c) in enumerate(tiles):
                g.wait_ge(sEv.h, evT_v[i])
                dSt2[i % 2].dma(g.dma_start(
                    out=wdst_T.rearrange("(g p) f -> p g f", p=128)[:, c * 8:(c + 1) * 8, r * 128:(r + 1) * 128],
                    in_=terT.rearrange("p (b g f) -> p b g f", b=2, g=8)[:, i % 2, :, :]))

        @block.tensor
        def _(te):
            for i in range(NT):
                te.wait_ge(sEv.h, q3_v[i])
                if i >= 1:
                    te.wait_ge(sEv.h, evT_v[i - 1])
                for cc in range(8):
                    ins = te.transpose(
                        p1[:, cc * 128:(cc + 1) * 128],
                        tern[:, (i % 2) * 1024 + cc * 128:(i % 2) * 1024 + (cc + 1) * 128],
                        s_idb[:])
                    if cc == 7:
                        sTx.inc(ins)

        assert sEv.n == evT_v[NT - 1] and sTx.n == tx_v[-1]
        psc.close()

    ph_list = [lambda: ph0()]
    for s in range(NSEQ):
        for f in (a1, a2, a3, a4, a5):
            ph_list.append(lambda f=f, s=s: f(s))
    ph_list.append(lambda: wtern(0, wg, wgT_h, INNER, C))
    ph_list.append(lambda: wtern(1, wu, wuT_h, INNER, C))
    ph_list.append(lambda: wtern(2, wd, wdT_h, C, INNER))
    nph = 0
    for f in ph_list:
        if nph >= UPTO:
            break
        f()
        barrier()
        nph += 1
    esA.close()

    esF = ExitStack()

    def sbF(name, shape, dt):
        return esF.enter_context(nc.sbuf_tensor(name, shape, dt))

    wgT = sbF("wgT", [128, 8 * INNER], BF16)
    wuT = sbF("wuT", [128, 8 * INNER], BF16)
    xnf = sbF("xnf", [128, 2 * C], F32)
    hbf = sbF("hbf", [128, INNER], F32)
    thf = sbF("thf", [128, INNER], F32)
    fst = sbF("fst", [128, 32], F32)
    esF1 = ExitStack()
    hqb = esF1.enter_context(nc.sbuf_tensor("hqb", [128, INNER], BF16))
    xqb = esF1.enter_context(nc.sbuf_tensor("xqb", [128, 2 * C], BF16))
    xqT = esF1.enter_context(nc.sbuf_tensor("xqT", [128, 2 * C], BF16))
    sgf = esF1.enter_context(nc.sbuf_tensor("sgf", [128, 512], F32))
    hqi_yb = [None, None]

    def f1():
        psc = ExitStack()
        pT = psc.enter_context(nc.psum_tensor("pT1", [128, 1024], BF16))
        pG = [psc.enter_context(nc.psum_tensor("pG%d" % i, [128, 512], F32)) for i in range(2)]
        pU = [psc.enter_context(nc.psum_tensor("pU%d" % i, [128, 512], F32)) for i in range(2)]

        fd0 = sFd.n
        xq_v, evT_v, hml_v, hs_v, hq_v = [], [], [], [], []
        n = fd0
        for tb in range(NT2):
            n += 7
            n += 2
            xq_v.append(n)
            n += 1
            evT_v.append(n)
            hml = []
            for oc in range(8):
                n += 1
                hml.append(n)
            hml_v.append(hml)
            n += 4
            hs_v.append(n)
            n += 1
            hq_v.append(n)
        fa0 = sFa.n
        silu_v = [[fa0 + tb * 9 + oc + 1 for oc in range(8)] for tb in range(NT2)]
        th_v = [fa0 + tb * 9 + 9 for tb in range(NT2)]
        f0 = sF.n
        tx_v, g_v, u_v = [], [], []
        n = f0
        for tb in range(NT2):
            n += 1
            tx_v.append(n)
            gl, ul = [], []
            for oc in range(8):
                n += 1
                gl.append(n)
                n += 1
                ul.append(n)
            g_v.append(gl)
            u_v.append(ul)
        hd0 = dSh.n
        hqd_v = [hd0 + 16 * (tb + 1) for tb in range(NT2)]
        xn_ld = {}

        @block.sync
        def _(sy):
            for half in range(2):
                dW.dma(sy.dma_start(
                    out=wgT.ap().rearrange("p (g f) -> p g f", g=8)[:, half * 4:(half + 1) * 4, :],
                    in_=wgT_h.rearrange("(g p) f -> p g f", p=128)[:, half * 4:(half + 1) * 4, :]))
                dW.dma(sy.dma_start(
                    out=wuT.ap().rearrange("p (g f) -> p g f", g=8)[:, half * 4:(half + 1) * 4, :],
                    in_=wuT_h.rearrange("(g p) f -> p g f", p=128)[:, half * 4:(half + 1) * 4, :]))
            w_ld = dW.n
            for tb in range(NT2):
                if tb >= 2:
                    sy.wait_ge(sFd.h, xq_v[tb - 2])
                dXn[tb % 2].dma(sy.dma_start(out=xnf[:, (tb % 2) * C:(tb % 2) * C + C],
                                    in_=xn_hbm[tb * 128:(tb + 1) * 128, :]))
                xn_ld[tb] = dXn[tb % 2].n

        w_ld = dW.n

        @block.vector
        def _(v):
            for tb in range(NT2):
                o = (tb % 2) * 16
                xt = xnf[:, (tb % 2) * C:(tb % 2) * C + C]
                v.wait_ge(dXn[tb % 2].h, xn_ld[tb])
                if tb >= 2:
                    v.wait_ge(sFa.h, th_v[tb - 2])
                sFd.inc(v.tensor_reduce(fst[:, o:o + 1], xt, AX.X, ALU.max,
                                        apply_absolute_value=True))
                v.drain()
                sFd.inc(v.tensor_scalar_max(fst[:, o:o + 1], fst[:, o:o + 1], 1e-5))
                v.drain()
                sFd.inc(v.reciprocal(fst[:, o + 1:o + 2], fst[:, o:o + 1]))
                v.drain()
                sFd.inc(v.tensor_scalar_mul(fst[:, o + 2:o + 3], fst[:, o + 1:o + 2], 127.0))
                sFd.inc(v.tensor_scalar_mul(fst[:, o + 3:o + 4], fst[:, o:o + 1], 1.0 / 127.0))
                v.drain()
                sFd.inc(v.tensor_mul(fst[:, o + 4:o + 5], fst[:, o + 3:o + 4], wsc[:, 0:1]))
                sFd.inc(v.tensor_mul(fst[:, o + 5:o + 6], fst[:, o + 3:o + 4], wsc[:, 2:3]))
                v.drain()
                sFd.inc(v.tensor_scalar(thf[:, 0:C], xt, fst[:, o + 2:o + 3], MAGIC,
                                        ALU.mult, ALU.add))
                v.drain()
                if tb >= 2:
                    v.wait_ge(sF.h, tx_v[tb - 2])
                sFd.inc(v.tensor_scalar_add(xqb[:, (tb % 2) * C:(tb % 2) * C + C],
                                            thf[:, 0:C], -MAGIC))
                v.wait_ge(sF.h, tx_v[tb])
                if tb >= 2:
                    v.wait_ge(sF.h, u_v[tb - 2][7])
                sFd.inc(v.tensor_copy(xqT[:, (tb % 2) * C:(tb % 2) * C + C], pT.ap()))
                for oc in range(8):
                    v.wait_ge(sF.h, u_v[tb][oc])
                    v.wait_ge(sFa.h, silu_v[tb][oc])
                    sFd.inc(v.scalar_tensor_tensor(
                        hbf[:, oc * 512:(oc + 1) * 512], pU[oc % 2][:],
                        fst[:, o + 5:o + 6], sgf[:], ALU.mult, ALU.mult))
                v.drain()
                sFd.inc(v.tensor_reduce(fst[:, o + 6:o + 7], hbf[:], AX.X, ALU.max,
                                        apply_absolute_value=True))
                v.drain()
                sFd.inc(v.reciprocal(fst[:, o + 7:o + 8], fst[:, o + 6:o + 7]))
                v.drain()
                sFd.inc(v.tensor_scalar_mul(fst[:, o + 8:o + 9], fst[:, o + 7:o + 8], 127.0))
                sFd.inc(v.tensor_scalar_mul(dsh_all[:, tb:tb + 1], fst[:, o + 6:o + 7], 1.0 / 127.0))
                v.drain()
                v.wait_ge(sFa.h, th_v[tb])
                if tb >= 1:
                    v.wait_ge(dSh.h, hqd_v[tb - 1])
                sFd.inc(v.tensor_scalar_add(hqb[:], thf[:], -MAGIC))

        @block.tensor
        def _(te):
            te.wait_ge(dW.h, w_ld)
            for tb in range(NT2):
                te.wait_ge(sFd.h, xq_v[tb])
                if tb >= 1:
                    te.wait_ge(sFd.h, evT_v[tb - 1])
                for cc in range(8):
                    ins = te.transpose(pT[:, cc * 128:(cc + 1) * 128],
                                       xqb[:, (tb % 2) * C + cc * 128:(tb % 2) * C + (cc + 1) * 128],
                                       s_idb[:])
                    if cc == 7:
                        sF.inc(ins)
                for oc in range(8):
                    gc = tb * 8 + oc
                    te.wait_ge(sFd.h, evT_v[tb])
                    if gc >= 2:
                        tb2, oc2 = divmod(gc - 2, 8)
                        te.wait_ge(sFa.h, silu_v[tb2][oc2])
                    for cc in range(8):
                        ins = te.matmul(pG[oc % 2][:],
                                        xqT[:, (tb % 2) * C + cc * 128:(tb % 2) * C + (cc + 1) * 128],
                                        wgT[:, cc * INNER + oc * 512:cc * INNER + (oc + 1) * 512],
                                        start=(cc == 0), stop=(cc == 7))
                        if cc == 7:
                            sF.inc(ins)
                    if gc >= 2:
                        tb2, oc2 = divmod(gc - 2, 8)
                        te.wait_ge(sFd.h, hml_v[tb2][oc2])
                    for cc in range(8):
                        ins = te.matmul(pU[oc % 2][:],
                                        xqT[:, (tb % 2) * C + cc * 128:(tb % 2) * C + (cc + 1) * 128],
                                        wuT[:, cc * INNER + oc * 512:cc * INNER + (oc + 1) * 512],
                                        start=(cc == 0), stop=(cc == 7))
                        if cc == 7:
                            sF.inc(ins)

        @block.scalar
        def _(sc):
            for tb in range(NT2):
                o = (tb % 2) * 16
                for oc in range(8):
                    gc = tb * 8 + oc
                    sc.wait_ge(sF.h, g_v[tb][oc])
                    if gc >= 1:
                        tb2, oc2 = divmod(gc - 1, 8)
                        sc.wait_ge(sFd.h, hml_v[tb2][oc2])
                    sFa.inc(sc.activation(sgf[:], pG[oc % 2][:], AF.Silu,
                                          scale=fst[:, o + 4:o + 5]))
                sc.wait_ge(sFd.h, hs_v[tb])
                sFa.inc(sc.activation(thf[:], hbf[:], AF.Identity,
                                      scale=fst[:, o + 8:o + 9], bias=s_magic[:, 0:1]))

        @block.gpsimd
        def _(g):
            for tb in range(NT2):
                g.wait_ge(sFd.h, hq_v[tb])
                dSh.dma(g.dma_start(out=hq_hbm[tb * 128:(tb + 1) * 128, :],
                                   in_=hqb[:]))

        assert sFd.n == hq_v[-1], (sFd.n, hq_v[-1])
        assert sF.n == u_v[-1][7]
        assert sFa.n == th_v[-1]
        psc.close()

    def f2():
        hqi_, yb_ = hqi_yb[0], hqi_yb[1]
        psc = ExitStack()
        pH = [psc.enter_context(nc.psum_tensor("pH%d" % i, [128, 1024], BF16)) for i in range(2)]
        pD = [psc.enter_context(nc.psum_tensor("pD%d" % i, [128, 1024], F32)) for i in range(2)]
        psq2 = psc.enter_context(nc.psum_tensor("psq2", [128, 1024], F32))
        wdT = wgT
        hqT = wuT.ap()[:, 0:2 * INNER]

        fd0 = sFd.n
        dshmd_v = fd0 + 1
        evh_v, ln_v = [], []
        n = fd0 + 1
        for tb in range(NT2):
            el = []
            for grp in range(4):
                n += 1
                el.append(n)
            evh_v.append(el)
            n += 8
            ln_v.append(n)
        fa0 = sFa.n
        sumsq_v = [fa0 + 2 * tb + 1 for tb in range(NT2)]
        sqrt_v = [fa0 + 2 * tb + 2 for tb in range(NT2)]
        f0 = sF.n
        txg_v, dmm_v = [], []
        n = f0
        for tb in range(NT2):
            tl = []
            for grp in range(4):
                n += 1
                tl.append(n)
            txg_v.append(tl)
            dl = []
            for n2 in range(2):
                n += 1
                dl.append(n)
            dmm_v.append(dl)
        dS0 = sOut.n
        y_v = [dS0 + 16 * (tb + 1) for tb in range(NT2)]
        hq_ld, xn_ld = {}, {}

        @block.vector
        def _(v):
            sFd.inc(v.tensor_scalar_mul(dsh_all[:, 32:64], dsh_all[:, 0:32],
                                        wsc[:, 4:5]))

        @block.sync
        def _(sy):
            for q in range(4):
                dW.dma(sy.dma_start(
                    out=wdT.ap().rearrange("p (g f) -> p g f", g=32)[:, q * 8:(q + 1) * 8, 0:C],
                    in_=wdT_h.rearrange("(g p) f -> p g f", p=128)[:, q * 8:(q + 1) * 8, :]))
            for tb in range(NT2):
                if tb >= 2:
                    sy.wait_ge(sF.h, dmm_v[tb - 2][1])
                dHq[tb % 2].dma(sy.dma_start(out=hqi_[:, (tb % 2) * INNER:(tb % 2) * INNER + INNER],
                                    in_=hq_hbm[tb * 128:(tb + 1) * 128, :]))
                hq_ld[tb] = dHq[tb % 2].n
                if tb >= 2:
                    sy.wait_ge(sFd.h, ln_v[tb - 2])
                dXn[tb % 2].dma(sy.dma_start(out=xnf[:, (tb % 2) * C:(tb % 2) * C + C],
                                    in_=xn_hbm[tb * 128:(tb + 1) * 128, :]))
                xn_ld[tb] = dXn[tb % 2].n

        wd_ld = dW.n

        @block.tensor
        def _(te):
            te.wait_ge(dW.h, wd_ld)
            te.wait_ge(sFd.h, dshmd_v)
            for tb in range(NT2):
                te.wait_ge(dHq[tb % 2].h, hq_ld[tb])
                for grp in range(4):
                    gi = tb * 4 + grp
                    if gi >= 2:
                        tbp, gp_ = divmod(gi - 2, 4)
                        te.wait_ge(sFd.h, evh_v[tbp][gp_])
                    for k in range(8):
                        ic = grp * 8 + k
                        ins = te.transpose(
                            pH[grp % 2][:, k * 128:(k + 1) * 128],
                            hqi_[:, (tb % 2) * INNER + ic * 128:(tb % 2) * INNER + (ic + 1) * 128],
                            s_idb[:])
                        if k == 7:
                            sF.inc(ins)
                for n2 in range(2):
                    if tb >= 2:
                        te.wait_ge(sFd.h, ln_v[tb - 2])
                    for ic in range(32):
                        if ic == 0 and n2 == 0:
                            te.wait_ge(sFd.h, evh_v[tb][3])
                        ins = te.matmul(
                            pD[tb % 2][:, n2 * 512:(n2 + 1) * 512],
                            hqT[:, (tb % 2) * INNER + ic * 128:(tb % 2) * INNER + (ic + 1) * 128],
                            wdT[:, ic * 1024 + n2 * 512:ic * 1024 + (n2 + 1) * 512],
                            start=(ic == 0), stop=(ic == 31))
                        if ic == 31:
                            sF.inc(ins)

        @block.vector
        def _(v):
            for tb in range(NT2):
                o = (tb % 2) * 16
                for grp in range(4):
                    v.wait_ge(sF.h, txg_v[tb][grp])
                    if tb >= 2:
                        v.wait_ge(sF.h, dmm_v[tb - 2][1])
                    sFd.inc(v.tensor_copy(
                        hqT.rearrange("p (b g f) -> p b g f", b=2, g=32)[:, tb % 2, grp * 8:(grp + 1) * 8, :],
                        pH[grp % 2].ap().rearrange("p (g f) -> p g f", g=8)))
                v.wait_ge(sF.h, dmm_v[tb][1])
                v.wait_ge(dXn[tb % 2].h, xn_ld[tb])
                if tb >= 1:
                    v.wait_ge(sFa.h, sumsq_v[tb - 1])
                r2t = thf[:, 0:C]
                sFd.inc(v.scalar_tensor_tensor(
                    r2t, pD[tb % 2][:], dsh_all[:, 32 + tb:33 + tb],
                    xnf[:, (tb % 2) * C:(tb % 2) * C + C], ALU.mult, ALU.add))
                v.drain()
                sFd.inc(v.reduce_sum(fst[:, o:o + 1], r2t, AX.X))
                v.drain()
                v.wait_ge(sFa.h, sumsq_v[tb])
                sFd.inc(v.tensor_scalar_mul(fst[:, o + 2:o + 3], fst[:, o:o + 1], 1.0 / C))
                sFd.inc(v.tensor_scalar_mul(fst[:, o + 3:o + 4], fst[:, o + 1:o + 2], 1.0 / C))
                v.drain()
                sFd.inc(v.tensor_mul(fst[:, o + 4:o + 5], fst[:, o + 2:o + 3], fst[:, o + 2:o + 3]))
                v.drain()
                sFd.inc(v.tensor_sub(fst[:, o + 5:o + 6], fst[:, o + 3:o + 4], fst[:, o + 4:o + 5]))
                v.drain()
                v.wait_ge(sFa.h, sqrt_v[tb])
                sFd.inc(v.reciprocal(fst[:, o + 7:o + 8], fst[:, o + 6:o + 7]))
                v.drain()
                if tb >= 1:
                    v.wait_ge(sOut.h, y_v[tb - 1])
                sFd.inc(v.tensor_scalar(yb_[:], r2t,
                                        fst[:, o + 2:o + 3], fst[:, o + 7:o + 8],
                                        ALU.subtract, ALU.mult))

        @block.scalar
        def _(sc):
            for tb in range(NT2):
                o = (tb % 2) * 16
                sc.wait_ge(sFd.h, ln_v[tb] - 7)
                sFa.inc(sc.activation(psq2[:], thf[:, 0:C], AF.Square,
                                      accum_out=fst[:, o + 1:o + 2]))
                sc.wait_ge(sFd.h, ln_v[tb] - 2)
                sFa.inc(sc.activation(fst[:, o + 6:o + 7], fst[:, o + 5:o + 6],
                                      AF.Sqrt, bias=s_eps[:, 0:1]))

        @block.gpsimd
        def _(g):
            for tb in range(NT2):
                g.wait_ge(sFd.h, ln_v[tb])
                sOut.dma(g.dma_start(out=y[tb * 128:(tb + 1) * 128, :],
                                     in_=yb_[:]))

        assert sFd.n == ln_v[-1]
        assert sOut.n == y_v[-1]
        psc.close()

    if nph < UPTO:
        f1()
        barrier()
        nph += 1
    esF1.close()
    hqi_yb = [esF.enter_context(nc.sbuf_tensor("hqi", [128, 2 * INNER], BF16)),
              esF.enter_context(nc.sbuf_tensor("yb", [128, C], F32))]
    if nph < UPTO:
        f2()
        barrier()
        nph += 1

    @block.sync
    def _(sy):
        sy.wait_ge(sOut.h, sOut.n)

    esF.close()
    es.close()
    return nc


def make_consts():
    idf = np.eye(128, dtype=np.float32)
    idb = np.eye(128, dtype=bfnp)
    tri = (np.arange(128)[:, None] <= np.arange(128)[None, :]).astype(bfnp)
    hmap = np.zeros((128, 128), dtype=bfnp)
    for cc in range(8):
        for p in range(128):
            hmap[p, cc * 16 + (2 * cc + p // 64)] = 1
    ones = np.ones((128, 128), dtype=np.float32)
    id16 = np.eye(16, dtype=np.float32)
    return dict(c_idf=idf, c_idb=idb, c_tri=tri, c_hmap=hmap, c_ones=ones,
                c_id16=id16)


def kernel(L0_attn_x, L0_attn_w_qkv, L0_attn_w_out, L0_ffn_w_gate, L0_ffn_w_up,
           L0_ffn_w_down):
    x = np.ascontiguousarray(np.asarray(L0_attn_x, dtype=np.float32))
    B = x.shape[0]
    ncore = 8
    per = B // ncore
    nc = build_program()
    base = dict(
        wqkv=np.ascontiguousarray(np.asarray(L0_attn_w_qkv, dtype=np.float32)),
        wout=np.ascontiguousarray(np.asarray(L0_attn_w_out, dtype=np.float32)),
        wg=np.ascontiguousarray(np.asarray(L0_ffn_w_gate, dtype=np.float32)),
        wu=np.ascontiguousarray(np.asarray(L0_ffn_w_up, dtype=np.float32)),
        wd=np.ascontiguousarray(np.asarray(L0_ffn_w_down, dtype=np.float32)),
        **make_consts())
    in_maps = []
    for c in range(ncore):
        m = dict(base)
        m["xin"] = np.ascontiguousarray(x[c * per:(c + 1) * per].reshape(per * T, C))
        in_maps.append(m)
    res = run_bass_kernel_spmd(nc, in_maps, core_ids=list(range(ncore)))
    outs = [r["y"].reshape(per, T, C) for r in res.results]
    return np.concatenate(outs, axis=0).astype(np.float32)


# version 04
# speedup vs baseline: 1.3955x; 1.1134x over previous
import numpy as np
import ml_dtypes
import concourse.bass as bass
import concourse.mybir as mybir
from concourse.bass_utils import run_bass_kernel_spmd
from contextlib import ExitStack

F32 = mybir.dt.float32
BF16 = mybir.dt.bfloat16
AF = mybir.ActivationFunctionType
ALU = mybir.AluOpType
AX = mybir.AxisListType
bfnp = ml_dtypes.bfloat16

MAGIC = float(np.float32(1.5 * 2 ** 23))
T, C, NH, DK, NSEQ, INNER = 2048, 1024, 16, 64, 2, 4096
NTB = T // 128
NQC = 4
NCC = C // 128
NPAIR = NH // 2
NT2 = NSEQ * NTB

DEBUG_OUT = False
UPTO = 99
A4_PAIRS = 8
A4_QCS = 4


class Sem:
    def __init__(self, es, nc, name):
        self.h = es.enter_context(nc.semaphore(name))
        self.n = 0

    def inc(self, instr, k=1):
        instr.then_inc(self.h, k)
        self.n += k
        return instr

    def dma(self, instr):
        return self.inc(instr, 16)


def build_program():
    nc = bass.Bass("TRN2", target_bir_lowering=False)

    xin = nc.dram_tensor("xin", [NSEQ * T, C], F32, kind="ExternalInput").ap()
    wqkv = nc.dram_tensor("wqkv", [C, 3 * C], F32, kind="ExternalInput").ap()
    wout = nc.dram_tensor("wout", [C, C], F32, kind="ExternalInput").ap()
    wg = nc.dram_tensor("wg", [INNER, C], F32, kind="ExternalInput").ap()
    wu = nc.dram_tensor("wu", [INNER, C], F32, kind="ExternalInput").ap()
    wd = nc.dram_tensor("wd", [C, INNER], F32, kind="ExternalInput").ap()
    c_idf = nc.dram_tensor("c_idf", [128, 128], F32, kind="ExternalInput").ap()
    c_idb = nc.dram_tensor("c_idb", [128, 128], BF16, kind="ExternalInput").ap()
    c_tri = nc.dram_tensor("c_tri", [128, 128], BF16, kind="ExternalInput").ap()
    c_hmap = nc.dram_tensor("c_hmap", [128, 128], BF16, kind="ExternalInput").ap()
    c_ones = nc.dram_tensor("c_ones", [128, 128], F32, kind="ExternalInput").ap()
    c_id16 = nc.dram_tensor("c_id16", [16, 16], F32, kind="ExternalInput").ap()
    y = nc.dram_tensor("y", [NSEQ * T, C], F32, kind="ExternalOutput").ap()

    KIND = "ExternalOutput" if DEBUG_OUT else "Internal"
    xn_hbm = nc.dram_tensor("xn_hbm", [NSEQ * T, C], F32, kind=KIND).ap()
    hq_hbm = nc.dram_tensor("hq_hbm", [NSEQ * T, INNER], BF16, kind=KIND).ap()
    wqkv_bh = nc.dram_tensor("wqkv_bh", [C, 3 * C], BF16, kind="Internal").ap()
    wout_bh = nc.dram_tensor("wout_bh", [C, C], BF16, kind="Internal").ap()
    wgT_h = nc.dram_tensor("wgT_h", [C, INNER], BF16, kind=KIND).ap()
    wuT_h = nc.dram_tensor("wuT_h", [C, INNER], BF16, kind=KIND).ap()
    wdT_h = nc.dram_tensor("wdT_h", [INNER, C], BF16, kind=KIND).ap()

    es = ExitStack()

    s_idf = es.enter_context(nc.sbuf_tensor("s_idf", [128, 128], F32))
    s_idb = es.enter_context(nc.sbuf_tensor("s_idb", [128, 128], BF16))
    s_tri = es.enter_context(nc.sbuf_tensor("s_tri", [128, 128], BF16))
    s_hmap = es.enter_context(nc.sbuf_tensor("s_hmap", [128, 128], BF16))
    s_ones = es.enter_context(nc.sbuf_tensor("s_ones", [128, 128], F32))
    s_onesc = es.enter_context(nc.sbuf_tensor("s_onesc", [128, 1], F32))
    s_id16 = es.enter_context(nc.sbuf_tensor("s_id16", [16, 16], F32))
    s_onesb = es.enter_context(nc.sbuf_tensor("s_onesb", [128, 64], BF16))
    s_eps = es.enter_context(nc.sbuf_tensor("s_eps", [128, 1], F32))
    s_magic = es.enter_context(nc.sbuf_tensor("s_magic", [128, 1], F32))
    wsc = es.enter_context(nc.sbuf_tensor("wsc", [128, 8], F32))
    dsh_all = es.enter_context(nc.sbuf_tensor("dsh_all", [128, 64], F32))
    w11 = es.enter_context(nc.sbuf_tensor("w11", [1, 4], F32))

    SD = {}
    for name in ("dX dW dS sTx sEv sAc sQk sS sEx sMk sPV sRc sBc sNm "
                 "sWo sGp sF sFd sFa sOut").split():
        SD[name] = Sem(es, nc, name)
    dXb = [Sem(es, nc, "dXb%d" % i) for i in range(2)]
    dWq = [Sem(es, nc, "dWq%d" % i) for i in range(2)]
    dWt = [Sem(es, nc, "dWt%d" % i) for i in range(2)]
    dXn = [Sem(es, nc, "dXn%d" % i) for i in range(2)]
    dHq = [Sem(es, nc, "dHq%d" % i) for i in range(2)]
    dSc = [Sem(es, nc, "dSc%d" % i) for i in range(2)]
    dSg = [Sem(es, nc, "dSg%d" % i) for i in range(2)]
    dSx = [Sem(es, nc, "dSx%d" % i) for i in range(2)]
    dSt2 = [Sem(es, nc, "dSt2_%d" % i) for i in range(2)]
    dSh = Sem(es, nc, "dSh")
    dX, dW, dS = SD["dX"], SD["dW"], SD["dS"]
    sTx, sEv, sAc, sQk = SD["sTx"], SD["sEv"], SD["sAc"], SD["sQk"]
    sS_, sEx, sMk, sPV = SD["sS"], SD["sEx"], SD["sMk"], SD["sPV"]
    sRc, sBc, sNm = SD["sRc"], SD["sBc"], SD["sNm"]
    sWo, sGp = SD["sWo"], SD["sGp"]
    sF, sFd, sFa, sOut = SD["sF"], SD["sFd"], SD["sFa"], SD["sOut"]
    ALL = (list(SD.values()) + dXb + dWq + dWt + dXn + dHq + dSc + dSg + dSx
           + dSt2 + [dSh])

    block = es.enter_context(nc.Block())

    def barrier():
        snaps = [(s.h, s.n) for s in ALL if s.n > 0]
        for attr in ("sync", "tensor", "vector", "scalar", "gpsimd"):
            def f(eng, snaps=snaps):
                for h, v in snaps:
                    eng.wait_ge(h, v)
            getattr(block, attr)(f)

    esA = ExitStack()

    def sbA(name, shape, dt):
        return esA.enter_context(nc.sbuf_tensor(name, shape, dt))

    xbuf = sbA("xbuf", [128, 2 * C], F32)
    wst = sbA("wst", [128, 2 * 1024], F32)
    wcb = sbA("wcb", [128, 2 * 1024], BF16)
    ovl = sbA("ovl", [128, 8192], BF16)
    wqs = ovl.ap()[:, 0:4096]
    wout_b = ovl.ap()[:, 0:8192]
    xfm = sbA("xfm", [128, 8 * T], BF16)
    attnO = xfm
    qfm = sbA("qfm", [128, 8 * T], BF16)
    kfm = sbA("kfm", [128, 8 * T], BF16)
    vsb = sbA("vsb", [128, NTB * NH * 65], BF16)
    pt = ovl.ap()[:, 0:2048]
    qsq = ovl.ap()[:, 0:2048]
    rcp_t = sbA("rcp_t", [128, 1024], F32)
    rcpf = rcp_t.ap()[64:65, 0:1024]
    rcp_b16 = sbA("rcp_b16", [128, 1024], BF16)
    rcpa = rcp_b16.ap()[64:65, 0:512]
    rcpb = rcp_b16.ap()[64:65, 512:1024]
    bcsb = sbA("bcsb", [64, 1024], F32)
    stg = sbA("stg", [64, 2 * 512], BF16)
    biasq = sbA("biasq", [16, 4], F32)
    nrow = sbA("nrow", [1, 16], F32)
    biasP = sbA("biasP", [128, 24], F32)
    r1 = sbA("r1", [128, 2 * C], F32)
    stat = sbA("stat", [128, 16], F32)
    sqs = r1.ap()[:, 0:C]
    xnb = sbA("xnb", [128, 2 * C], F32)
    absacc = sbA("absacc", [128, 33], F32)
    tern = wcb.ap()[:, 0:2048]
    terT = ovl.ap()[:, 4096:6144]

    def ph0():
        @block.sync
        def _(sy):
            dW.dma(sy.dma_start(out=s_idf[:], in_=c_idf))
            dW.dma(sy.dma_start(out=s_idb[:], in_=c_idb))
            dW.dma(sy.dma_start(out=s_tri[:], in_=c_tri))
            dW.dma(sy.dma_start(out=s_hmap[:], in_=c_hmap))
            dW.dma(sy.dma_start(out=s_ones[:], in_=c_ones))
            dW.dma(sy.dma_start(out=s_id16[:], in_=c_id16))
        cw = dW.n

        @block.vector
        def _(v):
            sEv.inc(v.memset(s_onesb[:], 1.0))
            sEv.inc(v.memset(s_eps[:], 1e-5))
            sEv.inc(v.memset(s_magic[:], MAGIC))
            sEv.inc(v.memset(s_onesc[:], 1.0))
            sEv.inc(v.memset(
                vsb.ap().rearrange("p (g c) -> p g c", c=65)[:, :, 64:65], 1.0))

        jobs = [("qkv", t, h) for t in range(8) for h in range(3)]
        jobs += [("out", t, 0) for t in range(8)]
        NJ = len(jobs)
        ld_v = {}
        ev0 = sEv.n
        ev_v = [ev0 + i + 1 for i in range(NJ)]
        st_v = {}

        @block.sync
        def _(sy):
            for i, (tag, t, h) in enumerate(jobs):
                if i >= 2:
                    sy.wait_ge(sEv.h, ev_v[i - 2])
                b = wst[:, (i % 2) * 1024:(i % 2) * 1024 + 1024]
                if tag == "qkv":
                    src = wqkv[t * 128:(t + 1) * 128, h * 1024:(h + 1) * 1024]
                else:
                    src = wout[t * 128:(t + 1) * 128, :]
                dWt[i % 2].dma(sy.dma_start(out=b, in_=src))
                ld_v[i] = dWt[i % 2].n

        @block.vector
        def _(v):
            for i, (tag, t, h) in enumerate(jobs):
                v.wait_ge(dWt[i % 2].h, ld_v[i])
                b = wst[:, (i % 2) * 1024:(i % 2) * 1024 + 1024]
                if i >= 2 and (i - 2) in st_v:
                    v.wait_ge(dSc[i % 2].h, st_v[i - 2])
                dst = wcb[:, (i % 2) * 1024:(i % 2) * 1024 + 1024]
                sEv.inc(v.tensor_copy(dst, b))

        @block.gpsimd
        def _(g):
            for i, (tag, t, h) in enumerate(jobs):
                g.wait_ge(sEv.h, ev_v[i])
                src = wcb[:, (i % 2) * 1024:(i % 2) * 1024 + 1024]
                if tag == "qkv":
                    dst = wqkv_bh[t * 128:(t + 1) * 128, h * 1024:(h + 1) * 1024]
                else:
                    dst = wout_bh[t * 128:(t + 1) * 128, :]
                dSc[i % 2].dma(g.dma_start(out=dst, in_=src))
                st_v[i] = dSc[i % 2].n

    def a1(s):
        psc = ExitStack()
        ptx = psc.enter_context(nc.psum_tensor("ptx%d" % s, [128, 1024], F32))
        tx0 = sTx.n
        tx_v = [tx0 + i + 1 for i in range(NTB)]
        ev0 = sEv.n
        ev_v = [ev0 + i + 1 for i in range(NTB)]
        ld_v = {}

        @block.sync
        def _(sy):
            for tb in range(NTB):
                if tb >= 2:
                    sy.wait_ge(sTx.h, tx_v[tb - 2])
                dXb[tb % 2].dma(sy.dma_start(
                    out=xbuf[:, (tb % 2) * C:(tb % 2) * C + C],
                    in_=xin[s * T + tb * 128: s * T + (tb + 1) * 128, :]))
                ld_v[tb] = dXb[tb % 2].n

        @block.tensor
        def _(te):
            for tb in range(NTB):
                te.wait_ge(dXb[tb % 2].h, ld_v[tb])
                if tb >= 1:
                    te.wait_ge(sEv.h, ev_v[tb - 1])
                for cc in range(NCC):
                    ins = te.transpose(
                        ptx[:, cc * 128:(cc + 1) * 128],
                        xbuf[:, (tb % 2) * C + cc * 128:(tb % 2) * C + (cc + 1) * 128],
                        s_idf[:])
                    if cc == 7:
                        sTx.inc(ins)

        @block.vector
        def _(v):
            for tb in range(NTB):
                v.wait_ge(sTx.h, tx_v[tb])
                sEv.inc(v.tensor_copy(
                    xfm.ap().rearrange("p (g t) -> p g t", t=T)[:, :, tb * 128:(tb + 1) * 128],
                    ptx.ap().rearrange("p (g c) -> p g c", c=128)))
        assert sTx.n == tx_v[-1] and sEv.n == ev_v[-1]
        psc.close()

    def a2(s):
        psc = ExitStack()
        pqk = [psc.enter_context(nc.psum_tensor("pqk%d_%d" % (s, i), [128, 512], F32))
               for i in range(2)]
        pv = [psc.enter_context(nc.psum_tensor("pv%d_%d" % (s, i), [128, 512], F32))
              for i in range(2)]

        qk0 = sQk.n
        qk_v = {}
        n = qk0
        for of in range(16):
            for tc in range(4):
                n += 1
                qk_v[(of, tc)] = n
        v_v = {}
        for nv in range(2):
            for tb in range(NTB):
                n += 1
                v_v[(nv, tb)] = n
        ac0 = sAc.n
        ac_v = {}
        n = ac0
        for of in range(16):
            for tc in range(4):
                n += 1
                ac_v[(of, tc)] = n
        ev0 = sEv.n
        evv_v = {}
        n = ev0
        for nv in range(2):
            for tb in range(NTB):
                n += 1
                evv_v[(nv, tb)] = n
        wl_v = {}

        @block.sync
        def _(sy):
            for of in range(16):
                if of >= 2:
                    sy.wait_ge(sQk.h, qk_v[(of - 2, 3)])
                dWq[of % 2].dma(sy.dma_start(
                    out=wqs.rearrange("p (b f) -> p b f", b=2)[:, of % 2, 0:1024],
                    in_=wqkv_bh.rearrange("(g p) f -> p g f", p=128)[:, :, of * 128:(of + 1) * 128]))
                wl_v[of] = dWq[of % 2].n
            for nv in range(2):
                if nv == 0:
                    sy.wait_ge(sQk.h, qk_v[(14, 3)])
                    sy.wait_ge(sQk.h, qk_v[(15, 3)])
                else:
                    sy.wait_ge(sQk.h, v_v[(0, NTB - 1)])
                for hf in range(2):
                    dWq[hf].dma(sy.dma_start(
                        out=wqs.rearrange("p (b f) -> p b f", b=2)[:, hf, :],
                        in_=wqkv_bh.rearrange("(g p) f -> p g f", p=128)[:, hf * 4:(hf + 1) * 4, 2048 + nv * 512:2048 + (nv + 1) * 512]))
                wl_v[16 + nv] = (dWq[0].n, dWq[1].n)

        @block.tensor
        def _(te):
            for of in range(16):
                te.wait_ge(dWq[of % 2].h, wl_v[of])
                for tc in range(4):
                    m = of * 4 + tc
                    if m >= 2:
                        of2, tc2 = divmod(m - 2, 4)
                        te.wait_ge(sAc.h, ac_v[(of2, tc2)])
                    for cc in range(NCC):
                        ins = te.matmul(
                            pqk[m % 2][:],
                            wqs[:, (of % 2) * 2048 + cc * 128:(of % 2) * 2048 + (cc + 1) * 128],
                            xfm[:, cc * T + tc * 512:cc * T + (tc + 1) * 512],
                            start=(cc == 0), stop=(cc == 7))
                        if cc == 7:
                            sQk.inc(ins)
            for nv in range(2):
                te.wait_ge(dWq[0].h, wl_v[16 + nv][0])
                te.wait_ge(dWq[1].h, wl_v[16 + nv][1])
                for tb in range(NTB):
                    m = nv * NTB + tb
                    if m >= 2:
                        nv2, tb2 = divmod(m - 2, NTB)
                        te.wait_ge(sEv.h, evv_v[(nv2, tb2)])
                    for cc in range(NCC):
                        ins = te.matmul(
                            pv[m % 2][:],
                            xfm[:, cc * T + tb * 128:cc * T + (tb + 1) * 128],
                            wqs[:, (cc // 4) * 2048 + (cc % 4) * 512:(cc // 4) * 2048 + (cc % 4) * 512 + 512],
                            start=(cc == 0), stop=(cc == 7))
                        if cc == 7:
                            sQk.inc(ins)

        @block.scalar
        def _(sc):
            for of in range(16):
                dstt = qfm if of < 8 else kfm
                ch = of % 8
                for tc in range(4):
                    sc.wait_ge(sQk.h, qk_v[(of, tc)])
                    sAc.inc(sc.activation(
                        dstt[:, ch * T + tc * 512:ch * T + (tc + 1) * 512],
                        pqk[(of * 4 + tc) % 2][:], AF.Identity))

        @block.vector
        def _(v):
            for nv in range(2):
                for tb in range(NTB):
                    v.wait_ge(sQk.h, v_v[(nv, tb)])
                    sEv.inc(v.tensor_copy(
                        vsb.ap().rearrange("p (k h c) -> p k h c", h=NH, c=65)[:, tb, nv * 8:(nv + 1) * 8, 0:64],
                        pv[(nv * NTB + tb) % 2].ap().rearrange("p (h c) -> p h c", c=64)))
        assert sQk.n == v_v[(1, NTB - 1)]
        psc.close()

    def a3(s):
        psc = ExitStack()
        pn = psc.enter_context(nc.psum_tensor("pn%d" % s, [16, 2048], F32))
        pr = psc.enter_context(nc.psum_tensor("pr%d" % s, [1, 16], F32))
        pbb = psc.enter_context(nc.psum_tensor("pbb%d" % s, [128, 16], F32))

        base = sEv.n
        sq_v = {}
        for cc in range(NCC):
            sq_v[(0, cc)] = base + cc + 1
        redq_v = base + 9
        for cc in range(NCC):
            sq_v[(1, cc)] = base + 10 + cc
        redk_v = base + 18
        mul_v = base + 19
        negb_v = base + 20
        nrow_v = base + 21
        bcp_v = base + 22
        min_v = base + 23
        t0 = sTx.n
        mm_v = {}
        for cc in range(NCC):
            mm_v[(0, cc)] = t0 + cc + 1
            mm_v[(1, cc)] = t0 + 9 + cc
        pr_v = t0 + 17
        pbb_v = t0 + 18
        sqrt_v = sAc.n + 1

        @block.vector
        def _(v):
            for cc in range(NCC):
                if cc >= 1:
                    v.wait_ge(sTx.h, mm_v[(0, cc - 1)])
                sEv.inc(v.tensor_mul(qsq[:], qfm[:, cc * T:(cc + 1) * T],
                                     qfm[:, cc * T:(cc + 1) * T]))
            v.wait_ge(sTx.h, mm_v[(0, NCC - 1)])
            sEv.inc(v.tensor_reduce(biasq[:, 0:1], pn[0:16, :], AX.X, ALU.max))
            for cc in range(NCC):
                if cc >= 1:
                    v.wait_ge(sTx.h, mm_v[(1, cc - 1)])
                sEv.inc(v.tensor_mul(qsq[:], kfm[:, cc * T:(cc + 1) * T],
                                     kfm[:, cc * T:(cc + 1) * T]))
            v.wait_ge(sTx.h, mm_v[(1, NCC - 1)])
            sEv.inc(v.tensor_reduce(biasq[:, 1:2], pn[0:16, :], AX.X, ALU.max))
            v.drain()
            sEv.inc(v.tensor_mul(biasq[:, 2:3], biasq[:, 0:1], biasq[:, 1:2]))
            v.drain()
            v.wait_ge(sAc.h, sqrt_v)
            sEv.inc(v.tensor_scalar_mul(biasq[:, 0:1], biasq[:, 3:4], -0.125))
            v.drain()
            v.wait_ge(sTx.h, pr_v)
            sEv.inc(v.tensor_copy(nrow[:], pr[:]))
            v.drain()
            v.wait_ge(sTx.h, pbb_v)
            sEv.inc(v.tensor_copy(biasP[:, 0:16], pbb[:]))
            v.drain()
            sEv.inc(v.tensor_tensor(biasP[:, 16:24], biasP[:, 0:16:2],
                                    biasP[:, 1:16:2], ALU.min))
            v.drain()

        @block.tensor
        def _(te):
            for w in range(2):
                if w == 1:
                    te.wait_ge(sEv.h, redq_v)
                for cc in range(NCC):
                    te.wait_ge(sEv.h, sq_v[(w, cc)])
                    for nq in range(4):
                        ins = te.matmul(pn[:, nq * 512:(nq + 1) * 512],
                                        s_hmap[:, cc * 16:(cc + 1) * 16],
                                        qsq[:, nq * 512:(nq + 1) * 512],
                                        start=(cc == 0), stop=(cc == 7))
                        if nq == 3:
                            sTx.inc(ins)
            te.wait_ge(sEv.h, negb_v)
            sTx.inc(te.matmul(pr[:], biasq[:, 0:1], s_id16[:], start=True, stop=True))
            te.wait_ge(sEv.h, nrow_v)
            sTx.inc(te.matmul(pbb[:], s_ones[0:1, :], nrow[:], start=True, stop=True))

        @block.scalar
        def _(sc):
            sc.wait_ge(sEv.h, mul_v)
            sAc.inc(sc.activation(biasq[:, 3:4], biasq[:, 2:3], AF.Sqrt))

        assert sEv.n == min_v, (sEv.n, min_v)
        assert sTx.n == pbb_v
        psc.close()

    def a4(s):
        psc = ExitStack()
        pS = [psc.enter_context(nc.psum_tensor("pS%d_%d" % (s, i), [128, 1024], F32))
              for i in range(2)]
        pO = [psc.enter_context(nc.psum_tensor("pO%d_%d" % (s, i), [65, 1024], F32))
              for i in range(2)]

        QCS = A4_QCS
        events = []
        for pair in range(A4_PAIRS):
            for qc in range(QCS):
                for kb in range(4 * qc + 4):
                    events.append((pair, qc, kb))
        E = len(events)
        eidx = {ev: i for i, ev in enumerate(events)}
        part = [kb >= 4 * qc for (pair, qc, kb) in events]
        jof = [kb - 4 * qc for (pair, qc, kb) in events]

        exp0, s0_, pv0 = sEx.n, sS_.n, sPV.n
        exp_v = [exp0 + i + 1 for i in range(E)]
        s_v = [s0_ + i + 1 for i in range(E)]
        pv_v = [pv0 + i + 1 for i in range(E)]
        NQG = A4_PAIRS * QCS
        nm0, rc0, bc0, ca0 = sNm.n, sRc.n, sBc.n, sAc.n
        nm_v = [nm0 + i + 1 for i in range(NQG)]
        rc_v = [rc0 + i + 1 for i in range(NQG)]
        bc_v = [bc0 + i + 1 for i in range(NQG)]
        cast_v = [ca0 + i + 1 for i in range(NQG)]
        stg_v = {}
        stg_base = [dSg[0].n, dSg[1].n]
        for i in range(NQG):
            stg_v[i] = stg_base[i % 2] + 16 * (i // 2 + 1)
        bias_rdy = sEv.n
        ev0 = sEv.n
        evac_v = [ev0 + i + 1 for i in range(NQG)]

        def cols(i):
            return (jof[i] * 128 if part[i] else 0, 512)

        elast = {}
        for gq in range(NQG):
            pr_, qc_ = divmod(gq, QCS)
            elast[gq] = eidx[(pr_, qc_, 4 * qc_ + 3)]

        def emit_pv(te, ip):
            pair, qc, kb = events[ip]
            gq = pair * QCS + qc
            c0, c1 = cols(ip)
            te.wait_ge(sEx.h, exp_v[ip])
            if kb == 0 and gq >= 2:
                te.wait_ge(sNm.h, nm_v[gq - 2])
            for hh in range(2):
                h = pair * 2 + hh
                ins = te.matmul(
                    pO[gq % 2][0:65, hh * 512 + c0:hh * 512 + c1],
                    vsb.ap().rearrange("p (k h c) -> p k h c", h=NH, c=65)[:, kb, h, :],
                    pt[:, (ip % 2) * 1024 + hh * 512 + c0:(ip % 2) * 1024 + hh * 512 + c1],
                    start=(kb == 0), stop=(kb == 4 * qc + 3),
                    skip_group_check=True)
                if hh == 1:
                    sPV.inc(ins)

        @block.tensor
        def _(te):
            pend = []

            def do_bc(gq):
                par = elast[gq] % 2
                te.wait_ge(sAc.h, cast_v[gq])
                te.matmul(pS[par][0:64, 0:512], s_onesb[64:65, :], rcpa,
                          start=True, stop=True, skip_group_check=True)
                sBc.inc(te.matmul(pS[par][0:64, 512:1024], s_onesb[64:65, :], rcpb,
                                  start=True, stop=True, skip_group_check=True))

            bc_dirty = {}
            for i, (pair, qc, kb) in enumerate(events):
                c0, c1 = cols(i)
                if i >= 2:
                    te.wait_ge(sEx.h, exp_v[i - 2])
                if (i % 2) in bc_dirty:
                    te.wait_ge(sEv.h, bc_dirty.pop(i % 2))
                for hh in range(2):
                    ins = te.matmul(
                        pS[i % 2][:, hh * 512 + c0:hh * 512 + c1],
                        kfm[hh * 64:(hh + 1) * 64, pair * T + kb * 128:pair * T + (kb + 1) * 128],
                        qfm[hh * 64:(hh + 1) * 64, pair * T + qc * 512 + c0:pair * T + qc * 512 + c1],
                        start=True, stop=not part[i], tile_position=(hh * 64, 0),
                        skip_group_check=True)
                    if hh == 1 and not part[i]:
                        sS_.inc(ins)
                if part[i]:
                    j = jof[i]
                    for hh in range(2):
                        ins = te.matmul(
                            pS[i % 2][:, hh * 512 + j * 128:hh * 512 + (j + 1) * 128],
                            s_idb[:], s_tri[:],
                            start=False, stop=True, skip_group_check=True)
                        if hh == 1:
                            sS_.inc(ins)
                while pend:
                    gqp = pend.pop(0)
                    do_bc(gqp)
                    bc_dirty[elast[gqp] % 2] = evac_v[gqp]
                if kb >= 1:
                    emit_pv(te, i - 1)
                if kb == 4 * qc + 3:
                    emit_pv(te, i)
                    pend.append(pair * QCS + qc)
            while pend:
                do_bc(pend.pop(0))

        @block.scalar
        def _(sc):
            sc.wait_ge(sEv.h, bias_rdy)
            for i, (pair, qc, kb) in enumerate(events):
                c0, c1 = cols(i)
                sc.wait_ge(sS_.h, s_v[i])
                if i >= 2:
                    sc.wait_ge(sPV.h, pv_v[i - 2])
                sEx.inc(sc.activation(
                    pt.rearrange("p (b h c) -> p b h c", b=2, c=512)[:, i % 2, :, c0:c1],
                    pS[i % 2].ap().rearrange("p (h c) -> p h c", c=512)[:, :, c0:c1],
                    AF.Exp, bias=biasP[:, 16 + pair:17 + pair], scale=0.125))
                if kb == 4 * qc + 3:
                    gq2 = pair * QCS + qc
                    sc.wait_ge(sRc.h, rc_v[gq2])
                    if gq2 >= 1:
                        sc.wait_ge(sBc.h, bc_v[gq2 - 1])
                    sAc.inc(sc.activation(rcp_b16[64:65, 0:1024], rcpf, AF.Identity))

        @block.gpsimd
        def _(g):
            for i, (pair, qc, kb) in enumerate(events):
                if kb == 4 * qc + 3:
                    gq = pair * QCS + qc
                    g.wait_ge(sNm.h, nm_v[gq])
                    dSg[gq % 2].dma(g.dma_start(
                        out=attnO[64:128, pair * T + qc * 512:pair * T + (qc + 1) * 512],
                        in_=stg[:, (gq % 2) * 512:(gq % 2) * 512 + 512]))

        @block.vector
        def _(v):
            for gq in range(NQG):
                pair, qc = divmod(gq, QCS)
                i_last = eidx[(pair, qc, 4 * qc + 3)]
                v.wait_ge(sPV.h, pv_v[i_last])
                if gq >= 1:
                    v.wait_ge(sAc.h, cast_v[gq - 1])
                v.reciprocal(rcp_t.ap()[64:65, 0:512], pO[gq % 2][64:65, 0:512])
                sRc.inc(v.reciprocal(rcp_t.ap()[64:65, 512:1024], pO[gq % 2][64:65, 512:1024]))
                v.wait_ge(sBc.h, bc_v[gq])
                sEv.inc(v.tensor_copy(bcsb[:], pS[elast[gq] % 2][0:64, :]))
                v.drain()
                if gq >= 2:
                    v.wait_ge(dSg[gq % 2].h, stg_v[gq - 2])
                v.tensor_mul(attnO[0:64, pair * T + qc * 512:pair * T + (qc + 1) * 512],
                             pO[gq % 2][0:64, 0:512], bcsb[0:64, 0:512])
                sNm.inc(v.tensor_mul(stg[:, (gq % 2) * 512:(gq % 2) * 512 + 512],
                                     pO[gq % 2][0:64, 512:1024], bcsb[0:64, 512:1024]))

        assert sEx.n == exp_v[-1] and sPV.n == pv_v[-1]
        assert sNm.n == nm_v[-1] and sBc.n == bc_v[-1]
        psc.close()

    def a5(s):
        psc = ExitStack()
        pw = [psc.enter_context(nc.psum_tensor("pw%d_%d" % (s, i), [128, 1024], F32))
              for i in range(2)]
        psq = psc.enter_context(nc.psum_tensor("psq%d" % s, [128, 1024], F32))
        wo0 = sWo.n
        wo_v = [wo0 + i + 1 for i in range(NTB)]
        ac0 = sAc.n
        ac_sumsq = [ac0 + 2 * i + 1 for i in range(NTB)]
        ac_sqrt = [ac0 + 2 * i + 2 for i in range(NTB)]
        ev0 = sEv.n
        stt_v, var_v, xn_v = [], [], []
        n = ev0
        for tb in range(NTB):
            n += 1
            stt_v.append(n)
            n += 1
            n += 4
            var_v.append(n)
            n += 1
            n += 1
            xn_v.append(n)
        xnd_v = {}
        xnd_base = [dSx[0].n, dSx[1].n]
        for i in range(NTB):
            xnd_v[i] = xnd_base[i % 2] + 16 * (i // 2 + 1)
        ld_v = {}

        @block.sync
        def _(sy):
            dW.dma(sy.dma_start(
                out=wout_b.rearrange("p (g f) -> p g f", g=8),
                in_=wout_bh.rearrange("(g p) f -> p g f", p=128)))
            for tb in range(NTB):
                if tb >= 2:
                    sy.wait_ge(sEv.h, xn_v[tb - 2])
                dXb[tb % 2].dma(sy.dma_start(
                    out=xbuf[:, (tb % 2) * C:(tb % 2) * C + C],
                    in_=xin[s * T + tb * 128: s * T + (tb + 1) * 128, :]))
                ld_v[tb] = dXb[tb % 2].n
        wob_ld = dW.n

        @block.tensor
        def _(te):
            te.wait_ge(dW.h, wob_ld)
            for tb in range(NTB):
                if tb >= 2:
                    te.wait_ge(sEv.h, stt_v[tb - 2])
                for cc in range(NCC):
                    for n2 in range(2):
                        ins = te.matmul(
                            pw[tb % 2][:, n2 * 512:(n2 + 1) * 512],
                            attnO[:, cc * T + tb * 128:cc * T + (tb + 1) * 128],
                            wout_b[:, cc * C + n2 * 512:cc * C + (n2 + 1) * 512],
                            start=(cc == 0), stop=(cc == 7))
                        if cc == 7 and n2 == 1:
                            sWo.inc(ins)

        @block.scalar
        def _(sc):
            for tb in range(NTB):
                o = (tb % 2) * 8
                sc.wait_ge(sEv.h, stt_v[tb])
                sAc.inc(sc.activation(psq[:], r1[:, (tb % 2) * C:(tb % 2) * C + C],
                                      AF.Square, accum_out=stat[:, o + 1:o + 2]))
                sc.wait_ge(sEv.h, var_v[tb])
                sAc.inc(sc.activation(stat[:, o + 6:o + 7], stat[:, o + 5:o + 6],
                                      AF.Sqrt, bias=s_eps[:, 0:1]))

        @block.vector
        def _(v):
            for tb in range(NTB):
                o = (tb % 2) * 8
                r1t = r1[:, (tb % 2) * C:(tb % 2) * C + C]
                v.wait_ge(sWo.h, wo_v[tb])
                v.wait_ge(dXb[tb % 2].h, ld_v[tb])
                if tb >= 2:
                    v.wait_ge(sAc.h, ac_sumsq[tb - 2])
                sEv.inc(v.scalar_tensor_tensor(
                    r1t, pw[tb % 2][:], 1.0, xbuf[:, (tb % 2) * C:(tb % 2) * C + C],
                    ALU.mult, ALU.add))
                v.drain()
                sEv.inc(v.reduce_sum(stat[:, o:o + 1], r1t, AX.X))
                v.drain()
                v.wait_ge(sAc.h, ac_sumsq[tb])
                sEv.inc(v.tensor_scalar_mul(stat[:, o + 2:o + 3], stat[:, o:o + 1], 1.0 / C))
                sEv.inc(v.tensor_scalar_mul(stat[:, o + 3:o + 4], stat[:, o + 1:o + 2], 1.0 / C))
                v.drain()
                sEv.inc(v.tensor_mul(stat[:, o + 4:o + 5], stat[:, o + 2:o + 3], stat[:, o + 2:o + 3]))
                v.drain()
                sEv.inc(v.tensor_sub(stat[:, o + 5:o + 6], stat[:, o + 3:o + 4], stat[:, o + 4:o + 5]))
                v.drain()
                v.wait_ge(sAc.h, ac_sqrt[tb])
                sEv.inc(v.reciprocal(stat[:, o + 7:o + 8], stat[:, o + 6:o + 7]))
                v.drain()
                if tb >= 2:
                    v.wait_ge(dSx[tb % 2].h, xnd_v[tb - 2])
                sEv.inc(v.tensor_scalar(xnb[:, (tb % 2) * C:(tb % 2) * C + C], r1t,
                                        stat[:, o + 2:o + 3], stat[:, o + 7:o + 8],
                                        ALU.subtract, ALU.mult))

        @block.gpsimd
        def _(g):
            for tb in range(NTB):
                g.wait_ge(sEv.h, xn_v[tb])
                dSx[tb % 2].dma(g.dma_start(
                    out=xn_hbm[s * T + tb * 128: s * T + (tb + 1) * 128, :],
                    in_=xnb[:, (tb % 2) * C:(tb % 2) * C + C]))

        assert sEv.n == xn_v[-1]
        psc.close()

    def wtern(wi, wsrc, wdst_T, R, Ccols):
        tiles = [(r, c) for r in range(R // 128) for c in range(Ccols // 1024)]
        NT = len(tiles)
        psc = ExitStack()
        p1 = psc.enter_context(nc.psum_tensor("pww%d" % wi, [128, 1024], BF16))
        pm = psc.enter_context(nc.psum_tensor("pwm%d" % wi, [1, 1], F32))
        pb = psc.enter_context(nc.psum_tensor("pwb%d" % wi, [128, 2], F32))

        ev0 = sEv.n
        red_v = [ev0 + i + 1 for i in range(NT)]
        redall_v = ev0 + NT + 1
        m1_v = redall_v + 1
        m2_v = redall_v + 2
        sc_v = redall_v + 3
        tx0 = sTx.n
        pm_v = tx0 + 1
        pb_v = tx0 + 2
        ld1, ld2 = {}, {}

        @block.sync
        def _(sy):
            for i, (r, c) in enumerate(tiles):
                if i >= 2:
                    sy.wait_ge(sEv.h, red_v[i - 2])
                dWt[i % 2].dma(sy.dma_start(
                    out=wst[:, (i % 2) * 1024:(i % 2) * 1024 + 1024],
                    in_=wsrc[r * 128:(r + 1) * 128, c * 1024:(c + 1) * 1024]))
                ld1[i] = dWt[i % 2].n

        @block.vector
        def _(v):
            for i in range(NT):
                v.wait_ge(dWt[i % 2].h, ld1[i])
                sEv.inc(v.reduce_sum(absacc[:, i:i + 1],
                                     wst[:, (i % 2) * 1024:(i % 2) * 1024 + 1024],
                                     AX.X, apply_absolute_value=True))
            v.drain()
            sEv.inc(v.reduce_sum(absacc[:, 32:33], absacc[:, 0:NT], AX.X))
            v.drain()
            v.wait_ge(sTx.h, pm_v)
            sEv.inc(v.tensor_scalar(w11[:, 0:1], pm[:], 1.0 / (R * Ccols), 1e-5,
                                    ALU.mult, ALU.max))
            v.drain()
            sEv.inc(v.reciprocal(w11[:, 1:2], w11[:, 0:1]))
            v.drain()
            v.wait_ge(sTx.h, pb_v)
            sEv.inc(v.tensor_copy(wsc[:, wi * 2:wi * 2 + 2], pb[:]))

        @block.tensor
        def _(te):
            te.wait_ge(sEv.h, redall_v)
            sTx.inc(te.matmul(pm[:], s_onesc[:], absacc[:, 32:33],
                              start=True, stop=True))
            te.wait_ge(sEv.h, m2_v)
            sTx.inc(te.matmul(pb[:], s_ones[0:1, :], w11[0:1, 0:2],
                              start=True, stop=True))

        ev1 = sEv.n
        q3_v, evT_v = [], []
        n = ev1
        for i in range(NT):
            n += 3
            q3_v.append(n)
            n += 1
            evT_v.append(n)
        tx1 = sTx.n
        tx_v = [tx1 + i + 1 for i in range(NT)]
        st_v = {}
        st_base = [dSt2[0].n, dSt2[1].n]
        for i in range(NT):
            st_v[i] = st_base[i % 2] + 16 * (i // 2 + 1)

        @block.sync
        def _(sy):
            for i, (r, c) in enumerate(tiles):
                if i < 2:
                    sy.wait_ge(sEv.h, red_v[NT - 1])
                if i >= 2:
                    sy.wait_ge(sEv.h, q3_v[i - 2])
                dWt[i % 2].dma(sy.dma_start(
                    out=wst[:, (i % 2) * 1024:(i % 2) * 1024 + 1024],
                    in_=wsrc[r * 128:(r + 1) * 128, c * 1024:(c + 1) * 1024]))
                ld2[i] = dWt[i % 2].n

        @block.vector
        def _(v):
            v.wait_ge(sEv.h, sc_v)
            for i in range(NT):
                v.wait_ge(dWt[i % 2].h, ld2[i])
                src = wst[:, (i % 2) * 1024:(i % 2) * 1024 + 1024]
                sEv.inc(v.tensor_scalar(sqs[:], src, wsc[:, wi * 2 + 1:wi * 2 + 2],
                                        -1.0, ALU.mult, ALU.max))
                v.drain()
                sEv.inc(v.tensor_scalar(sqs[:], sqs[:], 1.0, MAGIC,
                                        ALU.min, ALU.add))
                v.drain()
                if i >= 2:
                    v.wait_ge(sTx.h, tx_v[i - 2])
                sEv.inc(v.tensor_scalar_add(
                    tern[:, (i % 2) * 1024:(i % 2) * 1024 + 1024], sqs[:], -MAGIC))
                v.wait_ge(sTx.h, tx_v[i])
                if i >= 2:
                    v.wait_ge(dSt2[i % 2].h, st_v[i - 2])
                sEv.inc(v.tensor_copy(
                    terT[:, (i % 2) * 1024:(i % 2) * 1024 + 1024], p1.ap()))

        @block.gpsimd
        def _(g):
            for i, (r, c) in enumerate(tiles):
                g.wait_ge(sEv.h, evT_v[i])
                dSt2[i % 2].dma(g.dma_start(
                    out=wdst_T.rearrange("(g p) f -> p g f", p=128)[:, c * 8:(c + 1) * 8, r * 128:(r + 1) * 128],
                    in_=terT.rearrange("p (b g f) -> p b g f", b=2, g=8)[:, i % 2, :, :]))

        @block.tensor
        def _(te):
            for i in range(NT):
                te.wait_ge(sEv.h, q3_v[i])
                if i >= 1:
                    te.wait_ge(sEv.h, evT_v[i - 1])
                for cc in range(8):
                    ins = te.transpose(
                        p1[:, cc * 128:(cc + 1) * 128],
                        tern[:, (i % 2) * 1024 + cc * 128:(i % 2) * 1024 + (cc + 1) * 128],
                        s_idb[:])
                    if cc == 7:
                        sTx.inc(ins)

        assert sEv.n == evT_v[NT - 1] and sTx.n == tx_v[-1]
        psc.close()

    ph_list = [lambda: ph0()]
    for s in range(NSEQ):
        for f in (a1, a2, a3, a4, a5):
            ph_list.append(lambda f=f, s=s: f(s))
    ph_list.append(lambda: wtern(0, wg, wgT_h, INNER, C))
    ph_list.append(lambda: wtern(1, wu, wuT_h, INNER, C))
    ph_list.append(lambda: wtern(2, wd, wdT_h, C, INNER))
    nph = 0
    for f in ph_list:
        if nph >= UPTO:
            break
        f()
        barrier()
        nph += 1
    esA.close()

    esF = ExitStack()

    def sbF(name, shape, dt):
        return esF.enter_context(nc.sbuf_tensor(name, shape, dt))

    wgT = sbF("wgT", [128, 8 * INNER], BF16)
    wuT = sbF("wuT", [128, 8 * INNER], BF16)
    xnf = sbF("xnf", [128, 2 * C], F32)
    hbf = sbF("hbf", [128, INNER], F32)
    thf = sbF("thf", [128, INNER], F32)
    fst = sbF("fst", [128, 32], F32)
    esF1 = ExitStack()
    hqb = esF1.enter_context(nc.sbuf_tensor("hqb", [128, INNER], BF16))
    xqb = esF1.enter_context(nc.sbuf_tensor("xqb", [128, 2 * C], BF16))
    xqT = esF1.enter_context(nc.sbuf_tensor("xqT", [128, 2 * C], BF16))
    sgf = esF1.enter_context(nc.sbuf_tensor("sgf", [128, 512], F32))
    hqi_yb = [None, None]

    def f1():
        psc = ExitStack()
        pT = psc.enter_context(nc.psum_tensor("pT1", [128, 1024], BF16))
        pG = [psc.enter_context(nc.psum_tensor("pG%d" % i, [128, 512], F32)) for i in range(2)]
        pU = [psc.enter_context(nc.psum_tensor("pU%d" % i, [128, 512], F32)) for i in range(2)]

        order = [("q", 0, k) for k in range(9)]
        for tb in range(NT2):
            order.append(("e", tb, 0))
            if tb + 1 < NT2:
                order += [("q", tb + 1, k) for k in range(9)]
            order += [("h", tb, oc) for oc in range(8)]
            order += [("t", tb, k) for k in range(5)]
        fd0 = sFd.n
        opv = {}
        for j, op in enumerate(order):
            opv[op] = fd0 + j + 1
        xq_v = [opv[("q", tb, 8)] for tb in range(NT2)]
        evT_v = [opv[("e", tb, 0)] for tb in range(NT2)]
        hml_v = [[opv[("h", tb, oc)] for oc in range(8)] for tb in range(NT2)]
        hs_v = [opv[("t", tb, 3)] for tb in range(NT2)]
        hq_v = [opv[("t", tb, 4)] for tb in range(NT2)]

        fa0 = sFa.n
        silu_v = [[fa0 + tb * 9 + oc + 1 for oc in range(8)] for tb in range(NT2)]
        th_v = [fa0 + tb * 9 + 9 for tb in range(NT2)]
        f0 = sF.n
        tx_v, g_v, u_v = [], [], []
        n = f0
        for tb in range(NT2):
            n += 1
            tx_v.append(n)
            gl, ul = [], []
            for oc in range(8):
                n += 1
                gl.append(n)
                n += 1
                ul.append(n)
            g_v.append(gl)
            u_v.append(ul)
        hd0 = dSh.n
        hqd_v = [hd0 + 16 * (tb + 1) for tb in range(NT2)]
        xn_ld = {}

        @block.sync
        def _(sy):
            for half in range(2):
                dW.dma(sy.dma_start(
                    out=wgT.ap().rearrange("p (g f) -> p g f", g=8)[:, half * 4:(half + 1) * 4, :],
                    in_=wgT_h.rearrange("(g p) f -> p g f", p=128)[:, half * 4:(half + 1) * 4, :]))
                dW.dma(sy.dma_start(
                    out=wuT.ap().rearrange("p (g f) -> p g f", g=8)[:, half * 4:(half + 1) * 4, :],
                    in_=wuT_h.rearrange("(g p) f -> p g f", p=128)[:, half * 4:(half + 1) * 4, :]))
            for tb in range(NT2):
                if tb >= 2:
                    sy.wait_ge(sFd.h, xq_v[tb - 2])
                dXn[tb % 2].dma(sy.dma_start(out=xnf[:, (tb % 2) * C:(tb % 2) * C + C],
                                    in_=xn_hbm[tb * 128:(tb + 1) * 128, :]))
                xn_ld[tb] = dXn[tb % 2].n

        w_ld = dW.n

        def emit_quant(v, tb):
            o = (tb % 2) * 16
            xt = xnf[:, (tb % 2) * C:(tb % 2) * C + C]
            v.wait_ge(dXn[tb % 2].h, xn_ld[tb])
            if tb >= 2:
                v.wait_ge(sFa.h, th_v[tb - 2])
            sFd.inc(v.tensor_reduce(fst[:, o:o + 1], xt, AX.X, ALU.max,
                                    apply_absolute_value=True))
            v.drain()
            sFd.inc(v.tensor_scalar_max(fst[:, o:o + 1], fst[:, o:o + 1], 1e-5))
            v.drain()
            sFd.inc(v.reciprocal(fst[:, o + 1:o + 2], fst[:, o:o + 1]))
            v.drain()
            sFd.inc(v.tensor_scalar_mul(fst[:, o + 2:o + 3], fst[:, o + 1:o + 2], 127.0))
            sFd.inc(v.tensor_scalar_mul(fst[:, o + 3:o + 4], fst[:, o:o + 1], 1.0 / 127.0))
            v.drain()
            sFd.inc(v.tensor_mul(fst[:, o + 4:o + 5], fst[:, o + 3:o + 4], wsc[:, 0:1]))
            sFd.inc(v.tensor_mul(fst[:, o + 5:o + 6], fst[:, o + 3:o + 4], wsc[:, 2:3]))
            v.drain()
            sFd.inc(v.tensor_scalar(thf[:, 0:C], xt, fst[:, o + 2:o + 3], MAGIC,
                                    ALU.mult, ALU.add))
            v.drain()
            if tb >= 1:
                v.wait_ge(sF.h, tx_v[tb - 1])
            sFd.inc(v.tensor_scalar_add(xqb[:, (tb % 2) * C:(tb % 2) * C + C],
                                        thf[:, 0:C], -MAGIC))

        @block.vector
        def _(v):
            emit_quant(v, 0)
            for tb in range(NT2):
                o = (tb % 2) * 16
                v.wait_ge(sF.h, tx_v[tb])
                if tb >= 2:
                    v.wait_ge(sF.h, u_v[tb - 2][7])
                sFd.inc(v.tensor_copy(xqT[:, (tb % 2) * C:(tb % 2) * C + C], pT.ap()))
                if tb + 1 < NT2:
                    emit_quant(v, tb + 1)
                for oc in range(8):
                    v.wait_ge(sF.h, u_v[tb][oc])
                    v.wait_ge(sFa.h, silu_v[tb][oc])
                    sFd.inc(v.scalar_tensor_tensor(
                        hbf[:, oc * 512:(oc + 1) * 512], pU[oc % 2][:],
                        fst[:, o + 5:o + 6], sgf[:], ALU.mult, ALU.mult))
                v.drain()
                sFd.inc(v.tensor_reduce(fst[:, o + 6:o + 7], hbf[:], AX.X, ALU.max,
                                        apply_absolute_value=True))
                v.drain()
                sFd.inc(v.reciprocal(fst[:, o + 7:o + 8], fst[:, o + 6:o + 7]))
                v.drain()
                sFd.inc(v.tensor_scalar_mul(fst[:, o + 8:o + 9], fst[:, o + 7:o + 8], 127.0))
                sFd.inc(v.tensor_scalar_mul(dsh_all[:, tb:tb + 1], fst[:, o + 6:o + 7], 1.0 / 127.0))
                v.drain()
                v.wait_ge(sFa.h, th_v[tb])
                if tb >= 1:
                    v.wait_ge(dSh.h, hqd_v[tb - 1])
                sFd.inc(v.tensor_scalar_add(hqb[:], thf[:], -MAGIC))

        @block.tensor
        def _(te):
            te.wait_ge(dW.h, w_ld)
            for tb in range(NT2):
                te.wait_ge(sFd.h, xq_v[tb])
                if tb >= 1:
                    te.wait_ge(sFd.h, evT_v[tb - 1])
                for cc in range(8):
                    ins = te.transpose(pT[:, cc * 128:(cc + 1) * 128],
                                       xqb[:, (tb % 2) * C + cc * 128:(tb % 2) * C + (cc + 1) * 128],
                                       s_idb[:])
                    if cc == 7:
                        sF.inc(ins)
                for oc in range(8):
                    gc = tb * 8 + oc
                    te.wait_ge(sFd.h, evT_v[tb])
                    if gc >= 2:
                        tb2, oc2 = divmod(gc - 2, 8)
                        te.wait_ge(sFa.h, silu_v[tb2][oc2])
                    for cc in range(8):
                        ins = te.matmul(pG[oc % 2][:],
                                        xqT[:, (tb % 2) * C + cc * 128:(tb % 2) * C + (cc + 1) * 128],
                                        wgT[:, cc * INNER + oc * 512:cc * INNER + (oc + 1) * 512],
                                        start=(cc == 0), stop=(cc == 7))
                        if cc == 7:
                            sF.inc(ins)
                    if gc >= 2:
                        tb2, oc2 = divmod(gc - 2, 8)
                        te.wait_ge(sFd.h, hml_v[tb2][oc2])
                    for cc in range(8):
                        ins = te.matmul(pU[oc % 2][:],
                                        xqT[:, (tb % 2) * C + cc * 128:(tb % 2) * C + (cc + 1) * 128],
                                        wuT[:, cc * INNER + oc * 512:cc * INNER + (oc + 1) * 512],
                                        start=(cc == 0), stop=(cc == 7))
                        if cc == 7:
                            sF.inc(ins)

        @block.scalar
        def _(sc):
            for tb in range(NT2):
                o = (tb % 2) * 16
                for oc in range(8):
                    gc = tb * 8 + oc
                    sc.wait_ge(sF.h, g_v[tb][oc])
                    if gc >= 1:
                        tb2, oc2 = divmod(gc - 1, 8)
                        sc.wait_ge(sFd.h, hml_v[tb2][oc2])
                    sFa.inc(sc.activation(sgf[:], pG[oc % 2][:], AF.Silu,
                                          scale=fst[:, o + 4:o + 5]))
                sc.wait_ge(sFd.h, hs_v[tb])
                sFa.inc(sc.activation(thf[:], hbf[:], AF.Identity,
                                      scale=fst[:, o + 8:o + 9], bias=s_magic[:, 0:1]))

        @block.gpsimd
        def _(g):
            for tb in range(NT2):
                g.wait_ge(sFd.h, hq_v[tb])
                dSh.dma(g.dma_start(out=hq_hbm[tb * 128:(tb + 1) * 128, :],
                                   in_=hqb[:]))

        assert sFd.n == opv[order[-1]], (sFd.n, opv[order[-1]])
        assert sF.n == u_v[-1][7]
        assert sFa.n == th_v[-1]
        psc.close()

    def f2():
        hqi_, yb_ = hqi_yb[0], hqi_yb[1]
        psc = ExitStack()
        pH = [psc.enter_context(nc.psum_tensor("pH%d" % i, [128, 1024], BF16)) for i in range(2)]
        pD = [psc.enter_context(nc.psum_tensor("pD%d" % i, [128, 1024], F32)) for i in range(2)]
        psq2 = psc.enter_context(nc.psum_tensor("psq2", [128, 1024], F32))
        wdT = wgT
        hqT = wuT.ap()[:, 0:2 * INNER]

        fd0 = sFd.n
        dshmd_v = fd0 + 1
        evh_v, ln_v = [], []
        n = fd0 + 1
        for tb in range(NT2):
            el = []
            for grp in range(4):
                n += 1
                el.append(n)
            evh_v.append(el)
            n += 8
            ln_v.append(n)
        fa0 = sFa.n
        sumsq_v = [fa0 + 2 * tb + 1 for tb in range(NT2)]
        sqrt_v = [fa0 + 2 * tb + 2 for tb in range(NT2)]
        f0 = sF.n
        txg_v, dmm_v = [], []
        n = f0
        for tb in range(NT2):
            tl = []
            for grp in range(4):
                n += 1
                tl.append(n)
            txg_v.append(tl)
            dl = []
            for n2 in range(2):
                n += 1
                dl.append(n)
            dmm_v.append(dl)
        dS0 = sOut.n
        y_v = [dS0 + 16 * (tb + 1) for tb in range(NT2)]
        hq_ld, xn_ld = {}, {}

        @block.vector
        def _(v):
            sFd.inc(v.tensor_scalar_mul(dsh_all[:, 32:64], dsh_all[:, 0:32],
                                        wsc[:, 4:5]))

        @block.sync
        def _(sy):
            for q in range(4):
                dW.dma(sy.dma_start(
                    out=wdT.ap().rearrange("p (g f) -> p g f", g=32)[:, q * 8:(q + 1) * 8, 0:C],
                    in_=wdT_h.rearrange("(g p) f -> p g f", p=128)[:, q * 8:(q + 1) * 8, :]))
            for tb in range(NT2):
                if tb >= 2:
                    sy.wait_ge(sF.h, dmm_v[tb - 2][1])
                dHq[tb % 2].dma(sy.dma_start(out=hqi_[:, (tb % 2) * INNER:(tb % 2) * INNER + INNER],
                                    in_=hq_hbm[tb * 128:(tb + 1) * 128, :]))
                hq_ld[tb] = dHq[tb % 2].n
                if tb >= 2:
                    sy.wait_ge(sFd.h, ln_v[tb - 2])
                dXn[tb % 2].dma(sy.dma_start(out=xnf[:, (tb % 2) * C:(tb % 2) * C + C],
                                    in_=xn_hbm[tb * 128:(tb + 1) * 128, :]))
                xn_ld[tb] = dXn[tb % 2].n

        wd_ld = dW.n

        @block.tensor
        def _(te):
            te.wait_ge(dW.h, wd_ld)
            te.wait_ge(sFd.h, dshmd_v)
            for tb in range(NT2):
                te.wait_ge(dHq[tb % 2].h, hq_ld[tb])
                for grp in range(4):
                    gi = tb * 4 + grp
                    if gi >= 2:
                        tbp, gp_ = divmod(gi - 2, 4)
                        te.wait_ge(sFd.h, evh_v[tbp][gp_])
                    for k in range(8):
                        ic = grp * 8 + k
                        ins = te.transpose(
                            pH[grp % 2][:, k * 128:(k + 1) * 128],
                            hqi_[:, (tb % 2) * INNER + ic * 128:(tb % 2) * INNER + (ic + 1) * 128],
                            s_idb[:])
                        if k == 7:
                            sF.inc(ins)
                for n2 in range(2):
                    if tb >= 2:
                        te.wait_ge(sFd.h, ln_v[tb - 2])
                    for ic in range(32):
                        if ic == 0 and n2 == 0:
                            te.wait_ge(sFd.h, evh_v[tb][3])
                        ins = te.matmul(
                            pD[tb % 2][:, n2 * 512:(n2 + 1) * 512],
                            hqT[:, (tb % 2) * INNER + ic * 128:(tb % 2) * INNER + (ic + 1) * 128],
                            wdT[:, ic * 1024 + n2 * 512:ic * 1024 + (n2 + 1) * 512],
                            start=(ic == 0), stop=(ic == 31))
                        if ic == 31:
                            sF.inc(ins)

        @block.vector
        def _(v):
            for tb in range(NT2):
                o = (tb % 2) * 16
                for grp in range(4):
                    v.wait_ge(sF.h, txg_v[tb][grp])
                    if tb >= 2:
                        v.wait_ge(sF.h, dmm_v[tb - 2][1])
                    sFd.inc(v.tensor_copy(
                        hqT.rearrange("p (b g f) -> p b g f", b=2, g=32)[:, tb % 2, grp * 8:(grp + 1) * 8, :],
                        pH[grp % 2].ap().rearrange("p (g f) -> p g f", g=8)))
                v.wait_ge(sF.h, dmm_v[tb][1])
                v.wait_ge(dXn[tb % 2].h, xn_ld[tb])
                if tb >= 1:
                    v.wait_ge(sFa.h, sumsq_v[tb - 1])
                r2t = thf[:, 0:C]
                sFd.inc(v.scalar_tensor_tensor(
                    r2t, pD[tb % 2][:], dsh_all[:, 32 + tb:33 + tb],
                    xnf[:, (tb % 2) * C:(tb % 2) * C + C], ALU.mult, ALU.add))
                v.drain()
                sFd.inc(v.reduce_sum(fst[:, o:o + 1], r2t, AX.X))
                v.drain()
                v.wait_ge(sFa.h, sumsq_v[tb])
                sFd.inc(v.tensor_scalar_mul(fst[:, o + 2:o + 3], fst[:, o:o + 1], 1.0 / C))
                sFd.inc(v.tensor_scalar_mul(fst[:, o + 3:o + 4], fst[:, o + 1:o + 2], 1.0 / C))
                v.drain()
                sFd.inc(v.tensor_mul(fst[:, o + 4:o + 5], fst[:, o + 2:o + 3], fst[:, o + 2:o + 3]))
                v.drain()
                sFd.inc(v.tensor_sub(fst[:, o + 5:o + 6], fst[:, o + 3:o + 4], fst[:, o + 4:o + 5]))
                v.drain()
                v.wait_ge(sFa.h, sqrt_v[tb])
                sFd.inc(v.reciprocal(fst[:, o + 7:o + 8], fst[:, o + 6:o + 7]))
                v.drain()
                if tb >= 1:
                    v.wait_ge(sOut.h, y_v[tb - 1])
                sFd.inc(v.tensor_scalar(yb_[:], r2t,
                                        fst[:, o + 2:o + 3], fst[:, o + 7:o + 8],
                                        ALU.subtract, ALU.mult))

        @block.scalar
        def _(sc):
            for tb in range(NT2):
                o = (tb % 2) * 16
                sc.wait_ge(sFd.h, ln_v[tb] - 7)
                sFa.inc(sc.activation(psq2[:], thf[:, 0:C], AF.Square,
                                      accum_out=fst[:, o + 1:o + 2]))
                sc.wait_ge(sFd.h, ln_v[tb] - 2)
                sFa.inc(sc.activation(fst[:, o + 6:o + 7], fst[:, o + 5:o + 6],
                                      AF.Sqrt, bias=s_eps[:, 0:1]))

        @block.gpsimd
        def _(g):
            for tb in range(NT2):
                g.wait_ge(sFd.h, ln_v[tb])
                sOut.dma(g.dma_start(out=y[tb * 128:(tb + 1) * 128, :],
                                     in_=yb_[:]))

        assert sFd.n == ln_v[-1]
        assert sOut.n == y_v[-1]
        psc.close()

    if nph < UPTO:
        f1()
        barrier()
        nph += 1
    esF1.close()
    hqi_yb = [esF.enter_context(nc.sbuf_tensor("hqi", [128, 2 * INNER], BF16)),
              esF.enter_context(nc.sbuf_tensor("yb", [128, C], F32))]
    if nph < UPTO:
        f2()
        barrier()
        nph += 1

    @block.sync
    def _(sy):
        sy.wait_ge(sOut.h, sOut.n)

    esF.close()
    es.close()
    return nc


def make_consts():
    idf = np.eye(128, dtype=np.float32)
    idb = np.eye(128, dtype=bfnp)
    tri = np.where(np.arange(128)[:, None] > np.arange(128)[None, :],
                   np.float32(-30000.0), np.float32(0.0)).astype(bfnp)
    hmap = np.zeros((128, 128), dtype=bfnp)
    for cc in range(8):
        for p in range(128):
            hmap[p, cc * 16 + (2 * cc + p // 64)] = 1
    ones = np.ones((128, 128), dtype=np.float32)
    id16 = np.eye(16, dtype=np.float32)
    return dict(c_idf=idf, c_idb=idb, c_tri=tri, c_hmap=hmap, c_ones=ones,
                c_id16=id16)


def kernel(L0_attn_x, L0_attn_w_qkv, L0_attn_w_out, L0_ffn_w_gate, L0_ffn_w_up,
           L0_ffn_w_down):
    x = np.ascontiguousarray(np.asarray(L0_attn_x, dtype=np.float32))
    B = x.shape[0]
    ncore = 8
    per = B // ncore
    nc = build_program()
    base = dict(
        wqkv=np.ascontiguousarray(np.asarray(L0_attn_w_qkv, dtype=np.float32)),
        wout=np.ascontiguousarray(np.asarray(L0_attn_w_out, dtype=np.float32)),
        wg=np.ascontiguousarray(np.asarray(L0_ffn_w_gate, dtype=np.float32)),
        wu=np.ascontiguousarray(np.asarray(L0_ffn_w_up, dtype=np.float32)),
        wd=np.ascontiguousarray(np.asarray(L0_ffn_w_down, dtype=np.float32)),
        **make_consts())
    in_maps = []
    for c in range(ncore):
        m = dict(base)
        m["xin"] = np.ascontiguousarray(x[c * per:(c + 1) * per].reshape(per * T, C))
        in_maps.append(m)
    res = run_bass_kernel_spmd(nc, in_maps, core_ids=list(range(ncore)))
    outs = [r["y"].reshape(per, T, C) for r in res.results]
    return np.concatenate(outs, axis=0).astype(np.float32)


# version 05
# speedup vs baseline: 1.5242x; 1.0922x over previous
import numpy as np
import ml_dtypes
import concourse.bass as bass
import concourse.mybir as mybir
from concourse.bass_utils import run_bass_kernel_spmd
from contextlib import ExitStack

F32 = mybir.dt.float32
BF16 = mybir.dt.bfloat16
AF = mybir.ActivationFunctionType
ALU = mybir.AluOpType
AX = mybir.AxisListType
bfnp = ml_dtypes.bfloat16

MAGIC = float(np.float32(1.5 * 2 ** 23))
T, C, NH, DK, NSEQ, INNER = 2048, 1024, 16, 64, 2, 4096
NTB = T // 128
NQC = 4
NCC = C // 128
NPAIR = NH // 2
NT2 = NSEQ * NTB

DEBUG_OUT = False
UPTO = 99
A4_PAIRS = 8
A4_QCS = 4


class Sem:
    def __init__(self, es, nc, name):
        self.h = es.enter_context(nc.semaphore(name))
        self.n = 0

    def inc(self, instr, k=1):
        instr.then_inc(self.h, k)
        self.n += k
        return instr

    def dma(self, instr):
        return self.inc(instr, 16)


def build_program():
    nc = bass.Bass("TRN2", target_bir_lowering=False)

    xin = nc.dram_tensor("xin", [NSEQ * T, C], F32, kind="ExternalInput").ap()
    wqkv = nc.dram_tensor("wqkv", [C, 3 * C], F32, kind="ExternalInput").ap()
    wout = nc.dram_tensor("wout", [C, C], F32, kind="ExternalInput").ap()
    wg = nc.dram_tensor("wg", [INNER, C], F32, kind="ExternalInput").ap()
    wu = nc.dram_tensor("wu", [INNER, C], F32, kind="ExternalInput").ap()
    wd = nc.dram_tensor("wd", [C, INNER], F32, kind="ExternalInput").ap()
    c_idf = nc.dram_tensor("c_idf", [128, 128], F32, kind="ExternalInput").ap()
    c_idb = nc.dram_tensor("c_idb", [128, 128], BF16, kind="ExternalInput").ap()
    c_tri = nc.dram_tensor("c_tri", [128, 128], BF16, kind="ExternalInput").ap()
    c_hmap = nc.dram_tensor("c_hmap", [128, 128], BF16, kind="ExternalInput").ap()
    c_ones = nc.dram_tensor("c_ones", [128, 128], F32, kind="ExternalInput").ap()
    c_id16 = nc.dram_tensor("c_id16", [16, 16], F32, kind="ExternalInput").ap()
    y = nc.dram_tensor("y", [NSEQ * T, C], F32, kind="ExternalOutput").ap()

    KIND = "ExternalOutput" if DEBUG_OUT else "Internal"
    xn_hbm = nc.dram_tensor("xn_hbm", [NSEQ * T, C], F32, kind=KIND).ap()
    hq_hbm = nc.dram_tensor("hq_hbm", [NSEQ * T, INNER], BF16, kind=KIND).ap()
    wqkv_bh = nc.dram_tensor("wqkv_bh", [C, 3 * C], BF16, kind="Internal").ap()
    wout_bh = nc.dram_tensor("wout_bh", [C, C], BF16, kind="Internal").ap()
    wgT_h = nc.dram_tensor("wgT_h", [C, INNER], BF16, kind=KIND).ap()
    wuT_h = nc.dram_tensor("wuT_h", [C, INNER], BF16, kind=KIND).ap()
    wdT_h = nc.dram_tensor("wdT_h", [INNER, C], BF16, kind=KIND).ap()

    es = ExitStack()

    s_idf = es.enter_context(nc.sbuf_tensor("s_idf", [128, 128], F32))
    s_idb = es.enter_context(nc.sbuf_tensor("s_idb", [128, 128], BF16))
    s_tri = es.enter_context(nc.sbuf_tensor("s_tri", [128, 128], BF16))
    s_hmap = es.enter_context(nc.sbuf_tensor("s_hmap", [128, 128], BF16))
    s_ones = es.enter_context(nc.sbuf_tensor("s_ones", [128, 128], F32))
    s_onesc = es.enter_context(nc.sbuf_tensor("s_onesc", [128, 1], F32))
    s_id16 = es.enter_context(nc.sbuf_tensor("s_id16", [16, 16], F32))
    s_onesb = es.enter_context(nc.sbuf_tensor("s_onesb", [128, 64], BF16))
    s_eps = es.enter_context(nc.sbuf_tensor("s_eps", [128, 1], F32))
    s_magic = es.enter_context(nc.sbuf_tensor("s_magic", [128, 1], F32))
    wsc = es.enter_context(nc.sbuf_tensor("wsc", [128, 8], F32))
    dsh_all = es.enter_context(nc.sbuf_tensor("dsh_all", [128, 64], F32))
    w11 = es.enter_context(nc.sbuf_tensor("w11", [1, 4], F32))

    SD = {}
    for name in ("dX dW dS sTx sEv sAc sQk sS sEx sMk sPV sRc sBc sNm "
                 "sWo sGp sF sFd sFa sOut").split():
        SD[name] = Sem(es, nc, name)
    dXb = [Sem(es, nc, "dXb%d" % i) for i in range(2)]
    dWq = [Sem(es, nc, "dWq%d" % i) for i in range(2)]
    dWt = [Sem(es, nc, "dWt%d" % i) for i in range(2)]
    dXn = [Sem(es, nc, "dXn%d" % i) for i in range(2)]
    dHq = [Sem(es, nc, "dHq%d" % i) for i in range(2)]
    dSc = [Sem(es, nc, "dSc%d" % i) for i in range(2)]
    dSg = [Sem(es, nc, "dSg%d" % i) for i in range(2)]
    dSx = [Sem(es, nc, "dSx%d" % i) for i in range(2)]
    dSt2 = [Sem(es, nc, "dSt2_%d" % i) for i in range(2)]
    dSh = Sem(es, nc, "dSh")
    dX, dW, dS = SD["dX"], SD["dW"], SD["dS"]
    sTx, sEv, sAc, sQk = SD["sTx"], SD["sEv"], SD["sAc"], SD["sQk"]
    sS_, sEx, sMk, sPV = SD["sS"], SD["sEx"], SD["sMk"], SD["sPV"]
    sRc, sBc, sNm = SD["sRc"], SD["sBc"], SD["sNm"]
    sWo, sGp = SD["sWo"], SD["sGp"]
    sF, sFd, sFa, sOut = SD["sF"], SD["sFd"], SD["sFa"], SD["sOut"]
    ALL = (list(SD.values()) + dXb + dWq + dWt + dXn + dHq + dSc + dSg + dSx
           + dSt2 + [dSh])

    block = es.enter_context(nc.Block())

    def barrier():
        snaps = [(s.h, s.n) for s in ALL if s.n > 0]
        for attr in ("sync", "tensor", "vector", "scalar", "gpsimd"):
            def f(eng, snaps=snaps):
                for h, v in snaps:
                    eng.wait_ge(h, v)
            getattr(block, attr)(f)

    esA = ExitStack()

    def sbA(name, shape, dt):
        return esA.enter_context(nc.sbuf_tensor(name, shape, dt))

    xbuf = sbA("xbuf", [128, 2 * C], F32)
    wst = sbA("wst", [128, 2 * 1024], F32)
    wcb = sbA("wcb", [128, 2 * 1024], BF16)
    ovl = sbA("ovl", [128, 8192], BF16)
    wqs = ovl.ap()[:, 0:4096]
    wout_b = ovl.ap()[:, 0:8192]
    xfm = sbA("xfm", [128, 8 * T], BF16)
    attnO = xfm
    qfm = sbA("qfm", [128, 8 * T], BF16)
    kfm = sbA("kfm", [128, 8 * T], BF16)
    vsb = sbA("vsb", [128, NTB * NH * 65], BF16)
    pt = ovl.ap()[:, 0:3072]
    qsq = ovl.ap()[:, 0:2048]
    rcp_t = sbA("rcp_t", [128, 1024], F32)
    rcpf = rcp_t.ap()[64:65, 0:1024]
    rcp_b16 = sbA("rcp_b16", [128, 1024], BF16)
    rcpa = rcp_b16.ap()[64:65, 0:512]
    rcpb = rcp_b16.ap()[64:65, 512:1024]
    bcsb = sbA("bcsb", [64, 1024], F32)
    stg = sbA("stg", [64, 2 * 512], BF16)
    biasq = sbA("biasq", [16, 4], F32)
    nrow = sbA("nrow", [1, 16], F32)
    biasP = sbA("biasP", [128, 24], F32)
    r1 = sbA("r1", [128, 2 * C], F32)
    stat = sbA("stat", [128, 16], F32)
    sqs = r1.ap()[:, 0:C]
    xnb = sbA("xnb", [128, 2 * C], F32)
    absacc = sbA("absacc", [128, 33], F32)
    tern = wcb.ap()[:, 0:2048]
    terT = ovl.ap()[:, 4096:6144]

    def ph0():
        @block.sync
        def _(sy):
            dW.dma(sy.dma_start(out=s_idf[:], in_=c_idf))
            dW.dma(sy.dma_start(out=s_idb[:], in_=c_idb))
            dW.dma(sy.dma_start(out=s_tri[:], in_=c_tri))
            dW.dma(sy.dma_start(out=s_hmap[:], in_=c_hmap))
            dW.dma(sy.dma_start(out=s_ones[:], in_=c_ones))
            dW.dma(sy.dma_start(out=s_id16[:], in_=c_id16))
        cw = dW.n

        @block.vector
        def _(v):
            sEv.inc(v.memset(s_onesb[:], 1.0))
            sEv.inc(v.memset(s_eps[:], 1e-5))
            sEv.inc(v.memset(s_magic[:], MAGIC))
            sEv.inc(v.memset(s_onesc[:], 1.0))
            sEv.inc(v.memset(
                vsb.ap().rearrange("p (g c) -> p g c", c=65)[:, :, 64:65], 1.0))

        jobs = [("qkv", t, h) for t in range(8) for h in range(3)]
        jobs += [("out", t, 0) for t in range(8)]
        NJ = len(jobs)
        ld_v = {}
        ev0 = sEv.n
        ev_v = [ev0 + i + 1 for i in range(NJ)]
        st_v = {}

        @block.sync
        def _(sy):
            for i, (tag, t, h) in enumerate(jobs):
                if i >= 2:
                    sy.wait_ge(sEv.h, ev_v[i - 2])
                b = wst[:, (i % 2) * 1024:(i % 2) * 1024 + 1024]
                if tag == "qkv":
                    src = wqkv[t * 128:(t + 1) * 128, h * 1024:(h + 1) * 1024]
                else:
                    src = wout[t * 128:(t + 1) * 128, :]
                dWt[i % 2].dma(sy.dma_start(out=b, in_=src))
                ld_v[i] = dWt[i % 2].n

        @block.vector
        def _(v):
            for i, (tag, t, h) in enumerate(jobs):
                v.wait_ge(dWt[i % 2].h, ld_v[i])
                b = wst[:, (i % 2) * 1024:(i % 2) * 1024 + 1024]
                if i >= 2 and (i - 2) in st_v:
                    v.wait_ge(dSc[i % 2].h, st_v[i - 2])
                dst = wcb[:, (i % 2) * 1024:(i % 2) * 1024 + 1024]
                sEv.inc(v.tensor_copy(dst, b))

        @block.gpsimd
        def _(g):
            for i, (tag, t, h) in enumerate(jobs):
                g.wait_ge(sEv.h, ev_v[i])
                src = wcb[:, (i % 2) * 1024:(i % 2) * 1024 + 1024]
                if tag == "qkv":
                    dst = wqkv_bh[t * 128:(t + 1) * 128, h * 1024:(h + 1) * 1024]
                else:
                    dst = wout_bh[t * 128:(t + 1) * 128, :]
                dSc[i % 2].dma(g.dma_start(out=dst, in_=src))
                st_v[i] = dSc[i % 2].n

    def a1(s):
        psc = ExitStack()
        ptx = psc.enter_context(nc.psum_tensor("ptx%d" % s, [128, 1024], F32))
        tx0 = sTx.n
        tx_v = [tx0 + i + 1 for i in range(NTB)]
        ev0 = sEv.n
        ev_v = [ev0 + i + 1 for i in range(NTB)]
        ld_v = {}

        @block.sync
        def _(sy):
            for tb in range(NTB):
                if tb >= 2:
                    sy.wait_ge(sTx.h, tx_v[tb - 2])
                dXb[tb % 2].dma(sy.dma_start(
                    out=xbuf[:, (tb % 2) * C:(tb % 2) * C + C],
                    in_=xin[s * T + tb * 128: s * T + (tb + 1) * 128, :]))
                ld_v[tb] = dXb[tb % 2].n

        @block.tensor
        def _(te):
            for tb in range(NTB):
                te.wait_ge(dXb[tb % 2].h, ld_v[tb])
                if tb >= 1:
                    te.wait_ge(sEv.h, ev_v[tb - 1])
                for cc in range(NCC):
                    ins = te.transpose(
                        ptx[:, cc * 128:(cc + 1) * 128],
                        xbuf[:, (tb % 2) * C + cc * 128:(tb % 2) * C + (cc + 1) * 128],
                        s_idf[:])
                    if cc == 7:
                        sTx.inc(ins)

        @block.vector
        def _(v):
            for tb in range(NTB):
                v.wait_ge(sTx.h, tx_v[tb])
                sEv.inc(v.tensor_copy(
                    xfm.ap().rearrange("p (g t) -> p g t", t=T)[:, :, tb * 128:(tb + 1) * 128],
                    ptx.ap().rearrange("p (g c) -> p g c", c=128)))
        assert sTx.n == tx_v[-1] and sEv.n == ev_v[-1]
        psc.close()

    def a2(s):
        psc = ExitStack()
        pqk = [psc.enter_context(nc.psum_tensor("pqk%d_%d" % (s, i), [128, 512], F32))
               for i in range(2)]
        pv = [psc.enter_context(nc.psum_tensor("pv%d_%d" % (s, i), [128, 512], F32))
              for i in range(2)]

        qk0 = sQk.n
        qk_v = {}
        n = qk0
        for of in range(16):
            for tc in range(4):
                n += 1
                qk_v[(of, tc)] = n
        v_v = {}
        for nv in range(2):
            for tb in range(NTB):
                n += 1
                v_v[(nv, tb)] = n
        ac0 = sAc.n
        ac_v = {}
        n = ac0
        for of in range(16):
            for tc in range(4):
                n += 1
                ac_v[(of, tc)] = n
        ev0 = sEv.n
        evv_v = {}
        n = ev0
        for nv in range(2):
            for tb in range(NTB):
                n += 1
                evv_v[(nv, tb)] = n
        wl_v = {}

        @block.sync
        def _(sy):
            for of in range(16):
                if of >= 2:
                    sy.wait_ge(sQk.h, qk_v[(of - 2, 3)])
                dWq[of % 2].dma(sy.dma_start(
                    out=wqs.rearrange("p (b f) -> p b f", b=2)[:, of % 2, 0:1024],
                    in_=wqkv_bh.rearrange("(g p) f -> p g f", p=128)[:, :, of * 128:(of + 1) * 128]))
                wl_v[of] = dWq[of % 2].n
            for nv in range(2):
                if nv == 0:
                    sy.wait_ge(sQk.h, qk_v[(14, 3)])
                    sy.wait_ge(sQk.h, qk_v[(15, 3)])
                else:
                    sy.wait_ge(sQk.h, v_v[(0, NTB - 1)])
                for hf in range(2):
                    dWq[hf].dma(sy.dma_start(
                        out=wqs.rearrange("p (b f) -> p b f", b=2)[:, hf, :],
                        in_=wqkv_bh.rearrange("(g p) f -> p g f", p=128)[:, hf * 4:(hf + 1) * 4, 2048 + nv * 512:2048 + (nv + 1) * 512]))
                wl_v[16 + nv] = (dWq[0].n, dWq[1].n)

        @block.tensor
        def _(te):
            for of in range(16):
                te.wait_ge(dWq[of % 2].h, wl_v[of])
                for tc in range(4):
                    m = of * 4 + tc
                    if m >= 2:
                        of2, tc2 = divmod(m - 2, 4)
                        te.wait_ge(sAc.h, ac_v[(of2, tc2)])
                    for cc in range(NCC):
                        ins = te.matmul(
                            pqk[m % 2][:],
                            wqs[:, (of % 2) * 2048 + cc * 128:(of % 2) * 2048 + (cc + 1) * 128],
                            xfm[:, cc * T + tc * 512:cc * T + (tc + 1) * 512],
                            start=(cc == 0), stop=(cc == 7))
                        if cc == 7:
                            sQk.inc(ins)
            for nv in range(2):
                te.wait_ge(dWq[0].h, wl_v[16 + nv][0])
                te.wait_ge(dWq[1].h, wl_v[16 + nv][1])
                for tb in range(NTB):
                    m = nv * NTB + tb
                    if m >= 2:
                        nv2, tb2 = divmod(m - 2, NTB)
                        te.wait_ge(sEv.h, evv_v[(nv2, tb2)])
                    for cc in range(NCC):
                        ins = te.matmul(
                            pv[m % 2][:],
                            xfm[:, cc * T + tb * 128:cc * T + (tb + 1) * 128],
                            wqs[:, (cc // 4) * 2048 + (cc % 4) * 512:(cc // 4) * 2048 + (cc % 4) * 512 + 512],
                            start=(cc == 0), stop=(cc == 7))
                        if cc == 7:
                            sQk.inc(ins)

        @block.scalar
        def _(sc):
            for of in range(16):
                dstt = qfm if of < 8 else kfm
                ch = of % 8
                for tc in range(4):
                    sc.wait_ge(sQk.h, qk_v[(of, tc)])
                    sAc.inc(sc.activation(
                        dstt[:, ch * T + tc * 512:ch * T + (tc + 1) * 512],
                        pqk[(of * 4 + tc) % 2][:], AF.Identity))

        @block.vector
        def _(v):
            for nv in range(2):
                for tb in range(NTB):
                    v.wait_ge(sQk.h, v_v[(nv, tb)])
                    sEv.inc(v.tensor_copy(
                        vsb.ap().rearrange("p (k h c) -> p k h c", h=NH, c=65)[:, tb, nv * 8:(nv + 1) * 8, 0:64],
                        pv[(nv * NTB + tb) % 2].ap().rearrange("p (h c) -> p h c", c=64)))
        assert sQk.n == v_v[(1, NTB - 1)]
        psc.close()

    def a3(s):
        psc = ExitStack()
        pn = psc.enter_context(nc.psum_tensor("pn%d" % s, [16, 2048], F32))
        pr = psc.enter_context(nc.psum_tensor("pr%d" % s, [1, 16], F32))
        pbb = psc.enter_context(nc.psum_tensor("pbb%d" % s, [128, 16], F32))

        base = sEv.n
        sq_v = {}
        for cc in range(NCC):
            sq_v[(0, cc)] = base + cc + 1
        redq_v = base + 9
        for cc in range(NCC):
            sq_v[(1, cc)] = base + 10 + cc
        redk_v = base + 18
        mul_v = base + 19
        negb_v = base + 20
        nrow_v = base + 21
        bcp_v = base + 22
        min_v = base + 23
        t0 = sTx.n
        mm_v = {}
        for cc in range(NCC):
            mm_v[(0, cc)] = t0 + cc + 1
            mm_v[(1, cc)] = t0 + 9 + cc
        pr_v = t0 + 17
        pbb_v = t0 + 18
        sqrt_v = sAc.n + 1

        @block.vector
        def _(v):
            for cc in range(NCC):
                if cc >= 1:
                    v.wait_ge(sTx.h, mm_v[(0, cc - 1)])
                sEv.inc(v.tensor_mul(qsq[:], qfm[:, cc * T:(cc + 1) * T],
                                     qfm[:, cc * T:(cc + 1) * T]))
            v.wait_ge(sTx.h, mm_v[(0, NCC - 1)])
            sEv.inc(v.tensor_reduce(biasq[:, 0:1], pn[0:16, :], AX.X, ALU.max))
            for cc in range(NCC):
                if cc >= 1:
                    v.wait_ge(sTx.h, mm_v[(1, cc - 1)])
                sEv.inc(v.tensor_mul(qsq[:], kfm[:, cc * T:(cc + 1) * T],
                                     kfm[:, cc * T:(cc + 1) * T]))
            v.wait_ge(sTx.h, mm_v[(1, NCC - 1)])
            sEv.inc(v.tensor_reduce(biasq[:, 1:2], pn[0:16, :], AX.X, ALU.max))
            v.drain()
            sEv.inc(v.tensor_mul(biasq[:, 2:3], biasq[:, 0:1], biasq[:, 1:2]))
            v.drain()
            v.wait_ge(sAc.h, sqrt_v)
            sEv.inc(v.tensor_scalar_mul(biasq[:, 0:1], biasq[:, 3:4], -0.125))
            v.drain()
            v.wait_ge(sTx.h, pr_v)
            sEv.inc(v.tensor_copy(nrow[:], pr[:]))
            v.drain()
            v.wait_ge(sTx.h, pbb_v)
            sEv.inc(v.tensor_copy(biasP[:, 0:16], pbb[:]))
            v.drain()
            sEv.inc(v.tensor_tensor(biasP[:, 16:24], biasP[:, 0:16:2],
                                    biasP[:, 1:16:2], ALU.min))
            v.drain()

        @block.tensor
        def _(te):
            for w in range(2):
                if w == 1:
                    te.wait_ge(sEv.h, redq_v)
                for cc in range(NCC):
                    te.wait_ge(sEv.h, sq_v[(w, cc)])
                    for nq in range(4):
                        ins = te.matmul(pn[:, nq * 512:(nq + 1) * 512],
                                        s_hmap[:, cc * 16:(cc + 1) * 16],
                                        qsq[:, nq * 512:(nq + 1) * 512],
                                        start=(cc == 0), stop=(cc == 7))
                        if nq == 3:
                            sTx.inc(ins)
            te.wait_ge(sEv.h, negb_v)
            sTx.inc(te.matmul(pr[:], biasq[:, 0:1], s_id16[:], start=True, stop=True))
            te.wait_ge(sEv.h, nrow_v)
            sTx.inc(te.matmul(pbb[:], s_ones[0:1, :], nrow[:], start=True, stop=True))

        @block.scalar
        def _(sc):
            sc.wait_ge(sEv.h, mul_v)
            sAc.inc(sc.activation(biasq[:, 3:4], biasq[:, 2:3], AF.Sqrt))

        assert sEv.n == min_v, (sEv.n, min_v)
        assert sTx.n == pbb_v
        psc.close()

    def a4(s):
        psc = ExitStack()
        pS = [psc.enter_context(nc.psum_tensor("pS%d_%d" % (s, i), [128, 1024], F32))
              for i in range(2)]
        pO = [psc.enter_context(nc.psum_tensor("pO%d_%d" % (s, i), [65, 1024], F32))
              for i in range(2)]

        QCS = A4_QCS
        events = []
        for pair in range(A4_PAIRS):
            for qc in range(QCS):
                for kb in range(4 * qc + 4):
                    events.append((pair, qc, kb))
        E = len(events)
        eidx = {ev: i for i, ev in enumerate(events)}
        part = [kb >= 4 * qc for (pair, qc, kb) in events]
        jof = [kb - 4 * qc for (pair, qc, kb) in events]

        exp0, s0_, pv0 = sEx.n, sS_.n, sPV.n
        exp_v = [exp0 + i + 1 for i in range(E)]
        s_v = [s0_ + i + 1 for i in range(E)]
        pv_v = [pv0 + i + 1 for i in range(E)]
        NQG = A4_PAIRS * QCS
        nm0, rc0, bc0, ca0 = sNm.n, sRc.n, sBc.n, sAc.n
        nm_v = [nm0 + i + 1 for i in range(NQG)]
        rc_v = [rc0 + i + 1 for i in range(NQG)]
        bc_v = [bc0 + i + 1 for i in range(NQG)]
        cast_v = [ca0 + i + 1 for i in range(NQG)]
        stg_v = {}
        stg_base = [dSg[0].n, dSg[1].n]
        for i in range(NQG):
            stg_v[i] = stg_base[i % 2] + 16 * (i // 2 + 1)
        bias_rdy = sEv.n
        ev0 = sEv.n
        evac_v = [ev0 + i + 1 for i in range(NQG)]

        def cols(i):
            return (jof[i] * 128 if part[i] else 0, 512)

        elast = {}
        for gq in range(NQG):
            pr_, qc_ = divmod(gq, QCS)
            elast[gq] = eidx[(pr_, qc_, 4 * qc_ + 3)]

        def emit_pv(te, ip):
            pair, qc, kb = events[ip]
            gq = pair * QCS + qc
            c0, c1 = cols(ip)
            te.wait_ge(sEx.h, exp_v[ip])
            if kb == 0 and gq >= 2:
                te.wait_ge(sNm.h, nm_v[gq - 2])
            for hh in range(2):
                h = pair * 2 + hh
                ins = te.matmul(
                    pO[gq % 2][0:65, hh * 512 + c0:hh * 512 + c1],
                    vsb.ap().rearrange("p (k h c) -> p k h c", h=NH, c=65)[:, kb, h, :],
                    pt[:, (ip % 3) * 1024 + hh * 512 + c0:(ip % 3) * 1024 + hh * 512 + c1],
                    start=(kb == 0), stop=(kb == 4 * qc + 3),
                    skip_group_check=True)
                if hh == 1:
                    sPV.inc(ins)

        @block.tensor
        def _(te):
            pend = []

            def do_bc(gq):
                par = elast[gq] % 2
                te.wait_ge(sAc.h, cast_v[gq])
                te.matmul(pS[par][0:64, 0:512], s_onesb[64:65, :], rcpa,
                          start=True, stop=True, skip_group_check=True)
                sBc.inc(te.matmul(pS[par][0:64, 512:1024], s_onesb[64:65, :], rcpb,
                                  start=True, stop=True, skip_group_check=True))

            bc_dirty = {}
            nextpv = 0
            for i, (pair, qc, kb) in enumerate(events):
                c0, c1 = cols(i)
                if i >= 2:
                    te.wait_ge(sEx.h, exp_v[i - 2])
                if (i % 2) in bc_dirty:
                    te.wait_ge(sEv.h, bc_dirty.pop(i % 2))
                for hh in range(2):
                    ins = te.matmul(
                        pS[i % 2][:, hh * 512 + c0:hh * 512 + c1],
                        kfm[hh * 64:(hh + 1) * 64, pair * T + kb * 128:pair * T + (kb + 1) * 128],
                        qfm[hh * 64:(hh + 1) * 64, pair * T + qc * 512 + c0:pair * T + qc * 512 + c1],
                        start=True, stop=not part[i], tile_position=(hh * 64, 0),
                        skip_group_check=True)
                    if hh == 1 and not part[i]:
                        sS_.inc(ins)
                if part[i]:
                    j = jof[i]
                    for hh in range(2):
                        ins = te.matmul(
                            pS[i % 2][:, hh * 512 + j * 128:hh * 512 + (j + 1) * 128],
                            s_idb[:], s_tri[:],
                            start=False, stop=True, skip_group_check=True)
                        if hh == 1:
                            sS_.inc(ins)
                while nextpv <= i - 2:
                    emit_pv(te, nextpv)
                    nextpv += 1
                if i >= 1 and events[i - 1][2] == 4 * events[i - 1][1] + 3:
                    while nextpv <= i - 1:
                        emit_pv(te, nextpv)
                        nextpv += 1
                while pend:
                    gqp = pend.pop(0)
                    do_bc(gqp)
                    bc_dirty[elast[gqp] % 2] = evac_v[gqp]
                if kb == 4 * qc + 3:
                    pend.append(pair * QCS + qc)
            while nextpv < E:
                emit_pv(te, nextpv)
                nextpv += 1
            while pend:
                do_bc(pend.pop(0))

        @block.scalar
        def _(sc):
            sc.wait_ge(sEv.h, bias_rdy)
            for i, (pair, qc, kb) in enumerate(events):
                c0, c1 = cols(i)
                sc.wait_ge(sS_.h, s_v[i])
                if i >= 3:
                    sc.wait_ge(sPV.h, pv_v[i - 3])
                sEx.inc(sc.activation(
                    pt.rearrange("p (b h c) -> p b h c", b=3, c=512)[:, i % 3, :, c0:c1],
                    pS[i % 2].ap().rearrange("p (h c) -> p h c", c=512)[:, :, c0:c1],
                    AF.Exp, bias=biasP[:, 16 + pair:17 + pair], scale=0.125))
                if kb == 4 * qc + 3:
                    gq2 = pair * QCS + qc
                    sc.wait_ge(sPV.h, pv_v[i])
                    if gq2 >= 1:
                        sc.wait_ge(sBc.h, bc_v[gq2 - 1])
                    sc.activation(rcpf, pO[gq2 % 2][64:65, 0:1024], AF.Ln)
                    sc.drain()
                    sAc.inc(sc.activation(rcp_b16[64:65, 0:1024], rcpf,
                                          AF.Exp, scale=-1.0))

        @block.gpsimd
        def _(g):
            for i, (pair, qc, kb) in enumerate(events):
                if kb == 4 * qc + 3:
                    gq = pair * QCS + qc
                    g.wait_ge(sNm.h, nm_v[gq])
                    dSg[gq % 2].dma(g.dma_start(
                        out=attnO[64:128, pair * T + qc * 512:pair * T + (qc + 1) * 512],
                        in_=stg[:, (gq % 2) * 512:(gq % 2) * 512 + 512]))

        @block.vector
        def _(v):
            for gq in range(NQG):
                pair, qc = divmod(gq, QCS)
                i_last = eidx[(pair, qc, 4 * qc + 3)]
                v.wait_ge(sPV.h, pv_v[i_last])
                v.wait_ge(sBc.h, bc_v[gq])
                sEv.inc(v.tensor_copy(bcsb[:], pS[elast[gq] % 2][0:64, :]))
                v.drain()
                if gq >= 2:
                    v.wait_ge(dSg[gq % 2].h, stg_v[gq - 2])
                v.tensor_mul(attnO[0:64, pair * T + qc * 512:pair * T + (qc + 1) * 512],
                             pO[gq % 2][0:64, 0:512], bcsb[0:64, 0:512])
                sNm.inc(v.tensor_mul(stg[:, (gq % 2) * 512:(gq % 2) * 512 + 512],
                                     pO[gq % 2][0:64, 512:1024], bcsb[0:64, 512:1024]))

        assert sEx.n == exp_v[-1] and sPV.n == pv_v[-1]
        assert sNm.n == nm_v[-1] and sBc.n == bc_v[-1]
        psc.close()

    def a5(s):
        psc = ExitStack()
        pw = [psc.enter_context(nc.psum_tensor("pw%d_%d" % (s, i), [128, 1024], F32))
              for i in range(2)]
        psq = psc.enter_context(nc.psum_tensor("psq%d" % s, [128, 1024], F32))
        wo0 = sWo.n
        wo_v = [wo0 + i + 1 for i in range(NTB)]
        ac0 = sAc.n
        ac_sumsq = [ac0 + 2 * i + 1 for i in range(NTB)]
        ac_sqrt = [ac0 + 2 * i + 2 for i in range(NTB)]
        ev0 = sEv.n
        stt_v, var_v, xn_v = [], [], []
        n = ev0
        for tb in range(NTB):
            n += 1
            stt_v.append(n)
            n += 1
            n += 4
            var_v.append(n)
            n += 1
            n += 1
            xn_v.append(n)
        xnd_v = {}
        xnd_base = [dSx[0].n, dSx[1].n]
        for i in range(NTB):
            xnd_v[i] = xnd_base[i % 2] + 16 * (i // 2 + 1)
        ld_v = {}

        @block.sync
        def _(sy):
            dW.dma(sy.dma_start(
                out=wout_b.rearrange("p (g f) -> p g f", g=8),
                in_=wout_bh.rearrange("(g p) f -> p g f", p=128)))
            for tb in range(NTB):
                if tb >= 2:
                    sy.wait_ge(sEv.h, xn_v[tb - 2])
                dXb[tb % 2].dma(sy.dma_start(
                    out=xbuf[:, (tb % 2) * C:(tb % 2) * C + C],
                    in_=xin[s * T + tb * 128: s * T + (tb + 1) * 128, :]))
                ld_v[tb] = dXb[tb % 2].n
        wob_ld = dW.n

        @block.tensor
        def _(te):
            te.wait_ge(dW.h, wob_ld)
            for tb in range(NTB):
                if tb >= 2:
                    te.wait_ge(sEv.h, stt_v[tb - 2])
                for cc in range(NCC):
                    for n2 in range(2):
                        ins = te.matmul(
                            pw[tb % 2][:, n2 * 512:(n2 + 1) * 512],
                            attnO[:, cc * T + tb * 128:cc * T + (tb + 1) * 128],
                            wout_b[:, cc * C + n2 * 512:cc * C + (n2 + 1) * 512],
                            start=(cc == 0), stop=(cc == 7))
                        if cc == 7 and n2 == 1:
                            sWo.inc(ins)

        @block.scalar
        def _(sc):
            for tb in range(NTB):
                o = (tb % 2) * 8
                sc.wait_ge(sEv.h, stt_v[tb])
                sAc.inc(sc.activation(psq[:], r1[:, (tb % 2) * C:(tb % 2) * C + C],
                                      AF.Square, accum_out=stat[:, o + 1:o + 2]))
                sc.wait_ge(sEv.h, var_v[tb])
                sAc.inc(sc.activation(stat[:, o + 6:o + 7], stat[:, o + 5:o + 6],
                                      AF.Sqrt, bias=s_eps[:, 0:1]))

        @block.vector
        def _(v):
            for tb in range(NTB):
                o = (tb % 2) * 8
                r1t = r1[:, (tb % 2) * C:(tb % 2) * C + C]
                v.wait_ge(sWo.h, wo_v[tb])
                v.wait_ge(dXb[tb % 2].h, ld_v[tb])
                if tb >= 2:
                    v.wait_ge(sAc.h, ac_sumsq[tb - 2])
                sEv.inc(v.scalar_tensor_tensor(
                    r1t, pw[tb % 2][:], 1.0, xbuf[:, (tb % 2) * C:(tb % 2) * C + C],
                    ALU.mult, ALU.add))
                v.drain()
                sEv.inc(v.reduce_sum(stat[:, o:o + 1], r1t, AX.X))
                v.drain()
                v.wait_ge(sAc.h, ac_sumsq[tb])
                sEv.inc(v.tensor_scalar_mul(stat[:, o + 2:o + 3], stat[:, o:o + 1], 1.0 / C))
                sEv.inc(v.tensor_scalar_mul(stat[:, o + 3:o + 4], stat[:, o + 1:o + 2], 1.0 / C))
                v.drain()
                sEv.inc(v.tensor_mul(stat[:, o + 4:o + 5], stat[:, o + 2:o + 3], stat[:, o + 2:o + 3]))
                v.drain()
                sEv.inc(v.tensor_sub(stat[:, o + 5:o + 6], stat[:, o + 3:o + 4], stat[:, o + 4:o + 5]))
                v.drain()
                v.wait_ge(sAc.h, ac_sqrt[tb])
                sEv.inc(v.reciprocal(stat[:, o + 7:o + 8], stat[:, o + 6:o + 7]))
                v.drain()
                if tb >= 2:
                    v.wait_ge(dSx[tb % 2].h, xnd_v[tb - 2])
                sEv.inc(v.tensor_scalar(xnb[:, (tb % 2) * C:(tb % 2) * C + C], r1t,
                                        stat[:, o + 2:o + 3], stat[:, o + 7:o + 8],
                                        ALU.subtract, ALU.mult))

        @block.gpsimd
        def _(g):
            for tb in range(NTB):
                g.wait_ge(sEv.h, xn_v[tb])
                dSx[tb % 2].dma(g.dma_start(
                    out=xn_hbm[s * T + tb * 128: s * T + (tb + 1) * 128, :],
                    in_=xnb[:, (tb % 2) * C:(tb % 2) * C + C]))

        assert sEv.n == xn_v[-1]
        psc.close()

    def wtern(wi, wsrc, wdst_T, R, Ccols):
        tiles = [(r, c) for r in range(R // 128) for c in range(Ccols // 1024)]
        NT = len(tiles)
        psc = ExitStack()
        p1 = psc.enter_context(nc.psum_tensor("pww%d" % wi, [128, 1024], BF16))
        pm = psc.enter_context(nc.psum_tensor("pwm%d" % wi, [1, 1], F32))
        pb = psc.enter_context(nc.psum_tensor("pwb%d" % wi, [128, 2], F32))

        ev0 = sEv.n
        red_v = [ev0 + i + 1 for i in range(NT)]
        redall_v = ev0 + NT + 1
        m1_v = redall_v + 1
        m2_v = redall_v + 2
        sc_v = redall_v + 3
        tx0 = sTx.n
        pm_v = tx0 + 1
        pb_v = tx0 + 2
        ld1, ld2 = {}, {}

        @block.sync
        def _(sy):
            for i, (r, c) in enumerate(tiles):
                if i >= 2:
                    sy.wait_ge(sEv.h, red_v[i - 2])
                dWt[i % 2].dma(sy.dma_start(
                    out=wst[:, (i % 2) * 1024:(i % 2) * 1024 + 1024],
                    in_=wsrc[r * 128:(r + 1) * 128, c * 1024:(c + 1) * 1024]))
                ld1[i] = dWt[i % 2].n

        @block.vector
        def _(v):
            for i in range(NT):
                v.wait_ge(dWt[i % 2].h, ld1[i])
                sEv.inc(v.reduce_sum(absacc[:, i:i + 1],
                                     wst[:, (i % 2) * 1024:(i % 2) * 1024 + 1024],
                                     AX.X, apply_absolute_value=True))
            v.drain()
            sEv.inc(v.reduce_sum(absacc[:, 32:33], absacc[:, 0:NT], AX.X))
            v.drain()
            v.wait_ge(sTx.h, pm_v)
            sEv.inc(v.tensor_scalar(w11[:, 0:1], pm[:], 1.0 / (R * Ccols), 1e-5,
                                    ALU.mult, ALU.max))
            v.drain()
            sEv.inc(v.reciprocal(w11[:, 1:2], w11[:, 0:1]))
            v.drain()
            v.wait_ge(sTx.h, pb_v)
            sEv.inc(v.tensor_copy(wsc[:, wi * 2:wi * 2 + 2], pb[:]))

        @block.tensor
        def _(te):
            te.wait_ge(sEv.h, redall_v)
            sTx.inc(te.matmul(pm[:], s_onesc[:], absacc[:, 32:33],
                              start=True, stop=True))
            te.wait_ge(sEv.h, m2_v)
            sTx.inc(te.matmul(pb[:], s_ones[0:1, :], w11[0:1, 0:2],
                              start=True, stop=True))

        ev1 = sEv.n
        q3_v, evT_v = [], []
        n = ev1
        for i in range(NT):
            n += 3
            q3_v.append(n)
            n += 1
            evT_v.append(n)
        tx1 = sTx.n
        tx_v = [tx1 + i + 1 for i in range(NT)]
        st_v = {}
        st_base = [dSt2[0].n, dSt2[1].n]
        for i in range(NT):
            st_v[i] = st_base[i % 2] + 16 * (i // 2 + 1)

        @block.sync
        def _(sy):
            for i, (r, c) in enumerate(tiles):
                if i < 2:
                    sy.wait_ge(sEv.h, red_v[NT - 1])
                if i >= 2:
                    sy.wait_ge(sEv.h, q3_v[i - 2])
                dWt[i % 2].dma(sy.dma_start(
                    out=wst[:, (i % 2) * 1024:(i % 2) * 1024 + 1024],
                    in_=wsrc[r * 128:(r + 1) * 128, c * 1024:(c + 1) * 1024]))
                ld2[i] = dWt[i % 2].n

        @block.vector
        def _(v):
            v.wait_ge(sEv.h, sc_v)
            for i in range(NT):
                v.wait_ge(dWt[i % 2].h, ld2[i])
                src = wst[:, (i % 2) * 1024:(i % 2) * 1024 + 1024]
                sEv.inc(v.tensor_scalar(sqs[:], src, wsc[:, wi * 2 + 1:wi * 2 + 2],
                                        -1.0, ALU.mult, ALU.max))
                v.drain()
                sEv.inc(v.tensor_scalar(sqs[:], sqs[:], 1.0, MAGIC,
                                        ALU.min, ALU.add))
                v.drain()
                if i >= 2:
                    v.wait_ge(sTx.h, tx_v[i - 2])
                sEv.inc(v.tensor_scalar_add(
                    tern[:, (i % 2) * 1024:(i % 2) * 1024 + 1024], sqs[:], -MAGIC))
                v.wait_ge(sTx.h, tx_v[i])
                if i >= 2:
                    v.wait_ge(dSt2[i % 2].h, st_v[i - 2])
                sEv.inc(v.tensor_copy(
                    terT[:, (i % 2) * 1024:(i % 2) * 1024 + 1024], p1.ap()))

        @block.gpsimd
        def _(g):
            for i, (r, c) in enumerate(tiles):
                g.wait_ge(sEv.h, evT_v[i])
                dSt2[i % 2].dma(g.dma_start(
                    out=wdst_T.rearrange("(g p) f -> p g f", p=128)[:, c * 8:(c + 1) * 8, r * 128:(r + 1) * 128],
                    in_=terT.rearrange("p (b g f) -> p b g f", b=2, g=8)[:, i % 2, :, :]))

        @block.tensor
        def _(te):
            for i in range(NT):
                te.wait_ge(sEv.h, q3_v[i])
                if i >= 1:
                    te.wait_ge(sEv.h, evT_v[i - 1])
                for cc in range(8):
                    ins = te.transpose(
                        p1[:, cc * 128:(cc + 1) * 128],
                        tern[:, (i % 2) * 1024 + cc * 128:(i % 2) * 1024 + (cc + 1) * 128],
                        s_idb[:])
                    if cc == 7:
                        sTx.inc(ins)

        assert sEv.n == evT_v[NT - 1] and sTx.n == tx_v[-1]
        psc.close()

    ph_list = [lambda: ph0()]
    for s in range(NSEQ):
        for f in (a1, a2, a3, a4, a5):
            ph_list.append(lambda f=f, s=s: f(s))
    ph_list.append(lambda: wtern(0, wg, wgT_h, INNER, C))
    ph_list.append(lambda: wtern(1, wu, wuT_h, INNER, C))
    ph_list.append(lambda: wtern(2, wd, wdT_h, C, INNER))
    nph = 0
    for f in ph_list:
        if nph >= UPTO:
            break
        f()
        barrier()
        nph += 1
    esA.close()

    esF = ExitStack()

    def sbF(name, shape, dt):
        return esF.enter_context(nc.sbuf_tensor(name, shape, dt))

    wgT = sbF("wgT", [128, 8 * INNER], BF16)
    wuT = sbF("wuT", [128, 8 * INNER], BF16)
    xnf = sbF("xnf", [128, 2 * C], F32)
    hbf = sbF("hbf", [128, INNER], F32)
    thf = sbF("thf", [128, INNER], F32)
    fst = sbF("fst", [128, 32], F32)
    esF1 = ExitStack()
    hqb = esF1.enter_context(nc.sbuf_tensor("hqb", [128, INNER], BF16))
    xqb = esF1.enter_context(nc.sbuf_tensor("xqb", [128, 2 * C], BF16))
    xqT = esF1.enter_context(nc.sbuf_tensor("xqT", [128, 2 * C], BF16))
    sgf = esF1.enter_context(nc.sbuf_tensor("sgf", [128, 512], F32))
    hqi_yb = [None, None]

    def f1():
        psc = ExitStack()
        pT = psc.enter_context(nc.psum_tensor("pT1", [128, 1024], BF16))
        pG = [psc.enter_context(nc.psum_tensor("pG%d" % i, [128, 512], F32)) for i in range(2)]
        pU = [psc.enter_context(nc.psum_tensor("pU%d" % i, [128, 512], F32)) for i in range(2)]

        order = [("q", 0, k) for k in range(9)]
        for tb in range(NT2):
            order.append(("e", tb, 0))
            if tb + 1 < NT2:
                order += [("q", tb + 1, k) for k in range(9)]
            order += [("h", tb, oc) for oc in range(8)]
            order += [("t", tb, k) for k in range(5)]
        fd0 = sFd.n
        opv = {}
        for j, op in enumerate(order):
            opv[op] = fd0 + j + 1
        xq_v = [opv[("q", tb, 8)] for tb in range(NT2)]
        evT_v = [opv[("e", tb, 0)] for tb in range(NT2)]
        hml_v = [[opv[("h", tb, oc)] for oc in range(8)] for tb in range(NT2)]
        hs_v = [opv[("t", tb, 3)] for tb in range(NT2)]
        hq_v = [opv[("t", tb, 4)] for tb in range(NT2)]

        fa0 = sFa.n
        silu_v = [[fa0 + tb * 9 + oc + 1 for oc in range(8)] for tb in range(NT2)]
        th_v = [fa0 + tb * 9 + 9 for tb in range(NT2)]
        f0 = sF.n
        tx_v, g_v, u_v = [], [], []
        n = f0
        for tb in range(NT2):
            n += 1
            tx_v.append(n)
            gl, ul = [], []
            for oc in range(8):
                n += 1
                gl.append(n)
                n += 1
                ul.append(n)
            g_v.append(gl)
            u_v.append(ul)
        hd0 = dSh.n
        hqd_v = [hd0 + 16 * (tb + 1) for tb in range(NT2)]
        xn_ld = {}

        @block.sync
        def _(sy):
            for half in range(2):
                dW.dma(sy.dma_start(
                    out=wgT.ap().rearrange("p (g f) -> p g f", g=8)[:, half * 4:(half + 1) * 4, :],
                    in_=wgT_h.rearrange("(g p) f -> p g f", p=128)[:, half * 4:(half + 1) * 4, :]))
                dW.dma(sy.dma_start(
                    out=wuT.ap().rearrange("p (g f) -> p g f", g=8)[:, half * 4:(half + 1) * 4, :],
                    in_=wuT_h.rearrange("(g p) f -> p g f", p=128)[:, half * 4:(half + 1) * 4, :]))
            for tb in range(NT2):
                if tb >= 2:
                    sy.wait_ge(sFd.h, xq_v[tb - 2])
                dXn[tb % 2].dma(sy.dma_start(out=xnf[:, (tb % 2) * C:(tb % 2) * C + C],
                                    in_=xn_hbm[tb * 128:(tb + 1) * 128, :]))
                xn_ld[tb] = dXn[tb % 2].n

        w_ld = dW.n

        def emit_quant(v, tb):
            o = (tb % 2) * 16
            xt = xnf[:, (tb % 2) * C:(tb % 2) * C + C]
            v.wait_ge(dXn[tb % 2].h, xn_ld[tb])
            if tb >= 2:
                v.wait_ge(sFa.h, th_v[tb - 2])
            sFd.inc(v.tensor_reduce(fst[:, o:o + 1], xt, AX.X, ALU.max,
                                    apply_absolute_value=True))
            v.drain()
            sFd.inc(v.tensor_scalar_max(fst[:, o:o + 1], fst[:, o:o + 1], 1e-5))
            v.drain()
            sFd.inc(v.reciprocal(fst[:, o + 1:o + 2], fst[:, o:o + 1]))
            v.drain()
            sFd.inc(v.tensor_scalar_mul(fst[:, o + 2:o + 3], fst[:, o + 1:o + 2], 127.0))
            sFd.inc(v.tensor_scalar_mul(fst[:, o + 3:o + 4], fst[:, o:o + 1], 1.0 / 127.0))
            v.drain()
            sFd.inc(v.tensor_mul(fst[:, o + 4:o + 5], fst[:, o + 3:o + 4], wsc[:, 0:1]))
            sFd.inc(v.tensor_mul(fst[:, o + 5:o + 6], fst[:, o + 3:o + 4], wsc[:, 2:3]))
            v.drain()
            sFd.inc(v.tensor_scalar(thf[:, 0:C], xt, fst[:, o + 2:o + 3], MAGIC,
                                    ALU.mult, ALU.add))
            v.drain()
            if tb >= 1:
                v.wait_ge(sF.h, tx_v[tb - 1])
            sFd.inc(v.tensor_scalar_add(xqb[:, (tb % 2) * C:(tb % 2) * C + C],
                                        thf[:, 0:C], -MAGIC))

        @block.vector
        def _(v):
            emit_quant(v, 0)
            for tb in range(NT2):
                o = (tb % 2) * 16
                v.wait_ge(sF.h, tx_v[tb])
                if tb >= 2:
                    v.wait_ge(sF.h, u_v[tb - 2][7])
                sFd.inc(v.tensor_copy(xqT[:, (tb % 2) * C:(tb % 2) * C + C], pT.ap()))
                if tb + 1 < NT2:
                    emit_quant(v, tb + 1)
                for oc in range(8):
                    v.wait_ge(sF.h, u_v[tb][oc])
                    v.wait_ge(sFa.h, silu_v[tb][oc])
                    sFd.inc(v.scalar_tensor_tensor(
                        hbf[:, oc * 512:(oc + 1) * 512], pU[oc % 2][:],
                        fst[:, o + 5:o + 6], sgf[:], ALU.mult, ALU.mult))
                v.drain()
                sFd.inc(v.tensor_reduce(fst[:, o + 6:o + 7], hbf[:], AX.X, ALU.max,
                                        apply_absolute_value=True))
                v.drain()
                sFd.inc(v.reciprocal(fst[:, o + 7:o + 8], fst[:, o + 6:o + 7]))
                v.drain()
                sFd.inc(v.tensor_scalar_mul(fst[:, o + 8:o + 9], fst[:, o + 7:o + 8], 127.0))
                sFd.inc(v.tensor_scalar_mul(dsh_all[:, tb:tb + 1], fst[:, o + 6:o + 7], 1.0 / 127.0))
                v.drain()
                v.wait_ge(sFa.h, th_v[tb])
                if tb >= 1:
                    v.wait_ge(dSh.h, hqd_v[tb - 1])
                sFd.inc(v.tensor_scalar_add(hqb[:], thf[:], -MAGIC))

        @block.tensor
        def _(te):
            te.wait_ge(dW.h, w_ld)
            for tb in range(NT2):
                te.wait_ge(sFd.h, xq_v[tb])
                if tb >= 1:
                    te.wait_ge(sFd.h, evT_v[tb - 1])
                for cc in range(8):
                    ins = te.transpose(pT[:, cc * 128:(cc + 1) * 128],
                                       xqb[:, (tb % 2) * C + cc * 128:(tb % 2) * C + (cc + 1) * 128],
                                       s_idb[:])
                    if cc == 7:
                        sF.inc(ins)
                for oc in range(8):
                    gc = tb * 8 + oc
                    te.wait_ge(sFd.h, evT_v[tb])
                    if gc >= 2:
                        tb2, oc2 = divmod(gc - 2, 8)
                        te.wait_ge(sFa.h, silu_v[tb2][oc2])
                    for cc in range(8):
                        ins = te.matmul(pG[oc % 2][:],
                                        xqT[:, (tb % 2) * C + cc * 128:(tb % 2) * C + (cc + 1) * 128],
                                        wgT[:, cc * INNER + oc * 512:cc * INNER + (oc + 1) * 512],
                                        start=(cc == 0), stop=(cc == 7))
                        if cc == 7:
                            sF.inc(ins)
                    if gc >= 2:
                        tb2, oc2 = divmod(gc - 2, 8)
                        te.wait_ge(sFd.h, hml_v[tb2][oc2])
                    for cc in range(8):
                        ins = te.matmul(pU[oc % 2][:],
                                        xqT[:, (tb % 2) * C + cc * 128:(tb % 2) * C + (cc + 1) * 128],
                                        wuT[:, cc * INNER + oc * 512:cc * INNER + (oc + 1) * 512],
                                        start=(cc == 0), stop=(cc == 7))
                        if cc == 7:
                            sF.inc(ins)

        @block.scalar
        def _(sc):
            for tb in range(NT2):
                o = (tb % 2) * 16
                for oc in range(8):
                    gc = tb * 8 + oc
                    sc.wait_ge(sF.h, g_v[tb][oc])
                    if gc >= 1:
                        tb2, oc2 = divmod(gc - 1, 8)
                        sc.wait_ge(sFd.h, hml_v[tb2][oc2])
                    sFa.inc(sc.activation(sgf[:], pG[oc % 2][:], AF.Silu,
                                          scale=fst[:, o + 4:o + 5]))
                sc.wait_ge(sFd.h, hs_v[tb])
                sFa.inc(sc.activation(thf[:], hbf[:], AF.Identity,
                                      scale=fst[:, o + 8:o + 9], bias=s_magic[:, 0:1]))

        @block.gpsimd
        def _(g):
            for tb in range(NT2):
                g.wait_ge(sFd.h, hq_v[tb])
                dSh.dma(g.dma_start(out=hq_hbm[tb * 128:(tb + 1) * 128, :],
                                   in_=hqb[:]))

        assert sFd.n == opv[order[-1]], (sFd.n, opv[order[-1]])
        assert sF.n == u_v[-1][7]
        assert sFa.n == th_v[-1]
        psc.close()

    def f2():
        hqi_, yb_ = hqi_yb[0], hqi_yb[1]
        psc = ExitStack()
        pH = [psc.enter_context(nc.psum_tensor("pH%d" % i, [128, 1024], BF16)) for i in range(2)]
        pD = [psc.enter_context(nc.psum_tensor("pD%d" % i, [128, 1024], F32)) for i in range(2)]
        psq2 = psc.enter_context(nc.psum_tensor("psq2", [128, 1024], F32))
        wdT = wgT
        hqT = wuT.ap()[:, 0:2 * INNER]

        fd0 = sFd.n
        dshmd_v = fd0 + 1
        evh_v, ln_v = [], []
        n = fd0 + 1
        for tb in range(NT2):
            el = []
            for grp in range(4):
                n += 1
                el.append(n)
            evh_v.append(el)
            n += 8
            ln_v.append(n)
        fa0 = sFa.n
        sumsq_v = [fa0 + 2 * tb + 1 for tb in range(NT2)]
        sqrt_v = [fa0 + 2 * tb + 2 for tb in range(NT2)]
        f0 = sF.n
        txg_v, dmm_v = [], []
        n = f0
        for tb in range(NT2):
            tl = []
            for grp in range(4):
                n += 1
                tl.append(n)
            txg_v.append(tl)
            dl = []
            for n2 in range(2):
                n += 1
                dl.append(n)
            dmm_v.append(dl)
        dS0 = sOut.n
        y_v = [dS0 + 16 * (tb + 1) for tb in range(NT2)]
        hq_ld, xn_ld = {}, {}

        @block.vector
        def _(v):
            sFd.inc(v.tensor_scalar_mul(dsh_all[:, 32:64], dsh_all[:, 0:32],
                                        wsc[:, 4:5]))

        @block.sync
        def _(sy):
            for q in range(4):
                dW.dma(sy.dma_start(
                    out=wdT.ap().rearrange("p (g f) -> p g f", g=32)[:, q * 8:(q + 1) * 8, 0:C],
                    in_=wdT_h.rearrange("(g p) f -> p g f", p=128)[:, q * 8:(q + 1) * 8, :]))
            for tb in range(NT2):
                if tb >= 2:
                    sy.wait_ge(sF.h, dmm_v[tb - 2][1])
                dHq[tb % 2].dma(sy.dma_start(out=hqi_[:, (tb % 2) * INNER:(tb % 2) * INNER + INNER],
                                    in_=hq_hbm[tb * 128:(tb + 1) * 128, :]))
                hq_ld[tb] = dHq[tb % 2].n
                if tb >= 2:
                    sy.wait_ge(sFd.h, ln_v[tb - 2])
                dXn[tb % 2].dma(sy.dma_start(out=xnf[:, (tb % 2) * C:(tb % 2) * C + C],
                                    in_=xn_hbm[tb * 128:(tb + 1) * 128, :]))
                xn_ld[tb] = dXn[tb % 2].n

        wd_ld = dW.n

        @block.tensor
        def _(te):
            te.wait_ge(dW.h, wd_ld)
            te.wait_ge(sFd.h, dshmd_v)
            for tb in range(NT2):
                te.wait_ge(dHq[tb % 2].h, hq_ld[tb])
                for grp in range(4):
                    gi = tb * 4 + grp
                    if gi >= 2:
                        tbp, gp_ = divmod(gi - 2, 4)
                        te.wait_ge(sFd.h, evh_v[tbp][gp_])
                    for k in range(8):
                        ic = grp * 8 + k
                        ins = te.transpose(
                            pH[grp % 2][:, k * 128:(k + 1) * 128],
                            hqi_[:, (tb % 2) * INNER + ic * 128:(tb % 2) * INNER + (ic + 1) * 128],
                            s_idb[:])
                        if k == 7:
                            sF.inc(ins)
                for n2 in range(2):
                    if tb >= 2:
                        te.wait_ge(sFd.h, ln_v[tb - 2])
                    for ic in range(32):
                        if ic == 0 and n2 == 0:
                            te.wait_ge(sFd.h, evh_v[tb][3])
                        ins = te.matmul(
                            pD[tb % 2][:, n2 * 512:(n2 + 1) * 512],
                            hqT[:, (tb % 2) * INNER + ic * 128:(tb % 2) * INNER + (ic + 1) * 128],
                            wdT[:, ic * 1024 + n2 * 512:ic * 1024 + (n2 + 1) * 512],
                            start=(ic == 0), stop=(ic == 31))
                        if ic == 31:
                            sF.inc(ins)

        @block.vector
        def _(v):
            for tb in range(NT2):
                o = (tb % 2) * 16
                for grp in range(4):
                    v.wait_ge(sF.h, txg_v[tb][grp])
                    if tb >= 2:
                        v.wait_ge(sF.h, dmm_v[tb - 2][1])
                    sFd.inc(v.tensor_copy(
                        hqT.rearrange("p (b g f) -> p b g f", b=2, g=32)[:, tb % 2, grp * 8:(grp + 1) * 8, :],
                        pH[grp % 2].ap().rearrange("p (g f) -> p g f", g=8)))
                v.wait_ge(sF.h, dmm_v[tb][1])
                v.wait_ge(dXn[tb % 2].h, xn_ld[tb])
                if tb >= 1:
                    v.wait_ge(sFa.h, sumsq_v[tb - 1])
                r2t = thf[:, 0:C]
                sFd.inc(v.scalar_tensor_tensor(
                    r2t, pD[tb % 2][:], dsh_all[:, 32 + tb:33 + tb],
                    xnf[:, (tb % 2) * C:(tb % 2) * C + C], ALU.mult, ALU.add))
                v.drain()
                sFd.inc(v.reduce_sum(fst[:, o:o + 1], r2t, AX.X))
                v.drain()
                v.wait_ge(sFa.h, sumsq_v[tb])
                sFd.inc(v.tensor_scalar_mul(fst[:, o + 2:o + 3], fst[:, o:o + 1], 1.0 / C))
                sFd.inc(v.tensor_scalar_mul(fst[:, o + 3:o + 4], fst[:, o + 1:o + 2], 1.0 / C))
                v.drain()
                sFd.inc(v.tensor_mul(fst[:, o + 4:o + 5], fst[:, o + 2:o + 3], fst[:, o + 2:o + 3]))
                v.drain()
                sFd.inc(v.tensor_sub(fst[:, o + 5:o + 6], fst[:, o + 3:o + 4], fst[:, o + 4:o + 5]))
                v.drain()
                v.wait_ge(sFa.h, sqrt_v[tb])
                sFd.inc(v.reciprocal(fst[:, o + 7:o + 8], fst[:, o + 6:o + 7]))
                v.drain()
                if tb >= 1:
                    v.wait_ge(sOut.h, y_v[tb - 1])
                sFd.inc(v.tensor_scalar(yb_[:], r2t,
                                        fst[:, o + 2:o + 3], fst[:, o + 7:o + 8],
                                        ALU.subtract, ALU.mult))

        @block.scalar
        def _(sc):
            for tb in range(NT2):
                o = (tb % 2) * 16
                sc.wait_ge(sFd.h, ln_v[tb] - 7)
                sFa.inc(sc.activation(psq2[:], thf[:, 0:C], AF.Square,
                                      accum_out=fst[:, o + 1:o + 2]))
                sc.wait_ge(sFd.h, ln_v[tb] - 2)
                sFa.inc(sc.activation(fst[:, o + 6:o + 7], fst[:, o + 5:o + 6],
                                      AF.Sqrt, bias=s_eps[:, 0:1]))

        @block.gpsimd
        def _(g):
            for tb in range(NT2):
                g.wait_ge(sFd.h, ln_v[tb])
                sOut.dma(g.dma_start(out=y[tb * 128:(tb + 1) * 128, :],
                                     in_=yb_[:]))

        assert sFd.n == ln_v[-1]
        assert sOut.n == y_v[-1]
        psc.close()

    if nph < UPTO:
        f1()
        barrier()
        nph += 1
    esF1.close()
    hqi_yb = [esF.enter_context(nc.sbuf_tensor("hqi", [128, 2 * INNER], BF16)),
              esF.enter_context(nc.sbuf_tensor("yb", [128, C], F32))]
    if nph < UPTO:
        f2()
        barrier()
        nph += 1

    @block.sync
    def _(sy):
        sy.wait_ge(sOut.h, sOut.n)

    esF.close()
    es.close()
    return nc


def make_consts():
    idf = np.eye(128, dtype=np.float32)
    idb = np.eye(128, dtype=bfnp)
    tri = np.where(np.arange(128)[:, None] > np.arange(128)[None, :],
                   np.float32(-30000.0), np.float32(0.0)).astype(bfnp)
    hmap = np.zeros((128, 128), dtype=bfnp)
    for cc in range(8):
        for p in range(128):
            hmap[p, cc * 16 + (2 * cc + p // 64)] = 1
    ones = np.ones((128, 128), dtype=np.float32)
    id16 = np.eye(16, dtype=np.float32)
    return dict(c_idf=idf, c_idb=idb, c_tri=tri, c_hmap=hmap, c_ones=ones,
                c_id16=id16)


def kernel(L0_attn_x, L0_attn_w_qkv, L0_attn_w_out, L0_ffn_w_gate, L0_ffn_w_up,
           L0_ffn_w_down):
    x = np.ascontiguousarray(np.asarray(L0_attn_x, dtype=np.float32))
    B = x.shape[0]
    ncore = 8
    per = B // ncore
    nc = build_program()
    base = dict(
        wqkv=np.ascontiguousarray(np.asarray(L0_attn_w_qkv, dtype=np.float32)),
        wout=np.ascontiguousarray(np.asarray(L0_attn_w_out, dtype=np.float32)),
        wg=np.ascontiguousarray(np.asarray(L0_ffn_w_gate, dtype=np.float32)),
        wu=np.ascontiguousarray(np.asarray(L0_ffn_w_up, dtype=np.float32)),
        wd=np.ascontiguousarray(np.asarray(L0_ffn_w_down, dtype=np.float32)),
        **make_consts())
    in_maps = []
    for c in range(ncore):
        m = dict(base)
        m["xin"] = np.ascontiguousarray(x[c * per:(c + 1) * per].reshape(per * T, C))
        in_maps.append(m)
    res = run_bass_kernel_spmd(nc, in_maps, core_ids=list(range(ncore)))
    outs = [r["y"].reshape(per, T, C) for r in res.results]
    return np.concatenate(outs, axis=0).astype(np.float32)


# version 06
# speedup vs baseline: 1.5823x; 1.0381x over previous
import numpy as np
import ml_dtypes
import concourse.bass as bass
import concourse.mybir as mybir
from concourse.bass_utils import run_bass_kernel_spmd
from contextlib import ExitStack

F32 = mybir.dt.float32
BF16 = mybir.dt.bfloat16
AF = mybir.ActivationFunctionType
ALU = mybir.AluOpType
AX = mybir.AxisListType
bfnp = ml_dtypes.bfloat16

MAGIC = float(np.float32(1.5 * 2 ** 23))
T, C, NH, DK, NSEQ, INNER = 2048, 1024, 16, 64, 2, 4096
NTB = T // 128
NQC = 4
NCC = C // 128
NPAIR = NH // 2
NT2 = NSEQ * NTB

DEBUG_OUT = False
UPTO = 99
A4_PAIRS = 8
A4_QCS = 4


class Sem:
    def __init__(self, es, nc, name):
        self.h = es.enter_context(nc.semaphore(name))
        self.n = 0

    def inc(self, instr, k=1):
        instr.then_inc(self.h, k)
        self.n += k
        return instr

    def dma(self, instr):
        return self.inc(instr, 16)


def build_program():
    nc = bass.Bass("TRN2", target_bir_lowering=False)

    xin = nc.dram_tensor("xin", [NSEQ * T, C], F32, kind="ExternalInput").ap()
    wqkv = nc.dram_tensor("wqkv", [C, 3 * C], F32, kind="ExternalInput").ap()
    wout = nc.dram_tensor("wout", [C, C], F32, kind="ExternalInput").ap()
    wg = nc.dram_tensor("wg", [INNER, C], F32, kind="ExternalInput").ap()
    wu = nc.dram_tensor("wu", [INNER, C], F32, kind="ExternalInput").ap()
    wd = nc.dram_tensor("wd", [C, INNER], F32, kind="ExternalInput").ap()
    c_idf = nc.dram_tensor("c_idf", [128, 128], F32, kind="ExternalInput").ap()
    c_idb = nc.dram_tensor("c_idb", [128, 128], BF16, kind="ExternalInput").ap()
    c_tri = nc.dram_tensor("c_tri", [128, 128], BF16, kind="ExternalInput").ap()
    c_hmap = nc.dram_tensor("c_hmap", [128, 128], BF16, kind="ExternalInput").ap()
    c_ones = nc.dram_tensor("c_ones", [128, 128], F32, kind="ExternalInput").ap()
    c_id16 = nc.dram_tensor("c_id16", [16, 16], F32, kind="ExternalInput").ap()
    y = nc.dram_tensor("y", [NSEQ * T, C], F32, kind="ExternalOutput").ap()

    KIND = "ExternalOutput" if DEBUG_OUT else "Internal"
    xn_hbm = nc.dram_tensor("xn_hbm", [NSEQ * T, C], F32, kind=KIND).ap()
    hq_hbm = nc.dram_tensor("hq_hbm", [NSEQ * T, INNER], BF16, kind=KIND).ap()
    wqkv_bh = nc.dram_tensor("wqkv_bh", [C, 3 * C], BF16, kind="Internal").ap()
    wout_bh = nc.dram_tensor("wout_bh", [C, C], BF16, kind="Internal").ap()
    wgT_h = nc.dram_tensor("wgT_h", [C, INNER], BF16, kind=KIND).ap()
    wuT_h = nc.dram_tensor("wuT_h", [C, INNER], BF16, kind=KIND).ap()
    wdT_h = nc.dram_tensor("wdT_h", [INNER, C], BF16, kind=KIND).ap()

    es = ExitStack()

    s_idf = es.enter_context(nc.sbuf_tensor("s_idf", [128, 128], F32))
    s_idb = es.enter_context(nc.sbuf_tensor("s_idb", [128, 128], BF16))
    s_tri = es.enter_context(nc.sbuf_tensor("s_tri", [128, 128], BF16))
    s_hmap = es.enter_context(nc.sbuf_tensor("s_hmap", [128, 128], BF16))
    s_ones = es.enter_context(nc.sbuf_tensor("s_ones", [128, 128], F32))
    s_onesc = es.enter_context(nc.sbuf_tensor("s_onesc", [128, 1], F32))
    s_id16 = es.enter_context(nc.sbuf_tensor("s_id16", [16, 16], F32))
    s_onesb = es.enter_context(nc.sbuf_tensor("s_onesb", [128, 64], BF16))
    s_eps = es.enter_context(nc.sbuf_tensor("s_eps", [128, 1], F32))
    s_magic = es.enter_context(nc.sbuf_tensor("s_magic", [128, 1], F32))
    wsc = es.enter_context(nc.sbuf_tensor("wsc", [128, 8], F32))
    dsh_all = es.enter_context(nc.sbuf_tensor("dsh_all", [128, 64], F32))
    w11 = es.enter_context(nc.sbuf_tensor("w11", [1, 4], F32))

    SD = {}
    for name in ("dX dW dS sTx sEv sAc sQk sS sEx sMk sPV sRc sBc sNm "
                 "sWo sGp sF sFd sFa sOut").split():
        SD[name] = Sem(es, nc, name)
    dXb = [Sem(es, nc, "dXb%d" % i) for i in range(2)]
    dWq = [Sem(es, nc, "dWq%d" % i) for i in range(2)]
    dWt = [Sem(es, nc, "dWt%d" % i) for i in range(2)]
    dXn = [Sem(es, nc, "dXn%d" % i) for i in range(2)]
    dHq = [Sem(es, nc, "dHq%d" % i) for i in range(2)]
    dSc = [Sem(es, nc, "dSc%d" % i) for i in range(2)]
    dSg = [Sem(es, nc, "dSg%d" % i) for i in range(2)]
    dSx = [Sem(es, nc, "dSx%d" % i) for i in range(2)]
    dSt2 = [Sem(es, nc, "dSt2_%d" % i) for i in range(2)]
    dSh = Sem(es, nc, "dSh")
    dX, dW, dS = SD["dX"], SD["dW"], SD["dS"]
    sTx, sEv, sAc, sQk = SD["sTx"], SD["sEv"], SD["sAc"], SD["sQk"]
    sS_, sEx, sMk, sPV = SD["sS"], SD["sEx"], SD["sMk"], SD["sPV"]
    sRc, sBc, sNm = SD["sRc"], SD["sBc"], SD["sNm"]
    sWo, sGp = SD["sWo"], SD["sGp"]
    sF, sFd, sFa, sOut = SD["sF"], SD["sFd"], SD["sFa"], SD["sOut"]
    ALL = (list(SD.values()) + dXb + dWq + dWt + dXn + dHq + dSc + dSg + dSx
           + dSt2 + [dSh])

    block = es.enter_context(nc.Block())

    def barrier():
        snaps = [(s.h, s.n) for s in ALL if s.n > 0]
        for attr in ("sync", "tensor", "vector", "scalar", "gpsimd"):
            def f(eng, snaps=snaps):
                for h, v in snaps:
                    eng.wait_ge(h, v)
            getattr(block, attr)(f)

    esA = ExitStack()

    def sbA(name, shape, dt):
        return esA.enter_context(nc.sbuf_tensor(name, shape, dt))

    xbuf = sbA("xbuf", [128, 2 * C], F32)
    wst = sbA("wst", [128, 2 * 1024], F32)
    wcb = sbA("wcb", [128, 2 * 1024], BF16)
    ovl = sbA("ovl", [128, 8192], BF16)
    wqs = ovl.ap()[:, 0:4096]
    wout_b = ovl.ap()[:, 0:8192]
    xfm = sbA("xfm", [128, 8 * T], BF16)
    attnO = xfm
    qfm = sbA("qfm", [128, 8 * T], BF16)
    kfm = sbA("kfm", [128, 8 * T], BF16)
    vsb = sbA("vsb", [128, NTB * NH * 65], BF16)
    pt = ovl.ap()[:, 0:3072]
    qsq = ovl.ap()[:, 0:2048]
    rcp_t = sbA("rcp_t", [128, 1024], F32)
    rcpf = rcp_t.ap()[64:65, 0:1024]
    rcp_b16 = sbA("rcp_b16", [128, 1024], BF16)
    rcpa = rcp_b16.ap()[64:65, 0:512]
    rcpb = rcp_b16.ap()[64:65, 512:1024]
    bcsb = sbA("bcsb", [64, 1024], F32)
    stg = sbA("stg", [64, 2 * 512], BF16)
    biasq = sbA("biasq", [16, 4], F32)
    nrow = sbA("nrow", [1, 16], F32)
    biasP = sbA("biasP", [128, 24], F32)
    r1 = sbA("r1", [128, 2 * C], F32)
    stat = sbA("stat", [128, 16], F32)
    sqs = r1.ap()[:, 0:C]
    xnb = sbA("xnb", [128, 2 * C], F32)
    absacc = sbA("absacc", [128, 33], F32)
    tern = wcb.ap()[:, 0:2048]
    terT = ovl.ap()[:, 4096:6144]

    def ph0():
        @block.sync
        def _(sy):
            dW.dma(sy.dma_start(out=s_idf[:], in_=c_idf))
            dW.dma(sy.dma_start(out=s_idb[:], in_=c_idb))
            dW.dma(sy.dma_start(out=s_tri[:], in_=c_tri))
            dW.dma(sy.dma_start(out=s_hmap[:], in_=c_hmap))
            dW.dma(sy.dma_start(out=s_ones[:], in_=c_ones))
            dW.dma(sy.dma_start(out=s_id16[:], in_=c_id16))
        cw = dW.n

        @block.vector
        def _(v):
            sEv.inc(v.memset(s_onesb[:], 1.0))
            sEv.inc(v.memset(s_eps[:], 1e-5))
            sEv.inc(v.memset(s_magic[:], MAGIC))
            sEv.inc(v.memset(s_onesc[:], 1.0))
            sEv.inc(v.memset(
                vsb.ap().rearrange("p (g c) -> p g c", c=65)[:, :, 64:65], 1.0))

        jobs = [("qkv", t, h) for t in range(8) for h in range(3)]
        jobs += [("out", t, 0) for t in range(8)]
        NJ = len(jobs)
        ld_v = {}
        ev0 = sEv.n
        ev_v = [ev0 + i + 1 for i in range(NJ)]
        st_v = {}

        @block.sync
        def _(sy):
            for i, (tag, t, h) in enumerate(jobs):
                if i >= 2:
                    sy.wait_ge(sEv.h, ev_v[i - 2])
                b = wst[:, (i % 2) * 1024:(i % 2) * 1024 + 1024]
                if tag == "qkv":
                    src = wqkv[t * 128:(t + 1) * 128, h * 1024:(h + 1) * 1024]
                else:
                    src = wout[t * 128:(t + 1) * 128, :]
                dWt[i % 2].dma(sy.dma_start(out=b, in_=src))
                ld_v[i] = dWt[i % 2].n

        @block.vector
        def _(v):
            for i, (tag, t, h) in enumerate(jobs):
                v.wait_ge(dWt[i % 2].h, ld_v[i])
                b = wst[:, (i % 2) * 1024:(i % 2) * 1024 + 1024]
                if i >= 2 and (i - 2) in st_v:
                    v.wait_ge(dSc[i % 2].h, st_v[i - 2])
                dst = wcb[:, (i % 2) * 1024:(i % 2) * 1024 + 1024]
                sEv.inc(v.tensor_copy(dst, b))

        @block.gpsimd
        def _(g):
            for i, (tag, t, h) in enumerate(jobs):
                g.wait_ge(sEv.h, ev_v[i])
                src = wcb[:, (i % 2) * 1024:(i % 2) * 1024 + 1024]
                if tag == "qkv":
                    dst = wqkv_bh[t * 128:(t + 1) * 128, h * 1024:(h + 1) * 1024]
                else:
                    dst = wout_bh[t * 128:(t + 1) * 128, :]
                dSc[i % 2].dma(g.dma_start(out=dst, in_=src))
                st_v[i] = dSc[i % 2].n

    def a1(s):
        psc = ExitStack()
        ptx = psc.enter_context(nc.psum_tensor("ptx%d" % s, [128, 1024], F32))
        tx0 = sTx.n
        tx_v = [tx0 + i + 1 for i in range(NTB)]
        ev0 = sEv.n
        ev_v = [ev0 + i + 1 for i in range(NTB)]
        ld_v = {}

        @block.sync
        def _(sy):
            for tb in range(NTB):
                if tb >= 2:
                    sy.wait_ge(sTx.h, tx_v[tb - 2])
                dXb[tb % 2].dma(sy.dma_start(
                    out=xbuf[:, (tb % 2) * C:(tb % 2) * C + C],
                    in_=xin[s * T + tb * 128: s * T + (tb + 1) * 128, :]))
                ld_v[tb] = dXb[tb % 2].n

        @block.tensor
        def _(te):
            for tb in range(NTB):
                te.wait_ge(dXb[tb % 2].h, ld_v[tb])
                if tb >= 1:
                    te.wait_ge(sEv.h, ev_v[tb - 1])
                for cc in range(NCC):
                    ins = te.transpose(
                        ptx[:, cc * 128:(cc + 1) * 128],
                        xbuf[:, (tb % 2) * C + cc * 128:(tb % 2) * C + (cc + 1) * 128],
                        s_idf[:])
                    if cc == 7:
                        sTx.inc(ins)

        @block.vector
        def _(v):
            for tb in range(NTB):
                v.wait_ge(sTx.h, tx_v[tb])
                sEv.inc(v.tensor_copy(
                    xfm.ap().rearrange("p (g t) -> p g t", t=T)[:, :, tb * 128:(tb + 1) * 128],
                    ptx.ap().rearrange("p (g c) -> p g c", c=128)))
        assert sTx.n == tx_v[-1] and sEv.n == ev_v[-1]
        psc.close()

    def a2(s):
        psc = ExitStack()
        pqk = [psc.enter_context(nc.psum_tensor("pqk%d_%d" % (s, i), [128, 512], F32))
               for i in range(2)]
        pv = [psc.enter_context(nc.psum_tensor("pv%d_%d" % (s, i), [128, 512], F32))
              for i in range(2)]

        qk0 = sQk.n
        qk_v = {}
        n = qk0
        for of in range(16):
            for tc in range(4):
                n += 1
                qk_v[(of, tc)] = n
        v_v = {}
        for nv in range(2):
            for tb in range(NTB):
                n += 1
                v_v[(nv, tb)] = n
        ac0 = sAc.n
        ac_v = {}
        n = ac0
        for of in range(16):
            for tc in range(4):
                n += 1
                ac_v[(of, tc)] = n
        ev0 = sEv.n
        evv_v = {}
        n = ev0
        for nv in range(2):
            for tb in range(NTB):
                n += 1
                evv_v[(nv, tb)] = n
        wl_v = {}

        @block.sync
        def _(sy):
            for of in range(16):
                if of >= 2:
                    sy.wait_ge(sQk.h, qk_v[(of - 2, 3)])
                dWq[of % 2].dma(sy.dma_start(
                    out=wqs.rearrange("p (b f) -> p b f", b=2)[:, of % 2, 0:1024],
                    in_=wqkv_bh.rearrange("(g p) f -> p g f", p=128)[:, :, of * 128:(of + 1) * 128]))
                wl_v[of] = dWq[of % 2].n
            for nv in range(2):
                if nv == 0:
                    sy.wait_ge(sQk.h, qk_v[(14, 3)])
                    sy.wait_ge(sQk.h, qk_v[(15, 3)])
                else:
                    sy.wait_ge(sQk.h, v_v[(0, NTB - 1)])
                for hf in range(2):
                    dWq[hf].dma(sy.dma_start(
                        out=wqs.rearrange("p (b f) -> p b f", b=2)[:, hf, :],
                        in_=wqkv_bh.rearrange("(g p) f -> p g f", p=128)[:, hf * 4:(hf + 1) * 4, 2048 + nv * 512:2048 + (nv + 1) * 512]))
                wl_v[16 + nv] = (dWq[0].n, dWq[1].n)

        @block.tensor
        def _(te):
            for of in range(16):
                te.wait_ge(dWq[of % 2].h, wl_v[of])
                for tc in range(4):
                    m = of * 4 + tc
                    if m >= 2:
                        of2, tc2 = divmod(m - 2, 4)
                        te.wait_ge(sAc.h, ac_v[(of2, tc2)])
                    for cc in range(NCC):
                        ins = te.matmul(
                            pqk[m % 2][:],
                            wqs[:, (of % 2) * 2048 + cc * 128:(of % 2) * 2048 + (cc + 1) * 128],
                            xfm[:, cc * T + tc * 512:cc * T + (tc + 1) * 512],
                            start=(cc == 0), stop=(cc == 7))
                        if cc == 7:
                            sQk.inc(ins)
            for nv in range(2):
                te.wait_ge(dWq[0].h, wl_v[16 + nv][0])
                te.wait_ge(dWq[1].h, wl_v[16 + nv][1])
                for tb in range(NTB):
                    m = nv * NTB + tb
                    if m >= 2:
                        nv2, tb2 = divmod(m - 2, NTB)
                        te.wait_ge(sEv.h, evv_v[(nv2, tb2)])
                    for cc in range(NCC):
                        ins = te.matmul(
                            pv[m % 2][:],
                            xfm[:, cc * T + tb * 128:cc * T + (tb + 1) * 128],
                            wqs[:, (cc // 4) * 2048 + (cc % 4) * 512:(cc // 4) * 2048 + (cc % 4) * 512 + 512],
                            start=(cc == 0), stop=(cc == 7))
                        if cc == 7:
                            sQk.inc(ins)

        @block.scalar
        def _(sc):
            for of in range(16):
                dstt = qfm if of < 8 else kfm
                ch = of % 8
                for tc in range(4):
                    sc.wait_ge(sQk.h, qk_v[(of, tc)])
                    sAc.inc(sc.activation(
                        dstt[:, ch * T + tc * 512:ch * T + (tc + 1) * 512],
                        pqk[(of * 4 + tc) % 2][:], AF.Identity))

        @block.vector
        def _(v):
            for nv in range(2):
                for tb in range(NTB):
                    v.wait_ge(sQk.h, v_v[(nv, tb)])
                    sEv.inc(v.tensor_copy(
                        vsb.ap().rearrange("p (k h c) -> p k h c", h=NH, c=65)[:, tb, nv * 8:(nv + 1) * 8, 0:64],
                        pv[(nv * NTB + tb) % 2].ap().rearrange("p (h c) -> p h c", c=64)))
        assert sQk.n == v_v[(1, NTB - 1)]
        psc.close()

    def a3(s):
        psc = ExitStack()
        pn = psc.enter_context(nc.psum_tensor("pn%d" % s, [16, 2048], F32))
        pr = psc.enter_context(nc.psum_tensor("pr%d" % s, [1, 16], F32))
        pbb = psc.enter_context(nc.psum_tensor("pbb%d" % s, [128, 16], F32))

        base = sEv.n
        sq_v = {}
        for cc in range(NCC):
            sq_v[(0, cc)] = base + cc + 1
        redq_v = base + 9
        for cc in range(NCC):
            sq_v[(1, cc)] = base + 10 + cc
        redk_v = base + 18
        mul_v = base + 19
        negb_v = base + 20
        nrow_v = base + 21
        bcp_v = base + 22
        min_v = base + 23
        t0 = sTx.n
        mm_v = {}
        for cc in range(NCC):
            mm_v[(0, cc)] = t0 + cc + 1
            mm_v[(1, cc)] = t0 + 9 + cc
        pr_v = t0 + 17
        pbb_v = t0 + 18
        sqrt_v = sAc.n + 1

        @block.vector
        def _(v):
            for cc in range(NCC):
                if cc >= 1:
                    v.wait_ge(sTx.h, mm_v[(0, cc - 1)])
                sEv.inc(v.tensor_mul(qsq[:], qfm[:, cc * T:(cc + 1) * T],
                                     qfm[:, cc * T:(cc + 1) * T]))
            v.wait_ge(sTx.h, mm_v[(0, NCC - 1)])
            sEv.inc(v.tensor_reduce(biasq[:, 0:1], pn[0:16, :], AX.X, ALU.max))
            for cc in range(NCC):
                if cc >= 1:
                    v.wait_ge(sTx.h, mm_v[(1, cc - 1)])
                sEv.inc(v.tensor_mul(qsq[:], kfm[:, cc * T:(cc + 1) * T],
                                     kfm[:, cc * T:(cc + 1) * T]))
            v.wait_ge(sTx.h, mm_v[(1, NCC - 1)])
            sEv.inc(v.tensor_reduce(biasq[:, 1:2], pn[0:16, :], AX.X, ALU.max))
            v.drain()
            sEv.inc(v.tensor_mul(biasq[:, 2:3], biasq[:, 0:1], biasq[:, 1:2]))
            v.drain()
            v.wait_ge(sAc.h, sqrt_v)
            sEv.inc(v.tensor_scalar_mul(biasq[:, 0:1], biasq[:, 3:4], -0.125))
            v.drain()
            v.wait_ge(sTx.h, pr_v)
            sEv.inc(v.tensor_copy(nrow[:], pr[:]))
            v.drain()
            v.wait_ge(sTx.h, pbb_v)
            sEv.inc(v.tensor_copy(biasP[:, 0:16], pbb[:]))
            v.drain()
            sEv.inc(v.tensor_tensor(biasP[:, 16:24], biasP[:, 0:16:2],
                                    biasP[:, 1:16:2], ALU.min))
            v.drain()

        @block.tensor
        def _(te):
            for w in range(2):
                if w == 1:
                    te.wait_ge(sEv.h, redq_v)
                for cc in range(NCC):
                    te.wait_ge(sEv.h, sq_v[(w, cc)])
                    for nq in range(4):
                        ins = te.matmul(pn[:, nq * 512:(nq + 1) * 512],
                                        s_hmap[:, cc * 16:(cc + 1) * 16],
                                        qsq[:, nq * 512:(nq + 1) * 512],
                                        start=(cc == 0), stop=(cc == 7))
                        if nq == 3:
                            sTx.inc(ins)
            te.wait_ge(sEv.h, negb_v)
            sTx.inc(te.matmul(pr[:], biasq[:, 0:1], s_id16[:], start=True, stop=True))
            te.wait_ge(sEv.h, nrow_v)
            sTx.inc(te.matmul(pbb[:], s_ones[0:1, :], nrow[:], start=True, stop=True))

        @block.scalar
        def _(sc):
            sc.wait_ge(sEv.h, mul_v)
            sAc.inc(sc.activation(biasq[:, 3:4], biasq[:, 2:3], AF.Sqrt))

        assert sEv.n == min_v, (sEv.n, min_v)
        assert sTx.n == pbb_v
        psc.close()

    def a4(s):
        psc = ExitStack()
        pS = [psc.enter_context(nc.psum_tensor("pS%d_%d" % (s, i), [128, 1024], F32))
              for i in range(2)]
        pO = [psc.enter_context(nc.psum_tensor("pO%d_%d" % (s, i), [65, 1024], F32))
              for i in range(2)]

        QCS = A4_QCS
        events = []
        for pair in range(A4_PAIRS):
            for qc in range(QCS):
                for kb in range(4 * qc + 4):
                    events.append((pair, qc, kb))
        E = len(events)
        eidx = {ev: i for i, ev in enumerate(events)}
        part = [kb >= 4 * qc for (pair, qc, kb) in events]
        jof = [kb - 4 * qc for (pair, qc, kb) in events]

        exp0, s0_, pv0 = sEx.n, sS_.n, sPV.n
        exp_v = [exp0 + i + 1 for i in range(E)]
        s_v = [s0_ + i + 1 for i in range(E)]
        pv_v = [pv0 + i + 1 for i in range(E)]
        NQG = A4_PAIRS * QCS
        nm0, rc0, bc0, ca0 = sNm.n, sRc.n, sBc.n, sAc.n
        nm_v = [nm0 + i + 1 for i in range(NQG)]
        rc_v = [rc0 + i + 1 for i in range(NQG)]
        bc_v = [bc0 + i + 1 for i in range(NQG)]
        cast_v = [ca0 + i + 1 for i in range(NQG)]
        stg_v = {}
        stg_base = [dSg[0].n, dSg[1].n]
        for i in range(NQG):
            stg_v[i] = stg_base[i % 2] + 16 * (i // 2 + 1)
        bias_rdy = sEv.n
        ev0 = sEv.n
        evac_v = [ev0 + i + 1 for i in range(NQG)]

        def cols(i):
            return (jof[i] * 128 if part[i] else 0, 512)

        elast = {}
        for gq in range(NQG):
            pr_, qc_ = divmod(gq, QCS)
            elast[gq] = eidx[(pr_, qc_, 4 * qc_ + 3)]

        def emit_pv(te, ip):
            pair, qc, kb = events[ip]
            gq = pair * QCS + qc
            c0, c1 = cols(ip)
            te.wait_ge(sEx.h, exp_v[ip])
            if kb == 0 and gq >= 2:
                te.wait_ge(sNm.h, nm_v[gq - 2])
            for hh in range(2):
                h = pair * 2 + hh
                ins = te.matmul(
                    pO[gq % 2][0:65, hh * 512 + c0:hh * 512 + c1],
                    vsb.ap().rearrange("p (k h c) -> p k h c", h=NH, c=65)[:, kb, h, :],
                    pt[:, (ip % 3) * 1024 + hh * 512 + c0:(ip % 3) * 1024 + hh * 512 + c1],
                    start=(kb == 0), stop=(kb == 4 * qc + 3),
                    skip_group_check=True)
                if hh == 1:
                    sPV.inc(ins)

        @block.tensor
        def _(te):
            pend = []

            def do_bc(gq):
                par = elast[gq] % 2
                te.wait_ge(sAc.h, cast_v[gq])
                te.matmul(pS[par][0:64, 0:512], s_onesb[64:65, :], rcpa,
                          start=True, stop=True, skip_group_check=True)
                sBc.inc(te.matmul(pS[par][0:64, 512:1024], s_onesb[64:65, :], rcpb,
                                  start=True, stop=True, skip_group_check=True))

            bc_dirty = {}
            nextpv = 0
            for i, (pair, qc, kb) in enumerate(events):
                c0, c1 = cols(i)
                if i >= 2:
                    te.wait_ge(sEx.h, exp_v[i - 2])
                if (i % 2) in bc_dirty:
                    te.wait_ge(sEv.h, bc_dirty.pop(i % 2))
                for hh in range(2):
                    ins = te.matmul(
                        pS[i % 2][:, hh * 512 + c0:hh * 512 + c1],
                        kfm[hh * 64:(hh + 1) * 64, pair * T + kb * 128:pair * T + (kb + 1) * 128],
                        qfm[hh * 64:(hh + 1) * 64, pair * T + qc * 512 + c0:pair * T + qc * 512 + c1],
                        start=True, stop=not part[i], tile_position=(hh * 64, 0),
                        skip_group_check=True)
                    if hh == 1 and not part[i]:
                        sS_.inc(ins)
                if part[i]:
                    j = jof[i]
                    for hh in range(2):
                        ins = te.matmul(
                            pS[i % 2][:, hh * 512 + j * 128:hh * 512 + (j + 1) * 128],
                            s_idb[:], s_tri[:],
                            start=False, stop=True, skip_group_check=True)
                        if hh == 1:
                            sS_.inc(ins)
                while nextpv <= i - 2:
                    emit_pv(te, nextpv)
                    nextpv += 1
                if i >= 1 and events[i - 1][2] == 4 * events[i - 1][1] + 3:
                    while nextpv <= i - 1:
                        emit_pv(te, nextpv)
                        nextpv += 1
                while pend:
                    gqp = pend.pop(0)
                    do_bc(gqp)
                    bc_dirty[elast[gqp] % 2] = evac_v[gqp]
                if kb == 4 * qc + 3:
                    pend.append(pair * QCS + qc)
            while nextpv < E:
                emit_pv(te, nextpv)
                nextpv += 1
            while pend:
                do_bc(pend.pop(0))

        @block.scalar
        def _(sc):
            sc.wait_ge(sEv.h, bias_rdy)
            for i, (pair, qc, kb) in enumerate(events):
                c0, c1 = cols(i)
                sc.wait_ge(sS_.h, s_v[i])
                if i >= 3:
                    sc.wait_ge(sPV.h, pv_v[i - 3])
                sEx.inc(sc.activation(
                    pt.rearrange("p (b h c) -> p b h c", b=3, c=512)[:, i % 3, :, c0:c1],
                    pS[i % 2].ap().rearrange("p (h c) -> p h c", c=512)[:, :, c0:c1],
                    AF.Exp, bias=biasP[:, 16 + pair:17 + pair], scale=0.125))
                if kb == 4 * qc + 3:
                    gq2 = pair * QCS + qc
                    sc.wait_ge(sPV.h, pv_v[i])
                    if gq2 >= 1:
                        sc.wait_ge(sBc.h, bc_v[gq2 - 1])
                    sc.activation(rcpf, pO[gq2 % 2][64:65, 0:1024], AF.Ln)
                    sc.drain()
                    sAc.inc(sc.activation(rcp_b16[64:65, 0:1024], rcpf,
                                          AF.Exp, scale=-1.0))

        @block.gpsimd
        def _(g):
            for i, (pair, qc, kb) in enumerate(events):
                if kb == 4 * qc + 3:
                    gq = pair * QCS + qc
                    g.wait_ge(sNm.h, nm_v[gq])
                    dSg[gq % 2].dma(g.dma_start(
                        out=attnO[64:128, pair * T + qc * 512:pair * T + (qc + 1) * 512],
                        in_=stg[:, (gq % 2) * 512:(gq % 2) * 512 + 512]))

        @block.vector
        def _(v):
            for gq in range(NQG):
                pair, qc = divmod(gq, QCS)
                i_last = eidx[(pair, qc, 4 * qc + 3)]
                v.wait_ge(sPV.h, pv_v[i_last])
                v.wait_ge(sBc.h, bc_v[gq])
                sEv.inc(v.tensor_copy(bcsb[:], pS[elast[gq] % 2][0:64, :]))
                v.drain()
                if gq >= 2:
                    v.wait_ge(dSg[gq % 2].h, stg_v[gq - 2])
                v.tensor_mul(attnO[0:64, pair * T + qc * 512:pair * T + (qc + 1) * 512],
                             pO[gq % 2][0:64, 0:512], bcsb[0:64, 0:512])
                sNm.inc(v.tensor_mul(stg[:, (gq % 2) * 512:(gq % 2) * 512 + 512],
                                     pO[gq % 2][0:64, 512:1024], bcsb[0:64, 512:1024]))

        assert sEx.n == exp_v[-1] and sPV.n == pv_v[-1]
        assert sNm.n == nm_v[-1] and sBc.n == bc_v[-1]
        psc.close()

    def a5(s):
        psc = ExitStack()
        pw = [psc.enter_context(nc.psum_tensor("pw%d_%d" % (s, i), [128, 1024], F32))
              for i in range(2)]
        psq = psc.enter_context(nc.psum_tensor("psq%d" % s, [128, 1024], F32))
        wo0 = sWo.n
        wo_v = [wo0 + i + 1 for i in range(NTB)]
        ac0 = sAc.n
        ac_sumsq = [ac0 + 2 * i + 1 for i in range(NTB)]
        ac_sqrt = [ac0 + 2 * i + 2 for i in range(NTB)]
        ev0 = sEv.n
        stt_v, var_v, xn_v = [], [], []
        n = ev0
        for tb in range(NTB):
            n += 1
            stt_v.append(n)
            n += 1
            n += 4
            var_v.append(n)
            n += 1
            n += 1
            xn_v.append(n)
        xnd_v = {}
        xnd_base = [dSx[0].n, dSx[1].n]
        for i in range(NTB):
            xnd_v[i] = xnd_base[i % 2] + 16 * (i // 2 + 1)
        ld_v = {}

        @block.sync
        def _(sy):
            dW.dma(sy.dma_start(
                out=wout_b.rearrange("p (g f) -> p g f", g=8),
                in_=wout_bh.rearrange("(g p) f -> p g f", p=128)))
            for tb in range(NTB):
                if tb >= 2:
                    sy.wait_ge(sEv.h, xn_v[tb - 2])
                dXb[tb % 2].dma(sy.dma_start(
                    out=xbuf[:, (tb % 2) * C:(tb % 2) * C + C],
                    in_=xin[s * T + tb * 128: s * T + (tb + 1) * 128, :]))
                ld_v[tb] = dXb[tb % 2].n
        wob_ld = dW.n

        @block.tensor
        def _(te):
            te.wait_ge(dW.h, wob_ld)
            for tb in range(NTB):
                if tb >= 2:
                    te.wait_ge(sEv.h, stt_v[tb - 2])
                for cc in range(NCC):
                    for n2 in range(2):
                        ins = te.matmul(
                            pw[tb % 2][:, n2 * 512:(n2 + 1) * 512],
                            attnO[:, cc * T + tb * 128:cc * T + (tb + 1) * 128],
                            wout_b[:, cc * C + n2 * 512:cc * C + (n2 + 1) * 512],
                            start=(cc == 0), stop=(cc == 7))
                        if cc == 7 and n2 == 1:
                            sWo.inc(ins)

        @block.scalar
        def _(sc):
            for tb in range(NTB):
                o = (tb % 2) * 8
                sc.wait_ge(sEv.h, stt_v[tb])
                sAc.inc(sc.activation(psq[:], r1[:, (tb % 2) * C:(tb % 2) * C + C],
                                      AF.Square, accum_out=stat[:, o + 1:o + 2]))
                sc.wait_ge(sEv.h, var_v[tb])
                sAc.inc(sc.activation(stat[:, o + 6:o + 7], stat[:, o + 5:o + 6],
                                      AF.Sqrt, bias=s_eps[:, 0:1]))

        @block.vector
        def _(v):
            for tb in range(NTB):
                o = (tb % 2) * 8
                r1t = r1[:, (tb % 2) * C:(tb % 2) * C + C]
                v.wait_ge(sWo.h, wo_v[tb])
                v.wait_ge(dXb[tb % 2].h, ld_v[tb])
                if tb >= 2:
                    v.wait_ge(sAc.h, ac_sumsq[tb - 2])
                sEv.inc(v.scalar_tensor_tensor(
                    r1t, pw[tb % 2][:], 1.0, xbuf[:, (tb % 2) * C:(tb % 2) * C + C],
                    ALU.mult, ALU.add))
                v.drain()
                sEv.inc(v.reduce_sum(stat[:, o:o + 1], r1t, AX.X))
                v.drain()
                v.wait_ge(sAc.h, ac_sumsq[tb])
                sEv.inc(v.tensor_scalar_mul(stat[:, o + 2:o + 3], stat[:, o:o + 1], 1.0 / C))
                sEv.inc(v.tensor_scalar_mul(stat[:, o + 3:o + 4], stat[:, o + 1:o + 2], 1.0 / C))
                v.drain()
                sEv.inc(v.tensor_mul(stat[:, o + 4:o + 5], stat[:, o + 2:o + 3], stat[:, o + 2:o + 3]))
                v.drain()
                sEv.inc(v.tensor_sub(stat[:, o + 5:o + 6], stat[:, o + 3:o + 4], stat[:, o + 4:o + 5]))
                v.drain()
                v.wait_ge(sAc.h, ac_sqrt[tb])
                sEv.inc(v.reciprocal(stat[:, o + 7:o + 8], stat[:, o + 6:o + 7]))
                v.drain()
                if tb >= 2:
                    v.wait_ge(dSx[tb % 2].h, xnd_v[tb - 2])
                sEv.inc(v.tensor_scalar(xnb[:, (tb % 2) * C:(tb % 2) * C + C], r1t,
                                        stat[:, o + 2:o + 3], stat[:, o + 7:o + 8],
                                        ALU.subtract, ALU.mult))

        @block.gpsimd
        def _(g):
            for tb in range(NTB):
                g.wait_ge(sEv.h, xn_v[tb])
                dSx[tb % 2].dma(g.dma_start(
                    out=xn_hbm[s * T + tb * 128: s * T + (tb + 1) * 128, :],
                    in_=xnb[:, (tb % 2) * C:(tb % 2) * C + C]))

        assert sEv.n == xn_v[-1]
        psc.close()

    wps = {}

    def w_pass1(wi, wsrc, R, Ccols, prev_acc, prev_free):
        tiles = [(r, c) for r in range(R // 128) for c in range(Ccols // 1024)]
        NT = len(tiles)
        pm, pb = wps["pm"], wps["pb"]
        ev0 = sEv.n
        redall_v = ev0 + 1
        m1_v = ev0 + 2
        m2_v = ev0 + 3
        sc_v = ev0 + 4
        tx0 = sTx.n
        pm_v = tx0 + 1
        pb_v = tx0 + 2
        mk0 = sMk.n
        acc_v = [mk0 + i + 1 for i in range(NT)]
        ld1 = {}

        @block.scalar
        def _(sc):
            if prev_acc is not None:
                sc.wait_ge(sMk.h, prev_acc)
            if prev_free is not None:
                sc.wait_ge(sEv.h, prev_free)
            for i in range(2):
                r, c = tiles[i]
                dXb[i % 2].dma(sc.dma_start(
                    out=xbuf[:, (i % 2) * 1024:(i % 2) * 1024 + 1024],
                    in_=wsrc[r * 128:(r + 1) * 128, c * 1024:(c + 1) * 1024]))
                ld1[i] = dXb[i % 2].n
            for i in range(NT):
                sc.wait_ge(dXb[i % 2].h, ld1[i])
                sMk.inc(sc.activation(rcp_t.ap(),
                                      xbuf[:, (i % 2) * 1024:(i % 2) * 1024 + 1024],
                                      AF.Abs, accum_out=absacc[:, i:i + 1]))
                if i + 2 < NT:
                    sc.wait_ge(sMk.h, acc_v[i])
                    r2, c2 = tiles[i + 2]
                    dXb[i % 2].dma(sc.dma_start(
                        out=xbuf[:, (i % 2) * 1024:(i % 2) * 1024 + 1024],
                        in_=wsrc[r2 * 128:(r2 + 1) * 128, c2 * 1024:(c2 + 1) * 1024]))
                    ld1[i + 2] = dXb[i % 2].n

        @block.vector
        def _(v):
            v.wait_ge(sMk.h, acc_v[NT - 1])
            sEv.inc(v.reduce_sum(absacc[:, 32:33], absacc[:, 0:NT], AX.X))
            v.drain()
            v.wait_ge(sTx.h, pm_v)
            sEv.inc(v.tensor_scalar(w11[:, 0:1], pm[:], 1.0 / (R * Ccols), 1e-5,
                                    ALU.mult, ALU.max))
            v.drain()
            sEv.inc(v.reciprocal(w11[:, 1:2], w11[:, 0:1]))
            v.drain()
            v.wait_ge(sTx.h, pb_v)
            sEv.inc(v.tensor_copy(wsc[:, wi * 2:wi * 2 + 2], pb[:]))

        @block.tensor
        def _(te):
            te.wait_ge(sEv.h, redall_v)
            sTx.inc(te.matmul(pm[:], s_onesc[:], absacc[:, 32:33],
                              start=True, stop=True))
            te.wait_ge(sEv.h, m2_v)
            sTx.inc(te.matmul(pb[:], s_ones[0:1, :], w11[0:1, 0:2],
                              start=True, stop=True))

        return sc_v, redall_v, acc_v[NT - 1]

    def w_pass2(wi, wsrc, wdst_T, R, Ccols, sc_v, prev_q3):
        tiles = [(r, c) for r in range(R // 128) for c in range(Ccols // 1024)]
        NT = len(tiles)
        p1 = wps["p1"]
        ld2 = {}
        ev1 = sEv.n
        q3_v, evT_v = [], []
        n = ev1
        for i in range(NT):
            n += 3
            q3_v.append(n)
            n += 1
            evT_v.append(n)
        tx1 = sTx.n
        tx_v = [tx1 + i + 1 for i in range(NT)]
        st_v = {}
        st_base = [dSt2[0].n, dSt2[1].n]
        for i in range(NT):
            st_v[i] = st_base[i % 2] + 16 * (i // 2 + 1)

        @block.sync
        def _(sy):
            for i, (r, c) in enumerate(tiles):
                if i < 2 and prev_q3 is not None:
                    sy.wait_ge(sEv.h, prev_q3[NT - 2 + i])
                if i >= 2:
                    sy.wait_ge(sEv.h, q3_v[i - 2])
                dWt[i % 2].dma(sy.dma_start(
                    out=wst[:, (i % 2) * 1024:(i % 2) * 1024 + 1024],
                    in_=wsrc[r * 128:(r + 1) * 128, c * 1024:(c + 1) * 1024]))
                ld2[i] = dWt[i % 2].n

        @block.vector
        def _(v):
            v.wait_ge(sEv.h, sc_v)
            for i in range(NT):
                v.wait_ge(dWt[i % 2].h, ld2[i])
                src = wst[:, (i % 2) * 1024:(i % 2) * 1024 + 1024]
                sEv.inc(v.tensor_scalar(sqs[:], src, wsc[:, wi * 2 + 1:wi * 2 + 2],
                                        -1.0, ALU.mult, ALU.max))
                v.drain()
                sEv.inc(v.tensor_scalar(sqs[:], sqs[:], 1.0, MAGIC,
                                        ALU.min, ALU.add))
                v.drain()
                if i >= 2:
                    v.wait_ge(sTx.h, tx_v[i - 2])
                sEv.inc(v.tensor_scalar_add(
                    tern[:, (i % 2) * 1024:(i % 2) * 1024 + 1024], sqs[:], -MAGIC))
                v.wait_ge(sTx.h, tx_v[i])
                if i >= 2:
                    v.wait_ge(dSt2[i % 2].h, st_v[i - 2])
                sEv.inc(v.tensor_copy(
                    terT[:, (i % 2) * 1024:(i % 2) * 1024 + 1024], p1.ap()))

        @block.gpsimd
        def _(g):
            for i, (r, c) in enumerate(tiles):
                g.wait_ge(sEv.h, evT_v[i])
                dSt2[i % 2].dma(g.dma_start(
                    out=wdst_T.rearrange("(g p) f -> p g f", p=128)[:, c * 8:(c + 1) * 8, r * 128:(r + 1) * 128],
                    in_=terT.rearrange("p (b g f) -> p b g f", b=2, g=8)[:, i % 2, :, :]))

        @block.tensor
        def _(te):
            for i in range(NT):
                te.wait_ge(sEv.h, q3_v[i])
                if i >= 1:
                    te.wait_ge(sEv.h, evT_v[i - 1])
                for cc in range(8):
                    ins = te.transpose(
                        p1[:, cc * 128:(cc + 1) * 128],
                        tern[:, (i % 2) * 1024 + cc * 128:(i % 2) * 1024 + (cc + 1) * 128],
                        s_idb[:])
                    if cc == 7:
                        sTx.inc(ins)

        assert sEv.n == evT_v[NT - 1] and sTx.n == tx_v[-1]
        return q3_v

    def wphase():
        psc = ExitStack()
        wps["p1"] = psc.enter_context(nc.psum_tensor("pww", [128, 1024], BF16))
        wps["pm"] = psc.enter_context(nc.psum_tensor("pwm", [1, 1], F32))
        wps["pb"] = psc.enter_context(nc.psum_tensor("pwb", [128, 2], F32))
        prev_acc = prev_free = prev_q3 = None
        for wi, wsrc, wdst, R, Cc in ((0, wg, wgT_h, INNER, C),
                                      (1, wu, wuT_h, INNER, C),
                                      (2, wd, wdT_h, C, INNER)):
            sc_v, redall_v, lastacc = w_pass1(wi, wsrc, R, Cc, prev_acc, prev_free)
            prev_acc, prev_free = lastacc, redall_v
            prev_q3 = w_pass2(wi, wsrc, wdst, R, Cc, sc_v, prev_q3)
        psc.close()

    ph_list = [lambda: ph0()]
    for s in range(NSEQ):
        for f in (a1, a2, a3, a4, a5):
            ph_list.append(lambda f=f, s=s: f(s))
    ph_list.append(lambda: wphase())
    nph = 0
    for f in ph_list:
        if nph >= UPTO:
            break
        f()
        barrier()
        nph += 1
    esA.close()

    esF = ExitStack()

    def sbF(name, shape, dt):
        return esF.enter_context(nc.sbuf_tensor(name, shape, dt))

    wgT = sbF("wgT", [128, 8 * INNER], BF16)
    wuT = sbF("wuT", [128, 8 * INNER], BF16)
    xnf = sbF("xnf", [128, 2 * C], F32)
    hbf = sbF("hbf", [128, INNER], F32)
    thf = sbF("thf", [128, INNER], F32)
    fst = sbF("fst", [128, 32], F32)
    esF1 = ExitStack()
    hqb = esF1.enter_context(nc.sbuf_tensor("hqb", [128, INNER], BF16))
    xqb = esF1.enter_context(nc.sbuf_tensor("xqb", [128, 2 * C], BF16))
    xqT = esF1.enter_context(nc.sbuf_tensor("xqT", [128, 2 * C], BF16))
    sgf = esF1.enter_context(nc.sbuf_tensor("sgf", [128, 512], F32))
    hqi_yb = [None, None]

    def f1():
        psc = ExitStack()
        pT = psc.enter_context(nc.psum_tensor("pT1", [128, 1024], BF16))
        pG = [psc.enter_context(nc.psum_tensor("pG%d" % i, [128, 512], F32)) for i in range(2)]
        pU = [psc.enter_context(nc.psum_tensor("pU%d" % i, [128, 512], F32)) for i in range(2)]

        order = [("q", 0, k) for k in range(9)]
        for tb in range(NT2):
            order.append(("e", tb, 0))
            if tb + 1 < NT2:
                order += [("q", tb + 1, k) for k in range(9)]
            order += [("h", tb, oc) for oc in range(8)]
            order += [("t", tb, k) for k in range(5)]
        fd0 = sFd.n
        opv = {}
        for j, op in enumerate(order):
            opv[op] = fd0 + j + 1
        xq_v = [opv[("q", tb, 8)] for tb in range(NT2)]
        evT_v = [opv[("e", tb, 0)] for tb in range(NT2)]
        hml_v = [[opv[("h", tb, oc)] for oc in range(8)] for tb in range(NT2)]
        hs_v = [opv[("t", tb, 3)] for tb in range(NT2)]
        hq_v = [opv[("t", tb, 4)] for tb in range(NT2)]

        fa0 = sFa.n
        silu_v = [[fa0 + tb * 9 + oc + 1 for oc in range(8)] for tb in range(NT2)]
        th_v = [fa0 + tb * 9 + 9 for tb in range(NT2)]
        f0 = sF.n
        tx_v, g_v, u_v = [], [], []
        n = f0
        for tb in range(NT2):
            n += 1
            tx_v.append(n)
            gl, ul = [], []
            for oc in range(8):
                n += 1
                gl.append(n)
                n += 1
                ul.append(n)
            g_v.append(gl)
            u_v.append(ul)
        hd0 = dSh.n
        hqd_v = [hd0 + 16 * (tb + 1) for tb in range(NT2)]
        xn_ld = {}

        @block.sync
        def _(sy):
            for half in range(2):
                dW.dma(sy.dma_start(
                    out=wgT.ap().rearrange("p (g f) -> p g f", g=8)[:, half * 4:(half + 1) * 4, :],
                    in_=wgT_h.rearrange("(g p) f -> p g f", p=128)[:, half * 4:(half + 1) * 4, :]))
            for tb in range(NT2):
                if tb >= 2:
                    sy.wait_ge(sFd.h, xq_v[tb - 2])
                dXn[tb % 2].dma(sy.dma_start(out=xnf[:, (tb % 2) * C:(tb % 2) * C + C],
                                    in_=xn_hbm[tb * 128:(tb + 1) * 128, :]))
                xn_ld[tb] = dXn[tb % 2].n

        w_ld = dW.n

        @block.scalar
        def _(sc):
            for half in range(2):
                dHq[half].dma(sc.dma_start(
                    out=wuT.ap().rearrange("p (g f) -> p g f", g=8)[:, half * 4:(half + 1) * 4, :],
                    in_=wuT_h.rearrange("(g p) f -> p g f", p=128)[:, half * 4:(half + 1) * 4, :]))
        wu_ld = [dHq[0].n, dHq[1].n]

        def emit_quant(v, tb):
            o = (tb % 2) * 16
            xt = xnf[:, (tb % 2) * C:(tb % 2) * C + C]
            v.wait_ge(dXn[tb % 2].h, xn_ld[tb])
            if tb >= 2:
                v.wait_ge(sFa.h, th_v[tb - 2])
            sFd.inc(v.tensor_reduce(fst[:, o:o + 1], xt, AX.X, ALU.max,
                                    apply_absolute_value=True))
            v.drain()
            sFd.inc(v.tensor_scalar_max(fst[:, o:o + 1], fst[:, o:o + 1], 1e-5))
            v.drain()
            sFd.inc(v.reciprocal(fst[:, o + 1:o + 2], fst[:, o:o + 1]))
            v.drain()
            sFd.inc(v.tensor_scalar_mul(fst[:, o + 2:o + 3], fst[:, o + 1:o + 2], 127.0))
            sFd.inc(v.tensor_scalar_mul(fst[:, o + 3:o + 4], fst[:, o:o + 1], 1.0 / 127.0))
            v.drain()
            sFd.inc(v.tensor_mul(fst[:, o + 4:o + 5], fst[:, o + 3:o + 4], wsc[:, 0:1]))
            sFd.inc(v.tensor_mul(fst[:, o + 5:o + 6], fst[:, o + 3:o + 4], wsc[:, 2:3]))
            v.drain()
            sFd.inc(v.tensor_scalar(thf[:, 0:C], xt, fst[:, o + 2:o + 3], MAGIC,
                                    ALU.mult, ALU.add))
            v.drain()
            if tb >= 1:
                v.wait_ge(sF.h, tx_v[tb - 1])
            sFd.inc(v.tensor_scalar_add(xqb[:, (tb % 2) * C:(tb % 2) * C + C],
                                        thf[:, 0:C], -MAGIC))

        @block.vector
        def _(v):
            emit_quant(v, 0)
            for tb in range(NT2):
                o = (tb % 2) * 16
                v.wait_ge(sF.h, tx_v[tb])
                if tb >= 2:
                    v.wait_ge(sF.h, u_v[tb - 2][7])
                sFd.inc(v.tensor_copy(xqT[:, (tb % 2) * C:(tb % 2) * C + C], pT.ap()))
                if tb + 1 < NT2:
                    emit_quant(v, tb + 1)
                for oc in range(8):
                    v.wait_ge(sF.h, u_v[tb][oc])
                    v.wait_ge(sFa.h, silu_v[tb][oc])
                    sFd.inc(v.scalar_tensor_tensor(
                        hbf[:, oc * 512:(oc + 1) * 512], pU[oc % 2][:],
                        fst[:, o + 5:o + 6], sgf[:], ALU.mult, ALU.mult))
                v.drain()
                sFd.inc(v.tensor_reduce(fst[:, o + 6:o + 7], hbf[:], AX.X, ALU.max,
                                        apply_absolute_value=True))
                v.drain()
                sFd.inc(v.reciprocal(fst[:, o + 7:o + 8], fst[:, o + 6:o + 7]))
                v.drain()
                sFd.inc(v.tensor_scalar_mul(fst[:, o + 8:o + 9], fst[:, o + 7:o + 8], 127.0))
                sFd.inc(v.tensor_scalar_mul(dsh_all[:, tb:tb + 1], fst[:, o + 6:o + 7], 1.0 / 127.0))
                v.drain()
                v.wait_ge(sFa.h, th_v[tb])
                if tb >= 1:
                    v.wait_ge(dSh.h, hqd_v[tb - 1])
                sFd.inc(v.tensor_scalar_add(hqb[:], thf[:], -MAGIC))

        @block.tensor
        def _(te):
            te.wait_ge(dW.h, w_ld)
            te.wait_ge(dHq[0].h, wu_ld[0])
            te.wait_ge(dHq[1].h, wu_ld[1])
            for tb in range(NT2):
                te.wait_ge(sFd.h, xq_v[tb])
                if tb >= 1:
                    te.wait_ge(sFd.h, evT_v[tb - 1])
                for cc in range(8):
                    ins = te.transpose(pT[:, cc * 128:(cc + 1) * 128],
                                       xqb[:, (tb % 2) * C + cc * 128:(tb % 2) * C + (cc + 1) * 128],
                                       s_idb[:])
                    if cc == 7:
                        sF.inc(ins)
                for oc in range(8):
                    gc = tb * 8 + oc
                    te.wait_ge(sFd.h, evT_v[tb])
                    if gc >= 2:
                        tb2, oc2 = divmod(gc - 2, 8)
                        te.wait_ge(sFa.h, silu_v[tb2][oc2])
                    for cc in range(8):
                        ins = te.matmul(pG[oc % 2][:],
                                        xqT[:, (tb % 2) * C + cc * 128:(tb % 2) * C + (cc + 1) * 128],
                                        wgT[:, cc * INNER + oc * 512:cc * INNER + (oc + 1) * 512],
                                        start=(cc == 0), stop=(cc == 7))
                        if cc == 7:
                            sF.inc(ins)
                    if gc >= 2:
                        tb2, oc2 = divmod(gc - 2, 8)
                        te.wait_ge(sFd.h, hml_v[tb2][oc2])
                    for cc in range(8):
                        ins = te.matmul(pU[oc % 2][:],
                                        xqT[:, (tb % 2) * C + cc * 128:(tb % 2) * C + (cc + 1) * 128],
                                        wuT[:, cc * INNER + oc * 512:cc * INNER + (oc + 1) * 512],
                                        start=(cc == 0), stop=(cc == 7))
                        if cc == 7:
                            sF.inc(ins)

        @block.scalar
        def _(sc):
            for tb in range(NT2):
                o = (tb % 2) * 16
                for oc in range(8):
                    gc = tb * 8 + oc
                    sc.wait_ge(sF.h, g_v[tb][oc])
                    if gc >= 1:
                        tb2, oc2 = divmod(gc - 1, 8)
                        sc.wait_ge(sFd.h, hml_v[tb2][oc2])
                    sFa.inc(sc.activation(sgf[:], pG[oc % 2][:], AF.Silu,
                                          scale=fst[:, o + 4:o + 5]))
                sc.wait_ge(sFd.h, hs_v[tb])
                sFa.inc(sc.activation(thf[:], hbf[:], AF.Identity,
                                      scale=fst[:, o + 8:o + 9], bias=s_magic[:, 0:1]))

        @block.gpsimd
        def _(g):
            for tb in range(NT2):
                g.wait_ge(sFd.h, hq_v[tb])
                dSh.dma(g.dma_start(out=hq_hbm[tb * 128:(tb + 1) * 128, :],
                                   in_=hqb[:]))

        assert sFd.n == opv[order[-1]], (sFd.n, opv[order[-1]])
        assert sF.n == u_v[-1][7]
        assert sFa.n == th_v[-1]
        psc.close()

    def f2():
        hqi_, yb_ = hqi_yb[0], hqi_yb[1]
        psc = ExitStack()
        pH = [psc.enter_context(nc.psum_tensor("pH%d" % i, [128, 1024], BF16)) for i in range(2)]
        pD = [psc.enter_context(nc.psum_tensor("pD%d" % i, [128, 1024], F32)) for i in range(2)]
        psq2 = psc.enter_context(nc.psum_tensor("psq2", [128, 1024], F32))
        wdT = wgT
        hqT = wuT.ap()[:, 0:2 * INNER]

        fd0 = sFd.n
        dshmd_v = fd0 + 1
        evh_v, ln_v = [], []
        n = fd0 + 1
        for tb in range(NT2):
            el = []
            for grp in range(4):
                n += 1
                el.append(n)
            evh_v.append(el)
            n += 8
            ln_v.append(n)
        fa0 = sFa.n
        sumsq_v = [fa0 + 2 * tb + 1 for tb in range(NT2)]
        sqrt_v = [fa0 + 2 * tb + 2 for tb in range(NT2)]
        f0 = sF.n
        txg_v, dmm_v = [], []
        n = f0
        for tb in range(NT2):
            tl = []
            for grp in range(4):
                n += 1
                tl.append(n)
            txg_v.append(tl)
            dl = []
            for n2 in range(2):
                n += 1
                dl.append(n)
            dmm_v.append(dl)
        dS0 = sOut.n
        y_v = [dS0 + 16 * (tb + 1) for tb in range(NT2)]
        hq_ld, xn_ld = {}, {}

        @block.vector
        def _(v):
            sFd.inc(v.tensor_scalar_mul(dsh_all[:, 32:64], dsh_all[:, 0:32],
                                        wsc[:, 4:5]))

        @block.sync
        def _(sy):
            for q in range(4):
                dW.dma(sy.dma_start(
                    out=wdT.ap().rearrange("p (g f) -> p g f", g=32)[:, q * 8:(q + 1) * 8, 0:C],
                    in_=wdT_h.rearrange("(g p) f -> p g f", p=128)[:, q * 8:(q + 1) * 8, :]))
            for tb in range(NT2):
                if tb >= 2:
                    sy.wait_ge(sF.h, dmm_v[tb - 2][1])
                dHq[tb % 2].dma(sy.dma_start(out=hqi_[:, (tb % 2) * INNER:(tb % 2) * INNER + INNER],
                                    in_=hq_hbm[tb * 128:(tb + 1) * 128, :]))
                hq_ld[tb] = dHq[tb % 2].n
                if tb >= 2:
                    sy.wait_ge(sFd.h, ln_v[tb - 2])
                dXn[tb % 2].dma(sy.dma_start(out=xnf[:, (tb % 2) * C:(tb % 2) * C + C],
                                    in_=xn_hbm[tb * 128:(tb + 1) * 128, :]))
                xn_ld[tb] = dXn[tb % 2].n

        wd_ld = dW.n

        @block.tensor
        def _(te):
            te.wait_ge(dW.h, wd_ld)
            te.wait_ge(sFd.h, dshmd_v)
            for tb in range(NT2):
                te.wait_ge(dHq[tb % 2].h, hq_ld[tb])
                for grp in range(4):
                    gi = tb * 4 + grp
                    if gi >= 2:
                        tbp, gp_ = divmod(gi - 2, 4)
                        te.wait_ge(sFd.h, evh_v[tbp][gp_])
                    for k in range(8):
                        ic = grp * 8 + k
                        ins = te.transpose(
                            pH[grp % 2][:, k * 128:(k + 1) * 128],
                            hqi_[:, (tb % 2) * INNER + ic * 128:(tb % 2) * INNER + (ic + 1) * 128],
                            s_idb[:])
                        if k == 7:
                            sF.inc(ins)
                for n2 in range(2):
                    if tb >= 2:
                        te.wait_ge(sFd.h, ln_v[tb - 2])
                    for ic in range(32):
                        if ic == 0 and n2 == 0:
                            te.wait_ge(sFd.h, evh_v[tb][3])
                        ins = te.matmul(
                            pD[tb % 2][:, n2 * 512:(n2 + 1) * 512],
                            hqT[:, (tb % 2) * INNER + ic * 128:(tb % 2) * INNER + (ic + 1) * 128],
                            wdT[:, ic * 1024 + n2 * 512:ic * 1024 + (n2 + 1) * 512],
                            start=(ic == 0), stop=(ic == 31))
                        if ic == 31:
                            sF.inc(ins)

        @block.vector
        def _(v):
            for tb in range(NT2):
                o = (tb % 2) * 16
                for grp in range(4):
                    v.wait_ge(sF.h, txg_v[tb][grp])
                    if tb >= 2:
                        v.wait_ge(sF.h, dmm_v[tb - 2][1])
                    sFd.inc(v.tensor_copy(
                        hqT.rearrange("p (b g f) -> p b g f", b=2, g=32)[:, tb % 2, grp * 8:(grp + 1) * 8, :],
                        pH[grp % 2].ap().rearrange("p (g f) -> p g f", g=8)))
                v.wait_ge(sF.h, dmm_v[tb][1])
                v.wait_ge(dXn[tb % 2].h, xn_ld[tb])
                if tb >= 1:
                    v.wait_ge(sFa.h, sumsq_v[tb - 1])
                r2t = thf[:, 0:C]
                sFd.inc(v.scalar_tensor_tensor(
                    r2t, pD[tb % 2][:], dsh_all[:, 32 + tb:33 + tb],
                    xnf[:, (tb % 2) * C:(tb % 2) * C + C], ALU.mult, ALU.add))
                v.drain()
                sFd.inc(v.reduce_sum(fst[:, o:o + 1], r2t, AX.X))
                v.drain()
                v.wait_ge(sFa.h, sumsq_v[tb])
                sFd.inc(v.tensor_scalar_mul(fst[:, o + 2:o + 3], fst[:, o:o + 1], 1.0 / C))
                sFd.inc(v.tensor_scalar_mul(fst[:, o + 3:o + 4], fst[:, o + 1:o + 2], 1.0 / C))
                v.drain()
                sFd.inc(v.tensor_mul(fst[:, o + 4:o + 5], fst[:, o + 2:o + 3], fst[:, o + 2:o + 3]))
                v.drain()
                sFd.inc(v.tensor_sub(fst[:, o + 5:o + 6], fst[:, o + 3:o + 4], fst[:, o + 4:o + 5]))
                v.drain()
                v.wait_ge(sFa.h, sqrt_v[tb])
                sFd.inc(v.reciprocal(fst[:, o + 7:o + 8], fst[:, o + 6:o + 7]))
                v.drain()
                if tb >= 1:
                    v.wait_ge(sOut.h, y_v[tb - 1])
                sFd.inc(v.tensor_scalar(yb_[:], r2t,
                                        fst[:, o + 2:o + 3], fst[:, o + 7:o + 8],
                                        ALU.subtract, ALU.mult))

        @block.scalar
        def _(sc):
            for tb in range(NT2):
                o = (tb % 2) * 16
                sc.wait_ge(sFd.h, ln_v[tb] - 7)
                sFa.inc(sc.activation(psq2[:], thf[:, 0:C], AF.Square,
                                      accum_out=fst[:, o + 1:o + 2]))
                sc.wait_ge(sFd.h, ln_v[tb] - 2)
                sFa.inc(sc.activation(fst[:, o + 6:o + 7], fst[:, o + 5:o + 6],
                                      AF.Sqrt, bias=s_eps[:, 0:1]))

        @block.gpsimd
        def _(g):
            for tb in range(NT2):
                g.wait_ge(sFd.h, ln_v[tb])
                sOut.dma(g.dma_start(out=y[tb * 128:(tb + 1) * 128, :],
                                     in_=yb_[:]))

        assert sFd.n == ln_v[-1]
        assert sOut.n == y_v[-1]
        psc.close()

    if nph < UPTO:
        f1()
        barrier()
        nph += 1
    esF1.close()
    hqi_yb = [esF.enter_context(nc.sbuf_tensor("hqi", [128, 2 * INNER], BF16)),
              esF.enter_context(nc.sbuf_tensor("yb", [128, C], F32))]
    if nph < UPTO:
        f2()
        barrier()
        nph += 1

    @block.sync
    def _(sy):
        sy.wait_ge(sOut.h, sOut.n)

    esF.close()
    es.close()
    return nc


def make_consts():
    idf = np.eye(128, dtype=np.float32)
    idb = np.eye(128, dtype=bfnp)
    tri = np.where(np.arange(128)[:, None] > np.arange(128)[None, :],
                   np.float32(-30000.0), np.float32(0.0)).astype(bfnp)
    hmap = np.zeros((128, 128), dtype=bfnp)
    for cc in range(8):
        for p in range(128):
            hmap[p, cc * 16 + (2 * cc + p // 64)] = 1
    ones = np.ones((128, 128), dtype=np.float32)
    id16 = np.eye(16, dtype=np.float32)
    return dict(c_idf=idf, c_idb=idb, c_tri=tri, c_hmap=hmap, c_ones=ones,
                c_id16=id16)


def kernel(L0_attn_x, L0_attn_w_qkv, L0_attn_w_out, L0_ffn_w_gate, L0_ffn_w_up,
           L0_ffn_w_down):
    x = np.ascontiguousarray(np.asarray(L0_attn_x, dtype=np.float32))
    B = x.shape[0]
    ncore = 8
    per = B // ncore
    nc = build_program()
    base = dict(
        wqkv=np.ascontiguousarray(np.asarray(L0_attn_w_qkv, dtype=np.float32)),
        wout=np.ascontiguousarray(np.asarray(L0_attn_w_out, dtype=np.float32)),
        wg=np.ascontiguousarray(np.asarray(L0_ffn_w_gate, dtype=np.float32)),
        wu=np.ascontiguousarray(np.asarray(L0_ffn_w_up, dtype=np.float32)),
        wd=np.ascontiguousarray(np.asarray(L0_ffn_w_down, dtype=np.float32)),
        **make_consts())
    in_maps = []
    for c in range(ncore):
        m = dict(base)
        m["xin"] = np.ascontiguousarray(x[c * per:(c + 1) * per].reshape(per * T, C))
        in_maps.append(m)
    res = run_bass_kernel_spmd(nc, in_maps, core_ids=list(range(ncore)))
    outs = [r["y"].reshape(per, T, C) for r in res.results]
    return np.concatenate(outs, axis=0).astype(np.float32)


# version 07
# speedup vs baseline: 1.6642x; 1.0518x over previous
import numpy as np
import ml_dtypes
import concourse.bass as bass
import concourse.mybir as mybir
from concourse.bass_utils import run_bass_kernel_spmd
from contextlib import ExitStack

F32 = mybir.dt.float32
BF16 = mybir.dt.bfloat16
AF = mybir.ActivationFunctionType
ALU = mybir.AluOpType
AX = mybir.AxisListType
bfnp = ml_dtypes.bfloat16

MAGIC = float(np.float32(1.5 * 2 ** 23))
T, C, NH, DK, NSEQ, INNER = 2048, 1024, 16, 64, 2, 4096
NTB = T // 128
NQC = 4
NCC = C // 128
NPAIR = NH // 2
NT2 = NSEQ * NTB

DEBUG_OUT = False
UPTO = 99
A4_PAIRS = 8
A4_QCS = 4


class Sem:
    def __init__(self, es, nc, name):
        self.h = es.enter_context(nc.semaphore(name))
        self.n = 0

    def inc(self, instr, k=1):
        instr.then_inc(self.h, k)
        self.n += k
        return instr

    def dma(self, instr):
        return self.inc(instr, 16)


def build_program():
    nc = bass.Bass("TRN2", target_bir_lowering=False)

    xin = nc.dram_tensor("xin", [NSEQ * T, C], F32, kind="ExternalInput").ap()
    wqkv = nc.dram_tensor("wqkv", [C, 3 * C], F32, kind="ExternalInput").ap()
    wout = nc.dram_tensor("wout", [C, C], F32, kind="ExternalInput").ap()
    wg = nc.dram_tensor("wg", [INNER, C], F32, kind="ExternalInput").ap()
    wu = nc.dram_tensor("wu", [INNER, C], F32, kind="ExternalInput").ap()
    wd = nc.dram_tensor("wd", [C, INNER], F32, kind="ExternalInput").ap()
    c_idf = nc.dram_tensor("c_idf", [128, 128], F32, kind="ExternalInput").ap()
    c_idb = nc.dram_tensor("c_idb", [128, 128], BF16, kind="ExternalInput").ap()
    c_tri = nc.dram_tensor("c_tri", [128, 128], BF16, kind="ExternalInput").ap()
    c_hmap = nc.dram_tensor("c_hmap", [128, 128], BF16, kind="ExternalInput").ap()
    c_ones = nc.dram_tensor("c_ones", [128, 128], F32, kind="ExternalInput").ap()
    c_id16 = nc.dram_tensor("c_id16", [16, 16], F32, kind="ExternalInput").ap()
    y = nc.dram_tensor("y", [NSEQ * T, C], F32, kind="ExternalOutput").ap()

    KIND = "ExternalOutput" if DEBUG_OUT else "Internal"
    xn_hbm = nc.dram_tensor("xn_hbm", [NSEQ * T, C], F32, kind=KIND).ap()
    hq_hbm = nc.dram_tensor("hq_hbm", [NSEQ * T, INNER], BF16, kind=KIND).ap()
    wqkv_bh = nc.dram_tensor("wqkv_bh", [C, 3 * C], BF16, kind="Internal").ap()
    wout_bh = nc.dram_tensor("wout_bh", [C, C], BF16, kind="Internal").ap()
    wgT_h = nc.dram_tensor("wgT_h", [C, INNER], BF16, kind=KIND).ap()
    wuT_h = nc.dram_tensor("wuT_h", [C, INNER], BF16, kind=KIND).ap()
    wdT_h = nc.dram_tensor("wdT_h", [INNER, C], BF16, kind=KIND).ap()

    es = ExitStack()

    s_idf = es.enter_context(nc.sbuf_tensor("s_idf", [128, 128], F32))
    s_idb = es.enter_context(nc.sbuf_tensor("s_idb", [128, 128], BF16))
    s_tri = es.enter_context(nc.sbuf_tensor("s_tri", [128, 128], BF16))
    s_hmap = es.enter_context(nc.sbuf_tensor("s_hmap", [128, 128], BF16))
    s_ones = es.enter_context(nc.sbuf_tensor("s_ones", [128, 128], F32))
    s_onesc = es.enter_context(nc.sbuf_tensor("s_onesc", [128, 1], F32))
    s_id16 = es.enter_context(nc.sbuf_tensor("s_id16", [16, 16], F32))
    s_onesb = es.enter_context(nc.sbuf_tensor("s_onesb", [128, 64], BF16))
    s_eps = es.enter_context(nc.sbuf_tensor("s_eps", [128, 1], F32))
    s_magic = es.enter_context(nc.sbuf_tensor("s_magic", [128, 1], F32))
    wsc = es.enter_context(nc.sbuf_tensor("wsc", [128, 8], F32))
    dsh_all = es.enter_context(nc.sbuf_tensor("dsh_all", [128, 64], F32))
    w11 = es.enter_context(nc.sbuf_tensor("w11", [1, 4], F32))

    SD = {}
    for name in ("dX dW dS sTx sEv sAc sQk sS sEx sMk sPV sRc sBc sNm "
                 "sWo sGp sF sFd sFa sOut").split():
        SD[name] = Sem(es, nc, name)
    dXb = [Sem(es, nc, "dXb%d" % i) for i in range(2)]
    dWq = [Sem(es, nc, "dWq%d" % i) for i in range(2)]
    dWt = [Sem(es, nc, "dWt%d" % i) for i in range(2)]
    dXn = [Sem(es, nc, "dXn%d" % i) for i in range(2)]
    dHq = [Sem(es, nc, "dHq%d" % i) for i in range(2)]
    dSc = [Sem(es, nc, "dSc%d" % i) for i in range(2)]
    dSg = [Sem(es, nc, "dSg%d" % i) for i in range(2)]
    dSx = [Sem(es, nc, "dSx%d" % i) for i in range(2)]
    dSt2 = [Sem(es, nc, "dSt2_%d" % i) for i in range(2)]
    dSh = Sem(es, nc, "dSh")
    dX, dW, dS = SD["dX"], SD["dW"], SD["dS"]
    sTx, sEv, sAc, sQk = SD["sTx"], SD["sEv"], SD["sAc"], SD["sQk"]
    sS_, sEx, sMk, sPV = SD["sS"], SD["sEx"], SD["sMk"], SD["sPV"]
    sRc, sBc, sNm = SD["sRc"], SD["sBc"], SD["sNm"]
    sWo, sGp = SD["sWo"], SD["sGp"]
    sF, sFd, sFa, sOut = SD["sF"], SD["sFd"], SD["sFa"], SD["sOut"]
    ALL = (list(SD.values()) + dXb + dWq + dWt + dXn + dHq + dSc + dSg + dSx
           + dSt2 + [dSh])

    block = es.enter_context(nc.Block())

    def barrier():
        snaps = [(s.h, s.n) for s in ALL if s.n > 0]
        for attr in ("sync", "tensor", "vector", "scalar", "gpsimd"):
            def f(eng, snaps=snaps):
                for h, v in snaps:
                    eng.wait_ge(h, v)
            getattr(block, attr)(f)

    esA = ExitStack()

    def sbA(name, shape, dt):
        return esA.enter_context(nc.sbuf_tensor(name, shape, dt))

    xbuf = sbA("xbuf", [128, 2 * C], F32)
    wst = sbA("wst", [128, 2 * 1024], F32)
    wcb = sbA("wcb", [128, 2 * 1024], BF16)
    ovl = sbA("ovl", [128, 8192], BF16)
    wqs = ovl.ap()[:, 0:4096]
    wout_b = ovl.ap()[:, 0:8192]
    xfm = sbA("xfm", [128, 8 * T], BF16)
    attnO = xfm
    qfm = sbA("qfm", [128, 8 * T], BF16)
    kfm = sbA("kfm", [128, 8 * T], BF16)
    vsb = sbA("vsb", [128, NTB * NH * 65], BF16)
    pt = ovl.ap()[:, 0:3072]
    qsq = ovl.ap()[:, 0:2048]
    rcp_t = sbA("rcp_t", [128, 1024], F32)
    rcpf = rcp_t.ap()[64:65, 0:1024]
    rcp_b16 = sbA("rcp_b16", [128, 1024], BF16)
    rcpa = rcp_b16.ap()[64:65, 0:512]
    rcpb = rcp_b16.ap()[64:65, 512:1024]
    bcsb = sbA("bcsb", [64, 1024], F32)
    stg = sbA("stg", [64, 2 * 512], BF16)
    biasq = sbA("biasq", [16, 4], F32)
    nrow = sbA("nrow", [1, 16], F32)
    biasP = sbA("biasP", [128, 24], F32)
    r1 = sbA("r1", [128, 2 * C], F32)
    stat = sbA("stat", [128, 16], F32)
    sqs = r1.ap()[:, 0:C]
    xnb = sbA("xnb", [128, 2 * C], F32)
    absacc = sbA("absacc", [128, 33], F32)
    tern = wcb.ap()[:, 0:2048]
    terT = ovl.ap()[:, 4096:6144]

    def ph0():
        @block.sync
        def _(sy):
            dW.dma(sy.dma_start(out=s_idf[:], in_=c_idf))
            dW.dma(sy.dma_start(out=s_idb[:], in_=c_idb))
            dW.dma(sy.dma_start(out=s_tri[:], in_=c_tri))
            dW.dma(sy.dma_start(out=s_hmap[:], in_=c_hmap))
            dW.dma(sy.dma_start(out=s_ones[:], in_=c_ones))
            dW.dma(sy.dma_start(out=s_id16[:], in_=c_id16))
        cw = dW.n

        @block.vector
        def _(v):
            sEv.inc(v.memset(s_onesb[:], 1.0))
            sEv.inc(v.memset(s_eps[:], 1e-5))
            sEv.inc(v.memset(s_magic[:], MAGIC))
            sEv.inc(v.memset(s_onesc[:], 1.0))
            sEv.inc(v.memset(
                vsb.ap().rearrange("p (g c) -> p g c", c=65)[:, :, 64:65], 1.0))

        jobs = [("qkv", t, h) for t in range(8) for h in range(3)]
        jobs += [("out", t, 0) for t in range(8)]
        NJ = len(jobs)
        ld_v = {}
        ev0 = sEv.n
        ev_v = [ev0 + i + 1 for i in range(NJ)]
        st_v = {}

        @block.sync
        def _(sy):
            for i, (tag, t, h) in enumerate(jobs):
                if i >= 2:
                    sy.wait_ge(sEv.h, ev_v[i - 2])
                b = wst[:, (i % 2) * 1024:(i % 2) * 1024 + 1024]
                if tag == "qkv":
                    src = wqkv[t * 128:(t + 1) * 128, h * 1024:(h + 1) * 1024]
                else:
                    src = wout[t * 128:(t + 1) * 128, :]
                dWt[i % 2].dma(sy.dma_start(out=b, in_=src))
                ld_v[i] = dWt[i % 2].n

        @block.vector
        def _(v):
            for i, (tag, t, h) in enumerate(jobs):
                v.wait_ge(dWt[i % 2].h, ld_v[i])
                b = wst[:, (i % 2) * 1024:(i % 2) * 1024 + 1024]
                if i >= 2 and (i - 2) in st_v:
                    v.wait_ge(dSc[i % 2].h, st_v[i - 2])
                dst = wcb[:, (i % 2) * 1024:(i % 2) * 1024 + 1024]
                sEv.inc(v.tensor_copy(dst, b))

        @block.gpsimd
        def _(g):
            for i, (tag, t, h) in enumerate(jobs):
                g.wait_ge(sEv.h, ev_v[i])
                src = wcb[:, (i % 2) * 1024:(i % 2) * 1024 + 1024]
                if tag == "qkv":
                    dst = wqkv_bh[t * 128:(t + 1) * 128, h * 1024:(h + 1) * 1024]
                else:
                    dst = wout_bh[t * 128:(t + 1) * 128, :]
                dSc[i % 2].dma(g.dma_start(out=dst, in_=src))
                st_v[i] = dSc[i % 2].n

    def a1(s):
        psc = ExitStack()
        ptx = psc.enter_context(nc.psum_tensor("ptx%d" % s, [128, 1024], F32))
        tx0 = sTx.n
        tx_v = [tx0 + i + 1 for i in range(NTB)]
        ev0 = sEv.n
        ev_v = [ev0 + i + 1 for i in range(NTB)]
        ld_v = {}

        @block.sync
        def _(sy):
            for tb in range(NTB):
                if tb >= 2:
                    sy.wait_ge(sTx.h, tx_v[tb - 2])
                dXb[tb % 2].dma(sy.dma_start(
                    out=xbuf[:, (tb % 2) * C:(tb % 2) * C + C],
                    in_=xin[s * T + tb * 128: s * T + (tb + 1) * 128, :]))
                ld_v[tb] = dXb[tb % 2].n

        @block.tensor
        def _(te):
            for tb in range(NTB):
                te.wait_ge(dXb[tb % 2].h, ld_v[tb])
                if tb >= 1:
                    te.wait_ge(sEv.h, ev_v[tb - 1])
                for cc in range(NCC):
                    ins = te.transpose(
                        ptx[:, cc * 128:(cc + 1) * 128],
                        xbuf[:, (tb % 2) * C + cc * 128:(tb % 2) * C + (cc + 1) * 128],
                        s_idf[:])
                    if cc == 7:
                        sTx.inc(ins)

        @block.vector
        def _(v):
            for tb in range(NTB):
                v.wait_ge(sTx.h, tx_v[tb])
                sEv.inc(v.tensor_copy(
                    xfm.ap().rearrange("p (g t) -> p g t", t=T)[:, :, tb * 128:(tb + 1) * 128],
                    ptx.ap().rearrange("p (g c) -> p g c", c=128)))
        assert sTx.n == tx_v[-1] and sEv.n == ev_v[-1]
        psc.close()

    def a2(s):
        psc = ExitStack()
        pqk = [psc.enter_context(nc.psum_tensor("pqk%d_%d" % (s, i), [128, 512], F32))
               for i in range(2)]
        pv = [psc.enter_context(nc.psum_tensor("pv%d_%d" % (s, i), [128, 512], F32))
              for i in range(2)]

        qk0 = sQk.n
        qk_v = {}
        n = qk0
        for of in range(16):
            for tc in range(4):
                n += 1
                qk_v[(of, tc)] = n
        v_v = {}
        for nv in range(2):
            for tb in range(NTB):
                n += 1
                v_v[(nv, tb)] = n
        ac0 = sAc.n
        ac_v = {}
        n = ac0
        for of in range(16):
            for tc in range(4):
                n += 1
                ac_v[(of, tc)] = n
        ev0 = sEv.n
        evv_v = {}
        n = ev0
        for nv in range(2):
            for tb in range(NTB):
                n += 1
                evv_v[(nv, tb)] = n
        wl_v = {}

        @block.sync
        def _(sy):
            for of in range(16):
                if of >= 2:
                    sy.wait_ge(sQk.h, qk_v[(of - 2, 3)])
                dWq[of % 2].dma(sy.dma_start(
                    out=wqs.rearrange("p (b f) -> p b f", b=2)[:, of % 2, 0:1024],
                    in_=wqkv_bh.rearrange("(g p) f -> p g f", p=128)[:, :, of * 128:(of + 1) * 128]))
                wl_v[of] = dWq[of % 2].n
            for nv in range(2):
                if nv == 0:
                    sy.wait_ge(sQk.h, qk_v[(14, 3)])
                    sy.wait_ge(sQk.h, qk_v[(15, 3)])
                else:
                    sy.wait_ge(sQk.h, v_v[(0, NTB - 1)])
                for hf in range(2):
                    dWq[hf].dma(sy.dma_start(
                        out=wqs.rearrange("p (b f) -> p b f", b=2)[:, hf, :],
                        in_=wqkv_bh.rearrange("(g p) f -> p g f", p=128)[:, hf * 4:(hf + 1) * 4, 2048 + nv * 512:2048 + (nv + 1) * 512]))
                wl_v[16 + nv] = (dWq[0].n, dWq[1].n)

        @block.tensor
        def _(te):
            for of in range(16):
                te.wait_ge(dWq[of % 2].h, wl_v[of])
                for tc in range(4):
                    m = of * 4 + tc
                    if m >= 2:
                        of2, tc2 = divmod(m - 2, 4)
                        te.wait_ge(sAc.h, ac_v[(of2, tc2)])
                    for cc in range(NCC):
                        ins = te.matmul(
                            pqk[m % 2][:],
                            wqs[:, (of % 2) * 2048 + cc * 128:(of % 2) * 2048 + (cc + 1) * 128],
                            xfm[:, cc * T + tc * 512:cc * T + (tc + 1) * 512],
                            start=(cc == 0), stop=(cc == 7))
                        if cc == 7:
                            sQk.inc(ins)
            for nv in range(2):
                te.wait_ge(dWq[0].h, wl_v[16 + nv][0])
                te.wait_ge(dWq[1].h, wl_v[16 + nv][1])
                for tb in range(NTB):
                    m = nv * NTB + tb
                    if m >= 2:
                        nv2, tb2 = divmod(m - 2, NTB)
                        te.wait_ge(sEv.h, evv_v[(nv2, tb2)])
                    for cc in range(NCC):
                        ins = te.matmul(
                            pv[m % 2][:],
                            xfm[:, cc * T + tb * 128:cc * T + (tb + 1) * 128],
                            wqs[:, (cc // 4) * 2048 + (cc % 4) * 512:(cc // 4) * 2048 + (cc % 4) * 512 + 512],
                            start=(cc == 0), stop=(cc == 7))
                        if cc == 7:
                            sQk.inc(ins)

        @block.scalar
        def _(sc):
            for of in range(16):
                dstt = qfm if of < 8 else kfm
                ch = of % 8
                for tc in range(4):
                    sc.wait_ge(sQk.h, qk_v[(of, tc)])
                    sAc.inc(sc.activation(
                        dstt[:, ch * T + tc * 512:ch * T + (tc + 1) * 512],
                        pqk[(of * 4 + tc) % 2][:], AF.Identity))

        @block.vector
        def _(v):
            for nv in range(2):
                for tb in range(NTB):
                    v.wait_ge(sQk.h, v_v[(nv, tb)])
                    sEv.inc(v.tensor_copy(
                        vsb.ap().rearrange("p (k h c) -> p k h c", h=NH, c=65)[:, tb, nv * 8:(nv + 1) * 8, 0:64],
                        pv[(nv * NTB + tb) % 2].ap().rearrange("p (h c) -> p h c", c=64)))
        assert sQk.n == v_v[(1, NTB - 1)]
        psc.close()

    def a3(s):
        psc = ExitStack()
        pn = psc.enter_context(nc.psum_tensor("pn%d" % s, [16, 2048], F32))
        pr = psc.enter_context(nc.psum_tensor("pr%d" % s, [1, 16], F32))
        pbb = psc.enter_context(nc.psum_tensor("pbb%d" % s, [128, 16], F32))

        base = sEv.n
        sq_v = {}
        for cc in range(NCC):
            sq_v[(0, cc)] = base + cc + 1
        redq_v = base + 9
        for cc in range(NCC):
            sq_v[(1, cc)] = base + 10 + cc
        redk_v = base + 18
        mul_v = base + 19
        negb_v = base + 20
        nrow_v = base + 21
        bcp_v = base + 22
        min_v = base + 23
        t0 = sTx.n
        mm_v = {}
        for cc in range(NCC):
            mm_v[(0, cc)] = t0 + cc + 1
            mm_v[(1, cc)] = t0 + 9 + cc
        pr_v = t0 + 17
        pbb_v = t0 + 18
        sqrt_v = sAc.n + 1

        @block.vector
        def _(v):
            for cc in range(NCC):
                if cc >= 1:
                    v.wait_ge(sTx.h, mm_v[(0, cc - 1)])
                sEv.inc(v.tensor_mul(qsq[:], qfm[:, cc * T:(cc + 1) * T],
                                     qfm[:, cc * T:(cc + 1) * T]))
            v.wait_ge(sTx.h, mm_v[(0, NCC - 1)])
            sEv.inc(v.tensor_reduce(biasq[:, 0:1], pn[0:16, :], AX.X, ALU.max))
            for cc in range(NCC):
                if cc >= 1:
                    v.wait_ge(sTx.h, mm_v[(1, cc - 1)])
                sEv.inc(v.tensor_mul(qsq[:], kfm[:, cc * T:(cc + 1) * T],
                                     kfm[:, cc * T:(cc + 1) * T]))
            v.wait_ge(sTx.h, mm_v[(1, NCC - 1)])
            sEv.inc(v.tensor_reduce(biasq[:, 1:2], pn[0:16, :], AX.X, ALU.max))
            v.drain()
            sEv.inc(v.tensor_mul(biasq[:, 2:3], biasq[:, 0:1], biasq[:, 1:2]))
            v.drain()
            v.wait_ge(sAc.h, sqrt_v)
            sEv.inc(v.tensor_scalar_mul(biasq[:, 0:1], biasq[:, 3:4], -0.125))
            v.drain()
            v.wait_ge(sTx.h, pr_v)
            sEv.inc(v.tensor_copy(nrow[:], pr[:]))
            v.drain()
            v.wait_ge(sTx.h, pbb_v)
            sEv.inc(v.tensor_copy(biasP[:, 0:16], pbb[:]))
            v.drain()
            sEv.inc(v.tensor_tensor(biasP[:, 16:24], biasP[:, 0:16:2],
                                    biasP[:, 1:16:2], ALU.min))
            v.drain()

        @block.tensor
        def _(te):
            for w in range(2):
                if w == 1:
                    te.wait_ge(sEv.h, redq_v)
                for cc in range(NCC):
                    te.wait_ge(sEv.h, sq_v[(w, cc)])
                    for nq in range(4):
                        ins = te.matmul(pn[:, nq * 512:(nq + 1) * 512],
                                        s_hmap[:, cc * 16:(cc + 1) * 16],
                                        qsq[:, nq * 512:(nq + 1) * 512],
                                        start=(cc == 0), stop=(cc == 7))
                        if nq == 3:
                            sTx.inc(ins)
            te.wait_ge(sEv.h, negb_v)
            sTx.inc(te.matmul(pr[:], biasq[:, 0:1], s_id16[:], start=True, stop=True))
            te.wait_ge(sEv.h, nrow_v)
            sTx.inc(te.matmul(pbb[:], s_ones[0:1, :], nrow[:], start=True, stop=True))

        @block.scalar
        def _(sc):
            sc.wait_ge(sEv.h, mul_v)
            sAc.inc(sc.activation(biasq[:, 3:4], biasq[:, 2:3], AF.Sqrt))

        assert sEv.n == min_v, (sEv.n, min_v)
        assert sTx.n == pbb_v
        psc.close()

    def a4(s):
        psc = ExitStack()
        pS = [psc.enter_context(nc.psum_tensor("pS%d_%d" % (s, i), [128, 1024], F32))
              for i in range(2)]
        pO = [psc.enter_context(nc.psum_tensor("pO%d_%d" % (s, i), [65, 1024], F32))
              for i in range(2)]

        QCS = A4_QCS
        events = []
        for pair in range(A4_PAIRS):
            for qc in range(QCS):
                for kb in range(4 * qc + 4):
                    events.append((pair, qc, kb))
        E = len(events)
        eidx = {ev: i for i, ev in enumerate(events)}
        part = [kb >= 4 * qc for (pair, qc, kb) in events]
        jof = [kb - 4 * qc for (pair, qc, kb) in events]

        exp0, s0_, pv0 = sEx.n, sS_.n, sPV.n
        exp_v = [exp0 + i + 1 for i in range(E)]
        s_v = [s0_ + i + 1 for i in range(E)]
        pv_v = [pv0 + i + 1 for i in range(E)]
        NQG = A4_PAIRS * QCS
        nm0, rc0, bc0, ca0 = sNm.n, sRc.n, sBc.n, sAc.n
        nm_v = [nm0 + i + 1 for i in range(NQG)]
        rc_v = [rc0 + i + 1 for i in range(NQG)]
        bc_v = [bc0 + i + 1 for i in range(NQG)]
        cast_v = [ca0 + i + 1 for i in range(NQG)]
        stg_v = {}
        stg_base = [dSg[0].n, dSg[1].n]
        for i in range(NQG):
            stg_v[i] = stg_base[i % 2] + 16 * (i // 2 + 1)
        bias_rdy = sEv.n
        ev0 = sEv.n
        evac_v = [ev0 + i + 1 for i in range(NQG)]

        def cols(i):
            return (jof[i] * 128 if part[i] else 0, 512)

        elast = {}
        for gq in range(NQG):
            pr_, qc_ = divmod(gq, QCS)
            elast[gq] = eidx[(pr_, qc_, 4 * qc_ + 3)]

        def emit_pv(te, ip):
            pair, qc, kb = events[ip]
            gq = pair * QCS + qc
            c0, c1 = cols(ip)
            te.wait_ge(sEx.h, exp_v[ip])
            if kb == 0 and gq >= 2:
                te.wait_ge(sNm.h, nm_v[gq - 2])
            for hh in range(2):
                h = pair * 2 + hh
                ins = te.matmul(
                    pO[gq % 2][0:65, hh * 512 + c0:hh * 512 + c1],
                    vsb.ap().rearrange("p (k h c) -> p k h c", h=NH, c=65)[:, kb, h, :],
                    pt[:, (ip % 3) * 1024 + hh * 512 + c0:(ip % 3) * 1024 + hh * 512 + c1],
                    start=(kb == 0), stop=(kb == 4 * qc + 3),
                    skip_group_check=True)
                if hh == 1:
                    sPV.inc(ins)

        @block.tensor
        def _(te):
            pend = []

            def do_bc(gq):
                par = elast[gq] % 2
                te.wait_ge(sAc.h, cast_v[gq])
                te.matmul(pS[par][0:64, 0:512], s_onesb[64:65, :], rcpa,
                          start=True, stop=True, skip_group_check=True)
                sBc.inc(te.matmul(pS[par][0:64, 512:1024], s_onesb[64:65, :], rcpb,
                                  start=True, stop=True, skip_group_check=True))

            bc_dirty = {}
            nextpv = 0
            for i, (pair, qc, kb) in enumerate(events):
                c0, c1 = cols(i)
                if i >= 2:
                    te.wait_ge(sEx.h, exp_v[i - 2])
                if (i % 2) in bc_dirty:
                    te.wait_ge(sEv.h, bc_dirty.pop(i % 2))
                for hh in range(2):
                    ins = te.matmul(
                        pS[i % 2][:, hh * 512 + c0:hh * 512 + c1],
                        kfm[hh * 64:(hh + 1) * 64, pair * T + kb * 128:pair * T + (kb + 1) * 128],
                        qfm[hh * 64:(hh + 1) * 64, pair * T + qc * 512 + c0:pair * T + qc * 512 + c1],
                        start=True, stop=not part[i], tile_position=(hh * 64, 0),
                        skip_group_check=True)
                    if hh == 1 and not part[i]:
                        sS_.inc(ins)
                if part[i]:
                    j = jof[i]
                    for hh in range(2):
                        ins = te.matmul(
                            pS[i % 2][:, hh * 512 + j * 128:hh * 512 + (j + 1) * 128],
                            s_idb[:], s_tri[:],
                            start=False, stop=True, skip_group_check=True)
                        if hh == 1:
                            sS_.inc(ins)
                while nextpv <= i - 2:
                    emit_pv(te, nextpv)
                    nextpv += 1
                if i >= 1 and events[i - 1][2] == 4 * events[i - 1][1] + 3:
                    while nextpv <= i - 1:
                        emit_pv(te, nextpv)
                        nextpv += 1
                while pend:
                    gqp = pend.pop(0)
                    do_bc(gqp)
                    bc_dirty[elast[gqp] % 2] = evac_v[gqp]
                if kb == 4 * qc + 3:
                    pend.append(pair * QCS + qc)
            while nextpv < E:
                emit_pv(te, nextpv)
                nextpv += 1
            while pend:
                do_bc(pend.pop(0))

        @block.scalar
        def _(sc):
            sc.wait_ge(sEv.h, bias_rdy)
            for i, (pair, qc, kb) in enumerate(events):
                c0, c1 = cols(i)
                sc.wait_ge(sS_.h, s_v[i])
                if i >= 3:
                    sc.wait_ge(sPV.h, pv_v[i - 3])
                sEx.inc(sc.activation(
                    pt.rearrange("p (b h c) -> p b h c", b=3, c=512)[:, i % 3, :, c0:c1],
                    pS[i % 2].ap().rearrange("p (h c) -> p h c", c=512)[:, :, c0:c1],
                    AF.Exp, bias=biasP[:, 16 + pair:17 + pair], scale=0.125))
                if kb == 4 * qc + 3:
                    gq2 = pair * QCS + qc
                    sc.wait_ge(sPV.h, pv_v[i])
                    if gq2 >= 1:
                        sc.wait_ge(sBc.h, bc_v[gq2 - 1])
                    sc.activation(rcpf, pO[gq2 % 2][64:65, 0:1024], AF.Ln)
                    sc.drain()
                    sAc.inc(sc.activation(rcp_b16[64:65, 0:1024], rcpf,
                                          AF.Exp, scale=-1.0))

        @block.gpsimd
        def _(g):
            for i, (pair, qc, kb) in enumerate(events):
                if kb == 4 * qc + 3:
                    gq = pair * QCS + qc
                    g.wait_ge(sNm.h, nm_v[gq])
                    dSg[gq % 2].dma(g.dma_start(
                        out=attnO[64:128, pair * T + qc * 512:pair * T + (qc + 1) * 512],
                        in_=stg[:, (gq % 2) * 512:(gq % 2) * 512 + 512]))

        @block.vector
        def _(v):
            for gq in range(NQG):
                pair, qc = divmod(gq, QCS)
                i_last = eidx[(pair, qc, 4 * qc + 3)]
                v.wait_ge(sPV.h, pv_v[i_last])
                v.wait_ge(sBc.h, bc_v[gq])
                sEv.inc(v.tensor_copy(bcsb[:], pS[elast[gq] % 2][0:64, :]))
                v.drain()
                if gq >= 2:
                    v.wait_ge(dSg[gq % 2].h, stg_v[gq - 2])
                v.tensor_mul(attnO[0:64, pair * T + qc * 512:pair * T + (qc + 1) * 512],
                             pO[gq % 2][0:64, 0:512], bcsb[0:64, 0:512])
                sNm.inc(v.tensor_mul(stg[:, (gq % 2) * 512:(gq % 2) * 512 + 512],
                                     pO[gq % 2][0:64, 512:1024], bcsb[0:64, 512:1024]))

        assert sEx.n == exp_v[-1] and sPV.n == pv_v[-1]
        assert sNm.n == nm_v[-1] and sBc.n == bc_v[-1]
        psc.close()

    def a5(s):
        psc = ExitStack()
        pw = [psc.enter_context(nc.psum_tensor("pw%d_%d" % (s, i), [128, 1024], F32))
              for i in range(2)]
        psq = psc.enter_context(nc.psum_tensor("psq%d" % s, [128, 1024], F32))
        wo0 = sWo.n
        wo_v = [wo0 + i + 1 for i in range(NTB)]
        ac0 = sAc.n
        ac_sumsq = [ac0 + 2 * i + 1 for i in range(NTB)]
        ac_sqrt = [ac0 + 2 * i + 2 for i in range(NTB)]
        ev0 = sEv.n
        stt_v, var_v, xn_v = [], [], []
        n = ev0
        for tb in range(NTB):
            n += 1
            stt_v.append(n)
            n += 1
            n += 4
            var_v.append(n)
            n += 1
            n += 1
            xn_v.append(n)
        xnd_v = {}
        xnd_base = [dSx[0].n, dSx[1].n]
        for i in range(NTB):
            xnd_v[i] = xnd_base[i % 2] + 16 * (i // 2 + 1)
        ld_v = {}

        @block.sync
        def _(sy):
            dW.dma(sy.dma_start(
                out=wout_b.rearrange("p (g f) -> p g f", g=8),
                in_=wout_bh.rearrange("(g p) f -> p g f", p=128)))
            for tb in range(NTB):
                if tb >= 2:
                    sy.wait_ge(sEv.h, xn_v[tb - 2])
                dXb[tb % 2].dma(sy.dma_start(
                    out=xbuf[:, (tb % 2) * C:(tb % 2) * C + C],
                    in_=xin[s * T + tb * 128: s * T + (tb + 1) * 128, :]))
                ld_v[tb] = dXb[tb % 2].n
        wob_ld = dW.n

        @block.tensor
        def _(te):
            te.wait_ge(dW.h, wob_ld)
            for tb in range(NTB):
                if tb >= 2:
                    te.wait_ge(sEv.h, stt_v[tb - 2])
                for cc in range(NCC):
                    for n2 in range(2):
                        ins = te.matmul(
                            pw[tb % 2][:, n2 * 512:(n2 + 1) * 512],
                            attnO[:, cc * T + tb * 128:cc * T + (tb + 1) * 128],
                            wout_b[:, cc * C + n2 * 512:cc * C + (n2 + 1) * 512],
                            start=(cc == 0), stop=(cc == 7))
                        if cc == 7 and n2 == 1:
                            sWo.inc(ins)

        @block.scalar
        def _(sc):
            for tb in range(NTB):
                o = (tb % 2) * 8
                sc.wait_ge(sEv.h, stt_v[tb])
                sAc.inc(sc.activation(psq[:], r1[:, (tb % 2) * C:(tb % 2) * C + C],
                                      AF.Square, accum_out=stat[:, o + 1:o + 2]))
                sc.wait_ge(sEv.h, var_v[tb])
                sAc.inc(sc.activation(stat[:, o + 6:o + 7], stat[:, o + 5:o + 6],
                                      AF.Sqrt, bias=s_eps[:, 0:1]))

        @block.vector
        def _(v):
            for tb in range(NTB):
                o = (tb % 2) * 8
                r1t = r1[:, (tb % 2) * C:(tb % 2) * C + C]
                v.wait_ge(sWo.h, wo_v[tb])
                v.wait_ge(dXb[tb % 2].h, ld_v[tb])
                if tb >= 2:
                    v.wait_ge(sAc.h, ac_sumsq[tb - 2])
                sEv.inc(v.scalar_tensor_tensor(
                    r1t, pw[tb % 2][:], 1.0, xbuf[:, (tb % 2) * C:(tb % 2) * C + C],
                    ALU.mult, ALU.add))
                v.drain()
                sEv.inc(v.reduce_sum(stat[:, o:o + 1], r1t, AX.X))
                v.drain()
                v.wait_ge(sAc.h, ac_sumsq[tb])
                sEv.inc(v.tensor_scalar_mul(stat[:, o + 2:o + 3], stat[:, o:o + 1], 1.0 / C))
                sEv.inc(v.tensor_scalar_mul(stat[:, o + 3:o + 4], stat[:, o + 1:o + 2], 1.0 / C))
                v.drain()
                sEv.inc(v.tensor_mul(stat[:, o + 4:o + 5], stat[:, o + 2:o + 3], stat[:, o + 2:o + 3]))
                v.drain()
                sEv.inc(v.tensor_sub(stat[:, o + 5:o + 6], stat[:, o + 3:o + 4], stat[:, o + 4:o + 5]))
                v.drain()
                v.wait_ge(sAc.h, ac_sqrt[tb])
                sEv.inc(v.reciprocal(stat[:, o + 7:o + 8], stat[:, o + 6:o + 7]))
                v.drain()
                if tb >= 2:
                    v.wait_ge(dSx[tb % 2].h, xnd_v[tb - 2])
                sEv.inc(v.tensor_scalar(xnb[:, (tb % 2) * C:(tb % 2) * C + C], r1t,
                                        stat[:, o + 2:o + 3], stat[:, o + 7:o + 8],
                                        ALU.subtract, ALU.mult))

        @block.gpsimd
        def _(g):
            for tb in range(NTB):
                g.wait_ge(sEv.h, xn_v[tb])
                dSx[tb % 2].dma(g.dma_start(
                    out=xn_hbm[s * T + tb * 128: s * T + (tb + 1) * 128, :],
                    in_=xnb[:, (tb % 2) * C:(tb % 2) * C + C]))

        assert sEv.n == xn_v[-1]
        psc.close()

    wps = {}

    def w_pass1(wi, wsrc, R, Ccols, prev_acc, prev_free):
        tiles = [(r, c) for r in range(R // 128) for c in range(Ccols // 1024)]
        NT = len(tiles)
        pm, pb = wps["pm"], wps["pb"]
        ev0 = sEv.n
        redall_v = ev0 + 1
        m1_v = ev0 + 2
        m2_v = ev0 + 3
        sc_v = ev0 + 4
        tx0 = sTx.n
        pm_v = tx0 + 1
        pb_v = tx0 + 2
        mk0 = sMk.n
        acc_v = [mk0 + i + 1 for i in range(NT)]
        ld1 = {}

        @block.scalar
        def _(sc):
            if prev_acc is not None:
                sc.wait_ge(sMk.h, prev_acc)
            if prev_free is not None:
                sc.wait_ge(sEv.h, prev_free)
            for i in range(2):
                r, c = tiles[i]
                dXb[i % 2].dma(sc.dma_start(
                    out=xbuf[:, (i % 2) * 1024:(i % 2) * 1024 + 1024],
                    in_=wsrc[r * 128:(r + 1) * 128, c * 1024:(c + 1) * 1024]))
                ld1[i] = dXb[i % 2].n
            for i in range(NT):
                sc.wait_ge(dXb[i % 2].h, ld1[i])
                sMk.inc(sc.activation(rcp_t.ap(),
                                      xbuf[:, (i % 2) * 1024:(i % 2) * 1024 + 1024],
                                      AF.Abs, accum_out=absacc[:, i:i + 1]))
                if i + 2 < NT:
                    sc.wait_ge(sMk.h, acc_v[i])
                    r2, c2 = tiles[i + 2]
                    dXb[i % 2].dma(sc.dma_start(
                        out=xbuf[:, (i % 2) * 1024:(i % 2) * 1024 + 1024],
                        in_=wsrc[r2 * 128:(r2 + 1) * 128, c2 * 1024:(c2 + 1) * 1024]))
                    ld1[i + 2] = dXb[i % 2].n

        @block.vector
        def _(v):
            v.wait_ge(sMk.h, acc_v[NT - 1])
            sEv.inc(v.reduce_sum(absacc[:, 32:33], absacc[:, 0:NT], AX.X))
            v.drain()
            v.wait_ge(sTx.h, pm_v)
            sEv.inc(v.tensor_scalar(w11[:, 0:1], pm[:], 1.0 / (R * Ccols), 1e-5,
                                    ALU.mult, ALU.max))
            v.drain()
            sEv.inc(v.reciprocal(w11[:, 1:2], w11[:, 0:1]))
            v.drain()
            v.wait_ge(sTx.h, pb_v)
            sEv.inc(v.tensor_copy(wsc[:, wi * 2:wi * 2 + 2], pb[:]))

        @block.tensor
        def _(te):
            te.wait_ge(sEv.h, redall_v)
            sTx.inc(te.matmul(pm[:], s_onesc[:], absacc[:, 32:33],
                              start=True, stop=True))
            te.wait_ge(sEv.h, m2_v)
            sTx.inc(te.matmul(pb[:], s_ones[0:1, :], w11[0:1, 0:2],
                              start=True, stop=True))

        return sc_v, redall_v, acc_v[NT - 1]

    def w_pass2(wi, wsrc, wdst_T, R, Ccols, sc_v, prev_q3):
        tiles = [(r, c) for r in range(R // 128) for c in range(Ccols // 1024)]
        NT = len(tiles)
        p1 = wps["p1"]
        ld2 = {}
        ev1 = sEv.n
        q3_v, evT_v = [], []
        n = ev1
        for i in range(NT):
            n += 3
            q3_v.append(n)
            n += 1
            evT_v.append(n)
        tx1 = sTx.n
        tx_v = [tx1 + i + 1 for i in range(NT)]
        st_v = {}
        st_base = [dSt2[0].n, dSt2[1].n]
        for i in range(NT):
            st_v[i] = st_base[i % 2] + 16 * (i // 2 + 1)

        @block.sync
        def _(sy):
            for i, (r, c) in enumerate(tiles):
                if i < 2 and prev_q3 is not None:
                    sy.wait_ge(sEv.h, prev_q3[NT - 2 + i])
                if i >= 2:
                    sy.wait_ge(sEv.h, q3_v[i - 2])
                dWt[i % 2].dma(sy.dma_start(
                    out=wst[:, (i % 2) * 1024:(i % 2) * 1024 + 1024],
                    in_=wsrc[r * 128:(r + 1) * 128, c * 1024:(c + 1) * 1024]))
                ld2[i] = dWt[i % 2].n

        @block.vector
        def _(v):
            v.wait_ge(sEv.h, sc_v)
            for i in range(NT):
                v.wait_ge(dWt[i % 2].h, ld2[i])
                src = wst[:, (i % 2) * 1024:(i % 2) * 1024 + 1024]
                sEv.inc(v.tensor_scalar(sqs[:], src, wsc[:, wi * 2 + 1:wi * 2 + 2],
                                        -1.0, ALU.mult, ALU.max))
                v.drain()
                sEv.inc(v.tensor_scalar(sqs[:], sqs[:], 1.0, MAGIC,
                                        ALU.min, ALU.add))
                v.drain()
                if i >= 2:
                    v.wait_ge(sTx.h, tx_v[i - 2])
                sEv.inc(v.tensor_scalar_add(
                    tern[:, (i % 2) * 1024:(i % 2) * 1024 + 1024], sqs[:], -MAGIC))
                v.wait_ge(sTx.h, tx_v[i])
                if i >= 2:
                    v.wait_ge(dSt2[i % 2].h, st_v[i - 2])
                sEv.inc(v.tensor_copy(
                    terT[:, (i % 2) * 1024:(i % 2) * 1024 + 1024], p1.ap()))

        @block.gpsimd
        def _(g):
            for i, (r, c) in enumerate(tiles):
                g.wait_ge(sEv.h, evT_v[i])
                dSt2[i % 2].dma(g.dma_start(
                    out=wdst_T.rearrange("(g p) f -> p g f", p=128)[:, c * 8:(c + 1) * 8, r * 128:(r + 1) * 128],
                    in_=terT.rearrange("p (b g f) -> p b g f", b=2, g=8)[:, i % 2, :, :]))

        @block.tensor
        def _(te):
            for i in range(NT):
                te.wait_ge(sEv.h, q3_v[i])
                if i >= 1:
                    te.wait_ge(sEv.h, evT_v[i - 1])
                for cc in range(8):
                    ins = te.transpose(
                        p1[:, cc * 128:(cc + 1) * 128],
                        tern[:, (i % 2) * 1024 + cc * 128:(i % 2) * 1024 + (cc + 1) * 128],
                        s_idb[:])
                    if cc == 7:
                        sTx.inc(ins)

        assert sEv.n == evT_v[NT - 1] and sTx.n == tx_v[-1]
        return q3_v

    def wphase():
        psc = ExitStack()
        wps["p1"] = psc.enter_context(nc.psum_tensor("pww", [128, 1024], BF16))
        wps["pm"] = psc.enter_context(nc.psum_tensor("pwm", [1, 1], F32))
        wps["pb"] = psc.enter_context(nc.psum_tensor("pwb", [128, 2], F32))
        prev_acc = prev_free = prev_q3 = None
        for wi, wsrc, wdst, R, Cc in ((0, wg, wgT_h, INNER, C),
                                      (1, wu, wuT_h, INNER, C),
                                      (2, wd, wdT_h, C, INNER)):
            sc_v, redall_v, lastacc = w_pass1(wi, wsrc, R, Cc, prev_acc, prev_free)
            prev_acc, prev_free = lastacc, redall_v
            prev_q3 = w_pass2(wi, wsrc, wdst, R, Cc, sc_v, prev_q3)
        psc.close()

    ph_list = [lambda: ph0()]
    for s in range(NSEQ):
        for f in (a1, a2, a3, a4, a5):
            ph_list.append(lambda f=f, s=s: f(s))
    ph_list.append(lambda: wphase())
    nph = 0
    for f in ph_list:
        if nph >= UPTO:
            break
        f()
        barrier()
        nph += 1
    esA.close()

    esF = ExitStack()

    def sbF(name, shape, dt):
        return esF.enter_context(nc.sbuf_tensor(name, shape, dt))

    wgT = sbF("wgT", [128, 8 * INNER], BF16)
    wuT = sbF("wuT", [128, 8 * INNER], BF16)
    xnf = sbF("xnf", [128, 2 * C], F32)
    hbf = sbF("hbf", [128, INNER], F32)
    thf = sbF("thf", [128, INNER], F32)
    fst = sbF("fst", [128, 32], F32)
    esF1 = ExitStack()
    hqb = esF1.enter_context(nc.sbuf_tensor("hqb", [128, INNER], BF16))
    xqb = esF1.enter_context(nc.sbuf_tensor("xqb", [128, 2 * C], BF16))
    xqT = esF1.enter_context(nc.sbuf_tensor("xqT", [128, 2 * C], BF16))
    sgf = esF1.enter_context(nc.sbuf_tensor("sgf", [128, 512], F32))
    hqi_yb = [None, None]

    def f1():
        psc = ExitStack()
        pT = psc.enter_context(nc.psum_tensor("pT1", [128, 1024], BF16))
        pG = [psc.enter_context(nc.psum_tensor("pG%d" % i, [128, 512], F32)) for i in range(3)]
        pU = [psc.enter_context(nc.psum_tensor("pU%d" % i, [128, 512], F32)) for i in range(3)]

        order = [("q", 0, k) for k in range(9)]
        order.append(("e", 0, 0))
        order += [("q", 1, k) for k in range(9)]
        for tb in range(NT2):
            order += [("h", tb, oc) for oc in range(8)]
            if tb + 1 < NT2:
                order.append(("e", tb + 1, 0))
            order += [("t", tb, k) for k in range(4)]
            if tb + 2 < NT2:
                order += [("q", tb + 2, k) for k in range(9)]
            order.append(("r", tb, 0))
        fd0 = sFd.n
        opv = {}
        for j, op in enumerate(order):
            opv[op] = fd0 + j + 1
        xq_v = [opv[("q", tb, 8)] for tb in range(NT2)]
        evT_v = [opv[("e", tb, 0)] for tb in range(NT2)]
        hml_v = [[opv[("h", tb, oc)] for oc in range(8)] for tb in range(NT2)]
        hs_v = [opv[("t", tb, 3)] for tb in range(NT2)]
        rnd_v = [opv[("r", tb, 0)] for tb in range(NT2)]

        fa0 = sFa.n
        silu_v = [[fa0 + tb * 9 + oc + 1 for oc in range(8)] for tb in range(NT2)]
        th_v = [fa0 + tb * 9 + 9 for tb in range(NT2)]
        rcG0 = sRc.n
        round_v = [rcG0 + tb + 1 for tb in range(NT2)]
        f0 = sF.n
        tx_v, g_v, u_v = [], [], []
        n = f0
        for tb in range(NT2):
            n += 1
            tx_v.append(n)
            gl, ul = [], []
            for oc in range(8):
                n += 1
                gl.append(n)
                n += 1
                ul.append(n)
            g_v.append(gl)
            u_v.append(ul)
        hd0 = dSh.n
        hqd_v = [hd0 + 16 * (tb + 1) for tb in range(NT2)]
        xn_ld = {}

        @block.sync
        def _(sy):
            for half in range(2):
                dW.dma(sy.dma_start(
                    out=wgT.ap().rearrange("p (g f) -> p g f", g=8)[:, half * 4:(half + 1) * 4, :],
                    in_=wgT_h.rearrange("(g p) f -> p g f", p=128)[:, half * 4:(half + 1) * 4, :]))
            for tb in range(NT2):
                if tb >= 2:
                    sy.wait_ge(sFd.h, xq_v[tb - 2])
                dXn[tb % 2].dma(sy.dma_start(out=xnf[:, (tb % 2) * C:(tb % 2) * C + C],
                                    in_=xn_hbm[tb * 128:(tb + 1) * 128, :]))
                xn_ld[tb] = dXn[tb % 2].n

        w_ld = dW.n

        @block.scalar
        def _(sc):
            for half in range(2):
                dHq[half].dma(sc.dma_start(
                    out=wuT.ap().rearrange("p (g f) -> p g f", g=8)[:, half * 4:(half + 1) * 4, :],
                    in_=wuT_h.rearrange("(g p) f -> p g f", p=128)[:, half * 4:(half + 1) * 4, :]))
        wu_ld = [dHq[0].n, dHq[1].n]

        def emit_quant(v, tb):
            o = (tb % 2) * 16
            xt = xnf[:, (tb % 2) * C:(tb % 2) * C + C]
            v.wait_ge(dXn[tb % 2].h, xn_ld[tb])
            if tb >= 2:
                v.wait_ge(sFa.h, th_v[tb - 2])
            sFd.inc(v.tensor_reduce(fst[:, o:o + 1], xt, AX.X, ALU.max,
                                    apply_absolute_value=True))
            v.drain()
            sFd.inc(v.tensor_scalar_max(fst[:, o:o + 1], fst[:, o:o + 1], 1e-5))
            v.drain()
            sFd.inc(v.reciprocal(fst[:, o + 1:o + 2], fst[:, o:o + 1]))
            v.drain()
            sFd.inc(v.tensor_scalar_mul(fst[:, o + 2:o + 3], fst[:, o + 1:o + 2], 127.0))
            sFd.inc(v.tensor_scalar_mul(fst[:, o + 3:o + 4], fst[:, o:o + 1], 1.0 / 127.0))
            v.drain()
            sFd.inc(v.tensor_mul(fst[:, o + 4:o + 5], fst[:, o + 3:o + 4], wsc[:, 0:1]))
            sFd.inc(v.tensor_mul(fst[:, o + 5:o + 6], fst[:, o + 3:o + 4], wsc[:, 2:3]))
            v.drain()
            sFd.inc(v.tensor_scalar(hbf[:, 0:C], xt, fst[:, o + 2:o + 3], MAGIC,
                                    ALU.mult, ALU.add))
            v.drain()
            if tb >= 1:
                v.wait_ge(sF.h, tx_v[tb - 1])
            sFd.inc(v.tensor_scalar_add(xqb[:, (tb % 2) * C:(tb % 2) * C + C],
                                        hbf[:, 0:C], -MAGIC))

        @block.vector
        def _(v):
            def emit_evac(tb):
                v.wait_ge(sF.h, tx_v[tb])
                if tb >= 2:
                    v.wait_ge(sF.h, u_v[tb - 2][7])
                sFd.inc(v.tensor_copy(xqT[:, (tb % 2) * C:(tb % 2) * C + C], pT.ap()))

            emit_quant(v, 0)
            emit_evac(0)
            emit_quant(v, 1)
            for tb in range(NT2):
                o = (tb % 2) * 16
                for oc in range(8):
                    v.wait_ge(sF.h, u_v[tb][oc])
                    v.wait_ge(sFa.h, silu_v[tb][oc])
                    sFd.inc(v.scalar_tensor_tensor(
                        hbf[:, oc * 512:(oc + 1) * 512], pU[oc % 3][:],
                        fst[:, o + 5:o + 6], sgf[:], ALU.mult, ALU.mult))
                if tb + 1 < NT2:
                    emit_evac(tb + 1)
                v.drain()
                sFd.inc(v.tensor_reduce(fst[:, o + 6:o + 7], hbf[:], AX.X, ALU.max,
                                        apply_absolute_value=True))
                v.drain()
                sFd.inc(v.reciprocal(fst[:, o + 7:o + 8], fst[:, o + 6:o + 7]))
                v.drain()
                sFd.inc(v.tensor_scalar_mul(fst[:, o + 8:o + 9], fst[:, o + 7:o + 8], 127.0))
                sFd.inc(v.tensor_scalar_mul(dsh_all[:, tb:tb + 1], fst[:, o + 6:o + 7], 1.0 / 127.0))
                v.drain()
                if tb + 2 < NT2:
                    emit_quant(v, tb + 2)
                v.wait_ge(sFa.h, th_v[tb])
                if tb >= 1:
                    v.wait_ge(dSh.h, hqd_v[tb - 1])
                sFd.inc(v.tensor_scalar_add(hqb[:], thf[:], -MAGIC))

        @block.tensor
        def _(te):
            te.wait_ge(dW.h, w_ld)
            te.wait_ge(dHq[0].h, wu_ld[0])
            te.wait_ge(dHq[1].h, wu_ld[1])
            for tb in range(NT2):
                te.wait_ge(sFd.h, xq_v[tb])
                if tb >= 1:
                    te.wait_ge(sFd.h, evT_v[tb - 1])
                for cc in range(8):
                    ins = te.transpose(pT[:, cc * 128:(cc + 1) * 128],
                                       xqb[:, (tb % 2) * C + cc * 128:(tb % 2) * C + (cc + 1) * 128],
                                       s_idb[:])
                    if cc == 7:
                        sF.inc(ins)
                for oc in range(8):
                    gc = tb * 8 + oc
                    te.wait_ge(sFd.h, evT_v[tb])
                    if gc >= 3:
                        tb2, oc2 = divmod(gc - 3, 8)
                        te.wait_ge(sFa.h, silu_v[tb2][oc2])
                    for cc in range(8):
                        ins = te.matmul(pG[oc % 3][:],
                                        xqT[:, (tb % 2) * C + cc * 128:(tb % 2) * C + (cc + 1) * 128],
                                        wgT[:, cc * INNER + oc * 512:cc * INNER + (oc + 1) * 512],
                                        start=(cc == 0), stop=(cc == 7))
                        if cc == 7:
                            sF.inc(ins)
                    if gc >= 3:
                        tb2, oc2 = divmod(gc - 3, 8)
                        te.wait_ge(sFd.h, hml_v[tb2][oc2])
                    for cc in range(8):
                        ins = te.matmul(pU[oc % 3][:],
                                        xqT[:, (tb % 2) * C + cc * 128:(tb % 2) * C + (cc + 1) * 128],
                                        wuT[:, cc * INNER + oc * 512:cc * INNER + (oc + 1) * 512],
                                        start=(cc == 0), stop=(cc == 7))
                        if cc == 7:
                            sF.inc(ins)

        @block.scalar
        def _(sc):
            for tb in range(NT2):
                o = (tb % 2) * 16
                for oc in range(8):
                    gc = tb * 8 + oc
                    sc.wait_ge(sF.h, g_v[tb][oc])
                    if gc >= 1:
                        tb2, oc2 = divmod(gc - 1, 8)
                        sc.wait_ge(sFd.h, hml_v[tb2][oc2])
                    sFa.inc(sc.activation(sgf[:], pG[oc % 3][:], AF.Silu,
                                          scale=fst[:, o + 4:o + 5]))
                sc.wait_ge(sFd.h, hs_v[tb])
                if tb >= 1:
                    sc.wait_ge(sFd.h, rnd_v[tb - 1])
                sFa.inc(sc.activation(thf[:], hbf[:], AF.Identity,
                                      scale=fst[:, o + 8:o + 9], bias=s_magic[:, 0:1]))

        @block.gpsimd
        def _(g):
            for tb in range(NT2):
                g.wait_ge(sFd.h, rnd_v[tb])
                dSh.dma(g.dma_start(out=hq_hbm[tb * 128:(tb + 1) * 128, :],
                                   in_=hqb[:]))

        assert sFd.n == opv[order[-1]], (sFd.n, opv[order[-1]])
        assert sF.n == u_v[-1][7]
        assert sFa.n == th_v[-1]
        psc.close()

    def f2():
        hqi_, yb_ = hqi_yb[0], hqi_yb[1]
        psc = ExitStack()
        pH = [psc.enter_context(nc.psum_tensor("pH%d" % i, [128, 1024], BF16)) for i in range(2)]
        pD = [psc.enter_context(nc.psum_tensor("pD%d" % i, [128, 1024], F32)) for i in range(2)]
        psq2 = psc.enter_context(nc.psum_tensor("psq2", [128, 1024], F32))
        wdT = wgT
        hqT = wuT.ap()[:, 0:2 * INNER]

        fd0 = sFd.n
        dshmd_v = fd0 + 1
        evh_v, ln_v = [], []
        n = fd0 + 1
        for tb in range(NT2):
            el = []
            for grp in range(4):
                n += 1
                el.append(n)
            evh_v.append(el)
            n += 8
            ln_v.append(n)
        fa0 = sFa.n
        sumsq_v = [fa0 + 2 * tb + 1 for tb in range(NT2)]
        sqrt_v = [fa0 + 2 * tb + 2 for tb in range(NT2)]
        f0 = sF.n
        txg_v, dmm_v = [], []
        n = f0
        for tb in range(NT2):
            tl = []
            for grp in range(4):
                n += 1
                tl.append(n)
            txg_v.append(tl)
            dl = []
            for n2 in range(2):
                n += 1
                dl.append(n)
            dmm_v.append(dl)
        dS0 = sOut.n
        y_v = [dS0 + 16 * (tb + 1) for tb in range(NT2)]
        hq_ld, xn_ld = {}, {}

        @block.vector
        def _(v):
            sFd.inc(v.tensor_scalar_mul(dsh_all[:, 32:64], dsh_all[:, 0:32],
                                        wsc[:, 4:5]))

        @block.sync
        def _(sy):
            for q in range(4):
                dW.dma(sy.dma_start(
                    out=wdT.ap().rearrange("p (g f) -> p g f", g=32)[:, q * 8:(q + 1) * 8, 0:C],
                    in_=wdT_h.rearrange("(g p) f -> p g f", p=128)[:, q * 8:(q + 1) * 8, :]))
            for tb in range(NT2):
                if tb >= 2:
                    sy.wait_ge(sF.h, dmm_v[tb - 2][1])
                dHq[tb % 2].dma(sy.dma_start(out=hqi_[:, (tb % 2) * INNER:(tb % 2) * INNER + INNER],
                                    in_=hq_hbm[tb * 128:(tb + 1) * 128, :]))
                hq_ld[tb] = dHq[tb % 2].n
                if tb >= 2:
                    sy.wait_ge(sFd.h, ln_v[tb - 2])
                dXn[tb % 2].dma(sy.dma_start(out=xnf[:, (tb % 2) * C:(tb % 2) * C + C],
                                    in_=xn_hbm[tb * 128:(tb + 1) * 128, :]))
                xn_ld[tb] = dXn[tb % 2].n

        wd_ld = dW.n

        @block.tensor
        def _(te):
            te.wait_ge(dW.h, wd_ld)
            te.wait_ge(sFd.h, dshmd_v)
            for tb in range(NT2):
                te.wait_ge(dHq[tb % 2].h, hq_ld[tb])
                for grp in range(4):
                    gi = tb * 4 + grp
                    if gi >= 2:
                        tbp, gp_ = divmod(gi - 2, 4)
                        te.wait_ge(sFd.h, evh_v[tbp][gp_])
                    for k in range(8):
                        ic = grp * 8 + k
                        ins = te.transpose(
                            pH[grp % 2][:, k * 128:(k + 1) * 128],
                            hqi_[:, (tb % 2) * INNER + ic * 128:(tb % 2) * INNER + (ic + 1) * 128],
                            s_idb[:])
                        if k == 7:
                            sF.inc(ins)
                for n2 in range(2):
                    if tb >= 2:
                        te.wait_ge(sFd.h, ln_v[tb - 2])
                    for ic in range(32):
                        if ic == 0 and n2 == 0:
                            te.wait_ge(sFd.h, evh_v[tb][3])
                        ins = te.matmul(
                            pD[tb % 2][:, n2 * 512:(n2 + 1) * 512],
                            hqT[:, (tb % 2) * INNER + ic * 128:(tb % 2) * INNER + (ic + 1) * 128],
                            wdT[:, ic * 1024 + n2 * 512:ic * 1024 + (n2 + 1) * 512],
                            start=(ic == 0), stop=(ic == 31))
                        if ic == 31:
                            sF.inc(ins)

        @block.vector
        def _(v):
            for tb in range(NT2):
                o = (tb % 2) * 16
                for grp in range(4):
                    v.wait_ge(sF.h, txg_v[tb][grp])
                    if tb >= 2:
                        v.wait_ge(sF.h, dmm_v[tb - 2][1])
                    sFd.inc(v.tensor_copy(
                        hqT.rearrange("p (b g f) -> p b g f", b=2, g=32)[:, tb % 2, grp * 8:(grp + 1) * 8, :],
                        pH[grp % 2].ap().rearrange("p (g f) -> p g f", g=8)))
                v.wait_ge(sF.h, dmm_v[tb][1])
                v.wait_ge(dXn[tb % 2].h, xn_ld[tb])
                if tb >= 1:
                    v.wait_ge(sFa.h, sumsq_v[tb - 1])
                r2t = thf[:, 0:C]
                sFd.inc(v.scalar_tensor_tensor(
                    r2t, pD[tb % 2][:], dsh_all[:, 32 + tb:33 + tb],
                    xnf[:, (tb % 2) * C:(tb % 2) * C + C], ALU.mult, ALU.add))
                v.drain()
                sFd.inc(v.reduce_sum(fst[:, o:o + 1], r2t, AX.X))
                v.drain()
                v.wait_ge(sFa.h, sumsq_v[tb])
                sFd.inc(v.tensor_scalar_mul(fst[:, o + 2:o + 3], fst[:, o:o + 1], 1.0 / C))
                sFd.inc(v.tensor_scalar_mul(fst[:, o + 3:o + 4], fst[:, o + 1:o + 2], 1.0 / C))
                v.drain()
                sFd.inc(v.tensor_mul(fst[:, o + 4:o + 5], fst[:, o + 2:o + 3], fst[:, o + 2:o + 3]))
                v.drain()
                sFd.inc(v.tensor_sub(fst[:, o + 5:o + 6], fst[:, o + 3:o + 4], fst[:, o + 4:o + 5]))
                v.drain()
                v.wait_ge(sFa.h, sqrt_v[tb])
                sFd.inc(v.reciprocal(fst[:, o + 7:o + 8], fst[:, o + 6:o + 7]))
                v.drain()
                if tb >= 1:
                    v.wait_ge(sOut.h, y_v[tb - 1])
                sFd.inc(v.tensor_scalar(yb_[:], r2t,
                                        fst[:, o + 2:o + 3], fst[:, o + 7:o + 8],
                                        ALU.subtract, ALU.mult))

        @block.scalar
        def _(sc):
            for tb in range(NT2):
                o = (tb % 2) * 16
                sc.wait_ge(sFd.h, ln_v[tb] - 7)
                sFa.inc(sc.activation(psq2[:], thf[:, 0:C], AF.Square,
                                      accum_out=fst[:, o + 1:o + 2]))
                sc.wait_ge(sFd.h, ln_v[tb] - 2)
                sFa.inc(sc.activation(fst[:, o + 6:o + 7], fst[:, o + 5:o + 6],
                                      AF.Sqrt, bias=s_eps[:, 0:1]))

        @block.gpsimd
        def _(g):
            for tb in range(NT2):
                g.wait_ge(sFd.h, ln_v[tb])
                sOut.dma(g.dma_start(out=y[tb * 128:(tb + 1) * 128, :],
                                     in_=yb_[:]))

        assert sFd.n == ln_v[-1]
        assert sOut.n == y_v[-1]
        psc.close()

    if nph < UPTO:
        f1()
        barrier()
        nph += 1
    esF1.close()
    hqi_yb = [esF.enter_context(nc.sbuf_tensor("hqi", [128, 2 * INNER], BF16)),
              esF.enter_context(nc.sbuf_tensor("yb", [128, C], F32))]
    if nph < UPTO:
        f2()
        barrier()
        nph += 1

    @block.sync
    def _(sy):
        sy.wait_ge(sOut.h, sOut.n)

    esF.close()
    es.close()
    return nc


def make_consts():
    idf = np.eye(128, dtype=np.float32)
    idb = np.eye(128, dtype=bfnp)
    tri = np.where(np.arange(128)[:, None] > np.arange(128)[None, :],
                   np.float32(-30000.0), np.float32(0.0)).astype(bfnp)
    hmap = np.zeros((128, 128), dtype=bfnp)
    for cc in range(8):
        for p in range(128):
            hmap[p, cc * 16 + (2 * cc + p // 64)] = 1
    ones = np.ones((128, 128), dtype=np.float32)
    id16 = np.eye(16, dtype=np.float32)
    return dict(c_idf=idf, c_idb=idb, c_tri=tri, c_hmap=hmap, c_ones=ones,
                c_id16=id16)


def kernel(L0_attn_x, L0_attn_w_qkv, L0_attn_w_out, L0_ffn_w_gate, L0_ffn_w_up,
           L0_ffn_w_down):
    x = np.ascontiguousarray(np.asarray(L0_attn_x, dtype=np.float32))
    B = x.shape[0]
    ncore = 8
    per = B // ncore
    nc = build_program()
    base = dict(
        wqkv=np.ascontiguousarray(np.asarray(L0_attn_w_qkv, dtype=np.float32)),
        wout=np.ascontiguousarray(np.asarray(L0_attn_w_out, dtype=np.float32)),
        wg=np.ascontiguousarray(np.asarray(L0_ffn_w_gate, dtype=np.float32)),
        wu=np.ascontiguousarray(np.asarray(L0_ffn_w_up, dtype=np.float32)),
        wd=np.ascontiguousarray(np.asarray(L0_ffn_w_down, dtype=np.float32)),
        **make_consts())
    in_maps = []
    for c in range(ncore):
        m = dict(base)
        m["xin"] = np.ascontiguousarray(x[c * per:(c + 1) * per].reshape(per * T, C))
        in_maps.append(m)
    res = run_bass_kernel_spmd(nc, in_maps, core_ids=list(range(ncore)))
    outs = [r["y"].reshape(per, T, C) for r in res.results]
    return np.concatenate(outs, axis=0).astype(np.float32)


# version 08
# speedup vs baseline: 1.7573x; 1.0559x over previous
import numpy as np
import ml_dtypes
import concourse.bass as bass
import concourse.mybir as mybir
from concourse.bass_utils import run_bass_kernel_spmd
from contextlib import ExitStack

F32 = mybir.dt.float32
BF16 = mybir.dt.bfloat16
AF = mybir.ActivationFunctionType
ALU = mybir.AluOpType
AX = mybir.AxisListType
bfnp = ml_dtypes.bfloat16

MAGIC = float(np.float32(1.5 * 2 ** 23))
T, C, NH, DK, NSEQ, INNER = 2048, 1024, 16, 64, 2, 4096
NTB = T // 128
NQC = 4
NCC = C // 128
NPAIR = NH // 2
NT2 = NSEQ * NTB

DEBUG_OUT = False
UPTO = 99
A4_PAIRS = 8
A4_QCS = 4


class Sem:
    def __init__(self, es, nc, name):
        self.h = es.enter_context(nc.semaphore(name))
        self.n = 0

    def inc(self, instr, k=1):
        instr.then_inc(self.h, k)
        self.n += k
        return instr

    def dma(self, instr):
        return self.inc(instr, 16)


def build_program():
    nc = bass.Bass("TRN2", target_bir_lowering=False)

    xin = nc.dram_tensor("xin", [NSEQ * T, C], F32, kind="ExternalInput").ap()
    wqkv = nc.dram_tensor("wqkv", [C, 3 * C], F32, kind="ExternalInput").ap()
    wout = nc.dram_tensor("wout", [C, C], F32, kind="ExternalInput").ap()
    wg = nc.dram_tensor("wg", [INNER, C], F32, kind="ExternalInput").ap()
    wu = nc.dram_tensor("wu", [INNER, C], F32, kind="ExternalInput").ap()
    wd = nc.dram_tensor("wd", [C, INNER], F32, kind="ExternalInput").ap()
    c_idf = nc.dram_tensor("c_idf", [128, 128], F32, kind="ExternalInput").ap()
    c_idb = nc.dram_tensor("c_idb", [128, 128], BF16, kind="ExternalInput").ap()
    c_tri = nc.dram_tensor("c_tri", [128, 128], BF16, kind="ExternalInput").ap()
    c_hmap = nc.dram_tensor("c_hmap", [128, 128], BF16, kind="ExternalInput").ap()
    c_ones = nc.dram_tensor("c_ones", [128, 128], F32, kind="ExternalInput").ap()
    c_id16 = nc.dram_tensor("c_id16", [16, 16], F32, kind="ExternalInput").ap()
    y = nc.dram_tensor("y", [NSEQ * T, C], F32, kind="ExternalOutput").ap()

    KIND = "ExternalOutput" if DEBUG_OUT else "Internal"
    xn_hbm = nc.dram_tensor("xn_hbm", [NSEQ * T, C], F32, kind=KIND).ap()
    hq_hbm = nc.dram_tensor("hq_hbm", [NSEQ * T, INNER], BF16, kind=KIND).ap()
    wqkv_bh = nc.dram_tensor("wqkv_bh", [C, 3 * C], BF16, kind="Internal").ap()
    wout_bh = nc.dram_tensor("wout_bh", [C, C], BF16, kind="Internal").ap()
    wgT_h = nc.dram_tensor("wgT_h", [C, INNER], BF16, kind=KIND).ap()
    wuT_h = nc.dram_tensor("wuT_h", [C, INNER], BF16, kind=KIND).ap()
    wdT_h = nc.dram_tensor("wdT_h", [INNER, C], BF16, kind=KIND).ap()

    es = ExitStack()

    s_idf = es.enter_context(nc.sbuf_tensor("s_idf", [128, 128], F32))
    s_idb = es.enter_context(nc.sbuf_tensor("s_idb", [128, 128], BF16))
    s_tri = es.enter_context(nc.sbuf_tensor("s_tri", [128, 128], BF16))
    s_hmap = es.enter_context(nc.sbuf_tensor("s_hmap", [128, 128], BF16))
    s_ones = es.enter_context(nc.sbuf_tensor("s_ones", [128, 128], F32))
    s_onesc = es.enter_context(nc.sbuf_tensor("s_onesc", [128, 1], F32))
    s_id16 = es.enter_context(nc.sbuf_tensor("s_id16", [16, 16], F32))
    s_onesb = es.enter_context(nc.sbuf_tensor("s_onesb", [128, 64], BF16))
    s_eps = es.enter_context(nc.sbuf_tensor("s_eps", [128, 1], F32))
    s_magic = es.enter_context(nc.sbuf_tensor("s_magic", [128, 1], F32))
    wsc = es.enter_context(nc.sbuf_tensor("wsc", [128, 8], F32))
    dsh_all = es.enter_context(nc.sbuf_tensor("dsh_all", [128, 64], F32))
    w11 = es.enter_context(nc.sbuf_tensor("w11", [1, 4], F32))

    SD = {}
    for name in ("dX dW dS sTx sEv sAc sQk sS sEx sMk sPV sRc sBc sNm "
                 "sWo sGp sF sFd sFa sOut").split():
        SD[name] = Sem(es, nc, name)
    dXb = [Sem(es, nc, "dXb%d" % i) for i in range(2)]
    dWq = [Sem(es, nc, "dWq%d" % i) for i in range(2)]
    dWt = [Sem(es, nc, "dWt%d" % i) for i in range(2)]
    dXn = [Sem(es, nc, "dXn%d" % i) for i in range(2)]
    dHq = [Sem(es, nc, "dHq%d" % i) for i in range(2)]
    dSc = [Sem(es, nc, "dSc%d" % i) for i in range(2)]
    dSg = [Sem(es, nc, "dSg%d" % i) for i in range(2)]
    dSx = [Sem(es, nc, "dSx%d" % i) for i in range(2)]
    dSt2 = [Sem(es, nc, "dSt2_%d" % i) for i in range(2)]
    dSh = Sem(es, nc, "dSh")
    dX, dW, dS = SD["dX"], SD["dW"], SD["dS"]
    sTx, sEv, sAc, sQk = SD["sTx"], SD["sEv"], SD["sAc"], SD["sQk"]
    sS_, sEx, sMk, sPV = SD["sS"], SD["sEx"], SD["sMk"], SD["sPV"]
    sRc, sBc, sNm = SD["sRc"], SD["sBc"], SD["sNm"]
    sWo, sGp = SD["sWo"], SD["sGp"]
    sF, sFd, sFa, sOut = SD["sF"], SD["sFd"], SD["sFa"], SD["sOut"]
    ALL = (list(SD.values()) + dXb + dWq + dWt + dXn + dHq + dSc + dSg + dSx
           + dSt2 + [dSh])

    block = es.enter_context(nc.Block())

    def barrier():
        snaps = [(s.h, s.n) for s in ALL if s.n > 0]
        for attr in ("sync", "tensor", "vector", "scalar", "gpsimd"):
            def f(eng, snaps=snaps):
                for h, v in snaps:
                    eng.wait_ge(h, v)
            getattr(block, attr)(f)

    esA = ExitStack()

    def sbA(name, shape, dt):
        return esA.enter_context(nc.sbuf_tensor(name, shape, dt))

    xbuf = sbA("xbuf", [128, 2 * C], F32)
    wst = sbA("wst", [128, 2 * 1024], F32)
    wcb = sbA("wcb", [128, 2 * 1024], BF16)
    ovl = sbA("ovl", [128, 8192], BF16)
    wqs = ovl.ap()[:, 0:4096]
    wout_b = ovl.ap()[:, 0:8192]
    xfm = sbA("xfm", [128, 8 * T], BF16)
    attnO = xfm
    qfm = sbA("qfm", [128, 8 * T], BF16)
    kfm = sbA("kfm", [128, 8 * T], BF16)
    vsb = sbA("vsb", [128, NTB * NH * 65], BF16)
    pt = ovl.ap()[:, 0:3072]
    qsq = ovl.ap()[:, 0:2048]
    rcp_t = sbA("rcp_t", [128, 1024], F32)
    rcpf = rcp_t.ap()[64:65, 0:1024]
    rcp_b16 = sbA("rcp_b16", [128, 1024], BF16)
    rcpa = rcp_b16.ap()[64:65, 0:512]
    rcpb = rcp_b16.ap()[64:65, 512:1024]
    bcsb = sbA("bcsb", [64, 1024], F32)
    stg = sbA("stg", [64, 2 * 512], BF16)
    biasq = sbA("biasq", [16, 4], F32)
    nrow = sbA("nrow", [1, 16], F32)
    biasP = sbA("biasP", [128, 24], F32)
    r1 = sbA("r1", [128, 2 * C], F32)
    stat = sbA("stat", [128, 16], F32)
    sqs = r1.ap()[:, 0:C]
    xnb = sbA("xnb", [128, 2 * C], F32)
    absacc = sbA("absacc", [128, 33], F32)
    tern = wcb.ap()[:, 0:2048]
    terT = ovl.ap()[:, 4096:6144]

    def ph0():
        @block.sync
        def _(sy):
            dW.dma(sy.dma_start(out=s_idf[:], in_=c_idf))
            dW.dma(sy.dma_start(out=s_idb[:], in_=c_idb))
            dW.dma(sy.dma_start(out=s_tri[:], in_=c_tri))
            dW.dma(sy.dma_start(out=s_hmap[:], in_=c_hmap))
            dW.dma(sy.dma_start(out=s_ones[:], in_=c_ones))
            dW.dma(sy.dma_start(out=s_id16[:], in_=c_id16))
        cw = dW.n

        @block.vector
        def _(v):
            sEv.inc(v.memset(s_onesb[:], 1.0))
            sEv.inc(v.memset(s_eps[:], 1e-5))
            sEv.inc(v.memset(s_magic[:], MAGIC))
            sEv.inc(v.memset(s_onesc[:], 1.0))
            sEv.inc(v.memset(
                vsb.ap().rearrange("p (g c) -> p g c", c=65)[:, :, 64:65], 1.0))

        jobs = [("qkv", t, h) for t in range(8) for h in range(3)]
        jobs += [("out", t, 0) for t in range(8)]
        NJ = len(jobs)
        ld_v = {}
        ev0 = sEv.n
        ev_v = [ev0 + i + 1 for i in range(NJ)]
        st_v = {}

        @block.sync
        def _(sy):
            for i, (tag, t, h) in enumerate(jobs):
                if i >= 2:
                    sy.wait_ge(sEv.h, ev_v[i - 2])
                b = wst[:, (i % 2) * 1024:(i % 2) * 1024 + 1024]
                if tag == "qkv":
                    src = wqkv[t * 128:(t + 1) * 128, h * 1024:(h + 1) * 1024]
                else:
                    src = wout[t * 128:(t + 1) * 128, :]
                dWt[i % 2].dma(sy.dma_start(out=b, in_=src))
                ld_v[i] = dWt[i % 2].n

        @block.vector
        def _(v):
            for i, (tag, t, h) in enumerate(jobs):
                v.wait_ge(dWt[i % 2].h, ld_v[i])
                b = wst[:, (i % 2) * 1024:(i % 2) * 1024 + 1024]
                if i >= 2 and (i - 2) in st_v:
                    v.wait_ge(dSc[i % 2].h, st_v[i - 2])
                dst = wcb[:, (i % 2) * 1024:(i % 2) * 1024 + 1024]
                sEv.inc(v.tensor_copy(dst, b))

        @block.gpsimd
        def _(g):
            for i, (tag, t, h) in enumerate(jobs):
                g.wait_ge(sEv.h, ev_v[i])
                src = wcb[:, (i % 2) * 1024:(i % 2) * 1024 + 1024]
                if tag == "qkv":
                    dst = wqkv_bh[t * 128:(t + 1) * 128, h * 1024:(h + 1) * 1024]
                else:
                    dst = wout_bh[t * 128:(t + 1) * 128, :]
                dSc[i % 2].dma(g.dma_start(out=dst, in_=src))
                st_v[i] = dSc[i % 2].n

    def a1(s):
        psc = ExitStack()
        ptx = psc.enter_context(nc.psum_tensor("ptx%d" % s, [128, 1024], F32))
        tx0 = sTx.n
        tx_v = [tx0 + i + 1 for i in range(NTB)]
        ev0 = sEv.n
        ev_v = [ev0 + i + 1 for i in range(NTB)]
        ld_v = {}

        @block.sync
        def _(sy):
            for tb in range(NTB):
                if tb >= 2:
                    sy.wait_ge(sTx.h, tx_v[tb - 2])
                dXb[tb % 2].dma(sy.dma_start(
                    out=xbuf[:, (tb % 2) * C:(tb % 2) * C + C],
                    in_=xin[s * T + tb * 128: s * T + (tb + 1) * 128, :]))
                ld_v[tb] = dXb[tb % 2].n

        @block.tensor
        def _(te):
            for tb in range(NTB):
                te.wait_ge(dXb[tb % 2].h, ld_v[tb])
                if tb >= 1:
                    te.wait_ge(sEv.h, ev_v[tb - 1])
                for cc in range(NCC):
                    ins = te.transpose(
                        ptx[:, cc * 128:(cc + 1) * 128],
                        xbuf[:, (tb % 2) * C + cc * 128:(tb % 2) * C + (cc + 1) * 128],
                        s_idf[:])
                    if cc == 7:
                        sTx.inc(ins)

        @block.vector
        def _(v):
            for tb in range(NTB):
                v.wait_ge(sTx.h, tx_v[tb])
                sEv.inc(v.tensor_copy(
                    xfm.ap().rearrange("p (g t) -> p g t", t=T)[:, :, tb * 128:(tb + 1) * 128],
                    ptx.ap().rearrange("p (g c) -> p g c", c=128)))
        assert sTx.n == tx_v[-1] and sEv.n == ev_v[-1]
        psc.close()

    def a2(s):
        psc = ExitStack()
        pqk = [psc.enter_context(nc.psum_tensor("pqk%d_%d" % (s, i), [128, 512], F32))
               for i in range(2)]
        pv = [psc.enter_context(nc.psum_tensor("pv%d_%d" % (s, i), [128, 512], F32))
              for i in range(2)]

        qk0 = sQk.n
        qk_v = {}
        n = qk0
        for of in range(16):
            for tc in range(4):
                n += 1
                qk_v[(of, tc)] = n
        v_v = {}
        for nv in range(2):
            for tb in range(NTB):
                n += 1
                v_v[(nv, tb)] = n
        ac0 = sAc.n
        ac_v = {}
        n = ac0
        for of in range(16):
            for tc in range(4):
                n += 1
                ac_v[(of, tc)] = n
        ev0 = sEv.n
        evv_v = {}
        n = ev0
        for nv in range(2):
            for tb in range(NTB):
                n += 1
                evv_v[(nv, tb)] = n
        wl_v = {}

        @block.sync
        def _(sy):
            for of in range(16):
                if of >= 2:
                    sy.wait_ge(sQk.h, qk_v[(of - 2, 3)])
                dWq[of % 2].dma(sy.dma_start(
                    out=wqs.rearrange("p (b f) -> p b f", b=2)[:, of % 2, 0:1024],
                    in_=wqkv_bh.rearrange("(g p) f -> p g f", p=128)[:, :, of * 128:(of + 1) * 128]))
                wl_v[of] = dWq[of % 2].n
            for nv in range(2):
                if nv == 0:
                    sy.wait_ge(sQk.h, qk_v[(14, 3)])
                    sy.wait_ge(sQk.h, qk_v[(15, 3)])
                else:
                    sy.wait_ge(sQk.h, v_v[(0, NTB - 1)])
                for hf in range(2):
                    dWq[hf].dma(sy.dma_start(
                        out=wqs.rearrange("p (b f) -> p b f", b=2)[:, hf, :],
                        in_=wqkv_bh.rearrange("(g p) f -> p g f", p=128)[:, hf * 4:(hf + 1) * 4, 2048 + nv * 512:2048 + (nv + 1) * 512]))
                wl_v[16 + nv] = (dWq[0].n, dWq[1].n)

        @block.tensor
        def _(te):
            for of in range(16):
                te.wait_ge(dWq[of % 2].h, wl_v[of])
                for tc in range(4):
                    m = of * 4 + tc
                    if m >= 2:
                        of2, tc2 = divmod(m - 2, 4)
                        te.wait_ge(sAc.h, ac_v[(of2, tc2)])
                    for cc in range(NCC):
                        ins = te.matmul(
                            pqk[m % 2][:],
                            wqs[:, (of % 2) * 2048 + cc * 128:(of % 2) * 2048 + (cc + 1) * 128],
                            xfm[:, cc * T + tc * 512:cc * T + (tc + 1) * 512],
                            start=(cc == 0), stop=(cc == 7))
                        if cc == 7:
                            sQk.inc(ins)
            for nv in range(2):
                te.wait_ge(dWq[0].h, wl_v[16 + nv][0])
                te.wait_ge(dWq[1].h, wl_v[16 + nv][1])
                for tb in range(NTB):
                    m = nv * NTB + tb
                    if m >= 2:
                        nv2, tb2 = divmod(m - 2, NTB)
                        te.wait_ge(sEv.h, evv_v[(nv2, tb2)])
                    for cc in range(NCC):
                        ins = te.matmul(
                            pv[m % 2][:],
                            xfm[:, cc * T + tb * 128:cc * T + (tb + 1) * 128],
                            wqs[:, (cc // 4) * 2048 + (cc % 4) * 512:(cc // 4) * 2048 + (cc % 4) * 512 + 512],
                            start=(cc == 0), stop=(cc == 7))
                        if cc == 7:
                            sQk.inc(ins)

        @block.scalar
        def _(sc):
            for of in range(16):
                dstt = qfm if of < 8 else kfm
                ch = of % 8
                for tc in range(4):
                    sc.wait_ge(sQk.h, qk_v[(of, tc)])
                    sAc.inc(sc.activation(
                        dstt[:, ch * T + tc * 512:ch * T + (tc + 1) * 512],
                        pqk[(of * 4 + tc) % 2][:], AF.Identity))

        @block.vector
        def _(v):
            for nv in range(2):
                for tb in range(NTB):
                    v.wait_ge(sQk.h, v_v[(nv, tb)])
                    sEv.inc(v.tensor_copy(
                        vsb.ap().rearrange("p (k h c) -> p k h c", h=NH, c=65)[:, tb, nv * 8:(nv + 1) * 8, 0:64],
                        pv[(nv * NTB + tb) % 2].ap().rearrange("p (h c) -> p h c", c=64)))
        assert sQk.n == v_v[(1, NTB - 1)]
        psc.close()

    def a3(s):
        psc = ExitStack()
        pn = psc.enter_context(nc.psum_tensor("pn%d" % s, [16, 2048], F32))
        pr = psc.enter_context(nc.psum_tensor("pr%d" % s, [1, 16], F32))
        pbb = psc.enter_context(nc.psum_tensor("pbb%d" % s, [128, 16], F32))

        base = sEv.n
        sq_v = {}
        for cc in range(NCC):
            sq_v[(0, cc)] = base + cc + 1
        redq_v = base + 9
        for cc in range(NCC):
            sq_v[(1, cc)] = base + 10 + cc
        redk_v = base + 18
        mul_v = base + 19
        negb_v = base + 20
        nrow_v = base + 21
        bcp_v = base + 22
        min_v = base + 23
        t0 = sTx.n
        mm_v = {}
        for cc in range(NCC):
            mm_v[(0, cc)] = t0 + cc + 1
            mm_v[(1, cc)] = t0 + 9 + cc
        pr_v = t0 + 17
        pbb_v = t0 + 18
        sqrt_v = sAc.n + 1

        @block.vector
        def _(v):
            for cc in range(NCC):
                if cc >= 1:
                    v.wait_ge(sTx.h, mm_v[(0, cc - 1)])
                sEv.inc(v.tensor_mul(qsq[:], qfm[:, cc * T:(cc + 1) * T],
                                     qfm[:, cc * T:(cc + 1) * T]))
            v.wait_ge(sTx.h, mm_v[(0, NCC - 1)])
            sEv.inc(v.tensor_reduce(biasq[:, 0:1], pn[0:16, :], AX.X, ALU.max))
            for cc in range(NCC):
                if cc >= 1:
                    v.wait_ge(sTx.h, mm_v[(1, cc - 1)])
                sEv.inc(v.tensor_mul(qsq[:], kfm[:, cc * T:(cc + 1) * T],
                                     kfm[:, cc * T:(cc + 1) * T]))
            v.wait_ge(sTx.h, mm_v[(1, NCC - 1)])
            sEv.inc(v.tensor_reduce(biasq[:, 1:2], pn[0:16, :], AX.X, ALU.max))
            v.drain()
            sEv.inc(v.tensor_mul(biasq[:, 2:3], biasq[:, 0:1], biasq[:, 1:2]))
            v.drain()
            v.wait_ge(sAc.h, sqrt_v)
            sEv.inc(v.tensor_scalar_mul(biasq[:, 0:1], biasq[:, 3:4], -0.125))
            v.drain()
            v.wait_ge(sTx.h, pr_v)
            sEv.inc(v.tensor_copy(nrow[:], pr[:]))
            v.drain()
            v.wait_ge(sTx.h, pbb_v)
            sEv.inc(v.tensor_copy(biasP[:, 0:16], pbb[:]))
            v.drain()
            sEv.inc(v.tensor_tensor(biasP[:, 16:24], biasP[:, 0:16:2],
                                    biasP[:, 1:16:2], ALU.min))
            v.drain()

        @block.tensor
        def _(te):
            for w in range(2):
                if w == 1:
                    te.wait_ge(sEv.h, redq_v)
                for cc in range(NCC):
                    te.wait_ge(sEv.h, sq_v[(w, cc)])
                    for nq in range(4):
                        ins = te.matmul(pn[:, nq * 512:(nq + 1) * 512],
                                        s_hmap[:, cc * 16:(cc + 1) * 16],
                                        qsq[:, nq * 512:(nq + 1) * 512],
                                        start=(cc == 0), stop=(cc == 7))
                        if nq == 3:
                            sTx.inc(ins)
            te.wait_ge(sEv.h, negb_v)
            sTx.inc(te.matmul(pr[:], biasq[:, 0:1], s_id16[:], start=True, stop=True))
            te.wait_ge(sEv.h, nrow_v)
            sTx.inc(te.matmul(pbb[:], s_ones[0:1, :], nrow[:], start=True, stop=True))

        @block.scalar
        def _(sc):
            sc.wait_ge(sEv.h, mul_v)
            sAc.inc(sc.activation(biasq[:, 3:4], biasq[:, 2:3], AF.Sqrt))

        assert sEv.n == min_v, (sEv.n, min_v)
        assert sTx.n == pbb_v
        psc.close()

    def a4(s):
        psc = ExitStack()
        pS = [psc.enter_context(nc.psum_tensor("pS%d_%d" % (s, i), [128, 1024], F32))
              for i in range(2)]
        pO = [psc.enter_context(nc.psum_tensor("pO%d_%d" % (s, i), [65, 1024], F32))
              for i in range(2)]

        QCS = A4_QCS
        events = []
        for pair in range(A4_PAIRS):
            for qc in range(QCS):
                for kb in range(4 * qc + 4):
                    events.append((pair, qc, kb))
        E = len(events)
        eidx = {ev: i for i, ev in enumerate(events)}
        part = [kb >= 4 * qc for (pair, qc, kb) in events]
        jof = [kb - 4 * qc for (pair, qc, kb) in events]

        exp0, s0_, pv0 = sEx.n, sS_.n, sPV.n
        exp_v = [exp0 + i + 1 for i in range(E)]
        s_v = [s0_ + i + 1 for i in range(E)]
        pv_v = [pv0 + i + 1 for i in range(E)]
        NQG = A4_PAIRS * QCS
        nm0, rc0, bc0, ca0 = sNm.n, sRc.n, sBc.n, sAc.n
        nm_v = [nm0 + i + 1 for i in range(NQG)]
        rc_v = [rc0 + i + 1 for i in range(NQG)]
        bc_v = [bc0 + i + 1 for i in range(NQG)]
        cast_v = [ca0 + i + 1 for i in range(NQG)]
        stg_v = {}
        stg_base = [dSg[0].n, dSg[1].n]
        for i in range(NQG):
            stg_v[i] = stg_base[i % 2] + 16 * (i // 2 + 1)
        bias_rdy = sEv.n
        ev0 = sEv.n
        evac_v = [ev0 + i + 1 for i in range(NQG)]

        def cols(i):
            return (jof[i] * 128 if part[i] else 0, 512)

        elast = {}
        for gq in range(NQG):
            pr_, qc_ = divmod(gq, QCS)
            elast[gq] = eidx[(pr_, qc_, 4 * qc_ + 3)]

        def emit_pv(te, ip):
            pair, qc, kb = events[ip]
            gq = pair * QCS + qc
            c0, c1 = cols(ip)
            te.wait_ge(sEx.h, exp_v[ip])
            if kb == 0 and gq >= 2:
                te.wait_ge(sNm.h, nm_v[gq - 2])
            for hh in range(2):
                h = pair * 2 + hh
                ins = te.matmul(
                    pO[gq % 2][0:65, hh * 512 + c0:hh * 512 + c1],
                    vsb.ap().rearrange("p (k h c) -> p k h c", h=NH, c=65)[:, kb, h, :],
                    pt[:, (ip % 3) * 1024 + hh * 512 + c0:(ip % 3) * 1024 + hh * 512 + c1],
                    start=(kb == 0), stop=(kb == 4 * qc + 3),
                    skip_group_check=True)
                if hh == 1:
                    sPV.inc(ins)

        @block.tensor
        def _(te):
            pend = []

            def do_bc(gq):
                par = elast[gq] % 2
                te.wait_ge(sAc.h, cast_v[gq])
                te.matmul(pS[par][0:64, 0:512], s_onesb[64:65, :], rcpa,
                          start=True, stop=True, skip_group_check=True)
                sBc.inc(te.matmul(pS[par][0:64, 512:1024], s_onesb[64:65, :], rcpb,
                                  start=True, stop=True, skip_group_check=True))

            bc_dirty = {}
            nextpv = 0
            for i, (pair, qc, kb) in enumerate(events):
                c0, c1 = cols(i)
                if i >= 2:
                    te.wait_ge(sEx.h, exp_v[i - 2])
                if (i % 2) in bc_dirty:
                    te.wait_ge(sEv.h, bc_dirty.pop(i % 2))
                for hh in range(2):
                    ins = te.matmul(
                        pS[i % 2][:, hh * 512 + c0:hh * 512 + c1],
                        kfm[hh * 64:(hh + 1) * 64, pair * T + kb * 128:pair * T + (kb + 1) * 128],
                        qfm[hh * 64:(hh + 1) * 64, pair * T + qc * 512 + c0:pair * T + qc * 512 + c1],
                        start=True, stop=not part[i], tile_position=(hh * 64, 0),
                        skip_group_check=True)
                    if hh == 1 and not part[i]:
                        sS_.inc(ins)
                if part[i]:
                    j = jof[i]
                    for hh in range(2):
                        ins = te.matmul(
                            pS[i % 2][:, hh * 512 + j * 128:hh * 512 + (j + 1) * 128],
                            s_idb[:], s_tri[:],
                            start=False, stop=True, skip_group_check=True)
                        if hh == 1:
                            sS_.inc(ins)
                while nextpv <= i - 2:
                    emit_pv(te, nextpv)
                    nextpv += 1
                if i >= 1 and events[i - 1][2] == 4 * events[i - 1][1] + 3:
                    while nextpv <= i - 1:
                        emit_pv(te, nextpv)
                        nextpv += 1
                while pend:
                    gqp = pend.pop(0)
                    do_bc(gqp)
                    bc_dirty[elast[gqp] % 2] = evac_v[gqp]
                if kb == 4 * qc + 3:
                    pend.append(pair * QCS + qc)
            while nextpv < E:
                emit_pv(te, nextpv)
                nextpv += 1
            while pend:
                do_bc(pend.pop(0))

        @block.scalar
        def _(sc):
            sc.wait_ge(sEv.h, bias_rdy)
            for i, (pair, qc, kb) in enumerate(events):
                c0, c1 = cols(i)
                sc.wait_ge(sS_.h, s_v[i])
                if i >= 3:
                    sc.wait_ge(sPV.h, pv_v[i - 3])
                sEx.inc(sc.activation(
                    pt.rearrange("p (b h c) -> p b h c", b=3, c=512)[:, i % 3, :, c0:c1],
                    pS[i % 2].ap().rearrange("p (h c) -> p h c", c=512)[:, :, c0:c1],
                    AF.Exp, bias=biasP[:, 16 + pair:17 + pair], scale=0.125))
                if kb == 4 * qc + 3:
                    gq2 = pair * QCS + qc
                    sc.wait_ge(sPV.h, pv_v[i])
                    if gq2 >= 1:
                        sc.wait_ge(sBc.h, bc_v[gq2 - 1])
                    sc.activation(rcpf, pO[gq2 % 2][64:65, 0:1024], AF.Ln)
                    sc.drain()
                    sAc.inc(sc.activation(rcp_b16[64:65, 0:1024], rcpf,
                                          AF.Exp, scale=-1.0))

        @block.gpsimd
        def _(g):
            for i, (pair, qc, kb) in enumerate(events):
                if kb == 4 * qc + 3:
                    gq = pair * QCS + qc
                    g.wait_ge(sNm.h, nm_v[gq])
                    dSg[gq % 2].dma(g.dma_start(
                        out=attnO[64:128, pair * T + qc * 512:pair * T + (qc + 1) * 512],
                        in_=stg[:, (gq % 2) * 512:(gq % 2) * 512 + 512]))

        @block.vector
        def _(v):
            for gq in range(NQG):
                pair, qc = divmod(gq, QCS)
                i_last = eidx[(pair, qc, 4 * qc + 3)]
                v.wait_ge(sPV.h, pv_v[i_last])
                v.wait_ge(sBc.h, bc_v[gq])
                sEv.inc(v.tensor_copy(bcsb[:], pS[elast[gq] % 2][0:64, :]))
                v.drain()
                if gq >= 2:
                    v.wait_ge(dSg[gq % 2].h, stg_v[gq - 2])
                v.tensor_mul(attnO[0:64, pair * T + qc * 512:pair * T + (qc + 1) * 512],
                             pO[gq % 2][0:64, 0:512], bcsb[0:64, 0:512])
                sNm.inc(v.tensor_mul(stg[:, (gq % 2) * 512:(gq % 2) * 512 + 512],
                                     pO[gq % 2][0:64, 512:1024], bcsb[0:64, 512:1024]))

        assert sEx.n == exp_v[-1] and sPV.n == pv_v[-1]
        assert sNm.n == nm_v[-1] and sBc.n == bc_v[-1]
        psc.close()

    def a5(s):
        psc = ExitStack()
        pw = [psc.enter_context(nc.psum_tensor("pw%d_%d" % (s, i), [128, 1024], F32))
              for i in range(2)]
        psq = psc.enter_context(nc.psum_tensor("psq%d" % s, [128, 1024], F32))
        wo0 = sWo.n
        wo_v = [wo0 + i + 1 for i in range(NTB)]
        ac0 = sAc.n
        ac_sumsq = [ac0 + 2 * i + 1 for i in range(NTB)]
        ac_sqrt = [ac0 + 2 * i + 2 for i in range(NTB)]
        ev0 = sEv.n
        stt_v, var_v, xn_v = [], [], []
        n = ev0
        for tb in range(NTB):
            n += 1
            stt_v.append(n)
            n += 1
            n += 4
            var_v.append(n)
            n += 1
            n += 1
            xn_v.append(n)
        xnd_v = {}
        xnd_base = [dSx[0].n, dSx[1].n]
        for i in range(NTB):
            xnd_v[i] = xnd_base[i % 2] + 16 * (i // 2 + 1)
        ld_v = {}

        @block.sync
        def _(sy):
            dW.dma(sy.dma_start(
                out=wout_b.rearrange("p (g f) -> p g f", g=8),
                in_=wout_bh.rearrange("(g p) f -> p g f", p=128)))
            for tb in range(NTB):
                if tb >= 2:
                    sy.wait_ge(sEv.h, xn_v[tb - 2])
                dXb[tb % 2].dma(sy.dma_start(
                    out=xbuf[:, (tb % 2) * C:(tb % 2) * C + C],
                    in_=xin[s * T + tb * 128: s * T + (tb + 1) * 128, :]))
                ld_v[tb] = dXb[tb % 2].n
        wob_ld = dW.n

        @block.tensor
        def _(te):
            te.wait_ge(dW.h, wob_ld)
            for tb in range(NTB):
                if tb >= 2:
                    te.wait_ge(sEv.h, stt_v[tb - 2])
                for cc in range(NCC):
                    for n2 in range(2):
                        ins = te.matmul(
                            pw[tb % 2][:, n2 * 512:(n2 + 1) * 512],
                            attnO[:, cc * T + tb * 128:cc * T + (tb + 1) * 128],
                            wout_b[:, cc * C + n2 * 512:cc * C + (n2 + 1) * 512],
                            start=(cc == 0), stop=(cc == 7))
                        if cc == 7 and n2 == 1:
                            sWo.inc(ins)

        @block.scalar
        def _(sc):
            for tb in range(NTB):
                o = (tb % 2) * 8
                sc.wait_ge(sEv.h, stt_v[tb])
                sAc.inc(sc.activation(psq[:], r1[:, (tb % 2) * C:(tb % 2) * C + C],
                                      AF.Square, accum_out=stat[:, o + 1:o + 2]))
                sc.wait_ge(sEv.h, var_v[tb])
                sAc.inc(sc.activation(stat[:, o + 6:o + 7], stat[:, o + 5:o + 6],
                                      AF.Sqrt, bias=s_eps[:, 0:1]))

        @block.vector
        def _(v):
            for tb in range(NTB):
                o = (tb % 2) * 8
                r1t = r1[:, (tb % 2) * C:(tb % 2) * C + C]
                v.wait_ge(sWo.h, wo_v[tb])
                v.wait_ge(dXb[tb % 2].h, ld_v[tb])
                if tb >= 2:
                    v.wait_ge(sAc.h, ac_sumsq[tb - 2])
                sEv.inc(v.scalar_tensor_tensor(
                    r1t, pw[tb % 2][:], 1.0, xbuf[:, (tb % 2) * C:(tb % 2) * C + C],
                    ALU.mult, ALU.add))
                v.drain()
                sEv.inc(v.reduce_sum(stat[:, o:o + 1], r1t, AX.X))
                v.drain()
                v.wait_ge(sAc.h, ac_sumsq[tb])
                sEv.inc(v.tensor_scalar_mul(stat[:, o + 2:o + 3], stat[:, o:o + 1], 1.0 / C))
                sEv.inc(v.tensor_scalar_mul(stat[:, o + 3:o + 4], stat[:, o + 1:o + 2], 1.0 / C))
                v.drain()
                sEv.inc(v.tensor_mul(stat[:, o + 4:o + 5], stat[:, o + 2:o + 3], stat[:, o + 2:o + 3]))
                v.drain()
                sEv.inc(v.tensor_sub(stat[:, o + 5:o + 6], stat[:, o + 3:o + 4], stat[:, o + 4:o + 5]))
                v.drain()
                v.wait_ge(sAc.h, ac_sqrt[tb])
                sEv.inc(v.reciprocal(stat[:, o + 7:o + 8], stat[:, o + 6:o + 7]))
                v.drain()
                if tb >= 2:
                    v.wait_ge(dSx[tb % 2].h, xnd_v[tb - 2])
                sEv.inc(v.tensor_scalar(xnb[:, (tb % 2) * C:(tb % 2) * C + C], r1t,
                                        stat[:, o + 2:o + 3], stat[:, o + 7:o + 8],
                                        ALU.subtract, ALU.mult))

        @block.gpsimd
        def _(g):
            for tb in range(NTB):
                g.wait_ge(sEv.h, xn_v[tb])
                dSx[tb % 2].dma(g.dma_start(
                    out=xn_hbm[s * T + tb * 128: s * T + (tb + 1) * 128, :],
                    in_=xnb[:, (tb % 2) * C:(tb % 2) * C + C]))

        assert sEv.n == xn_v[-1]
        psc.close()

    wps = {}

    def w_pass1(wi, wsrc, R, Ccols, prev_acc, prev_free):
        tiles = [(r, c) for r in range(R // 128) for c in range(Ccols // 1024)]
        NT = len(tiles)
        pm, pb = wps["pm"], wps["pb"]
        ev0 = sEv.n
        redall_v = ev0 + 1
        m1_v = ev0 + 2
        m2_v = ev0 + 3
        sc_v = ev0 + 4
        tx0 = sTx.n
        pm_v = tx0 + 1
        pb_v = tx0 + 2
        mk0 = sMk.n
        acc_v = [mk0 + i + 1 for i in range(NT)]
        ld1 = {}

        @block.scalar
        def _(sc):
            if prev_acc is not None:
                sc.wait_ge(sMk.h, prev_acc)
            if prev_free is not None:
                sc.wait_ge(sEv.h, prev_free)
            for i in range(2):
                r, c = tiles[i]
                dXb[i % 2].dma(sc.dma_start(
                    out=xbuf[:, (i % 2) * 1024:(i % 2) * 1024 + 1024],
                    in_=wsrc[r * 128:(r + 1) * 128, c * 1024:(c + 1) * 1024]))
                ld1[i] = dXb[i % 2].n
            for i in range(NT):
                sc.wait_ge(dXb[i % 2].h, ld1[i])
                sMk.inc(sc.activation(rcp_t.ap(),
                                      xbuf[:, (i % 2) * 1024:(i % 2) * 1024 + 1024],
                                      AF.Abs, accum_out=absacc[:, i:i + 1]))
                if i + 2 < NT:
                    sc.wait_ge(sMk.h, acc_v[i])
                    r2, c2 = tiles[i + 2]
                    dXb[i % 2].dma(sc.dma_start(
                        out=xbuf[:, (i % 2) * 1024:(i % 2) * 1024 + 1024],
                        in_=wsrc[r2 * 128:(r2 + 1) * 128, c2 * 1024:(c2 + 1) * 1024]))
                    ld1[i + 2] = dXb[i % 2].n

        @block.vector
        def _(v):
            v.wait_ge(sMk.h, acc_v[NT - 1])
            sEv.inc(v.reduce_sum(absacc[:, 32:33], absacc[:, 0:NT], AX.X))
            v.drain()
            v.wait_ge(sTx.h, pm_v)
            sEv.inc(v.tensor_scalar(w11[:, 0:1], pm[:], 1.0 / (R * Ccols), 1e-5,
                                    ALU.mult, ALU.max))
            v.drain()
            sEv.inc(v.reciprocal(w11[:, 1:2], w11[:, 0:1]))
            v.drain()
            v.wait_ge(sTx.h, pb_v)
            sEv.inc(v.tensor_copy(wsc[:, wi * 2:wi * 2 + 2], pb[:]))

        @block.tensor
        def _(te):
            te.wait_ge(sEv.h, redall_v)
            sTx.inc(te.matmul(pm[:], s_onesc[:], absacc[:, 32:33],
                              start=True, stop=True))
            te.wait_ge(sEv.h, m2_v)
            sTx.inc(te.matmul(pb[:], s_ones[0:1, :], w11[0:1, 0:2],
                              start=True, stop=True))

        return sc_v, redall_v, acc_v[NT - 1]

    def w_pass2(wi, wsrc, wdst_T, R, Ccols, sc_v, prev_q3):
        tiles = [(r, c) for r in range(R // 128) for c in range(Ccols // 1024)]
        NT = len(tiles)
        p1 = wps["p1"]
        ld2 = {}
        ev1 = sEv.n
        q3_v, evT_v = [], []
        n = ev1
        for i in range(NT):
            n += 3
            q3_v.append(n)
            n += 1
            evT_v.append(n)
        tx1 = sTx.n
        tx_v = [tx1 + i + 1 for i in range(NT)]
        st_v = {}
        st_base = [dSt2[0].n, dSt2[1].n]
        for i in range(NT):
            st_v[i] = st_base[i % 2] + 16 * (i // 2 + 1)

        @block.sync
        def _(sy):
            for i, (r, c) in enumerate(tiles):
                if i < 2 and prev_q3 is not None:
                    sy.wait_ge(sEv.h, prev_q3[NT - 2 + i])
                if i >= 2:
                    sy.wait_ge(sEv.h, q3_v[i - 2])
                dWt[i % 2].dma(sy.dma_start(
                    out=wst[:, (i % 2) * 1024:(i % 2) * 1024 + 1024],
                    in_=wsrc[r * 128:(r + 1) * 128, c * 1024:(c + 1) * 1024]))
                ld2[i] = dWt[i % 2].n

        @block.vector
        def _(v):
            v.wait_ge(sEv.h, sc_v)
            for i in range(NT):
                v.wait_ge(dWt[i % 2].h, ld2[i])
                src = wst[:, (i % 2) * 1024:(i % 2) * 1024 + 1024]
                sEv.inc(v.tensor_scalar(sqs[:], src, wsc[:, wi * 2 + 1:wi * 2 + 2],
                                        -1.0, ALU.mult, ALU.max))
                v.drain()
                sEv.inc(v.tensor_scalar(sqs[:], sqs[:], 1.0, MAGIC,
                                        ALU.min, ALU.add))
                v.drain()
                if i >= 2:
                    v.wait_ge(sTx.h, tx_v[i - 2])
                sEv.inc(v.tensor_scalar_add(
                    tern[:, (i % 2) * 1024:(i % 2) * 1024 + 1024], sqs[:], -MAGIC))
                v.wait_ge(sTx.h, tx_v[i])
                if i >= 2:
                    v.wait_ge(dSt2[i % 2].h, st_v[i - 2])
                sEv.inc(v.tensor_copy(
                    terT[:, (i % 2) * 1024:(i % 2) * 1024 + 1024], p1.ap()))

        @block.gpsimd
        def _(g):
            for i, (r, c) in enumerate(tiles):
                g.wait_ge(sEv.h, evT_v[i])
                dSt2[i % 2].dma(g.dma_start(
                    out=wdst_T.rearrange("(g p) f -> p g f", p=128)[:, c * 8:(c + 1) * 8, r * 128:(r + 1) * 128],
                    in_=terT.rearrange("p (b g f) -> p b g f", b=2, g=8)[:, i % 2, :, :]))

        @block.tensor
        def _(te):
            for i in range(NT):
                te.wait_ge(sEv.h, q3_v[i])
                if i >= 1:
                    te.wait_ge(sEv.h, evT_v[i - 1])
                for cc in range(8):
                    ins = te.transpose(
                        p1[:, cc * 128:(cc + 1) * 128],
                        tern[:, (i % 2) * 1024 + cc * 128:(i % 2) * 1024 + (cc + 1) * 128],
                        s_idb[:])
                    if cc == 7:
                        sTx.inc(ins)

        assert sEv.n == evT_v[NT - 1] and sTx.n == tx_v[-1]
        return q3_v

    def wphase():
        psc = ExitStack()
        wps["p1"] = psc.enter_context(nc.psum_tensor("pww", [128, 1024], BF16))
        wps["pm"] = psc.enter_context(nc.psum_tensor("pwm", [1, 1], F32))
        wps["pb"] = psc.enter_context(nc.psum_tensor("pwb", [128, 2], F32))
        prev_acc = prev_free = prev_q3 = None
        for wi, wsrc, wdst, R, Cc in ((0, wg, wgT_h, INNER, C),
                                      (1, wu, wuT_h, INNER, C),
                                      (2, wd, wdT_h, C, INNER)):
            sc_v, redall_v, lastacc = w_pass1(wi, wsrc, R, Cc, prev_acc, prev_free)
            prev_acc, prev_free = lastacc, redall_v
            prev_q3 = w_pass2(wi, wsrc, wdst, R, Cc, sc_v, prev_q3)
        psc.close()

    ph_list = [lambda: ph0()]
    for s in range(NSEQ):
        for f in (a1, a2, a3, a4, a5):
            ph_list.append(lambda f=f, s=s: f(s))
    ph_list.append(lambda: wphase())
    nph = 0
    for f in ph_list:
        if nph >= UPTO:
            break
        f()
        barrier()
        nph += 1
    esA.close()

    esF = ExitStack()

    def sbF(name, shape, dt):
        return esF.enter_context(nc.sbuf_tensor(name, shape, dt))

    wgT = sbF("wgT", [128, 8 * INNER], BF16)
    wuT = sbF("wuT", [128, 8 * INNER], BF16)
    xnf = sbF("xnf", [128, 2 * C], F32)
    hbf = sbF("hbf", [128, INNER], F32)
    thf = sbF("thf", [128, INNER], F32)
    fst = sbF("fst", [128, 32], F32)
    esF1 = ExitStack()
    hqb = esF1.enter_context(nc.sbuf_tensor("hqb", [128, INNER], BF16))
    xqb = esF1.enter_context(nc.sbuf_tensor("xqb", [128, 2 * C], BF16))
    xqT = esF1.enter_context(nc.sbuf_tensor("xqT", [128, 2 * C], BF16))
    sgf = esF1.enter_context(nc.sbuf_tensor("sgf", [128, 512], F32))
    hqi_yb = [None, None]

    def f1():
        psc = ExitStack()
        pT = psc.enter_context(nc.psum_tensor("pT1", [128, 1024], BF16))
        pG = [psc.enter_context(nc.psum_tensor("pG%d" % i, [128, 512], F32)) for i in range(3)]
        pU = [psc.enter_context(nc.psum_tensor("pU%d" % i, [128, 512], F32)) for i in range(3)]

        order = [("q", 0, k) for k in range(9)]
        order.append(("e", 0, 0))
        order += [("q", 1, k) for k in range(9)]
        for tb in range(NT2):
            order += [("h", tb, oc) for oc in range(8)]
            if tb + 1 < NT2:
                order.append(("e", tb + 1, 0))
            order += [("t", tb, k) for k in range(4)]
            if tb + 2 < NT2:
                order += [("q", tb + 2, k) for k in range(9)]
            order.append(("r", tb, 0))
        fd0 = sFd.n
        opv = {}
        for j, op in enumerate(order):
            opv[op] = fd0 + j + 1
        xq_v = [opv[("q", tb, 8)] for tb in range(NT2)]
        evT_v = [opv[("e", tb, 0)] for tb in range(NT2)]
        hml_v = [[opv[("h", tb, oc)] for oc in range(8)] for tb in range(NT2)]
        hs_v = [opv[("t", tb, 3)] for tb in range(NT2)]
        rnd_v = [opv[("r", tb, 0)] for tb in range(NT2)]

        fa0 = sFa.n
        silu_v = [[fa0 + tb * 9 + oc + 1 for oc in range(8)] for tb in range(NT2)]
        th_v = [fa0 + tb * 9 + 9 for tb in range(NT2)]
        rcG0 = sRc.n
        round_v = [rcG0 + tb + 1 for tb in range(NT2)]
        f0 = sF.n
        tx_v, g_v, u_v = [], [], []
        n = f0
        for tb in range(NT2):
            n += 1
            tx_v.append(n)
            gl, ul = [], []
            for oc in range(8):
                n += 1
                gl.append(n)
                n += 1
                ul.append(n)
            g_v.append(gl)
            u_v.append(ul)
        hd0 = dSh.n
        hqd_v = [hd0 + 16 * (tb + 1) for tb in range(NT2)]
        xn_ld = {}

        @block.sync
        def _(sy):
            for half in range(2):
                dW.dma(sy.dma_start(
                    out=wgT.ap().rearrange("p (g f) -> p g f", g=8)[:, half * 4:(half + 1) * 4, :],
                    in_=wgT_h.rearrange("(g p) f -> p g f", p=128)[:, half * 4:(half + 1) * 4, :]))
            for tb in range(NT2):
                if tb >= 2:
                    sy.wait_ge(sFd.h, xq_v[tb - 2])
                dXn[tb % 2].dma(sy.dma_start(out=xnf[:, (tb % 2) * C:(tb % 2) * C + C],
                                    in_=xn_hbm[tb * 128:(tb + 1) * 128, :]))
                xn_ld[tb] = dXn[tb % 2].n

        w_ld = dW.n

        @block.scalar
        def _(sc):
            for half in range(2):
                dHq[half].dma(sc.dma_start(
                    out=wuT.ap().rearrange("p (g f) -> p g f", g=8)[:, half * 4:(half + 1) * 4, :],
                    in_=wuT_h.rearrange("(g p) f -> p g f", p=128)[:, half * 4:(half + 1) * 4, :]))
        wu_ld = [dHq[0].n, dHq[1].n]

        def emit_quant(v, tb):
            o = (tb % 2) * 16
            xt = xnf[:, (tb % 2) * C:(tb % 2) * C + C]
            v.wait_ge(dXn[tb % 2].h, xn_ld[tb])
            if tb >= 2:
                v.wait_ge(sFa.h, th_v[tb - 2])
            sFd.inc(v.tensor_reduce(fst[:, o:o + 1], xt, AX.X, ALU.max,
                                    apply_absolute_value=True))
            v.drain()
            sFd.inc(v.tensor_scalar_max(fst[:, o:o + 1], fst[:, o:o + 1], 1e-5))
            v.drain()
            sFd.inc(v.reciprocal(fst[:, o + 1:o + 2], fst[:, o:o + 1]))
            v.drain()
            sFd.inc(v.tensor_scalar_mul(fst[:, o + 2:o + 3], fst[:, o + 1:o + 2], 127.0))
            sFd.inc(v.tensor_scalar_mul(fst[:, o + 3:o + 4], fst[:, o:o + 1], 1.0 / 127.0))
            v.drain()
            sFd.inc(v.tensor_mul(fst[:, o + 4:o + 5], fst[:, o + 3:o + 4], wsc[:, 0:1]))
            sFd.inc(v.tensor_mul(fst[:, o + 5:o + 6], fst[:, o + 3:o + 4], wsc[:, 2:3]))
            v.drain()
            sFd.inc(v.tensor_scalar(hbf[:, 0:C], xt, fst[:, o + 2:o + 3], MAGIC,
                                    ALU.mult, ALU.add))
            v.drain()
            if tb >= 1:
                v.wait_ge(sF.h, tx_v[tb - 1])
            sFd.inc(v.tensor_scalar_add(xqb[:, (tb % 2) * C:(tb % 2) * C + C],
                                        hbf[:, 0:C], -MAGIC))

        @block.vector
        def _(v):
            def emit_evac(tb):
                v.wait_ge(sF.h, tx_v[tb])
                if tb >= 2:
                    v.wait_ge(sF.h, u_v[tb - 2][7])
                sFd.inc(v.tensor_copy(xqT[:, (tb % 2) * C:(tb % 2) * C + C], pT.ap()))

            emit_quant(v, 0)
            emit_evac(0)
            emit_quant(v, 1)
            for tb in range(NT2):
                o = (tb % 2) * 16
                for oc in range(8):
                    v.wait_ge(sF.h, u_v[tb][oc])
                    v.wait_ge(sFa.h, silu_v[tb][oc])
                    sFd.inc(v.scalar_tensor_tensor(
                        hbf[:, oc * 512:(oc + 1) * 512], pU[oc % 3][:],
                        fst[:, o + 5:o + 6], sgf[:], ALU.mult, ALU.mult))
                if tb + 1 < NT2:
                    emit_evac(tb + 1)
                v.drain()
                sFd.inc(v.tensor_reduce(fst[:, o + 6:o + 7], hbf[:], AX.X, ALU.max,
                                        apply_absolute_value=True))
                v.drain()
                sFd.inc(v.reciprocal(fst[:, o + 7:o + 8], fst[:, o + 6:o + 7]))
                v.drain()
                sFd.inc(v.tensor_scalar_mul(fst[:, o + 8:o + 9], fst[:, o + 7:o + 8], 127.0))
                sFd.inc(v.tensor_scalar_mul(dsh_all[:, tb:tb + 1], fst[:, o + 6:o + 7], 1.0 / 127.0))
                v.drain()
                if tb + 2 < NT2:
                    emit_quant(v, tb + 2)
                v.wait_ge(sFa.h, th_v[tb])
                if tb >= 1:
                    v.wait_ge(dSh.h, hqd_v[tb - 1])
                sFd.inc(v.tensor_scalar_add(hqb[:], thf[:], -MAGIC))

        @block.tensor
        def _(te):
            te.wait_ge(dW.h, w_ld)
            te.wait_ge(dHq[0].h, wu_ld[0])
            te.wait_ge(dHq[1].h, wu_ld[1])
            for tb in range(NT2):
                te.wait_ge(sFd.h, xq_v[tb])
                if tb >= 1:
                    te.wait_ge(sFd.h, evT_v[tb - 1])
                for cc in range(8):
                    ins = te.transpose(pT[:, cc * 128:(cc + 1) * 128],
                                       xqb[:, (tb % 2) * C + cc * 128:(tb % 2) * C + (cc + 1) * 128],
                                       s_idb[:])
                    if cc == 7:
                        sF.inc(ins)
                for oc in range(8):
                    gc = tb * 8 + oc
                    te.wait_ge(sFd.h, evT_v[tb])
                    if gc >= 3:
                        tb2, oc2 = divmod(gc - 3, 8)
                        te.wait_ge(sFa.h, silu_v[tb2][oc2])
                    for cc in range(8):
                        ins = te.matmul(pG[oc % 3][:],
                                        xqT[:, (tb % 2) * C + cc * 128:(tb % 2) * C + (cc + 1) * 128],
                                        wgT[:, cc * INNER + oc * 512:cc * INNER + (oc + 1) * 512],
                                        start=(cc == 0), stop=(cc == 7))
                        if cc == 7:
                            sF.inc(ins)
                    if gc >= 3:
                        tb2, oc2 = divmod(gc - 3, 8)
                        te.wait_ge(sFd.h, hml_v[tb2][oc2])
                    for cc in range(8):
                        ins = te.matmul(pU[oc % 3][:],
                                        xqT[:, (tb % 2) * C + cc * 128:(tb % 2) * C + (cc + 1) * 128],
                                        wuT[:, cc * INNER + oc * 512:cc * INNER + (oc + 1) * 512],
                                        start=(cc == 0), stop=(cc == 7))
                        if cc == 7:
                            sF.inc(ins)

        @block.scalar
        def _(sc):
            for tb in range(NT2):
                o = (tb % 2) * 16
                for oc in range(8):
                    gc = tb * 8 + oc
                    sc.wait_ge(sF.h, g_v[tb][oc])
                    if gc >= 1:
                        tb2, oc2 = divmod(gc - 1, 8)
                        sc.wait_ge(sFd.h, hml_v[tb2][oc2])
                    sFa.inc(sc.activation(sgf[:], pG[oc % 3][:], AF.Silu,
                                          scale=fst[:, o + 4:o + 5]))
                sc.wait_ge(sFd.h, hs_v[tb])
                if tb >= 1:
                    sc.wait_ge(sFd.h, rnd_v[tb - 1])
                sFa.inc(sc.activation(thf[:], hbf[:], AF.Identity,
                                      scale=fst[:, o + 8:o + 9], bias=s_magic[:, 0:1]))

        @block.gpsimd
        def _(g):
            for tb in range(NT2):
                g.wait_ge(sFd.h, rnd_v[tb])
                dSh.dma(g.dma_start(out=hq_hbm[tb * 128:(tb + 1) * 128, :],
                                   in_=hqb[:]))

        assert sFd.n == opv[order[-1]], (sFd.n, opv[order[-1]])
        assert sF.n == u_v[-1][7]
        assert sFa.n == th_v[-1]
        psc.close()

    def f2():
        hqi_, yb_ = hqi_yb[0], hqi_yb[1]
        psc = ExitStack()
        pH = [psc.enter_context(nc.psum_tensor("pH%d" % i, [128, 1024], BF16)) for i in range(2)]
        pD = [psc.enter_context(nc.psum_tensor("pD%d" % i, [128, 1024], F32)) for i in range(2)]
        psq2 = psc.enter_context(nc.psum_tensor("psq2", [128, 1024], F32))
        wdT = wgT
        hqT = wuT.ap()[:, 0:2 * INNER]

        fd0 = sFd.n
        dshmd_v = fd0 + 1
        evh_v = [None] * NT2
        ln_v = [None] * NT2
        n = fd0 + 1
        evh_v[0] = [n + 1, n + 2, n + 3, n + 4]
        n += 4
        for tb in range(NT2):
            if tb + 1 < NT2:
                evh_v[tb + 1] = [n + 1, n + 2, n + 3, n + 4]
                n += 4
            n += 8
            ln_v[tb] = n
        fa0 = sFa.n
        sumsq_v = [fa0 + 2 * tb + 1 for tb in range(NT2)]
        sqrt_v = [fa0 + 2 * tb + 2 for tb in range(NT2)]
        f0 = sF.n
        txg_v, dmm_v = [], []
        n = f0
        for tb in range(NT2):
            tl = []
            for grp in range(4):
                n += 1
                tl.append(n)
            txg_v.append(tl)
            dl = []
            for n2 in range(2):
                n += 1
                dl.append(n)
            dmm_v.append(dl)
        dS0 = sOut.n
        y_v = [dS0 + 16 * (tb + 1) for tb in range(NT2)]
        hq_ld, xn_ld = {}, {}

        @block.vector
        def _(v):
            sFd.inc(v.tensor_scalar_mul(dsh_all[:, 32:64], dsh_all[:, 0:32],
                                        wsc[:, 4:5]))

        @block.sync
        def _(sy):
            for q in range(4):
                dW.dma(sy.dma_start(
                    out=wdT.ap().rearrange("p (g f) -> p g f", g=32)[:, q * 8:(q + 1) * 8, 0:C],
                    in_=wdT_h.rearrange("(g p) f -> p g f", p=128)[:, q * 8:(q + 1) * 8, :]))
            for tb in range(NT2):
                if tb >= 2:
                    sy.wait_ge(sF.h, dmm_v[tb - 2][1])
                dHq[tb % 2].dma(sy.dma_start(out=hqi_[:, (tb % 2) * INNER:(tb % 2) * INNER + INNER],
                                    in_=hq_hbm[tb * 128:(tb + 1) * 128, :]))
                hq_ld[tb] = dHq[tb % 2].n
                if tb >= 2:
                    sy.wait_ge(sFd.h, ln_v[tb - 2])
                dXn[tb % 2].dma(sy.dma_start(out=xnf[:, (tb % 2) * C:(tb % 2) * C + C],
                                    in_=xn_hbm[tb * 128:(tb + 1) * 128, :]))
                xn_ld[tb] = dXn[tb % 2].n

        wd_ld = dW.n

        @block.tensor
        def _(te):
            te.wait_ge(dW.h, wd_ld)
            te.wait_ge(sFd.h, dshmd_v)
            for tb in range(NT2):
                te.wait_ge(dHq[tb % 2].h, hq_ld[tb])
                for grp in range(4):
                    gi = tb * 4 + grp
                    if gi >= 2:
                        tbp, gp_ = divmod(gi - 2, 4)
                        te.wait_ge(sFd.h, evh_v[tbp][gp_])
                    for k in range(8):
                        ic = grp * 8 + k
                        ins = te.transpose(
                            pH[grp % 2][:, k * 128:(k + 1) * 128],
                            hqi_[:, (tb % 2) * INNER + ic * 128:(tb % 2) * INNER + (ic + 1) * 128],
                            s_idb[:])
                        if k == 7:
                            sF.inc(ins)
                for n2 in range(2):
                    if tb >= 2:
                        te.wait_ge(sFd.h, ln_v[tb - 2])
                    for ic in range(32):
                        if ic == 0 and n2 == 0:
                            te.wait_ge(sFd.h, evh_v[tb][3])
                        ins = te.matmul(
                            pD[tb % 2][:, n2 * 512:(n2 + 1) * 512],
                            hqT[:, (tb % 2) * INNER + ic * 128:(tb % 2) * INNER + (ic + 1) * 128],
                            wdT[:, ic * 1024 + n2 * 512:ic * 1024 + (n2 + 1) * 512],
                            start=(ic == 0), stop=(ic == 31))
                        if ic == 31:
                            sF.inc(ins)

        @block.vector
        def _(v):
            def emit_evh(tb):
                for grp in range(4):
                    v.wait_ge(sF.h, txg_v[tb][grp])
                    if tb >= 2:
                        v.wait_ge(sF.h, dmm_v[tb - 2][1])
                    sFd.inc(v.tensor_copy(
                        hqT.rearrange("p (b g f) -> p b g f", b=2, g=32)[:, tb % 2, grp * 8:(grp + 1) * 8, :],
                        pH[grp % 2].ap().rearrange("p (g f) -> p g f", g=8)))

            emit_evh(0)
            for tb in range(NT2):
                o = (tb % 2) * 16
                if tb + 1 < NT2:
                    emit_evh(tb + 1)
                v.wait_ge(sF.h, dmm_v[tb][1])
                v.wait_ge(dXn[tb % 2].h, xn_ld[tb])
                if tb >= 1:
                    v.wait_ge(sFa.h, sumsq_v[tb - 1])
                r2t = thf[:, 0:C]
                sFd.inc(v.scalar_tensor_tensor(
                    r2t, pD[tb % 2][:], dsh_all[:, 32 + tb:33 + tb],
                    xnf[:, (tb % 2) * C:(tb % 2) * C + C], ALU.mult, ALU.add))
                v.drain()
                sFd.inc(v.reduce_sum(fst[:, o:o + 1], r2t, AX.X))
                v.drain()
                v.wait_ge(sFa.h, sumsq_v[tb])
                sFd.inc(v.tensor_scalar_mul(fst[:, o + 2:o + 3], fst[:, o:o + 1], 1.0 / C))
                sFd.inc(v.tensor_scalar_mul(fst[:, o + 3:o + 4], fst[:, o + 1:o + 2], 1.0 / C))
                v.drain()
                sFd.inc(v.tensor_mul(fst[:, o + 4:o + 5], fst[:, o + 2:o + 3], fst[:, o + 2:o + 3]))
                v.drain()
                sFd.inc(v.tensor_sub(fst[:, o + 5:o + 6], fst[:, o + 3:o + 4], fst[:, o + 4:o + 5]))
                v.drain()
                v.wait_ge(sFa.h, sqrt_v[tb])
                sFd.inc(v.reciprocal(fst[:, o + 7:o + 8], fst[:, o + 6:o + 7]))
                v.drain()
                if tb >= 1:
                    v.wait_ge(sOut.h, y_v[tb - 1])
                sFd.inc(v.tensor_scalar(yb_[:], r2t,
                                        fst[:, o + 2:o + 3], fst[:, o + 7:o + 8],
                                        ALU.subtract, ALU.mult))

        @block.scalar
        def _(sc):
            for tb in range(NT2):
                o = (tb % 2) * 16
                sc.wait_ge(sFd.h, ln_v[tb] - 7)
                sFa.inc(sc.activation(psq2[:], thf[:, 0:C], AF.Square,
                                      accum_out=fst[:, o + 1:o + 2]))
                sc.wait_ge(sFd.h, ln_v[tb] - 2)
                sFa.inc(sc.activation(fst[:, o + 6:o + 7], fst[:, o + 5:o + 6],
                                      AF.Sqrt, bias=s_eps[:, 0:1]))

        @block.gpsimd
        def _(g):
            for tb in range(NT2):
                g.wait_ge(sFd.h, ln_v[tb])
                sOut.dma(g.dma_start(out=y[tb * 128:(tb + 1) * 128, :],
                                     in_=yb_[:]))

        assert sFd.n == ln_v[-1]
        assert sOut.n == y_v[-1]
        psc.close()

    if nph < UPTO:
        f1()
        barrier()
        nph += 1
    esF1.close()
    hqi_yb = [esF.enter_context(nc.sbuf_tensor("hqi", [128, 2 * INNER], BF16)),
              esF.enter_context(nc.sbuf_tensor("yb", [128, C], F32))]
    if nph < UPTO:
        f2()
        barrier()
        nph += 1

    @block.sync
    def _(sy):
        sy.wait_ge(sOut.h, sOut.n)

    esF.close()
    es.close()
    return nc


def make_consts():
    idf = np.eye(128, dtype=np.float32)
    idb = np.eye(128, dtype=bfnp)
    tri = np.where(np.arange(128)[:, None] > np.arange(128)[None, :],
                   np.float32(-30000.0), np.float32(0.0)).astype(bfnp)
    hmap = np.zeros((128, 128), dtype=bfnp)
    for cc in range(8):
        for p in range(128):
            hmap[p, cc * 16 + (2 * cc + p // 64)] = 1
    ones = np.ones((128, 128), dtype=np.float32)
    id16 = np.eye(16, dtype=np.float32)
    return dict(c_idf=idf, c_idb=idb, c_tri=tri, c_hmap=hmap, c_ones=ones,
                c_id16=id16)


def kernel(L0_attn_x, L0_attn_w_qkv, L0_attn_w_out, L0_ffn_w_gate, L0_ffn_w_up,
           L0_ffn_w_down):
    x = np.ascontiguousarray(np.asarray(L0_attn_x, dtype=np.float32))
    B = x.shape[0]
    ncore = 8
    per = B // ncore
    nc = build_program()
    base = dict(
        wqkv=np.ascontiguousarray(np.asarray(L0_attn_w_qkv, dtype=np.float32)),
        wout=np.ascontiguousarray(np.asarray(L0_attn_w_out, dtype=np.float32)),
        wg=np.ascontiguousarray(np.asarray(L0_ffn_w_gate, dtype=np.float32)),
        wu=np.ascontiguousarray(np.asarray(L0_ffn_w_up, dtype=np.float32)),
        wd=np.ascontiguousarray(np.asarray(L0_ffn_w_down, dtype=np.float32)),
        **make_consts())
    in_maps = []
    for c in range(ncore):
        m = dict(base)
        m["xin"] = np.ascontiguousarray(x[c * per:(c + 1) * per].reshape(per * T, C))
        in_maps.append(m)
    res = run_bass_kernel_spmd(nc, in_maps, core_ids=list(range(ncore)))
    outs = [r["y"].reshape(per, T, C) for r in res.results]
    return np.concatenate(outs, axis=0).astype(np.float32)
